# version 06
# baseline (speedup 1.0000x reference)
import numpy as np
import ml_dtypes
from contextlib import ExitStack

import concourse.bass as bass
import concourse.mybir as mybir
from concourse.bass_utils import run_bass_kernel_spmd

F32 = mybir.dt.float32
BF16 = mybir.dt.bfloat16
U32 = mybir.dt.uint32
I32 = mybir.dt.int32
AF = mybir.ActivationFunctionType
ALU = mybir.AluOpType
IOA = bass.IndirectOffsetOnAxis

B, L, D = 4, 4096, 1024
T = 2048
NCORE = 8
TG = 512
NTG = T // TG
NCH = D // 128
POLL_BOUND = 400000
P1_DEBUG = False
P2_DEBUG = False
FUSED_DEBUG = False
FUSED_STAGES = 9
bf16 = ml_dtypes.bfloat16

ENGINES = ("tensor", "vector", "scalar", "gpsimd", "sync")
DMA_QUEUES = ("sync", "gpsimd", "scalar")


class Op:
    __slots__ = ("engine", "emit", "reads", "writes", "amount", "deps", "idx",
                 "tag", "pool", "pool_pre")

    def __init__(self, engine, emit, reads, writes, amount, tag=""):
        self.engine = engine
        self.emit = emit
        self.reads = reads
        self.writes = writes
        self.amount = amount
        self.deps = []
        self.idx = 0
        self.tag = tag


class Sched:

    NPOOL = {"sync": 14, "gpsimd": 14, "scalar": 6}

    def __init__(self, ctx):
        self.ctx = ctx
        self.ops = []
        self.cum = {e: 0 for e in ENGINES}
        self.last_writer = {}
        self.readers = {}
        self.sems = {e: ctx.sem(f"E_{e}") for e in ENGINES}
        self.pool = {q: [ctx.sem(f"P_{q}{i}") for i in range(self.NPOOL[q])]
                     for q in DMA_QUEUES}
        self.pool_cum = {q: [0] * self.NPOOL[q] for q in DMA_QUEUES}
        self.pool_next = {q: 0 for q in DMA_QUEUES}
        self.fence = []

    def set_fence(self):
        self.fence = self.barrier_token()

    def add(self, engine, emit, reads=(), writes=(), dma=False, tag="",
            extra_deps=()):
        op = Op(engine, emit, tuple(reads), tuple(writes), 16 if dma else 1, tag)
        deps = {}

        def need(tok):
            if tok is None:
                return
            kind, key, c = tok
            if kind == "eng" and key == engine and not dma:
                return
            k = (kind, key)
            if deps.get(k, 0) < c:
                deps[k] = c
        for k in op.reads:
            need(self.last_writer.get(k))
        for k in op.writes:
            need(self.last_writer.get(k))
            for r in self.readers.get(k, ()):
                need(r)
        for d in extra_deps:
            need(d)
        for d in self.fence:
            need(d)
        op.deps = sorted(deps.items())
        if dma:
            q = engine
            i = self.pool_next[q]
            self.pool_next[q] = (i + 1) % self.NPOOL[q]
            op.pool = (q, i)
            op.pool_pre = self.pool_cum[q][i]
            self.pool_cum[q][i] += 16
            me = ("dma", (q, i), self.pool_cum[q][i])
        else:
            op.pool = None
            self.cum[engine] += 1
            op.idx = self.cum[engine]
            me = ("eng", engine, self.cum[engine])
        self.ops.append(op)
        for k in op.writes:
            self.last_writer[k] = me
            self.readers[k] = []
        for k in op.reads:
            self.readers.setdefault(k, []).append(me)
        return me

    def barrier_token(self):
        toks = [("eng", e, c) for e, c in self.cum.items() if c > 0]
        for q in DMA_QUEUES:
            for i, c in enumerate(self.pool_cum[q]):
                if c > 0:
                    toks.append(("dma", (q, i), c))
        return toks

    def _sem_of(self, kind, key):
        if kind == "eng":
            return self.sems[key].h
        q, i = key
        return self.pool[q][i].h

    def flush(self, blk):
        per_engine = {e: [] for e in ENGINES}
        for op in self.ops:
            per_engine[op.engine].append(op)
        for ename in ENGINES:
            ops = per_engine[ename]
            if not ops and ename not in DMA_QUEUES:
                continue

            def make(ops=ops, ename=ename):
                def body(eng):
                    waited = {}
                    for op in ops:
                        for (kind, key), c in op.deps:
                            if waited.get((kind, key), 0) < c:
                                eng.wait_ge(self._sem_of(kind, key), c)
                                waited[(kind, key)] = c
                        if op.pool is not None:
                            q, i = op.pool
                            if op.pool_pre > 0 and waited.get(("dma", (q, i)), 0) < op.pool_pre:
                                eng.wait_ge(self.pool[q][i].h, op.pool_pre)
                                waited[("dma", (q, i))] = op.pool_pre
                            inst = op.emit(eng)
                            assert inst is not None, f"dma op {op.tag} returned None"
                            inst.then_inc(self.pool[q][i].h, 16)
                        else:
                            inst = op.emit(eng)
                            if inst is None:
                                eng.sem_inc(self.sems[ename].h, 1)
                            else:
                                inst.then_inc(self.sems[ename].h, 1)
                    if ename in DMA_QUEUES:
                        for i, c in enumerate(self.pool_cum[ename]):
                            if c > 0 and waited.get(("dma", (ename, i)), 0) < c:
                                eng.wait_ge(self.pool[ename][i].h, c)
                return body
            getattr(blk, ename)(make())


class Ctx:

    def __init__(self, nc):
        self.nc = nc
        self.st = ExitStack()
        self._i = 0
        self.cursor = (nc.sbuf_base + 31) // 32 * 32
        self.top = nc.sbuf_top
        self.peak = self.cursor
        self.pbanks = [self.ps(f"pbank{i}", [128, 512], F32) for i in range(8)]

    def sem(self, name=None):
        if name is None:
            name = f"s{self._i}"
            self._i += 1
        return _SemH(self.st.enter_context(self.nc.semaphore(name)))

    def sb(self, name, shape, dt, at=None, region=None):
        nbytes = int(np.prod(shape[1:])) * mybir.dt.size(dt)
        nbytes = (nbytes + 31) // 32 * 32
        if at is None:
            off = self.cursor
            self.cursor += nbytes
            self.peak = max(self.peak, self.cursor)
        else:
            off = at
        assert off + nbytes <= self.top, (
            f"SBUF overflow allocating {name}: off={off} nbytes={nbytes} top={self.top}")
        self._i += 1
        t = self.nc.alloc_sbuf_tensor_at(f"{name}_{self._i}", shape, dt, offset=off)
        REGION[id(t)] = region if region is not None else id(t)
        OFFSET[id(t)] = off
        return t

    def overlay(self, name, shape, dt, over):
        return self.sb(name, shape, dt, at=OFFSET[id(over)], region=REGION[id(over)])

    def mark(self):
        return self.cursor

    def reset(self, m):
        self.cursor = m

    def ps(self, name, shape, dt=F32):
        return self.st.enter_context(self.nc.psum_tensor(name, shape, dt))


class _SemH:
    __slots__ = ("h",)

    def __init__(self, h):
        self.h = h


REGION = {}
OFFSET = {}


def K(t):
    return REGION.get(id(t), id(t))


def to_fm(x2d):
    return np.ascontiguousarray(np.asarray(x2d).T)


def rows_col(a, b=None, off=0):
    if b is None:
        a, b = 0, a
    return (np.arange(a, b, dtype=np.uint32) + off)[:, None]


class Exchange:

    def __init__(self, ctx, sch, n_flags):
        self.ctx = ctx
        self.sch = sch
        nc = ctx.nc
        self.n_flags = n_flags
        self.FLG = nc.dram_tensor("FLGshare", [max(2 * n_flags, 2), 1], I32,
                                  addr_space="Shared")
        self.s_magic = ctx.sb("xgs_magic", [2, max(n_flags, 1)], I32)
        self.s_myslot = ctx.sb("xgs_myslot", [2, max(n_flags, 1)], U32)
        self.s_peerslot = ctx.sb("xgs_peerslot", [2, max(n_flags, 1)], U32)
        self.s_pf = ctx.sb("xg_pf", [2, 1], I32)
        self.iters = ctx.sb("xg_iters", [1, max(n_flags, 1)], I32)
        self.loaded = None

    def emit_load(self, io):
        sch = self.sch
        sch.add("sync", lambda q: q.dma_start(out=self.s_magic[:, :], in_=io["xg_magic"]),
                writes=(K(self.s_magic),), dma=True, tag="xgld0")
        sch.add("sync", lambda q: q.dma_start(out=self.s_myslot[:, :], in_=io["xg_myslot"]),
                writes=(K(self.s_myslot),), dma=True, tag="xgld1")
        self.loaded = sch.add(
            "sync", lambda q: q.dma_start(out=self.s_peerslot[:, :], in_=io["xg_peerslot"]),
            writes=(K(self.s_peerslot),), dma=True, tag="xgld2")

    def push(self, k, data_deps):
        sch = self.sch
        ctx = self.ctx
        dma = ctx.sem()

        def em(g):
            g.indirect_dma_start(
                out=self.FLG.ap(),
                out_offset=IOA(ap=self.s_myslot[:, k:k + 1], axis=0),
                in_=self.s_magic[:, k:k + 1],
                in_offset=None,
            ).then_inc(dma.h, 16)
            g.wait_ge(dma.h, 16)
            return None
        return sch.add("gpsimd", em, reads=(K(self.s_magic), K(self.s_myslot)),
                       writes=(), tag=f"push{k}",
                       extra_deps=list(data_deps) + [self.loaded])

    def poll(self, k):
        sch = self.sch
        ctx = self.ctx
        poll = ctx.sem()

        def em(g):
            with (g.register(f"xit{k}") as rit, g.register(f"xc{k}") as rc,
                  g.register(f"xw{k}") as rw, g.register(f"xf{k}") as rf,
                  g.register(f"xm{k}") as rm):
                g.reg_mov(rit, 0)
                g.reg_mov(rw, 0)
                g.reg_load(rm, self.s_magic[0:1, k:k + 1])
                g.reg_mov(rc, 1)
                with g.While(rc):
                    g.indirect_dma_start(
                        out=self.s_pf[:, :], out_offset=None,
                        in_=self.FLG.ap(),
                        in_offset=IOA(ap=self.s_peerslot[:, k:k + 1], axis=0),
                    ).then_inc(poll.h, 16)
                    g.reg_add(rw, rw, 16)
                    g.wait_ge(poll.h, rw)
                    g.reg_load(rf, self.s_pf[0:1, 0:1])
                    g.reg_add(rit, rit, 1)
                    with g.register(f"xne{k}") as rne, g.register(f"xlt{k}") as rlt:
                        g.reg_alu(rne, rf, rm, ALU.not_equal)
                        g.reg_alu(rlt, rit, POLL_BOUND, ALU.is_lt)
                        g.reg_alu(rc, rne, rlt, ALU.bitwise_and)
                g.reg_save(self.iters[0:1, k:k + 1], rit)
            return None
        return sch.add("gpsimd", em,
                       reads=(K(self.s_magic), K(self.s_peerslot)),
                       writes=(K(self.s_pf),), tag=f"poll{k}",
                       extra_deps=[self.loaded])

    def host_inputs(self, h, rng):
        n = max(self.n_flags, 1)
        magics = rng.integers(1, 2 ** 30, size=(n,), dtype=np.int64).astype(np.int32)
        magic = np.stack([magics, magics], axis=0)
        ks = np.arange(n, dtype=np.uint32)
        my = np.stack([2 * ks + h, 2 * ks + h], axis=0).astype(np.uint32)
        peer = np.stack([2 * ks + (1 - h), 2 * ks + (1 - h)], axis=0).astype(np.uint32)
        return {"xg_magic": magic, "xg_myslot": my, "xg_peerslot": peer}

    @staticmethod
    def input_decls(nc, n_flags):
        n = max(n_flags, 1)
        return {
            "xg_magic": nc.dram_tensor("xg_magic", [2, n], I32, kind="ExternalInput").ap(),
            "xg_myslot": nc.dram_tensor("xg_myslot", [2, n], U32, kind="ExternalInput").ap(),
            "xg_peerslot": nc.dram_tensor("xg_peerslot", [2, n], U32, kind="ExternalInput").ap(),
        }


def emit_p1(ctx, sch, xg, io, xgk0=0):
    nc = ctx.nc
    m_p1 = ctx.mark()

    xb = [ctx.sb(f"p1xb{c}", [128, T], BF16) for c in range(NCH)]
    wq = [ctx.sb(f"p1wq{c}", [128, D], BF16) for c in range(NCH)]
    wk = [ctx.sb(f"p1wk{c}", [128, D], BF16) for c in range(NCH)]
    wv = [ctx.sb(f"p1wv{c}", [128, D], BF16) for c in range(NCH)]
    pj = [ctx.sb(f"p1pj{c}", [128, 256], BF16) for c in range(NCH)]
    qf = [ctx.sb(f"p1qf{c}", [128, TG], BF16) for c in range(NCH)]
    kf = [ctx.sb(f"p1kf{c}", [128, TG], BF16) for c in range(NCH)]
    qprime = [ctx.sb(f"p1qpr{p}", [128, T], BF16) for p in range(2)]
    kprime = [ctx.sb(f"p1kpr{t}", [128, 256], BF16) for t in range(4)]
    vt = [ctx.sb(f"p1vt{t}", [128, D], BF16) for t in range(4)]
    ones_b = ctx.sb("p1ones_b", [128, 128], BF16)
    ones_f = ctx.sb("p1ones_f", [128, 128], F32)
    kvs = [ctx.sb(f"p1kvs{c}", [128, D], F32) for c in range(2)]
    kvb = [ctx.sb(f"p1kvb{c}", [128, D], BF16) for c in range(2)]
    kvp = ctx.sb("p1kvp", [128, D], F32)
    zrow = ctx.sb("p1zrow", [1, T], F32)
    zrec = ctx.sb("p1zrec", [1, T], F32)
    scr = [ctx.sb(f"p1scr{i}", [128, TG], F32) for i in range(6)]
    srow = [ctx.sb(f"p1srow{i}", [1, TG], F32) for i in range(6)]
    yg = [ctx.sb(f"p1yg{c}", [128, TG], F32) for c in range(NCH)]
    xs = [ctx.sb(f"p1xs{i}", [128, TG], F32) for i in range(2)]
    s1 = [ctx.sb(f"p1s1{c}", [128, TG], F32) for c in range(2)]
    kvrows_my = ctx.sb("p1kvrows_my", [128, 2], U32)
    kvrows_peer = ctx.sb("p1kvrows_peer", [128, 2], U32)

    KVSH = nc.dram_tensor("KVshare", [512, D], F32, addr_space="Shared")

    pb = ctx.pbanks

    for c in range(NCH):
        sch.add("gpsimd", lambda g, c=c: g.dma_start(
            out=xb[c][:, :], in_=io["x"][128 * c:128 * (c + 1), :]),
            writes=(K(xb[c]),), dma=True, tag=f"ldxb{c}")
        sch.add("gpsimd", lambda g, c=c: g.dma_start(
            out=wq[c][:, :], in_=io["WqT"][128 * c:128 * (c + 1), :]),
            writes=(K(wq[c]),), dma=True)
        sch.add("gpsimd", lambda g, c=c: g.dma_start(
            out=wk[c][:, :], in_=io["WkT"][128 * c:128 * (c + 1), :]),
            writes=(K(wk[c]),), dma=True)
        sch.add("gpsimd", lambda g, c=c: g.dma_start(
            out=wv[c][:, :], in_=io["WvT"][128 * c:128 * (c + 1), :]),
            writes=(K(wv[c]),), dma=True)
        sch.add("gpsimd", lambda g, c=c: g.dma_start(
            out=pj[c][:, :], in_=io["proj"][128 * c:128 * (c + 1), :]),
            writes=(K(pj[c]),), dma=True)
    sch.add("sync", lambda q: q.dma_start(out=kvrows_my[:, :], in_=io["kvrows_my"]),
            writes=(K(kvrows_my),), dma=True)
    sch.add("sync", lambda q: q.dma_start(out=kvrows_peer[:, :], in_=io["kvrows_peer"]),
            writes=(K(kvrows_peer),), dma=True)
    sch.add("gpsimd", lambda g: g.memset(ones_b[:, :], 1.0), writes=(K(ones_b),))
    if P1_DEBUG:
        def dbg_store(name, buf, region=None):
            ap = buf[:, :] if region is None else region
            sch.add("sync", lambda q, ap=ap, name=name: q.dma_start(
                out=io[name], in_=ap), reads=(K(buf),), writes=(name,), dma=True)
        io["_dbg_store"] = dbg_store
    sch.add("gpsimd", lambda g: g.memset(ones_f[:, :], 1.0), writes=(K(ones_f),))

    def MMG(bank, pairs, tag=""):
        def em(t):
            n = len(pairs)
            last = None
            for i, (lf, rf) in enumerate(pairs):
                last = t.matmul(bank[:, :] if callable(bank) else bank,
                                lf(t), rf(t), start=(i == 0), stop=(i == n - 1))
            return last
        return em

    for gi in range(NTG):
        lo = gi * TG
        for e in range(NCH):
            bq = pb[e % 2]
            sch.add("tensor", MMG(bq[:, :], [
                (lambda t, d=d, e=e: wq[d][:, 128 * e:128 * (e + 1)],
                 lambda t, d=d, lo=lo: xb[d][:, lo:lo + TG]) for d in range(NCH)]),
                reads=tuple(K(wq[d]) for d in range(NCH)) + tuple(K(xb[d]) for d in range(NCH)),
                writes=(K(bq),), tag=f"qmm{gi}_{e}")
            sch.add("scalar", lambda s, e=e, bq=bq: s.copy(qf[e][:, :], bq[:, :]),
                    reads=(K(bq),), writes=(K(qf[e]),), tag=f"qev{gi}_{e}")
            bk = pb[2 + e % 2]
            sch.add("tensor", MMG(bk[:, :], [
                (lambda t, d=d, e=e: wk[d][:, 128 * e:128 * (e + 1)],
                 lambda t, d=d, lo=lo: xb[d][:, lo:lo + TG]) for d in range(NCH)]),
                reads=tuple(K(wk[d]) for d in range(NCH)) + tuple(K(xb[d]) for d in range(NCH)),
                writes=(K(bk),), tag=f"kmm{gi}_{e}")
            sch.add("vector", lambda v, e=e, bk=bk: v.tensor_copy(kf[e][:, :], bk[:, :]),
                    reads=(K(bk),), writes=(K(kf[e]),), tag=f"kev{gi}_{e}")
        for p in range(2):
            bqp = pb[p]
            sch.add("tensor", MMG(bqp[:, :], [
                (lambda t, e=e, p=p: pj[e][:, 128 * p:128 * (p + 1)],
                 lambda t, e=e: qf[e][:, :]) for e in range(NCH)]),
                reads=tuple(K(pj[e]) for e in range(NCH)) + tuple(K(qf[e]) for e in range(NCH)),
                writes=(K(bqp),), tag=f"qpmm{gi}_{p}")
            sc0, sc1 = scr[2 * p], scr[2 * p + 1]
            sch.add("vector", lambda v, bqp=bqp, sc0=sc0: v.tensor_scalar_min(
                sc0[:, :], bqp[:, :], 0.0), reads=(K(bqp),), writes=(K(sc0),))
            sch.add("scalar", lambda s, sc0=sc0, sc1=sc1: s.activation(
                out=sc1[:, :], in_=sc0[:, :], func=AF.Exp),
                reads=(K(sc0),), writes=(K(sc1),))
            sch.add("vector", lambda v, bqp=bqp, sc1=sc1, p=p, lo=lo: v.scalar_tensor_tensor(
                out=qprime[p][:, lo:lo + TG], in0=bqp[:, :], scalar=0.0,
                in1=sc1[:, :], op0=ALU.max, op1=ALU.add),
                reads=(K(bqp), K(sc1)), writes=(K(qprime[p]),), tag=f"qpr{gi}_{p}")
        bz = pb[2]
        sch.add("tensor", MMG(bz[0:1, :], [
            (lambda t, p=p: ones_b[:, 0:1],
             lambda t, p=p, lo=lo: qprime[p][:, lo:lo + TG]) for p in range(2)]),
            reads=(K(ones_b), K(qprime[0]), K(qprime[1])),
            writes=(K(bz),), tag=f"zmm{gi}")
        sch.add("vector", lambda v, bz=bz, lo=lo: v.tensor_copy(zrow[0:1, lo:lo + TG], bz[0:1, :]),
                reads=(K(bz),), writes=(K(zrow),))
        for lt in range(4):
            ls = lo + 128 * lt
            bkp = pb[3]
            sch.add("tensor", MMG(bkp[:, 0:256], [
                (lambda t, e=e, lt=lt: kf[e][:, 128 * lt:128 * (lt + 1)],
                 lambda t, e=e: pj[e][:, :]) for e in range(NCH)]),
                reads=tuple(K(kf[e]) for e in range(NCH)) + tuple(K(pj[e]) for e in range(NCH)),
                writes=(K(bkp),), tag=f"kpmm{gi}_{lt}")
            sc0, sc1 = scr[4], scr[5]
            sch.add("vector", lambda v, bkp=bkp, sc0=sc0: v.tensor_scalar_min(
                sc0[:, 0:256], bkp[:, 0:256], 0.0), reads=(K(bkp),), writes=(K(sc0),))
            sch.add("scalar", lambda s, sc0=sc0, sc1=sc1: s.activation(
                out=sc1[:, 0:256], in_=sc0[:, 0:256], func=AF.Exp),
                reads=(K(sc0),), writes=(K(sc1),))
            sch.add("vector", lambda v, bkp=bkp, sc1=sc1, lt=lt: v.scalar_tensor_tensor(
                out=kprime[lt][:, :], in0=bkp[:, 0:256], scalar=0.0,
                in1=sc1[:, 0:256], op0=ALU.max, op1=ALU.add),
                reads=(K(bkp), K(sc1)), writes=(K(kprime[lt]),), tag=f"kpr{gi}_{lt}")
            for vh in range(2):
                bv = pb[3]
                sch.add("tensor", MMG(bv[:, :], [
                    (lambda t, d=d, ls=ls: xb[d][:, ls:ls + 128],
                     lambda t, d=d, vh=vh: wv[d][:, TG * vh:TG * (vh + 1)]) for d in range(NCH)]),
                    reads=tuple(K(xb[d]) for d in range(NCH)) + tuple(K(wv[d]) for d in range(NCH)),
                    writes=(K(bv),), tag=f"vmm{gi}_{lt}_{vh}")
                sch.add("vector", lambda v, bv=bv, lt=lt, vh=vh: v.tensor_copy(
                    vt[lt][:, TG * vh:TG * (vh + 1)], bv[:, :]),
                    reads=(K(bv),), writes=(K(vt[lt]),))
            first = (gi == 0 and lt == 0)
            last = (gi == NTG - 1 and lt == 3)
            for p in range(2):
                for vh in range(2):
                    bank = pb[4 + 2 * p + vh]
                    def em(t, p=p, vh=vh, lt=lt, first=first, last=last, bank=bank):
                        return t.matmul(bank[:, :], kprime[lt][:, 128 * p:128 * (p + 1)],
                                        vt[lt][:, TG * vh:TG * (vh + 1)],
                                        start=first, stop=last)
                    sch.add("tensor", em, reads=(K(kprime[lt]), K(vt[lt])),
                            writes=(K(bank),), tag=f"kvmm{gi}_{lt}_{p}{vh}")

    kv_deps = []
    for p in range(2):
        for vh in range(2):
            bank = pb[4 + 2 * p + vh]
            sch.add("vector", lambda v, p=p, vh=vh, bank=bank: v.tensor_copy(
                kvs[p][:, TG * vh:TG * (vh + 1)], bank[:, :]),
                reads=(K(bank),), writes=(K(kvs[p]),), tag=f"kvev{p}{vh}")
    for p in range(2):
        tok = sch.add("gpsimd", lambda g, p=p: g.indirect_dma_start(
            out=KVSH.ap(), out_offset=IOA(ap=kvrows_my[:, p:p + 1], axis=0),
            in_=kvs[p][:, :], in_offset=None),
            reads=(K(kvs[p]), K(kvrows_my)), writes=("KVSH",), dma=True,
            tag=f"kvpush{p}")
        kv_deps.append(tok)
    xg.push(xgk0 + 0, kv_deps)
    ptok = xg.poll(xgk0 + 0)
    for p in range(2):
        sch.add("gpsimd", lambda g, p=p: g.indirect_dma_start(
            out=kvp[:, :], out_offset=None,
            in_=KVSH.ap(),
            in_offset=IOA(ap=kvrows_peer[:, p:p + 1], axis=0)),
            reads=("KVSH", K(kvrows_peer)), writes=(K(kvp),), dma=True,
            extra_deps=[ptok], tag=f"kvpull{p}")
        sch.add("vector", lambda v, p=p: v.tensor_add(
            kvs[p][:, :], kvs[p][:, :], kvp[:, :]),
            reads=(K(kvs[p]), K(kvp)), writes=(K(kvs[p]),))
        sch.add("scalar", lambda s, p=p: s.copy(kvb[p][:, :], kvs[p][:, :]),
                reads=(K(kvs[p]),), writes=(K(kvb[p]),))
        if P1_DEBUG:
            io["_dbg_store"](f"dbg_kv{p}", kvs[p])

    if P1_DEBUG:
        io["_dbg_store"]("dbg_z", zrow, zrow[0:1, :])
        for p in range(2):
            io["_dbg_store"](f"dbg_qpr{p}", qprime[p])
    sch.add("vector", lambda v: v.tensor_scalar_add(zrow[0:1, :], zrow[0:1, :], 1e-8),
            reads=(K(zrow),), writes=(K(zrow),))
    sch.add("vector", lambda v: v.reciprocal(zrec[0:1, :], zrow[0:1, :]),
            reads=(K(zrow),), writes=(K(zrec),))
    for gi in range(NTG):
        lo = gi * TG
        bsum = pb[6]
        bsq = pb[7]
        zib = scr[5]
        bbc = pb[2]
        sch.add("tensor", lambda t, lo=lo, bbc=bbc: t.matmul(
            bbc[:, :], ones_f[0:1, :], zrec[0:1, lo:lo + TG], start=True, stop=True),
            reads=(K(ones_f), K(zrec)), writes=(K(bbc),), tag=f"zbc{gi}")
        sch.add("scalar", lambda s, bbc=bbc, zib=zib: s.copy(zib[:, :], bbc[:, :]),
                reads=(K(bbc),), writes=(K(zib),))
        for e in range(NCH):
            bn = pb[e % 2]
            sch.add("tensor", MMG(bn[:, :], [
                (lambda t, p=p, e=e: kvb[p][:, 128 * e:128 * (e + 1)],
                 lambda t, p=p, lo=lo: qprime[p][:, lo:lo + TG]) for p in range(2)]),
                reads=(K(kvb[0]), K(kvb[1]), K(qprime[0]), K(qprime[1])),
                writes=(K(bn),), tag=f"num{gi}_{e}")
            xse = xs[e % 2]
            sch.add("sync", lambda q, e=e, lo=lo, xse=xse: q.dma_start(
                out=xse[:, :], in_=io["x"][128 * e:128 * (e + 1), lo:lo + TG]),
                reads=(), writes=(K(xse),), dma=True, tag=f"xld{gi}_{e}")
            sc = scr[e % 2]
            sch.add("vector", lambda v, bn=bn, sc=sc, zib=zib: v.tensor_mul(
                sc[:, :], bn[:, :], zib[:, :]),
                reads=(K(bn), K(zib)), writes=(K(sc),))
            sch.add("gpsimd", lambda g, sc=sc, e=e, xse=xse: g.tensor_add(
                yg[e][:, :], sc[:, :], xse[:, :]),
                reads=(K(sc), K(xse)), writes=(K(yg[e]),), tag=f"y{gi}_{e}")
            if P1_DEBUG and gi == 0 and e == 0:
                io["_dbg_store"]("dbg_y00", yg[0])
            def em_sum(t, e=e, bsum=bsum):
                return t.matmul(bsum[0:1, :], ones_f[:, 0:1], yg[e][:, :],
                                start=(e == 0), stop=(e == NCH - 1))
            sch.add("tensor", em_sum, reads=(K(ones_f), K(yg[e])), writes=(K(bsum),),
                    tag=f"sum{gi}_{e}")
            scq = scr[2 + e % 2]
            sch.add("scalar", lambda s, e=e, scq=scq: s.activation(
                out=scq[:, :], in_=yg[e][:, :], func=AF.Square),
                reads=(K(yg[e]),), writes=(K(scq),))
            def em_sq(t, e=e, scq=scq, bsq=bsq):
                return t.matmul(bsq[0:1, :], ones_f[:, 0:1], scq[:, :],
                                start=(e == 0), stop=(e == NCH - 1))
            sch.add("tensor", em_sq, reads=(K(ones_f), K(scq)), writes=(K(bsq),),
                    tag=f"sq{gi}_{e}")
        sch.add("scalar", lambda s, bsum=bsum: s.activation(
            out=srow[0][0:1, :], in_=bsum[0:1, :], func=AF.Copy,
            scale=1.0 / D), reads=(K(bsum),), writes=(K(srow[0]),))
        sch.add("scalar", lambda s, bsq=bsq: s.activation(
            out=srow[1][0:1, :], in_=bsq[0:1, :], func=AF.Copy,
            scale=1.0 / D), reads=(K(bsq),), writes=(K(srow[1]),))
        sch.add("scalar", lambda s: s.activation(
            out=srow[2][0:1, :], in_=srow[0][0:1, :], func=AF.Square),
            reads=(K(srow[0]),), writes=(K(srow[2]),))
        sch.add("vector", lambda v: v.tensor_sub(
            srow[3][0:1, :], srow[1][0:1, :], srow[2][0:1, :]),
            reads=(K(srow[1]), K(srow[2])), writes=(K(srow[3]),))
        sch.add("vector", lambda v: v.tensor_scalar_add(
            srow[3][0:1, :], srow[3][0:1, :], 1e-5),
            reads=(K(srow[3]),), writes=(K(srow[3]),))
        sch.add("scalar", lambda s: s.activation(
            out=srow[4][0:1, :], in_=srow[3][0:1, :], func=AF.Ln),
            reads=(K(srow[3]),), writes=(K(srow[4]),))
        sch.add("scalar", lambda s: s.activation(
            out=srow[5][0:1, :], in_=srow[4][0:1, :], func=AF.Exp, scale=-0.5),
            reads=(K(srow[4]),), writes=(K(srow[5]),), tag=f"rstd{gi}")
        if P1_DEBUG and gi == 0:
            io["_dbg_store"]("dbg_mu0", srow[0], srow[0][0:1, :])
            io["_dbg_store"]("dbg_rstd0", srow[5], srow[5][0:1, :])
        sch.add("vector", lambda v: v.tensor_mul(
            srow[2][0:1, :], srow[0][0:1, :], srow[5][0:1, :]),
            reads=(K(srow[0]), K(srow[5])), writes=(K(srow[2]),), tag=f"mr{gi}")
        br = pb[4]
        bm = pb[5]
        sch.add("tensor", lambda t, br=br: t.matmul(
            br[:, :], ones_f[0:1, :], srow[5][0:1, :], start=True, stop=True),
            reads=(K(ones_f), K(srow[5])), writes=(K(br),))
        sch.add("tensor", lambda t, bm=bm: t.matmul(
            bm[:, :], ones_f[0:1, :], srow[2][0:1, :], start=True, stop=True),
            reads=(K(ones_f), K(srow[2])), writes=(K(bm),))
        for e in range(NCH):
            so = s1[e % 2]
            sch.add("vector", lambda v, e=e, br=br, so=so: v.tensor_mul(
                so[:, :], yg[e][:, :], br[:, :]),
                reads=(K(yg[e]), K(br)), writes=(K(so),))
            sch.add("vector", lambda v, bm=bm, so=so: v.tensor_sub(
                so[:, :], so[:, :], bm[:, :]),
                reads=(K(so), K(bm)), writes=(K(so),))
            sch.add("sync", lambda q, e=e, lo=lo, so=so: q.dma_start(
                out=io["out"][128 * e:128 * (e + 1), lo:lo + TG], in_=so[:, :]),
                reads=(K(so),), writes=(f"out{e}_{gi}",), dma=True,
                tag=f"st{gi}_{e}")
    ctx.reset(m_p1)
    sch.set_fence()


def p1_host_inputs(core, inputs, rng):
    b, h = core // 2, core % 2
    x = np.asarray(inputs["L0_attn_x"][b, 2048 * h:2048 * (h + 1), :], np.float32)
    d = {
        "x": to_fm(x),
        "WqT": np.ascontiguousarray(np.asarray(inputs["L0_attn_Wq"], np.float32).T),
        "WkT": np.ascontiguousarray(np.asarray(inputs["L0_attn_Wk"], np.float32).T),
        "WvT": np.ascontiguousarray(np.asarray(inputs["L0_attn_Wv"], np.float32).T),
        "proj": np.asarray(inputs["L0_attn_proj"], np.float32),
        "kvrows_my": np.concatenate([rows_col(128, off=256 * h),
                                     rows_col(128, off=256 * h + 128)], axis=1).astype(np.uint32),
        "kvrows_peer": np.concatenate([rows_col(128, off=256 * (1 - h)),
                                       rows_col(128, off=256 * (1 - h) + 128)], axis=1).astype(np.uint32),
    }
    return d


def p1_decls(nc):
    return {
        "x": nc.dram_tensor("x", [D, T], F32, kind="ExternalInput").ap(),
        "WqT": nc.dram_tensor("WqT", [D, D], F32, kind="ExternalInput").ap(),
        "WkT": nc.dram_tensor("WkT", [D, D], F32, kind="ExternalInput").ap(),
        "WvT": nc.dram_tensor("WvT", [D, D], F32, kind="ExternalInput").ap(),
        "proj": nc.dram_tensor("proj", [D, 256], F32, kind="ExternalInput").ap(),
        "kvrows_my": nc.dram_tensor("kvrows_my", [128, 2], U32, kind="ExternalInput").ap(),
        "kvrows_peer": nc.dram_tensor("kvrows_peer", [128, 2], U32, kind="ExternalInput").ap(),
        "out": nc.dram_tensor("out_s1", [D, T], F32, kind="ExternalOutput").ap(),
    } | ({
        "dbg_kv0": nc.dram_tensor("dbg_kv0", [128, D], F32, kind="ExternalOutput").ap(),
        "dbg_kv1": nc.dram_tensor("dbg_kv1", [128, D], F32, kind="ExternalOutput").ap(),
        "dbg_z": nc.dram_tensor("dbg_z", [1, T], F32, kind="ExternalOutput").ap(),
        "dbg_qpr0": nc.dram_tensor("dbg_qpr0", [128, T], BF16, kind="ExternalOutput").ap(),
        "dbg_qpr1": nc.dram_tensor("dbg_qpr1", [128, T], BF16, kind="ExternalOutput").ap(),
        "dbg_y00": nc.dram_tensor("dbg_y00", [128, TG], F32, kind="ExternalOutput").ap(),
        "dbg_mu0": nc.dram_tensor("dbg_mu0", [1, TG], F32, kind="ExternalOutput").ap(),
        "dbg_rstd0": nc.dram_tensor("dbg_rstd0", [1, TG], F32, kind="ExternalOutput").ap(),
    } if P1_DEBUG else {})


def build_p1():
    nc = bass.Bass("TRN2", target_bir_lowering=False)
    ctx = Ctx(nc)
    io = p1_decls(nc)
    io.update(Exchange.input_decls(nc, 1))
    with nc.Block() as blk:
        sch = Sched(ctx)
        xg = Exchange(ctx, sch, 1)
        xg.emit_load(io)
        emit_p1(ctx, sch, xg, io)
        sch.flush(blk)
    ctx.st.close()
    return nc


def run_p1(inputs):
    rng = np.random.default_rng()
    nc = build_p1()
    in_maps = [p1_host_inputs(c, inputs, rng) for c in range(NCORE)]
    seed = int(np.random.default_rng().integers(2 ** 31))
    for c in range(NCORE):
        pair_rng = np.random.default_rng(seed + (c // 2))
        xgi = Exchange.__new__(Exchange)
        xgi.n_flags = 1
        in_maps[c].update(Exchange.host_inputs(xgi, c % 2, pair_rng))
    res = run_bass_kernel_spmd(nc, in_maps, core_ids=list(range(NCORE)))
    return res.results



H0 = 8
HD0 = 64
NW = 4
NJ = 22


def l0ffn_const_decls(nc):
    d = {}
    d["cst_tcos"] = nc.dram_tensor("cst_tcos", [128, T], F32, kind="ExternalInput").ap()
    d["cst_tsin"] = nc.dram_tensor("cst_tsin", [128, T], F32, kind="ExternalInput").ap()
    d["cst_tsq"] = nc.dram_tensor("cst_tsq", [128, T], F32, kind="ExternalInput").ap()
    d["cst_tski"] = nc.dram_tensor("cst_tski", [128, T], F32, kind="ExternalInput").ap()
    d["cst_masks"] = nc.dram_tensor("cst_masks", [128, 8 * TG], F32, kind="ExternalInput").ap()
    d["cst_perm"] = nc.dram_tensor("cst_perm", [128, 128], F32, kind="ExternalInput").ap()
    d["cst_vkill"] = nc.dram_tensor("cst_vkill", [128, 1], F32, kind="ExternalInput").ap()
    d["cst_hscale"] = nc.dram_tensor("cst_hscale", [128, 1], F32, kind="ExternalInput").ap()
    return d


def l0ffn_const_host(core):
    h = core % 2
    base_pos = 2048 * h
    dd = np.arange(64)
    inv_freq = 10000.0 ** (-(2.0 * (dd % 32)) / 64.0)
    base = (2.0 * (dd % 32) + 0.4 * 64) / (1.4 * 64)
    i = base_pos + np.arange(T)
    freqs = i[None, :] * inv_freq[:, None]
    power = (i - L // 2) / float(512 // 2)
    scale = base[:, None] ** power[None, :]
    tc = np.cos(freqs).astype(np.float32)
    ts = np.sin(freqs).astype(np.float32)
    two = lambda a: np.concatenate([a, a], axis=0).astype(np.float32)
    masks = np.zeros((128, 8, TG), np.float32)
    jr = np.arange(128)[:, None]
    qq = np.arange(TG)[None, :]
    for kt in range(8):
        dlt = 512 - 128 * kt - jr + qq
        masks[:, kt, :] = ((dlt >= 0) & (dlt <= 512)).astype(np.float32)
    perm = np.zeros((128, 128), np.float32)
    for blk in range(2):
        o = 64 * blk
        for j2 in range(32):
            perm[o + 32 + j2, o + j2] = -1.0
            perm[o + j2, o + 32 + j2] = 1.0
    hpat2 = np.zeros((128, 2), np.float32)
    hpat2[0:64, 0] = 1.0
    hpat2[64:128, 1] = 1.0
    bpat2 = hpat2.T.copy()
    e01 = np.eye(2, dtype=np.float32)
    vkill = np.full((128, 1), 1.0 if h == 1 else 0.0, np.float32)
    hscale = vkill.copy()
    return {
        "cst_tcos": two(tc), "cst_tsin": two(ts),
        "cst_tsq": two(scale), "cst_tski": two(1.0 / scale),
        "cst_masks": masks.reshape(128, 8 * TG),
        "cst_perm": perm, "cst_hpat2": hpat2, "cst_bpat2": bpat2,
        "cst_e01": e01, "cst_vkill": vkill, "cst_hscale": hscale,
    }


class GlobalCst:
    def __init__(self, ctx, sch, io):
        self.ones_f = ctx.sb("g_ones_f", [128, 128], F32)
        self.ones_b = ctx.sb("g_ones_b", [128, 128], BF16)
        self.hpat2 = ctx.sb("g_hpat2", [128, 2], F32)
        self.bpat2 = ctx.sb("g_bpat2", [2, 128], F32)
        self.e01 = ctx.sb("g_e01", [2, 2], F32)
        sch.add("gpsimd", lambda g: g.memset(self.ones_f[:, :], 1.0), writes=(K(self.ones_f),))
        sch.add("gpsimd", lambda g: g.memset(self.ones_b[:, :], 1.0), writes=(K(self.ones_b),))
        for nm, buf in [("cst_hpat2", self.hpat2), ("cst_bpat2", self.bpat2),
                        ("cst_e01", self.e01)]:
            sch.add("sync", lambda q, nm=nm, buf=buf: q.dma_start(out=buf[:, :], in_=io[nm]),
                    writes=(K(buf),), dma=True, tag=nm)


class L0Consts:

    def __init__(self, ctx, sch, io, g):
        for a in ("ones_f", "ones_b", "hpat2", "bpat2", "e01"):
            setattr(self, a, getattr(g, a))
        self.tcos = ctx.sb("tcos", [128, T], BF16)
        self.tsin = ctx.sb("tsin", [128, T], BF16)
        self.tsq = ctx.sb("tsq", [128, T], BF16)
        self.tski = ctx.sb("tski", [128, T], BF16)
        self.masks = ctx.sb("masks", [128, 8 * TG], BF16)
        self.perm = ctx.sb("permc", [128, 128], F32)
        self.vkill = ctx.sb("vkill", [128, 1], F32)
        self.hscale = ctx.sb("hscale", [128, 1], F32)
        for name, buf in [("cst_perm", self.perm),
                          ("cst_vkill", self.vkill), ("cst_hscale", self.hscale)]:
            sch.add("sync", lambda q, buf=buf, name=name: q.dma_start(
                out=buf[:, :], in_=io[name]), writes=(K(buf),), dma=True, tag=name)
        for name, buf in [("cst_tcos", self.tcos), ("cst_tsin", self.tsin),
                          ("cst_tsq", self.tsq), ("cst_tski", self.tski),
                          ("cst_masks", self.masks)]:
            sch.add("gpsimd", lambda g, buf=buf, name=name: g.dma_start(
                out=buf[:, :], in_=io[name]), writes=(K(buf),), dma=True, tag=name)


def l0ffn_layer_decls(nc, li):
    d = {}
    d[f"wqkv{li}"] = nc.dram_tensor(f"wqkv{li}", [D, 1536], F32, kind="ExternalInput").ap()
    d[f"wo{li}"] = nc.dram_tensor(f"wo{li}", [512, D], F32, kind="ExternalInput").ap()
    d[f"w1p{li}"] = nc.dram_tensor(f"w1p{li}", [D, 2 * NJ * 128], F32, kind="ExternalInput").ap()
    d[f"b1p{li}"] = nc.dram_tensor(f"b1p{li}", [128, 2 * NJ], F32, kind="ExternalInput").ap()
    d[f"w2{li}"] = nc.dram_tensor(f"w2{li}", [2730, D], F32, kind="ExternalInput").ap()
    d[f"qs{li}"] = nc.dram_tensor(f"qs{li}", [128, 1], F32, kind="ExternalInput").ap()
    d[f"ks{li}"] = nc.dram_tensor(f"ks{li}", [128, 1], F32, kind="ExternalInput").ap()
    d[f"lg{li}"] = nc.dram_tensor(f"lg{li}", [128, 8], F32, kind="ExternalInput").ap()
    d[f"lb{li}"] = nc.dram_tensor(f"lb{li}", [128, 8], F32, kind="ExternalInput").ap()
    d[f"b2c{li}"] = nc.dram_tensor(f"b2c{li}", [128, 8], F32, kind="ExternalInput").ap()
    return d


def l0ffn_layer_host(inputs, li):
    w1 = np.asarray(inputs["L0_ffn_w1"][li], np.float32)
    b1 = np.asarray(inputs["L0_ffn_b1"][li], np.float32)
    HID = 2730
    w1p = np.zeros((D, 2 * NJ * 128), np.float32)
    b1p = np.zeros((128, 2 * NJ), np.float32)
    for j in range(NJ):
        nj = 128 if j < 21 else 42
        a = w1[:, 128 * j:128 * j + nj]
        g = w1[:, HID + 128 * j:HID + 128 * j + nj]
        base = 256 * j
        w1p[:, base:base + nj] = a
        w1p[:, base + 128:base + 128 + nj] = g
        b1p[0:nj, 2 * j] = b1[128 * j:128 * j + nj]
        b1p[0:nj, 2 * j + 1] = b1[HID + 128 * j:HID + 128 * j + nj]
    qs = np.asarray(inputs["L0_ffn_q_scale"][li], np.float32)
    ks = np.asarray(inputs["L0_ffn_k_scale"][li], np.float32)
    col = lambda v: np.concatenate([v, v])[:, None].astype(np.float32)
    pack8 = lambda v: np.asarray(v, np.float32).reshape(8, 128).T.copy()
    return {
        f"wqkv{li}": np.asarray(inputs["L0_ffn_wqkv"][li], np.float32),
        f"wo{li}": np.asarray(inputs["L0_ffn_wo"][li], np.float32),
        f"w1p{li}": w1p, f"b1p{li}": b1p,
        f"w2{li}": np.asarray(inputs["L0_ffn_w2"][li], np.float32),
        f"qs{li}": col(qs), f"ks{li}": col(ks),
        f"lg{li}": pack8(inputs["L0_ffn_ln_g"][li]),
        f"lb{li}": pack8(inputs["L0_ffn_ln_b"][li]),
        f"b2c{li}": pack8(inputs["L0_ffn_b2"][li]),
    }


def emit_l0ffn_layer(ctx, sch, xg, cst, io, li, xgk):
    nc = ctx.nc
    pb = ctx.pbanks
    EXT = 2049

    XHA = nc.dram_tensor(f"XHA{li}", [256, 8], BF16, addr_space="Shared")
    XHK = nc.dram_tensor(f"XHK{li}", [2 * 512, 512], BF16, addr_space="Shared")
    XHV = nc.dram_tensor(f"XHV{li}", [2 * 512, 520], BF16, addr_space="Shared")
    XBM = nc.dram_tensor(f"XBM{li}", [4, 4], F32, addr_space="Shared")
    XMB = nc.dram_tensor(f"XMB{li}", [256, 8], BF16, addr_space="Shared")

    m0 = ctx.mark()
    ymid = ctx.sb(f"ymid{li}", [128, 8 * EXT], BF16)
    amax = ctx.sb(f"amax{li}", [2, 16], F32)
    bmax = ctx.sb(f"bmax{li}", [2, 20], F32)
    bval = ctx.sb(f"bval{li}", [2, 16], F32)
    mrow = ctx.sb(f"mrow{li}", [1, 32], F32)
    bcol = ctx.sb(f"bcol{li}", [128, 32], F32)
    xrows = ctx.sb(f"xrows{li}", [128, 22], U32)
    sch.add("sync", lambda q: q.dma_start(out=xrows[:, :], in_=io["xrows"]),
            writes=(K(xrows),), dma=True, tag="xrowsld")

    m1 = ctx.mark()
    qrot = [ctx.sb(f"qrot{li}_{hp}", [128, T], BF16) for hp in range(4)]
    krot = [ctx.sb(f"krot{li}_{hp}", [128, 512 + T], BF16) for hp in range(4)]
    vbuf = [ctx.sb(f"vbuf{li}_{t}", [128, 520], BF16) for t in range(20)]
    wq3 = ctx.sb(f"wq3_{li}", [128, 8, 1536], BF16)
    wo4 = ctx.sb(f"wo4_{li}", [128, 4, 1024], BF16)

    sch.add("gpsimd", lambda g: g.dma_start(
        out=wq3[:, :, :], in_=io[f"wqkv{li}"].rearrange("(c p) n -> p c n", p=128)),
        writes=(K(wq3),), dma=True, tag="wqkvld")
    sch.add("gpsimd", lambda g: g.dma_start(
        out=wo4[:, :, :], in_=io[f"wo{li}"].rearrange("(c p) n -> p c n", p=128)),
        writes=(K(wo4),), dma=True, tag="wold")
    qs = ctx.sb(f"qsc{li}", [128, 1], F32)
    ks = ctx.sb(f"ksc{li}", [128, 1], F32)
    for nm, buf in [(f"qs{li}", qs), (f"ks{li}", ks)]:
        sch.add("sync", lambda q, nm=nm, buf=buf: q.dma_start(out=buf[:, :], in_=io[nm]),
                writes=(K(buf),), dma=True)
    for t in range(20):
        sch.add("gpsimd", lambda g, t=t: g.memset(vbuf[t][:, :], 1.0),
                writes=(K(vbuf[t]),), tag=f"vones{t}")

    m2 = ctx.mark()
    ybf = ctx.sb(f"ybf{li}", [128, 8 * EXT], BF16)
    scr2set = [[ctx.sb(f"s1scr{li}_{s}_{i}", [128, TG], F32) for i in range(6)]
               for s in range(2)]
    scr = scr2set[0]
    stg = ctx.sb(f"stg{li}", [128, 8], BF16)
    rq2set = [ctx.sb(f"rq2_{li}_{s}", [2, TG], F32) for s in range(2)]
    rq2 = rq2set[0]

    def ysl(buf, c, lo, n, shifted):
        off = lo + (0 if (shifted and c >= 4) else 1)
        return buf[:, EXT * c + off: EXT * c + off + n]

    for c in range(NCH):
        sch.add("gpsimd", lambda g, c=c: g.dma_start(
            out=ybf[:, EXT * c + 1:EXT * c + 1 + T],
            in_=io["yin"][128 * c:128 * (c + 1), :]),
            writes=(K(ybf),), dma=True, tag=f"yld{c}")

    stgp = ctx.sb(f"stgp{li}", [128, 8], BF16)
    sch.add("vector", lambda v: v.tensor_copy(
        stgp[:, :], ybf.rearrange("p (c n) -> p c n", n=EXT)[:, :, 2048]),
        reads=(K(ybf),), writes=(K(stgp),), tag="e2a_stage")
    ptk = sch.add("gpsimd", lambda g: g.indirect_dma_start(
        out=XHA.ap(), out_offset=IOA(ap=xrows[:, 0:1], axis=0),
        in_=stgp[:, :], in_offset=None),
        reads=(K(stgp), K(xrows)), writes=("XHA" + str(li),), dma=True, tag="e2a_push")
    xg.push(xgk + 0, [ptk])
    pk = xg.poll(xgk + 0)
    sch.add("gpsimd", lambda g: g.indirect_dma_start(
        out=stg[:, :], out_offset=None,
        in_=XHA.ap(), in_offset=IOA(ap=xrows[:, 1:2], axis=0)),
        reads=("XHA" + str(li), K(xrows)), writes=(K(stg),), dma=True,
        extra_deps=[pk], tag="e2a_pull")
    sch.add("gpsimd", lambda g: g.tensor_scalar_mul(stg[:, :], stg[:, :], cst.hscale[:, 0:1]),
            reads=(K(stg), K(cst.hscale)), writes=(K(stg),))
    sch.add("vector", lambda v: v.tensor_copy(
        ybf.rearrange("p (c n) -> p c n", n=EXT)[:, :, 0], stg[:, :]),
        reads=(K(stg),), writes=(K(ybf),), tag="e2a_wr")

    def qk_side(gi, hp, is_q):
        cid = gi * 8 + hp * 2 + (0 if is_q else 1)
        scr = scr2set[cid % 2]
        rq2 = rq2set[cid % 2]
        pbA = pb[cid % 4]
        pbB = pb[4 + cid % 4]
        lo = gi * TG
        colbase = (0 if is_q else 512) + 128 * hp
        bmm = pbA
        tab_c, tab_s = cst.tcos, cst.tsin
        tsc = cst.tsq if is_q else cst.tski
        sccol = qs if is_q else ks
        dst = qrot[hp] if is_q else krot[hp]
        dst_lo = lo if is_q else 512 + lo
        tab = amax if is_q else bmax
        tabcol = (4 * gi + hp) if is_q else (4 * (gi + 1) + hp)
        sch.add("tensor", MMG_(pb, bmm[:, :], [
            (lambda t, d=d, colbase=colbase: wq3[:, d, colbase:colbase + 128],
             lambda t, d=d, lo=lo: ysl(ybf, d, lo, TG, True)) for d in range(NCH)]),
            reads=(K(wq3), K(ybf)), writes=(K(bmm),), tag=f"qk{is_q}{gi}_{hp}")
        sch.add("scalar", lambda s, bmm=bmm: s.activation(
            out=scr[0][:, :], in_=bmm[:, :], func=AF.Square),
            reads=(K(bmm),), writes=(K(scr[0]),))
        sch.add("tensor", lambda t, scr=scr, pbB=pbB: t.matmul(
            pbB[0:2, :], cst.hpat2[:, :], scr[0][:, :], start=True, stop=True),
            reads=(K(cst.hpat2), K(scr[0])), writes=(K(pbB),))
        sch.add("vector", lambda v, rq2=rq2, pbB=pbB: v.tensor_scalar_max(rq2[0:2, :], pbB[0:2, :], 1e-24),
                reads=(K(pbB),), writes=(K(rq2),))
        sch.add("scalar", lambda s, rq2=rq2: s.activation(out=rq2[0:2, :], in_=rq2[0:2, :], func=AF.Ln),
                reads=(K(rq2),), writes=(K(rq2),))
        sch.add("scalar", lambda s, rq2=rq2: s.activation(out=rq2[0:2, :], in_=rq2[0:2, :], func=AF.Exp, scale=-0.5),
                reads=(K(rq2),), writes=(K(rq2),))
        sch.add("tensor", lambda t, rq2=rq2, pbB=pbB: t.matmul(
            pbB[:, :], cst.bpat2[:, :], rq2[0:2, :], start=True, stop=True),
            reads=(K(cst.bpat2), K(rq2)), writes=(K(pbB),))
        sch.add("scalar", lambda s, scr=scr, pbB=pbB: s.copy(scr[1][:, :], pbB[:, :]),
                reads=(K(pbB),), writes=(K(scr[1]),))
        sch.add("vector", lambda v, bmm=bmm, sccol=sccol, scr=scr: v.scalar_tensor_tensor(
            out=scr[2][:, :], in0=bmm[:, :], scalar=sccol[:, 0:1],
            in1=scr[1][:, :], op0=ALU.mult, op1=ALU.mult),
            reads=(K(bmm), K(scr[1]), K(sccol)), writes=(K(scr[2]),))
        sch.add("tensor", lambda t, scr=scr, pbB=pbB: t.matmul(
            pbB[:, :], cst.perm[:, :], scr[2][:, :], start=True, stop=True),
            reads=(K(cst.perm), K(scr[2])), writes=(K(pbB),))
        sch.add("vector", lambda v, lo=lo, tab_c=tab_c, scr=scr: v.tensor_mul(
            scr[3][:, :], scr[2][:, :], tab_c[:, lo:lo + TG]),
            reads=(K(scr[2]), K(tab_c)), writes=(K(scr[3]),))
        sch.add("vector", lambda v, lo=lo, tab_s=tab_s, scr=scr, pbB=pbB: v.tensor_mul(
            scr[4][:, :], pbB[:, :], tab_s[:, lo:lo + TG]),
            reads=(K(pbB), K(tab_s)), writes=(K(scr[4]),))
        sch.add("gpsimd", lambda g, scr=scr: g.tensor_add(scr[5][:, :], scr[3][:, :], scr[4][:, :]),
                reads=(K(scr[3]), K(scr[4])), writes=(K(scr[5]),))
        sch.add("scalar", lambda s, scr=scr: s.activation(
            out=scr[0][:, :], in_=scr[5][:, :], func=AF.Square),
            reads=(K(scr[5]),), writes=(K(scr[0]),))
        sch.add("tensor", lambda t, scr=scr, pbB=pbB: t.matmul(
            pbB[0:2, :], cst.hpat2[:, :], scr[0][:, :], start=True, stop=True),
            reads=(K(cst.hpat2), K(scr[0])), writes=(K(pbB),))
        sch.add("vector", lambda v, rq2=rq2, pbB=pbB: v.reduce_max(rq2[0:2, 0:1], pbB[0:2, :],
                                                 axis=mybir.AxisListType.X),
                reads=(K(pbB),), writes=(K(rq2),))
        sch.add("scalar", lambda s, rq2=rq2: s.activation(
            out=rq2[0:2, 0:1], in_=rq2[0:2, 0:1], func=AF.Ln),
            reads=(K(rq2),), writes=(K(rq2),))
        sch.add("scalar", lambda s, tab=tab, tabcol=tabcol, rq2=rq2: s.activation(
            out=tab[0:2, tabcol:tabcol + 1], in_=rq2[0:2, 0:1], func=AF.Exp, scale=0.5),
            reads=(K(rq2),), writes=(K(tab),))
        sch.add("vector", lambda v, dst=dst, dst_lo=dst_lo, lo=lo, tsc=tsc, scr=scr: v.tensor_mul(
            dst[:, dst_lo:dst_lo + TG], scr[5][:, :], tsc[:, lo:lo + TG]),
            reads=(K(scr[5]), K(tsc)), writes=(K(dst),), tag=f"rot{is_q}{gi}{hp}")

    for gi in range(NTG):
        for hp in range(4):
            qk_side(gi, hp, True)
            qk_side(gi, hp, False)
        for lt in range(4):
            ls = gi * TG + 128 * lt
            bv = pb[lt % 4]
            sch.add("tensor", MMG_(pb, bv[:, :], [
                (lambda t, d=d, ls=ls: ysl(ybf, d, ls, 128, True),
                 lambda t, d=d: wq3[:, d, 1024:1536]) for d in range(NCH)]),
                reads=(K(ybf), K(wq3)), writes=(K(bv),), tag=f"vmm{gi}_{lt}")
            ti = 4 + 4 * gi + lt
            sch.add("vector", lambda v, bv=bv, ti=ti: v.tensor_copy(
                vbuf[ti].rearrange("p (h n) -> p h n", n=65)[:, :, 0:64], bv[:, :]),
                reads=(K(bv),), writes=(K(vbuf[ti]),), tag=f"vev{gi}_{lt}")

    deps = []
    for hp in range(4):
        deps.append(sch.add("gpsimd", lambda g, hp=hp: g.indirect_dma_start(
            out=XHK.ap(), out_offset=IOA(ap=xrows[:, 4 + hp:5 + hp], axis=0),
            in_=krot[hp][:, 2048:2560], in_offset=None),
            reads=(K(krot[hp]), K(xrows)), writes=(f"XHK{li}",), dma=True,
            tag=f"e2b_pk{hp}"))
    for i, t in enumerate(range(16, 20)):
        deps.append(sch.add("gpsimd", lambda g, t=t, i=i: g.indirect_dma_start(
            out=XHV.ap(), out_offset=IOA(ap=xrows[:, 8 + i:9 + i], axis=0),
            in_=vbuf[t][:, :], in_offset=None),
            reads=(K(vbuf[t]), K(xrows)), writes=(f"XHV{li}",), dma=True,
            tag=f"e2b_pv{i}"))
    deps.append(sch.add("gpsimd", lambda g: g.indirect_dma_start(
        out=XBM.ap(), out_offset=IOA(ap=xrows[0:2, 12:13], axis=0),
        in_=bmax[0:2, 16:20], in_offset=None),
        reads=(K(bmax), K(xrows)), writes=(f"XBM{li}",), dma=True, tag="e2b_pb"))
    xg.push(xgk + 1, deps)
    pk2 = xg.poll(xgk + 1)
    for hp in range(4):
        sch.add("gpsimd", lambda g, hp=hp: g.indirect_dma_start(
            out=krot[hp][:, 0:512], out_offset=None,
            in_=XHK.ap(), in_offset=IOA(ap=xrows[:, 13 + hp:14 + hp], axis=0)),
            reads=(f"XHK{li}", K(xrows)), writes=(K(krot[hp]),), dma=True,
            extra_deps=[pk2], tag=f"e2b_gk{hp}")
        sch.add("gpsimd", lambda g, hp=hp: g.tensor_scalar_mul(
            krot[hp][:, 0:512], krot[hp][:, 0:512], cst.vkill[:, 0:1]),
            reads=(K(krot[hp]), K(cst.vkill)), writes=(K(krot[hp]),))
    for i in range(4):
        sch.add("gpsimd", lambda g, i=i: g.indirect_dma_start(
            out=vbuf[i][:, 0:520], out_offset=None,
            in_=XHV.ap(), in_offset=IOA(ap=xrows[:, 17 + i:18 + i], axis=0)),
            reads=(f"XHV{li}", K(xrows)), writes=(K(vbuf[i]),), dma=True,
            extra_deps=[pk2], tag=f"e2b_gv{i}")
        sch.add("gpsimd", lambda g, i=i: g.tensor_scalar_mul(
            vbuf[i][:, :], vbuf[i][:, :], cst.vkill[:, 0:1]),
            reads=(K(vbuf[i]), K(cst.vkill)), writes=(K(vbuf[i]),))
    sch.add("gpsimd", lambda g: g.indirect_dma_start(
        out=bmax[0:2, 0:4], out_offset=None,
        in_=XBM.ap(), in_offset=IOA(ap=xrows[0:2, 21:22], axis=0)),
        reads=(f"XBM{li}", K(xrows)), writes=(K(bmax),), dma=True,
        extra_deps=[pk2], tag="e2b_gb")

    sch.add("vector", lambda v: v.tensor_max(bval[0:2, :], bmax[0:2, 4:20], bmax[0:2, 0:16]),
            reads=(K(bmax),), writes=(K(bval),))
    sch.add("vector", lambda v: v.tensor_mul(bval[0:2, :], bval[0:2, :], amax[0:2, :]),
            reads=(K(bval), K(amax)), writes=(K(bval),))
    sch.add("vector", lambda v: v.tensor_scalar(
        out=bval[0:2, :], in0=bval[0:2, :], scalar1=-8.0, scalar2=45.0,
        op0=ALU.mult, op1=ALU.add),
        reads=(K(bval),), writes=(K(bval),))
    for r in range(2):
        sch.add("tensor", lambda t, r=r: t.matmul(
            pb[2][0:1, 16 * r:16 * r + 16], cst.e01[:, r:r + 1], bval[0:2, :],
            start=True, stop=True),
            reads=(K(cst.e01), K(bval)), writes=(K(pb[2]),), tag=f"mrow{r}")
    sch.add("vector", lambda v: v.tensor_copy(mrow[0:1, :], pb[2][0:1, 0:32]),
            reads=(K(pb[2]),), writes=(K(mrow),))
    sch.add("tensor", lambda t: t.matmul(
        pb[3][:, 0:32], cst.ones_f[0:1, :], mrow[0:1, :], start=True, stop=True),
        reads=(K(cst.ones_f), K(mrow)), writes=(K(pb[3]),))
    sch.add("scalar", lambda s: s.copy(bcol[:, :], pb[3][:, 0:32]),
            reads=(K(pb[3]),), writes=(K(bcol),))

    ctx.reset(m2)
    sch.set_fence()
    pT = [ctx.sb(f"pT{li}_{i}", [128, TG], BF16) for i in range(4)]
    ofb = [ctx.sb(f"ofb{li}_{i}", [128, TG], BF16) for i in range(4)]
    recs = [ctx.sb(f"recs{li}_{i}", [1, TG], F32) for i in range(2)]
    scrR = ctx.sb(f"scrR{li}", [128, TG], F32)
    scrX = [ctx.sb(f"scrX{li}_{i}", [128, TG], F32) for i in range(2)]
    scrY = [ctx.sb(f"scrY{li}_{i}", [128, TG], F32) for i in range(2)]

    for w in range(NW):
        for hp in range(4):
            bO = [pb[4], pb[5]]
            for kt in range(8):
                bS = [pb[2 * (kt % 2)], pb[2 * (kt % 2) + 1]]
                for par in range(2):
                    rs = 64 * par
                    def em_s(t, w=w, hp=hp, kt=kt, rs=rs, par=par, bS=bS):
                        return t.matmul(
                            bS[par][:, :],
                            krot[hp][rs:rs + 64, 512 * w + 128 * kt:512 * w + 128 * kt + 128],
                            qrot[hp][rs:rs + 64, 512 * w:512 * w + 512],
                            start=True, stop=True)
                    sch.add("tensor", em_s, reads=(K(krot[hp]), K(qrot[hp])),
                            writes=(K(bS[par]),), tag=f"S{w}_{hp}_{kt}_{par}")
                    hgl = 2 * hp + par
                    bidx = 16 * par + 4 * w + hp
                    def em_e(s, par=par, bidx=bidx, kt=kt, bS=bS):
                        return s.activation(
                            out=pT[2 * (kt % 2) + par][:, :], in_=bS[par][:, :],
                            func=AF.Exp, bias=bcol[:, bidx:bidx + 1], scale=8.0)
                    sch.add("scalar", em_e, reads=(K(bS[par]), K(bcol)),
                            writes=(K(pT[2 * (kt % 2) + par]),), tag=f"E{w}_{hp}_{kt}_{par}")
                    def em_m(g, par=par, kt=kt):
                        buf = pT[2 * (kt % 2) + par]
                        return g.tensor_mul(buf[:, :], buf[:, :],
                                            cst.masks[:, TG * kt:TG * (kt + 1)])
                    sch.add("vector", em_m, reads=(K(pT[2 * (kt % 2) + par]), K(cst.masks)),
                            writes=(K(pT[2 * (kt % 2) + par]),), tag=f"M{w}_{hp}_{kt}_{par}")
                    def em_pv(t, w=w, kt=kt, hgl=hgl, par=par, bO=bO):
                        return t.matmul(
                            bO[par][0:65, :], vbuf[4 * w + kt][:, 65 * hgl:65 * hgl + 65],
                            pT[2 * (kt % 2) + par][:, :], start=(kt == 0), stop=(kt == 7))
                    sch.add("tensor", em_pv,
                            reads=(K(vbuf[4 * w + kt]), K(pT[2 * (kt % 2) + par])),
                            writes=(K(bO[par]),), tag=f"PV{w}_{hp}_{kt}_{par}")
            for par in range(2):
                sch.add("vector", lambda v, par=par, bO=bO: v.reciprocal(
                    recs[par][0:1, :], bO[par][64:65, :]),
                    reads=(K(bO[par]),), writes=(K(recs[par]),))
            def em_rb(t, bO=bO):
                t.matmul(pb[6][0:64, :], cst.ones_f[0:1, 0:64], recs[0][0:1, :],
                         start=True, stop=True)
                return t.matmul(pb[6][64:128, :], cst.ones_f[0:1, 0:64], recs[1][0:1, :],
                                start=True, stop=True)
            sch.add("tensor", em_rb, reads=(K(cst.ones_f), K(recs[0]), K(recs[1])),
                    writes=(K(pb[6]),), tag=f"RB{w}_{hp}")
            sch.add("scalar", lambda s: s.copy(scrR[:, :], pb[6][:, :]),
                    reads=(K(pb[6]),), writes=(K(scrR),))
            for par in range(2):
                rs = 64 * par
                sch.add("vector", lambda v, par=par, rs=rs, hp=hp, bO=bO: v.tensor_mul(
                    ofb[hp][rs:rs + 64, :], bO[par][0:64, :], scrR[rs:rs + 64, :]),
                    reads=(K(bO[par]), K(scrR)), writes=(K(ofb[hp]),), tag=f"OF{w}_{hp}_{par}")
        for oc in range(NCH):
            bY = pb[2 + oc % 2]
            sch.add("tensor", MMG_(pb, bY[:, :], [
                (lambda t, hp=hp2, oc=oc: wo4[:, hp, 128 * oc:128 * oc + 128],
                 lambda t, hp=hp2: ofb[hp][:, :]) for hp2 in range(4)]),
                reads=tuple(K(ofb[i]) for i in range(4)) + (K(wo4),),
                writes=(K(bY),), tag=f"WO{w}_{oc}")
            xse = scrX[oc % 2]
            sch.add("sync", lambda q, oc=oc, w=w, xse=xse: q.dma_start(
                out=xse[:, :], in_=io["yin"][128 * oc:128 * (oc + 1), TG * w:TG * (w + 1)]),
                writes=(K(xse),), dma=True, tag=f"yld2_{w}_{oc}")
            yse = scrY[oc % 2]
            sch.add("vector", lambda v, bY=bY, xse=xse, yse=yse: v.tensor_add(
                yse[:, :], bY[:, :], xse[:, :]),
                reads=(K(bY), K(xse)), writes=(K(yse),), tag=f"YM{w}_{oc}")
            sch.add("sync", lambda q, oc=oc, w=w, yse=yse: q.dma_start(
                out=io["ymid"][128 * oc:128 * (oc + 1), TG * w:TG * (w + 1)], in_=yse[:, :]),
                reads=(K(yse),), writes=(f"ymidd{li}_{w}_{oc}",), dma=True,
                tag=f"YMS{w}_{oc}")
            sch.add("scalar", lambda s, oc=oc, w=w, yse=yse: s.copy(
                ymid[:, EXT * oc + 1 + TG * w: EXT * oc + 1 + TG * (w + 1)], yse[:, :]),
                reads=(K(yse),), writes=(K(ymid),), tag=f"YMB{w}_{oc}")

    stg2 = ctx.sb(f"stg2{li}", [128, 8], BF16)
    stg2p = ctx.sb(f"stg2p{li}", [128, 8], BF16)
    sch.add("vector", lambda v: v.tensor_copy(
        stg2p[:, :], ymid.rearrange("p (c n) -> p c n", n=EXT)[:, :, 2048]),
        reads=(K(ymid),), writes=(K(stg2p),), tag="e3_stage")
    ptk3 = sch.add("gpsimd", lambda g: g.indirect_dma_start(
        out=XMB.ap(), out_offset=IOA(ap=xrows[:, 2:3], axis=0),
        in_=stg2p[:, :], in_offset=None),
        reads=(K(stg2p), K(xrows)), writes=(f"XMB{li}",), dma=True, tag="e3_push")
    xg.push(xgk + 2, [ptk3])
    pk3 = xg.poll(xgk + 2)
    sch.add("gpsimd", lambda g: g.indirect_dma_start(
        out=stg2[:, :], out_offset=None,
        in_=XMB.ap(), in_offset=IOA(ap=xrows[:, 3:4], axis=0)),
        reads=(f"XMB{li}", K(xrows)), writes=(K(stg2),), dma=True,
        extra_deps=[pk3], tag="e3_pull")
    if P2_DEBUG:
        sch.add("sync", lambda q: q.dma_start(out=io["dbg_stg2"], in_=stg2[:, :]),
                reads=(K(stg2),), writes=("dbg_stg2",), dma=True)
    sch.add("gpsimd", lambda g: g.tensor_scalar_mul(stg2[:, :], stg2[:, :], cst.hscale[:, 0:1]),
            reads=(K(stg2), K(cst.hscale)), writes=(K(stg2),))
    sch.add("vector", lambda v: v.tensor_copy(
        ymid.rearrange("p (c n) -> p c n", n=EXT)[:, :, 0], stg2[:, :]),
        reads=(K(stg2),), writes=(K(ymid),), tag="e3_wr")

    ctx.reset(m1)
    sch.set_fence()
    hnb = ctx.sb(f"hnb{li}", [128, 8 * TG], BF16)
    h2g = ctx.sb(f"h2g{li}", [128, NJ * TG], BF16)
    w1t = [ctx.sb(f"w1t{li}_{i}", [128, 8, 256], BF16) for i in range(2)]
    w2t = [ctx.sb(f"w2t{li}_{i}", [128, 1024], BF16) for i in range(2)]
    scr3 = [ctx.sb(f"s3scr{li}_{i}", [128, TG], F32) for i in range(6)]
    scr3m = ctx.sb(f"s3scrm{li}", [128, TG], F32)
    scr3b = [ctx.sb(f"s3scrb{li}_{i}", [128, TG], BF16) for i in range(2)]
    srow = [ctx.sb(f"s3row{li}_{i}", [1, TG], F32) for i in range(6)]
    lgc = ctx.sb(f"lgc{li}", [128, 8], F32)
    lbc = ctx.sb(f"lbc{li}", [128, 8], F32)
    b1c = ctx.sb(f"b1c{li}", [128, 2 * NJ], F32)
    b2c = ctx.sb(f"b2cc{li}", [128, 8], F32)
    for nm, buf in [(f"lg{li}", lgc), (f"lb{li}", lbc), (f"b1p{li}", b1c),
                    (f"b2c{li}", b2c)]:
        sch.add("sync", lambda q, nm=nm, buf=buf: q.dma_start(out=buf[:, :], in_=io[nm]),
                writes=(K(buf),), dma=True)

    for gi in range(NTG):
        lo = gi * TG
        bsum, bsq = pb[6], pb[7]
        for c in range(NCH):
            def em_sum(t, c=c, lo=lo, bsum=bsum):
                return t.matmul(bsum[0:1, :], cst.ones_b[:, 0:1],
                                ysl(ymid, c, lo, TG, True), start=(c == 0), stop=(c == NCH - 1))
            sch.add("tensor", em_sum, reads=(K(ymid), K(cst.ones_b)), writes=(K(bsum),),
                    tag=f"hsum{gi}_{c}")
            sq = scr3b[c % 2]
            sch.add("scalar", lambda s, c=c, lo=lo, sq=sq: s.activation(
                out=sq[:, :], in_=ysl(ymid, c, lo, TG, True), func=AF.Square),
                reads=(K(ymid),), writes=(K(sq),))
            def em_sq(t, c=c, sq=sq, bsq=bsq):
                return t.matmul(bsq[0:1, :], cst.ones_b[:, 0:1], sq[:, :],
                                start=(c == 0), stop=(c == NCH - 1))
            sch.add("tensor", em_sq, reads=(K(sq), K(cst.ones_b)), writes=(K(bsq),),
                    tag=f"hsq{gi}_{c}")
        sch.add("scalar", lambda s, bsum=bsum: s.activation(
            out=srow[0][0:1, :], in_=bsum[0:1, :], func=AF.Copy, scale=1.0 / D),
            reads=(K(bsum),), writes=(K(srow[0]),))
        sch.add("scalar", lambda s, bsq=bsq: s.activation(
            out=srow[1][0:1, :], in_=bsq[0:1, :], func=AF.Copy, scale=1.0 / D),
            reads=(K(bsq),), writes=(K(srow[1]),))
        sch.add("scalar", lambda s: s.activation(
            out=srow[2][0:1, :], in_=srow[0][0:1, :], func=AF.Square),
            reads=(K(srow[0]),), writes=(K(srow[2]),))
        sch.add("vector", lambda v: v.tensor_sub(srow[3][0:1, :], srow[1][0:1, :], srow[2][0:1, :]),
                reads=(K(srow[1]), K(srow[2])), writes=(K(srow[3]),))
        sch.add("vector", lambda v: v.tensor_scalar_add(srow[3][0:1, :], srow[3][0:1, :], 1e-5),
                reads=(K(srow[3]),), writes=(K(srow[3]),))
        sch.add("scalar", lambda s: s.activation(out=srow[4][0:1, :], in_=srow[3][0:1, :],
                                                 func=AF.Ln),
                reads=(K(srow[3]),), writes=(K(srow[4]),))
        sch.add("scalar", lambda s: s.activation(out=srow[5][0:1, :], in_=srow[4][0:1, :],
                                                 func=AF.Exp, scale=-0.5),
                reads=(K(srow[4]),), writes=(K(srow[5]),))
        sch.add("vector", lambda v: v.tensor_mul(srow[2][0:1, :], srow[0][0:1, :], srow[5][0:1, :]),
                reads=(K(srow[0]), K(srow[5])), writes=(K(srow[2]),))
        br, bm = pb[4], pb[5]
        sch.add("tensor", lambda t, br=br: t.matmul(
            br[:, :], cst.ones_f[0:1, :], srow[5][0:1, :], start=True, stop=True),
            reads=(K(cst.ones_f), K(srow[5])), writes=(K(br),))
        sch.add("tensor", lambda t, bm=bm: t.matmul(
            bm[:, :], cst.ones_f[0:1, :], srow[2][0:1, :], start=True, stop=True),
            reads=(K(cst.ones_f), K(srow[2])), writes=(K(bm),))
        sch.add("scalar", lambda s, bm=bm: s.copy(scr3m[:, :], bm[:, :]),
                reads=(K(bm),), writes=(K(scr3m),))
        for c in range(NCH):
            sa = scr3[c % 2]
            sb_ = scr3[4 + c % 2]
            sch.add("vector", lambda v, c=c, lo=lo, br=br, sa=sa: v.tensor_mul(
                sa[:, :], ysl(ymid, c, lo, TG, True), br[:, :]),
                reads=(K(ymid), K(br)), writes=(K(sa),))
            sch.add("gpsimd", lambda g, sa=sa, sb_=sb_: g.tensor_sub(sb_[:, :], sa[:, :], scr3m[:, :]),
                    reads=(K(sa), K(scr3m)), writes=(K(sb_),))
            sch.add("vector", lambda v, c=c, sb_=sb_: v.tensor_scalar(
                out=hnb[:, TG * c:TG * (c + 1)], in0=sb_[:, :],
                scalar1=lgc[:, c:c + 1], scalar2=lbc[:, c:c + 1],
                op0=ALU.mult, op1=ALU.add),
                reads=(K(sb_), K(lgc), K(lbc)), writes=(K(hnb),), tag=f"hnb{gi}_{c}")
        if P2_DEBUG and gi == 0:
            sch.add("sync", lambda q: q.dma_start(out=io["dbg_hnb"], in_=hnb[:, :]),
                    reads=(K(hnb),), writes=("dbg_hnb",), dma=True)
            sch.add("sync", lambda q: q.dma_start(out=io["dbg_mu"], in_=srow[0][0:1, :]),
                    reads=(K(srow[0]),), writes=("dbg_mu",), dma=True)
            sch.add("sync", lambda q: q.dma_start(out=io["dbg_rstd"], in_=srow[5][0:1, :]),
                    reads=(K(srow[5]),), writes=("dbg_rstd",), dma=True)
        for j in range(NJ):
            nj = 128 if j < 21 else 42
            wt = w1t[j % 2]
            sch.add("gpsimd", lambda g, j=j, wt=wt: g.dma_start(
                out=wt[:, :, :],
                in_=io[f"w1p{li}"][:, 256 * j:256 * (j + 1)].rearrange("(c p) n -> p c n", p=128)),
                writes=(K(wt),), dma=True, tag=f"w1ld{gi}_{j}")
            bA, bG = pb[2 * (j % 2)], pb[2 * (j % 2) + 1]
            sch.add("tensor", MMG_(pb, bA[0:nj, :], [
                (lambda t, c=c, wt=wt, nj=nj: wt[:, c, 0:nj],
                 lambda t, c=c: hnb[:, TG * c:TG * (c + 1)]) for c in range(NCH)]),
                reads=(K(wt), K(hnb)), writes=(K(bA),), tag=f"amm{gi}_{j}")
            sch.add("tensor", MMG_(pb, bG[0:nj, :], [
                (lambda t, c=c, wt=wt, nj=nj: wt[:, c, 128:128 + nj],
                 lambda t, c=c: hnb[:, TG * c:TG * (c + 1)]) for c in range(NCH)]),
                reads=(K(wt), K(hnb)), writes=(K(bG),), tag=f"gmm{gi}_{j}")
            gsc = scr3[2 + j % 2]
            sch.add("scalar", lambda s, j=j, nj=nj, bG=bG, gsc=gsc: s.activation(
                out=gsc[0:nj, :], in_=bG[0:nj, :], func=AF.Gelu,
                bias=b1c[0:nj, 2 * j + 1:2 * j + 2]),
                reads=(K(bG), K(b1c)), writes=(K(gsc),))
            sch.add("vector", lambda v, j=j, nj=nj, bA=bA, gsc=gsc: v.scalar_tensor_tensor(
                out=h2g[0:nj, TG * j:TG * (j + 1)], in0=bA[0:nj, :],
                scalar=b1c[0:nj, 2 * j:2 * j + 1], in1=gsc[0:nj, :],
                op0=ALU.add, op1=ALU.mult),
                reads=(K(bA), K(gsc), K(b1c)), writes=(K(h2g),), tag=f"h2{gi}_{j}")
        if P2_DEBUG and gi == 0:
            sch.add("sync", lambda q: q.dma_start(out=io["dbg_h2"], in_=h2g[:, 0:2 * TG]),
                    reads=(K(h2g),), writes=("dbg_h2",), dma=True)
        for j in range(NJ):
            nj = 128 if j < 21 else 42
            wt = w2t[j % 2]
            sch.add("gpsimd", lambda g, j=j, nj=nj, wt=wt: g.dma_start(
                out=wt[0:nj, :], in_=io[f"w2{li}"][128 * j:128 * j + nj, :]),
                writes=(K(wt),), dma=True, tag=f"w2ld{gi}_{j}")
            for oc in range(NCH):
                def em_w2(t, j=j, nj=nj, oc=oc, wt=wt):
                    return t.matmul(pb[oc][:, :], wt[0:nj, 128 * oc:128 * (oc + 1)],
                                    h2g[0:nj, TG * j:TG * (j + 1)],
                                    start=(j == 0), stop=(j == NJ - 1))
                sch.add("tensor", em_w2, reads=(K(wt), K(h2g)), writes=(K(pb[oc]),),
                        tag=f"w2mm{gi}_{j}_{oc}")
        for oc in range(NCH):
            xse = scr3[oc % 2]
            sch.add("sync", lambda q, oc=oc, lo=lo, xse=xse: q.dma_start(
                out=xse[:, :], in_=io["ymid"][128 * oc:128 * (oc + 1), lo:lo + TG]),
                writes=(K(xse),), dma=True, tag=f"ymld{gi}_{oc}")
            yse = scr3[4 + oc % 2]
            sch.add("vector", lambda v, oc=oc, xse=xse, yse=yse: v.scalar_tensor_tensor(
                out=yse[:, :], in0=pb[oc][:, :], scalar=b2c[:, oc:oc + 1],
                in1=xse[:, :], op0=ALU.add, op1=ALU.add),
                reads=(K(pb[oc]), K(xse), K(b2c)), writes=(K(yse),), tag=f"yo{gi}_{oc}")
            sch.add("sync", lambda q, oc=oc, lo=lo, yse=yse: q.dma_start(
                out=io["yout"][128 * oc:128 * (oc + 1), lo:lo + TG], in_=yse[:, :]),
                reads=(K(yse),), writes=(f"yout{li}_{gi}_{oc}",), dma=True,
                tag=f"yos{gi}_{oc}")
    if P2_DEBUG:
        sch.add("sync", lambda q: q.dma_start(out=io["dbg_polls"], in_=xg.iters[0:1, 0:3]),
                reads=(K(xg.iters),), writes=("dbg_polls",), dma=True)
    ctx.reset(m0)
    sch.set_fence()


def MMG_(pb, bank_ap, pairs):
    def em(t):
        n = len(pairs)
        last = None
        for i, (lf, rf) in enumerate(pairs):
            last = t.matmul(bank_ap, lf(t), rf(t), start=(i == 0), stop=(i == n - 1))
        return last
    return em


def xrows_host(core):
    h = core % 2
    p = np.arange(128, dtype=np.int64)
    cols = []
    cols.append(128 * h + p)
    cols.append(128 * (1 - h) + p)
    cols.append(128 * h + p)
    cols.append(128 * (1 - h) + p)
    for hp in range(4):
        cols.append(512 * h + 128 * hp + p)
    for i in range(4):
        cols.append(512 * h + 128 * i + p)
    cols.append(2 * h + p)
    for hp in range(4):
        cols.append(512 * (1 - h) + 128 * hp + p)
    for i in range(4):
        cols.append(512 * (1 - h) + 128 * i + p)
    cols.append(2 * (1 - h) + p)
    return np.stack(cols, axis=1).astype(np.uint32)


def build_p2(li=0):
    nc = bass.Bass("TRN2", target_bir_lowering=False)
    ctx = Ctx(nc)
    io = {}
    io["yin"] = nc.dram_tensor("yin", [D, T], F32, kind="ExternalInput").ap()
    io["ymid"] = nc.dram_tensor("ymid_d", [D, T], F32, kind="ExternalOutput").ap()
    io["yout"] = nc.dram_tensor("yout", [D, T], F32, kind="ExternalOutput").ap()
    io["xrows"] = nc.dram_tensor("xrows", [128, 22], U32, kind="ExternalInput").ap()
    if P2_DEBUG:
        io["dbg_hnb"] = nc.dram_tensor("dbg_hnb", [128, 8 * TG], BF16, kind="ExternalOutput").ap()
        io["dbg_mu"] = nc.dram_tensor("dbg_mu", [1, TG], F32, kind="ExternalOutput").ap()
        io["dbg_rstd"] = nc.dram_tensor("dbg_rstd", [1, TG], F32, kind="ExternalOutput").ap()
        io["dbg_h2"] = nc.dram_tensor("dbg_h2", [128, 2 * TG], BF16, kind="ExternalOutput").ap()
        io["dbg_stg2"] = nc.dram_tensor("dbg_stg2", [128, 8], BF16, kind="ExternalOutput").ap()
        io["dbg_polls"] = nc.dram_tensor("dbg_polls", [1, 3], I32, kind="ExternalOutput").ap()
    io.update(l0ffn_const_decls(nc))
    io.update(gcst_decls(nc))
    io.update(l0ffn_layer_decls(nc, li))
    io.update(Exchange.input_decls(nc, 3))
    with nc.Block() as blk:
        sch = Sched(ctx)
        xg = Exchange(ctx, sch, 3)
        xg.emit_load(io)
        gc = GlobalCst(ctx, sch, io)
        cst = L0Consts(ctx, sch, io, gc)
        emit_l0ffn_layer(ctx, sch, xg, cst, io, li, 0)
        sch.flush(blk)
    ctx.st.close()
    return nc


def run_p2(inputs, yin_percore, li=0):
    nc = build_p2(li)
    in_maps = []
    for c in range(NCORE):
        d = {"yin": yin_percore[c], "xrows": xrows_host(c)}
        d.update(l0ffn_const_host(c))
        d.update(l0ffn_layer_host(inputs, li))
        in_maps.append(d)
    seed = int(np.random.default_rng().integers(2 ** 31))
    for c in range(NCORE):
        pair_rng = np.random.default_rng(seed + (c // 2))
        xgi = Exchange.__new__(Exchange)
        xgi.n_flags = 3
        in_maps[c].update(Exchange.host_inputs(xgi, c % 2, pair_rng))
    res = run_bass_kernel_spmd(nc, in_maps, core_ids=list(range(NCORE)))
    return res.results


def emit_postln(ctx, sch, cst, io_a, io_b, io_out, tagp):
    pb = ctx.pbanks
    m = ctx.mark()
    yg = [ctx.sb(f"{tagp}yg{c}", [128, TG], F32) for c in range(NCH)]
    xs = [ctx.sb(f"{tagp}xs{i}", [128, TG], F32) for i in range(4)]
    so = [ctx.sb(f"{tagp}so{i}", [128, TG], F32) for i in range(2)]
    scq = [ctx.sb(f"{tagp}scq{i}", [128, TG], BF16) for i in range(2)]
    srow = [ctx.sb(f"{tagp}srow{i}", [1, TG], F32) for i in range(6)]
    for gi in range(NTG):
        lo = gi * TG
        bsum, bsq = pb[6], pb[7]
        for c in range(NCH):
            xa = xs[2 * (c % 2)]
            xb_ = xs[2 * (c % 2) + 1]
            sch.add("sync", lambda q, c=c, lo=lo, xa=xa: q.dma_start(
                out=xa[:, :], in_=io_a[128 * c:128 * (c + 1), lo:lo + TG]),
                writes=(K(xa),), dma=True, tag=f"{tagp}la{gi}_{c}")
            sch.add("scalar", lambda q, c=c, lo=lo, xb_=xb_: q.dma_start(
                out=xb_[:, :], in_=io_b[128 * c:128 * (c + 1), lo:lo + TG]),
                writes=(K(xb_),), dma=True, tag=f"{tagp}lb{gi}_{c}")
            sch.add("gpsimd", lambda g, c=c, xa=xa, xb_=xb_: g.tensor_add(
                yg[c][:, :], xa[:, :], xb_[:, :]),
                reads=(K(xa), K(xb_)), writes=(K(yg[c]),), tag=f"{tagp}y{gi}_{c}")
            def em_sum(t, c=c, bsum=bsum):
                return t.matmul(bsum[0:1, :], cst.ones_f[:, 0:1], yg[c][:, :],
                                start=(c == 0), stop=(c == NCH - 1))
            sch.add("tensor", em_sum, reads=(K(yg[c]), K(cst.ones_f)), writes=(K(bsum),))
            sq = scq[c % 2]
            sch.add("scalar", lambda s, c=c, sq=sq: s.activation(
                out=sq[:, :], in_=yg[c][:, :], func=AF.Square),
                reads=(K(yg[c]),), writes=(K(sq),))
            def em_sq(t, c=c, sq=sq, bsq=bsq):
                return t.matmul(bsq[0:1, :], cst.ones_b[:, 0:1], sq[:, :],
                                start=(c == 0), stop=(c == NCH - 1))
            sch.add("tensor", em_sq, reads=(K(sq), K(cst.ones_b)), writes=(K(bsq),))
        sch.add("scalar", lambda s, bsum=bsum: s.activation(
            out=srow[0][0:1, :], in_=bsum[0:1, :], func=AF.Copy, scale=1.0 / D),
            reads=(K(bsum),), writes=(K(srow[0]),))
        sch.add("scalar", lambda s, bsq=bsq: s.activation(
            out=srow[1][0:1, :], in_=bsq[0:1, :], func=AF.Copy, scale=1.0 / D),
            reads=(K(bsq),), writes=(K(srow[1]),))
        sch.add("scalar", lambda s: s.activation(
            out=srow[2][0:1, :], in_=srow[0][0:1, :], func=AF.Square),
            reads=(K(srow[0]),), writes=(K(srow[2]),))
        sch.add("vector", lambda v: v.tensor_sub(srow[3][0:1, :], srow[1][0:1, :], srow[2][0:1, :]),
                reads=(K(srow[1]), K(srow[2])), writes=(K(srow[3]),))
        sch.add("vector", lambda v: v.tensor_scalar_add(srow[3][0:1, :], srow[3][0:1, :], 1e-5),
                reads=(K(srow[3]),), writes=(K(srow[3]),))
        sch.add("scalar", lambda s: s.activation(out=srow[4][0:1, :], in_=srow[3][0:1, :],
                                                 func=AF.Ln),
                reads=(K(srow[3]),), writes=(K(srow[4]),))
        sch.add("scalar", lambda s: s.activation(out=srow[5][0:1, :], in_=srow[4][0:1, :],
                                                 func=AF.Exp, scale=-0.5),
                reads=(K(srow[4]),), writes=(K(srow[5]),))
        sch.add("vector", lambda v: v.tensor_mul(srow[2][0:1, :], srow[0][0:1, :], srow[5][0:1, :]),
                reads=(K(srow[0]), K(srow[5])), writes=(K(srow[2]),))
        br, bm = pb[4], pb[5]
        sch.add("tensor", lambda t, br=br: t.matmul(
            br[:, :], cst.ones_f[0:1, :], srow[5][0:1, :], start=True, stop=True),
            reads=(K(cst.ones_f), K(srow[5])), writes=(K(br),))
        sch.add("tensor", lambda t, bm=bm: t.matmul(
            bm[:, :], cst.ones_f[0:1, :], srow[2][0:1, :], start=True, stop=True),
            reads=(K(cst.ones_f), K(srow[2])), writes=(K(bm),))
        for c in range(NCH):
            sv = so[c % 2]
            sch.add("vector", lambda v, c=c, br=br, sv=sv: v.tensor_mul(
                sv[:, :], yg[c][:, :], br[:, :]),
                reads=(K(yg[c]), K(br)), writes=(K(sv),))
            sch.add("vector", lambda v, bm=bm, sv=sv: v.tensor_sub(
                sv[:, :], sv[:, :], bm[:, :]),
                reads=(K(sv), K(bm)), writes=(K(sv),))
            sch.add("sync", lambda q, c=c, lo=lo, sv=sv: q.dma_start(
                out=io_out[128 * c:128 * (c + 1), lo:lo + TG], in_=sv[:, :]),
                reads=(K(sv),), writes=(f"{tagp}o{gi}_{c}",), dma=True,
                tag=f"{tagp}st{gi}_{c}")
    ctx.reset(m)
    sch.set_fence()


def build_p3(li=1):
    nc = bass.Bass("TRN2", target_bir_lowering=False)
    ctx = Ctx(nc)
    io = {}
    io["yin"] = nc.dram_tensor("yin", [D, T], F32, kind="ExternalInput").ap()
    io["s1"] = nc.dram_tensor("s1", [D, T], F32, kind="ExternalInput").ap()
    io["ymid"] = nc.dram_tensor("ymid_d", [D, T], F32, kind="ExternalOutput").ap()
    io["yout"] = nc.dram_tensor("yout", [D, T], F32, kind="ExternalOutput").ap()
    io["s2"] = nc.dram_tensor("s2", [D, T], F32, kind="ExternalOutput").ap()
    io["xrows"] = nc.dram_tensor("xrows", [128, 22], U32, kind="ExternalInput").ap()
    io.update(l0ffn_const_decls(nc))
    io.update(gcst_decls(nc))
    io.update(l0ffn_layer_decls(nc, li))
    io.update(Exchange.input_decls(nc, 3))
    with nc.Block() as blk:
        sch = Sched(ctx)
        xg = Exchange(ctx, sch, 3)
        xg.emit_load(io)
        gc = GlobalCst(ctx, sch, io)
        cst = L0Consts(ctx, sch, io, gc)
        emit_l0ffn_layer(ctx, sch, xg, cst, io, li, 0)
        emit_postln(ctx, sch, cst, io["s1"], io["yout"], io["s2"], "pl3")
        sch.flush(blk)
    ctx.st.close()
    return nc


def run_p3(inputs, y1_percore, s1_percore, li=1):
    nc = build_p3(li)
    in_maps = []
    for c in range(NCORE):
        d = {"yin": y1_percore[c], "s1": s1_percore[c], "xrows": xrows_host(c)}
        d.update(l0ffn_const_host(c))
        d.update(l0ffn_layer_host(inputs, li))
        in_maps.append(d)
    seed = int(np.random.default_rng().integers(2 ** 31))
    for c in range(NCORE):
        pair_rng = np.random.default_rng(seed + (c // 2))
        xgi = Exchange.__new__(Exchange)
        xgi.n_flags = 3
        in_maps[c].update(Exchange.host_inputs(xgi, c % 2, pair_rng))
    res = run_bass_kernel_spmd(nc, in_maps, core_ids=list(range(NCORE)))
    return res.results


NH1 = 16
LAT = 128


def p4_decls(nc, with_stream=True):
    d = {}
    if with_stream:
        d["s2i"] = nc.dram_tensor("s2i", [D, T], F32, kind="ExternalInput").ap()
    d["Wq1p"] = nc.dram_tensor("Wq1p", [D, D], F32, kind="ExternalInput").ap()
    d["Wkv1"] = nc.dram_tensor("Wkv1", [D, LAT], F32, kind="ExternalInput").ap()
    d["Wk1p"] = nc.dram_tensor("Wk1p", [LAT, 768], F32, kind="ExternalInput").ap()
    d["Wv1"] = nc.dram_tensor("Wv1", [LAT, 512], F32, kind="ExternalInput").ap()
    d["Wo1p"] = nc.dram_tensor("Wo1p", [768, D], F32, kind="ExternalInput").ap()
    d["qg1"] = nc.dram_tensor("qg1", [128, 1], F32, kind="ExternalInput").ap()
    d["kg1"] = nc.dram_tensor("kg1", [128, 1], F32, kind="ExternalInput").ap()
    d["xrows4"] = nc.dram_tensor("xrows4", [128, 3], U32, kind="ExternalInput").ap()
    d["vkill4"] = nc.dram_tensor("vkill4", [128, 32], F32, kind="ExternalInput").ap()
    d["qsel32"] = nc.dram_tensor("qsel32", [128, 2], F32, kind="ExternalInput").ap()
    d["hpat3"] = nc.dram_tensor("hpat3", [128, 3], F32, kind="ExternalInput").ap()
    d["bpat3"] = nc.dram_tensor("bpat3", [3, 128], F32, kind="ExternalInput").ap()
    d["e3x"] = nc.dram_tensor("e3x", [3, 3], F32, kind="ExternalInput").ap()
    d["maskd0"] = nc.dram_tensor("maskd0", [128, 640], F32, kind="ExternalInput").ap()
    d["maskd1"] = nc.dram_tensor("maskd1", [128, 640], F32, kind="ExternalInput").ap()
    d["mkill4"] = nc.dram_tensor("mkill4", [128, 128], F32, kind="ExternalInput").ap()
    return d


def p4_host(inputs, core):
    h = core % 2
    Wq = np.asarray(inputs["L1_attn_Wq"], np.float32)
    Wk = np.asarray(inputs["L1_attn_Wk"], np.float32)
    Wo = np.asarray(inputs["L1_attn_Wo"], np.float32)
    Wk1p = np.zeros((LAT, 768), np.float32)
    Wo1p = np.zeros((768, D), np.float32)
    for H in range(NH1):
        cc, r = H // 3, H % 3
        Wk1p[:, 128 * cc + 32 * r:128 * cc + 32 * r + 32] = Wk[:, 32 * H:32 * H + 32]
        Wo1p[128 * cc + 32 * r:128 * cc + 32 * r + 32, :] = Wo[64 * H:64 * H + 32, :]
    qg = np.asarray(inputs["L1_attn_q_g"], np.float32)
    kg = np.asarray(inputs["L1_attn_k_g"], np.float32)
    kgcol = np.zeros((128, 1), np.float32)
    for r in range(3):
        kgcol[32 * r:32 * r + 32, 0] = kg
    p = np.arange(128, dtype=np.uint32)
    xrows4 = np.stack([128 * h + p, p, 128 + p], axis=1).astype(np.uint32)
    vkill4 = np.ones((128, 32), np.float32)
    if h == 0:
        vkill4[:, 16:] = 0.0
    qsel32 = np.zeros((128, 2), np.float32)
    qsel32[0:32, 0] = 1.0
    qsel32[64:96, 1] = 1.0
    hpat3 = np.zeros((128, 3), np.float32)
    bpat3 = np.zeros((3, 128), np.float32)
    for r in range(3):
        hpat3[32 * r:32 * r + 32, r] = 1.0
        bpat3[r, 32 * r:32 * r + 32] = 1.0
    jr = np.arange(128)[:, None]
    qq = np.arange(128)[None, :]
    tri = np.concatenate([np.zeros((128, 512), np.float32),
                          (qq >= jr).astype(np.float32)], axis=1)
    ones = np.ones((128, 640), np.float32)
    mkill = np.ones((128, 128), np.float32)
    if h == 0:
        for qc in range(4):
            for kt in range(32):
                if kt > 4 * qc + 3:
                    mkill[:, 32 * qc + kt] = 0.0
    return {
        "s2i": None,
        "Wq1p": np.ascontiguousarray(Wq),
        "Wkv1": np.asarray(inputs["L1_attn_Wkv"], np.float32),
        "Wk1p": Wk1p,
        "Wv1": np.asarray(inputs["L1_attn_Wv"], np.float32),
        "Wo1p": Wo1p,
        "qg1": np.concatenate([qg, qg])[:, None].astype(np.float32),
        "kg1": kgcol,
        "xrows4": xrows4, "vkill4": vkill4, "qsel32": qsel32,
        "hpat3": hpat3, "bpat3": bpat3, "e3x": np.eye(3, dtype=np.float32),
        "maskd0": tri if h == 0 else ones,
        "maskd1": tri if h == 1 else ones,
        "mkill4": mkill,
    }


def emit_p4(ctx, sch, xg, cst, io, xgk):
    nc = ctx.nc
    pb = ctx.pbanks
    TL = 2 * T

    XL = nc.dram_tensor("XLshare", [256, T], BF16, addr_space="Shared")

    m0 = ctx.mark()
    lat = ctx.sb("p4lat", [128, TL], BF16)
    kn = ctx.sb("p4kn", [128, 6 * TL], BF16)
    qsc = ctx.sb("p4qsc", [128, 6 * T], BF16)
    vbuf = [ctx.sb(f"p4v{t}", [128, 16 * 33], BF16) for t in range(32)]
    wq = ctx.sb("p4wq", [128, 8, D], BF16)
    wkv = ctx.sb("p4wkv", [128, 8, LAT], BF16)
    wk = ctx.sb("p4wk", [128, 768], BF16)
    wv = ctx.sb("p4wv", [128, 512], BF16)
    wo6 = ctx.sb("p4wo", [128, 6, D], BF16)
    qg = ctx.sb("p4qg", [128, 1], F32)
    kg = ctx.sb("p4kg", [128, 1], F32)
    xrows = ctx.sb("p4xr", [128, 3], U32)
    vkill = ctx.sb("p4vk", [128, 32], F32)
    qsel = ctx.sb("p4qsel", [128, 2], F32)
    hpat3 = ctx.sb("p4hp3", [128, 3], F32)
    bpat3 = ctx.sb("p4bp3", [3, 128], F32)
    maskd = [ctx.sb(f"p4md{i}", [128, 640], BF16) for i in range(2)]
    mkill = ctx.sb("p4mk", [128, 128], F32)
    qtab = ctx.sb("p4qtab", [2, 8], F32)
    ktab = ctx.sb("p4ktab", [3, 6], F32)
    qrow = ctx.sb("p4qrow", [1, 16], F32)
    krow = ctx.sb("p4krow", [1, 18], F32)
    bcol = ctx.sb("p4bcol", [128, 16], F32)
    for nm, buf, qd in [("Wq1p", wq, 1), ("Wkv1", wkv, 1), ("Wk1p", wk, 0),
                        ("Wv1", wv, 0), ("Wo1p", wo6, 1)]:
        if qd:
            sch.add("gpsimd", lambda g, nm=nm, buf=buf: g.dma_start(
                out=buf[:, :, :], in_=io[nm].rearrange("(c p) n -> p c n", p=128)),
                writes=(K(buf),), dma=True, tag=nm)
        else:
            sch.add("gpsimd", lambda g, nm=nm, buf=buf: g.dma_start(
                out=buf[:, :], in_=io[nm]), writes=(K(buf),), dma=True, tag=nm)
    for nm, buf in [("qg1", qg), ("kg1", kg), ("xrows4", xrows), ("vkill4", vkill),
                    ("qsel32", qsel), ("hpat3", hpat3), ("bpat3", bpat3)]:
        sch.add("sync", lambda q, nm=nm, buf=buf: q.dma_start(out=buf[:, :], in_=io[nm]),
                writes=(K(buf),), dma=True, tag=nm)
    for i in range(2):
        sch.add("gpsimd", lambda g, i=i: g.dma_start(out=maskd[i][:, :], in_=io[f"maskd{i}"]),
                writes=(K(maskd[i]),), dma=True)
    sch.add("sync", lambda q: q.dma_start(out=mkill[:, :], in_=io["mkill4"]),
            writes=(K(mkill),), dma=True)
    for t in range(32):
        sch.add("gpsimd", lambda g, t=t: g.memset(vbuf[t][:, :], 1.0),
                writes=(K(vbuf[t]),))
    sch.add("gpsimd", lambda g: g.memset(qtab[:, :], 0.0), writes=(K(qtab),))
    sch.add("gpsimd", lambda g: g.memset(ktab[:, :], 0.0), writes=(K(ktab),))

    m1 = ctx.mark()
    s2b = ctx.sb("p4s2b", [128, 8, T], BF16)
    scr = [ctx.sb(f"p4scr{i}", [128, TG], F32) for i in range(6)]
    rq2 = ctx.sb("p4rq2", [3, TG], F32)
    r21 = ctx.sb("p4r21", [3, 1], F32)

    for c in range(NCH):
        sch.add("gpsimd", lambda g, c=c: g.dma_start(
            out=s2b[:, c, :], in_=io["s2i"][128 * c:128 * (c + 1), :]),
            writes=(K(s2b),), dma=True, tag=f"s2ld{c}")

    latmy = ctx.sb("p4latmy", [128, T], BF16)
    for gi in range(NTG):
        lo = gi * TG
        bl = pb[0]
        sch.add("tensor", MMG_(pb, bl[:, :], [
            (lambda t, d=d: wkv[:, d, :],
             lambda t, d=d, lo=lo: s2b[:, d, lo:lo + TG]) for d in range(NCH)]),
            reads=(K(wkv), K(s2b)), writes=(K(bl),), tag=f"lat{gi}")
        sch.add("vector", lambda v, lo=lo, bl=bl: v.tensor_copy(latmy[:, lo:lo + TG], bl[:, :]),
                reads=(K(bl),), writes=(K(latmy),))
    ptk = sch.add("gpsimd", lambda g: g.indirect_dma_start(
        out=XL.ap(), out_offset=IOA(ap=xrows[:, 0:1], axis=0),
        in_=latmy[:, :], in_offset=None),
        reads=(K(latmy), K(xrows)), writes=("XL",), dma=True, tag="e6_push")
    xg.push(xgk, [ptk])

    for gi in range(NTG):
        lo = gi * TG
        for c in range(NCH):
            bq = pb[1]
            sch.add("tensor", MMG_(pb, bq[:, :], [
                (lambda t, d=d, c=c: wq[:, d, 128 * c:128 * (c + 1)],
                 lambda t, d=d, lo=lo: s2b[:, d, lo:lo + TG]) for d in range(NCH)]),
                reads=(K(wq), K(s2b)), writes=(K(bq),), tag=f"qmm{gi}_{c}")
            sch.add("scalar", lambda s, bq=bq: s.activation(
                out=scr[0][:, :], in_=bq[:, :], func=AF.Square),
                reads=(K(bq),), writes=(K(scr[0]),))
            sch.add("tensor", lambda t: t.matmul(
                pb[2][0:2, :], cst.hpat2[:, :], scr[0][:, :], start=True, stop=True),
                reads=(K(cst.hpat2), K(scr[0])), writes=(K(pb[2]),))
            sch.add("vector", lambda v: v.tensor_scalar(
                out=rq2[0:2, :], in0=pb[2][0:2, :], scalar1=1.0 / 64, scalar2=1e-6,
                op0=ALU.mult, op1=ALU.add),
                reads=(K(pb[2]),), writes=(K(rq2),))
            sch.add("scalar", lambda s: s.activation(out=rq2[0:2, :], in_=rq2[0:2, :],
                                                     func=AF.Ln),
                    reads=(K(rq2),), writes=(K(rq2),))
            sch.add("scalar", lambda s: s.activation(out=rq2[0:2, :], in_=rq2[0:2, :],
                                                     func=AF.Exp, scale=-0.5),
                    reads=(K(rq2),), writes=(K(rq2),))
            sch.add("tensor", lambda t: t.matmul(
                pb[3][:, :], cst.bpat2[:, :], rq2[0:2, :], start=True, stop=True),
                reads=(K(cst.bpat2), K(rq2)), writes=(K(pb[3]),))
            sch.add("scalar", lambda s: s.copy(scr[1][:, :], pb[3][:, :]),
                    reads=(K(pb[3]),), writes=(K(scr[1]),))
            sch.add("vector", lambda v, bq=bq: v.scalar_tensor_tensor(
                out=scr[2][:, :], in0=bq[:, :], scalar=qg[:, 0:1],
                in1=scr[1][:, :], op0=ALU.mult, op1=ALU.mult),
                reads=(K(bq), K(scr[1]), K(qg)), writes=(K(scr[2]),), tag=f"qn{gi}_{c}")
            for par in range(2):
                H = 2 * c + par
                cc, r = H // 3, H % 3
                sch.add("vector", lambda v, par=par, cc=cc, r=r, lo=lo: v.tensor_copy(
                    qsc[32 * r:32 * r + 32, T * cc + lo:T * cc + lo + TG],
                    scr[2][64 * par:64 * par + 32, :]),
                    reads=(K(scr[2]),), writes=(K(qsc),), tag=f"qx{gi}_{c}_{par}")
            sch.add("scalar", lambda s: s.activation(
                out=scr[3][:, :], in_=scr[2][:, :], func=AF.Square),
                reads=(K(scr[2]),), writes=(K(scr[3]),))
            sch.add("tensor", lambda t: t.matmul(
                pb[2][0:2, :], qsel[:, :], scr[3][:, :], start=True, stop=True),
                reads=(K(qsel), K(scr[3])), writes=(K(pb[2]),))
            sch.add("vector", lambda v: v.reduce_max(rq2[0:2, 0:1], pb[2][0:2, :],
                                                     axis=mybir.AxisListType.X),
                    reads=(K(pb[2]),), writes=(K(rq2),))
            sch.add("vector", lambda v, c=c: v.tensor_max(
                qtab[0:2, c:c + 1], qtab[0:2, c:c + 1], rq2[0:2, 0:1]),
                reads=(K(qtab), K(rq2)), writes=(K(qtab),), tag=f"qmax{gi}_{c}")

    pk = xg.poll(xgk)
    for blkk in range(2):
        sch.add("gpsimd", lambda g, blkk=blkk: g.indirect_dma_start(
            out=lat[:, T * blkk:T * (blkk + 1)], out_offset=None,
            in_=XL.ap(), in_offset=IOA(ap=xrows[:, 1 + blkk:2 + blkk], axis=0)),
            reads=("XL", K(xrows)), writes=(K(lat),), dma=True,
            extra_deps=[pk], tag=f"e6_pull{blkk}")

    for kg8 in range(8):
        ko = kg8 * TG
        for cc in range(6):
            bk = pb[1]
            sch.add("tensor", lambda t, cc=cc, ko=ko, bk=bk: t.matmul(
                bk[:, :], wk[:, 128 * cc:128 * (cc + 1)], lat[:, ko:ko + TG],
                start=True, stop=True),
                reads=(K(wk), K(lat)), writes=(K(bk),), tag=f"kmm{kg8}_{cc}")
            sch.add("scalar", lambda s, bk=bk: s.activation(
                out=scr[0][:, :], in_=bk[:, :], func=AF.Square),
                reads=(K(bk),), writes=(K(scr[0]),))
            sch.add("tensor", lambda t: t.matmul(
                pb[2][0:3, :], hpat3[:, :], scr[0][:, :], start=True, stop=True),
                reads=(K(hpat3), K(scr[0])), writes=(K(pb[2]),))
            sch.add("vector", lambda v: v.tensor_scalar(
                out=rq2[0:3, :], in0=pb[2][0:3, :], scalar1=1.0 / 32, scalar2=1e-6,
                op0=ALU.mult, op1=ALU.add),
                reads=(K(pb[2]),), writes=(K(rq2),))
            sch.add("scalar", lambda s: s.activation(out=rq2[0:3, :], in_=rq2[0:3, :],
                                                     func=AF.Ln),
                    reads=(K(rq2),), writes=(K(rq2),))
            sch.add("scalar", lambda s: s.activation(out=rq2[0:3, :], in_=rq2[0:3, :],
                                                     func=AF.Exp, scale=-0.5),
                    reads=(K(rq2),), writes=(K(rq2),))
            sch.add("tensor", lambda t: t.matmul(
                pb[3][:, :], bpat3[:, :], rq2[0:3, :], start=True, stop=True),
                reads=(K(bpat3), K(rq2)), writes=(K(pb[3]),))
            sch.add("scalar", lambda s: s.copy(scr[1][:, :], pb[3][:, :]),
                    reads=(K(pb[3]),), writes=(K(scr[1]),))
            sch.add("vector", lambda v, cc=cc, ko=ko, bk=bk: v.scalar_tensor_tensor(
                out=kn[:, TL * cc + ko:TL * cc + ko + TG], in0=bk[:, :], scalar=kg[:, 0:1],
                in1=scr[1][:, :], op0=ALU.mult, op1=ALU.mult),
                reads=(K(bk), K(scr[1]), K(kg)), writes=(K(kn),), tag=f"knw{kg8}_{cc}")
            sch.add("scalar", lambda s, cc=cc, ko=ko: s.activation(
                out=scr[3][:, :], in_=kn[:, TL * cc + ko:TL * cc + ko + TG], func=AF.Square),
                reads=(K(kn),), writes=(K(scr[3]),))
            sch.add("tensor", lambda t: t.matmul(
                pb[2][0:3, :], hpat3[:, :], scr[3][:, :], start=True, stop=True),
                reads=(K(hpat3), K(scr[3])), writes=(K(pb[2]),))
            sch.add("vector", lambda v: v.reduce_max(r21[0:3, 0:1], pb[2][0:3, :],
                                                     axis=mybir.AxisListType.X),
                    reads=(K(pb[2]),), writes=(K(r21),))
            sch.add("vector", lambda v, cc=cc: v.tensor_max(
                ktab[0:3, cc:cc + 1], ktab[0:3, cc:cc + 1], r21[0:3, 0:1]),
                reads=(K(ktab), K(r21)), writes=(K(ktab),), tag=f"kmax{kg8}_{cc}")
        for lt in range(4):
            ks = ko + 128 * lt
            bv = pb[5]
            sch.add("tensor", lambda t, ks=ks, bv=bv: t.matmul(
                bv[:, :], lat[:, ks:ks + 128], wv[:, :], start=True, stop=True),
                reads=(K(lat), K(wv)), writes=(K(bv),), tag=f"vmm{kg8}_{lt}")
            ti = 4 * kg8 + lt
            sch.add("vector", lambda v, bv=bv, ti=ti: v.tensor_copy(
                vbuf[ti].rearrange("p (h n) -> p h n", n=33)[:, :, 0:32], bv[:, :]),
                reads=(K(bv),), writes=(K(vbuf[ti]),))
            sch.add("vector", lambda g, ti=ti: g.tensor_scalar_mul(
                vbuf[ti][:, :], vbuf[ti][:, :], vkill[:, ti:ti + 1]),
                reads=(K(vbuf[ti]), K(vkill)), writes=(K(vbuf[ti]),), tag=f"vk{ti}")

    for tb, rr in ((qtab, 2), (ktab, 3)):
        sch.add("scalar", lambda s, tb=tb, rr=rr: s.activation(out=tb[0:rr, :], in_=tb[0:rr, :], func=AF.Ln),
                reads=(K(tb),), writes=(K(tb),))
        sch.add("scalar", lambda s, tb=tb, rr=rr: s.activation(out=tb[0:rr, :], in_=tb[0:rr, :], func=AF.Exp, scale=0.5),
                reads=(K(tb),), writes=(K(tb),))
    for r in range(2):
        sch.add("tensor", lambda t, r=r: t.matmul(
            pb[2][0:1, 0:8], cst.e01[:, r:r + 1], qtab[0:2, :], start=True, stop=True),
            reads=(K(cst.e01), K(qtab)), writes=(K(pb[2]),))
        sch.add("vector", lambda v, r=r: v.tensor_copy(
            qrow.rearrange("a (c two) -> a c two", two=2)[0:1, :, r], pb[2][0:1, 0:8]),
            reads=(K(pb[2]),), writes=(K(qrow),))
    e3 = ctx.sb("p4e3", [3, 3], F32)
    sch.add("sync", lambda q: q.dma_start(out=e3[:, :], in_=io["e3x"]),
            writes=(K(e3),), dma=True)
    for r in range(3):
        sch.add("tensor", lambda t, r=r: t.matmul(
            pb[2][0:1, 0:6], e3[:, r:r + 1], ktab[0:3, :], start=True, stop=True),
            reads=(K(e3), K(ktab)), writes=(K(pb[2]),))
        sch.add("vector", lambda v, r=r: v.tensor_copy(
            krow.rearrange("a (c three) -> a c three", three=3)[0:1, :, r],
            pb[2][0:1, 0:6]),
            reads=(K(pb[2]),), writes=(K(krow),))
    sch.add("vector", lambda v: v.tensor_mul(qrow[0:1, :], qrow[0:1, :], krow[0:1, 0:16]),
            reads=(K(qrow), K(krow)), writes=(K(qrow),))
    sch.add("vector", lambda v: v.tensor_scalar(
        out=qrow[0:1, :], in0=qrow[0:1, :], scalar1=-1.0 / 8, scalar2=45.0,
        op0=ALU.mult, op1=ALU.add),
        reads=(K(qrow),), writes=(K(qrow),))
    sch.add("tensor", lambda t: t.matmul(
        pb[3][:, 0:16], cst.ones_f[0:1, :], qrow[0:1, :], start=True, stop=True),
        reads=(K(cst.ones_f), K(qrow)), writes=(K(pb[3]),))
    sch.add("scalar", lambda s: s.copy(bcol[:, :], pb[3][:, 0:16]),
            reads=(K(pb[3]),), writes=(K(bcol),))

    ctx.reset(m1)
    sch.set_fence()
    pT = [ctx.sb(f"p4pT{i}", [128, TG], BF16) for i in range(4)]
    ofm = ctx.sb("p4ofm", [128, 6 * TG], BF16)
    recs = ctx.sb("p4recs", [1, TG], F32)
    rbc = ctx.sb("p4rbc", [128, TG], F32)
    so = [ctx.sb(f"p4so{i}", [128, TG], F32) for i in range(2)]

    for qc in range(NTG):
        nkt = 16 + 4 * (qc + 1)
        sch.add("gpsimd", lambda g: g.memset(ofm[:, :], 0.0), writes=(K(ofm),))
        for H in range(NH1):
            cc, r = H // 3, H % 3
            bO = pb[4 + H % 2]
            for kt in range(nkt):
                bS = pb[kt % 4]
                def em_s(t, H=H, cc=cc, r=r, kt=kt, qc=qc, bS=bS):
                    return t.matmul(
                        bS[:, :],
                        kn[32 * r:32 * r + 32, TL * cc + 128 * kt:TL * cc + 128 * kt + 128],
                        qsc[32 * r:32 * r + 32, T * cc + TG * qc:T * cc + TG * (qc + 1)],
                        start=True, stop=True)
                sch.add("tensor", em_s, reads=(K(kn), K(qsc)), writes=(K(bS),),
                        tag=f"S{qc}_{H}_{kt}")
                def em_e(s, H=H, kt=kt, bS=bS):
                    return s.activation(
                        out=pT[kt % 4][:, :], in_=bS[:, :], func=AF.Exp,
                        bias=bcol[:, H:H + 1], scale=0.125)
                sch.add("scalar", em_e, reads=(K(bS), K(bcol)),
                        writes=(K(pT[kt % 4]),), tag=f"E{qc}_{H}_{kt}")
                if 4 * qc + 3 < kt < 16:
                    def em_k(g, kt=kt, qc=qc):
                        return g.tensor_scalar_mul(
                            pT[kt % 4][:, :], pT[kt % 4][:, :],
                            mkill[:, 32 * qc + kt:32 * qc + kt + 1])
                    sch.add("vector", em_k, reads=(K(pT[kt % 4]), K(mkill)),
                            writes=(K(pT[kt % 4]),), tag=f"MK{qc}_{H}_{kt}")
                sub0 = 128 * kt - TG * qc
                sub1 = 128 * kt - T - TG * qc
                for mi, sub in ((0, sub0), (1, sub1)):
                    if 0 <= sub < TG:
                        def em_m(g, mi=mi, sub=sub, kt=kt):
                            return g.tensor_mul(
                                pT[kt % 4][:, 0:sub + 128], pT[kt % 4][:, 0:sub + 128],
                                maskd[mi][:, 512 - sub:640])
                        sch.add("vector", em_m,
                                reads=(K(pT[kt % 4]), K(maskd[mi])),
                                writes=(K(pT[kt % 4]),), tag=f"M{qc}_{H}_{kt}_{mi}")
                def em_pv(t, H=H, kt=kt, bO=bO, nkt=nkt):
                    return t.matmul(
                        bO[0:33, :], vbuf[kt][:, 33 * H:33 * H + 33],
                        pT[kt % 4][:, :], start=(kt == 0), stop=(kt == nkt - 1))
                sch.add("tensor", em_pv, reads=(K(vbuf[kt]), K(pT[kt % 4])),
                        writes=(K(bO),), tag=f"PV{qc}_{H}_{kt}")
            sch.add("vector", lambda v, bO=bO: v.reciprocal(recs[0:1, :], bO[32:33, :]),
                    reads=(K(bO),), writes=(K(recs),))
            sch.add("tensor", lambda t: t.matmul(
                pb[6][0:32, :], cst.ones_f[0:1, 0:32], recs[0:1, :], start=True, stop=True),
                reads=(K(cst.ones_f), K(recs)), writes=(K(pb[6]),))
            sch.add("scalar", lambda s: s.copy(rbc[0:32, :], pb[6][0:32, :]),
                    reads=(K(pb[6]),), writes=(K(rbc),))
            sch.add("vector", lambda v, cc=cc, r=r, bO=bO: v.tensor_mul(
                ofm[32 * r:32 * r + 32, TG * cc:TG * (cc + 1)], bO[0:32, :], rbc[0:32, :]),
                reads=(K(bO), K(rbc)), writes=(K(ofm),), tag=f"OF{qc}_{H}")
        for oc in range(NCH):
            bY = pb[6 + oc % 2]
            sch.add("tensor", MMG_(pb, bY[:, :], [
                (lambda t, cc=cc2, oc=oc: wo6[:, cc, 128 * oc:128 * (oc + 1)],
                 lambda t, cc=cc2: ofm[:, TG * cc:TG * (cc + 1)]) for cc2 in range(6)]),
                reads=(K(wo6), K(ofm)), writes=(K(bY),), tag=f"WO{qc}_{oc}")
            sv = so[oc % 2]
            sch.add("vector", lambda v, bY=bY, sv=sv: v.tensor_copy(sv[:, :], bY[:, :]),
                    reads=(K(bY),), writes=(K(sv),))
            sch.add("sync", lambda q, oc=oc, qc=qc, sv=sv: q.dma_start(
                out=io["aout"][128 * oc:128 * (oc + 1), TG * qc:TG * (qc + 1)], in_=sv[:, :]),
                reads=(K(sv),), writes=(f"aout{qc}_{oc}",), dma=True,
                tag=f"ast{qc}_{oc}")
    ctx.reset(m0)
    sch.set_fence()


def build_p4():
    nc = bass.Bass("TRN2", target_bir_lowering=False)
    ctx = Ctx(nc)
    io = p4_decls(nc)
    io["aout"] = nc.dram_tensor("aout", [D, T], F32, kind="ExternalOutput").ap()
    io["s3"] = nc.dram_tensor("s3", [D, T], F32, kind="ExternalOutput").ap()
    io["cst_hpat2"] = nc.dram_tensor("cst_hpat2", [128, 2], F32, kind="ExternalInput").ap()
    io["cst_bpat2"] = nc.dram_tensor("cst_bpat2", [2, 128], F32, kind="ExternalInput").ap()
    io["cst_e01"] = nc.dram_tensor("cst_e01", [2, 2], F32, kind="ExternalInput").ap()
    io.update(Exchange.input_decls(nc, 1))

    class MiniCst:
        pass
    with nc.Block() as blk:
        sch = Sched(ctx)
        xg = Exchange(ctx, sch, 1)
        xg.emit_load(io)
        cst = MiniCst()
        cst.ones_f = ctx.sb("c_ones_f", [128, 128], F32)
        cst.ones_b = ctx.sb("c_ones_b", [128, 128], BF16)
        cst.hpat2 = ctx.sb("hpat2", [128, 2], F32)
        cst.bpat2 = ctx.sb("bpat2", [2, 128], F32)
        cst.e01 = ctx.sb("e01", [2, 2], F32)
        sch.add("gpsimd", lambda g: g.memset(cst.ones_f[:, :], 1.0), writes=(K(cst.ones_f),))
        sch.add("gpsimd", lambda g: g.memset(cst.ones_b[:, :], 1.0), writes=(K(cst.ones_b),))
        for nm, buf in [("cst_hpat2", cst.hpat2), ("cst_bpat2", cst.bpat2),
                        ("cst_e01", cst.e01)]:
            sch.add("sync", lambda q, nm=nm, buf=buf: q.dma_start(out=buf[:, :], in_=io[nm]),
                    writes=(K(buf),), dma=True)
        emit_p4(ctx, sch, xg, cst, io, 0)
        emit_postln(ctx, sch, cst, io["s2i"], io["aout"], io["s3"], "pl4")
        sch.flush(blk)
    ctx.st.close()
    return nc


def run_p4(inputs, s2_percore):
    nc = build_p4()
    in_maps = []
    for c in range(NCORE):
        d = p4_host(inputs, c)
        d["s2i"] = s2_percore[c]
        d["cst_hpat2"] = l0ffn_const_host(c)["cst_hpat2"]
        d["cst_bpat2"] = l0ffn_const_host(c)["cst_bpat2"]
        d["cst_e01"] = l0ffn_const_host(c)["cst_e01"]
        in_maps.append(d)
    seed = int(np.random.default_rng().integers(2 ** 31))
    for c in range(NCORE):
        pair_rng = np.random.default_rng(seed + (c // 2))
        xgi = Exchange.__new__(Exchange)
        xgi.n_flags = 1
        in_maps[c].update(Exchange.host_inputs(xgi, c % 2, pair_rng))
    res = run_bass_kernel_spmd(nc, in_maps, core_ids=list(range(NCORE)))
    return res.results


def p5_decls(nc, with_stream=True):
    d = {}
    if with_stream:
        d["s3i"] = nc.dram_tensor("s3i", [D, T], F32, kind="ExternalInput").ap()
    d["Wr5T"] = nc.dram_tensor("Wr5T", [D, D], F32, kind="ExternalInput").ap()
    d["Wk5T"] = nc.dram_tensor("Wk5T", [D, D], F32, kind="ExternalInput").ap()
    d["Wv5T"] = nc.dram_tensor("Wv5T", [D, D], F32, kind="ExternalInput").ap()
    d["murc"] = nc.dram_tensor("murc", [128, 8], F32, kind="ExternalInput").ap()
    d["murc1"] = nc.dram_tensor("murc1", [128, 8], F32, kind="ExternalInput").ap()
    d["mukc"] = nc.dram_tensor("mukc", [128, 8], F32, kind="ExternalInput").ap()
    d["mukc1"] = nc.dram_tensor("mukc1", [128, 8], F32, kind="ExternalInput").ap()
    d["xrows5"] = nc.dram_tensor("xrows5", [128, 2], U32, kind="ExternalInput").ap()
    d["hscale5"] = nc.dram_tensor("hscale5", [128, 1], F32, kind="ExternalInput").ap()
    return d


def p5_host(inputs, core):
    h = core % 2
    pack8 = lambda v: np.asarray(v, np.float32).reshape(8, 128).T.copy()
    mur = np.asarray(inputs["L1_ffn_mu_r"], np.float32)
    muk = np.asarray(inputs["L1_ffn_mu_k"], np.float32)
    p = np.arange(128, dtype=np.uint32)
    return {
        "Wr5T": np.ascontiguousarray(np.asarray(inputs["L1_ffn_W_r"], np.float32).T),
        "Wk5T": np.ascontiguousarray(np.asarray(inputs["L1_ffn_W_k"], np.float32).T),
        "Wv5T": np.ascontiguousarray(np.asarray(inputs["L1_ffn_W_v"], np.float32).T),
        "murc": pack8(mur), "murc1": pack8(1.0 - mur),
        "mukc": pack8(muk), "mukc1": pack8(1.0 - muk),
        "xrows5": np.stack([128 * h + p, 128 * (1 - h) + p], axis=1).astype(np.uint32),
        "hscale5": np.full((128, 1), float(h), np.float32),
    }


def emit_p5(ctx, sch, cst, xg, io, xgk):
    nc = ctx.nc
    pb = ctx.pbanks
    EXT = 2049
    X5 = nc.dram_tensor("X5share", [256, 8], BF16, addr_space="Shared")

    m0 = ctx.mark()
    s3e = ctx.sb("p5s3e", [128, 8 * EXT], BF16)
    wr = ctx.sb("p5wr", [128, 8, D], BF16)
    wk_ = ctx.sb("p5wk", [128, 8, D], BF16)
    wv_ = ctx.sb("p5wv", [128, 8, D], BF16)
    murc = ctx.sb("p5mur", [128, 8], F32)
    murc1 = ctx.sb("p5mur1", [128, 8], F32)
    mukc = ctx.sb("p5muk", [128, 8], F32)
    mukc1 = ctx.sb("p5muk1", [128, 8], F32)
    xrows = ctx.sb("p5xr", [128, 2], U32)
    hscale = ctx.sb("p5hs", [128, 1], F32)
    stg = ctx.sb("p5stg", [128, 8], BF16)
    stgp = ctx.sb("p5stgp", [128, 8], BF16)
    for nm, buf in [("Wr5T", wr), ("Wk5T", wk_), ("Wv5T", wv_)]:
        sch.add("gpsimd", lambda g, nm=nm, buf=buf: g.dma_start(
            out=buf[:, :, :], in_=io[nm].rearrange("(c p) n -> p c n", p=128)),
            writes=(K(buf),), dma=True, tag=nm)
    for nm, buf in [("murc", murc), ("murc1", murc1), ("mukc", mukc),
                    ("mukc1", mukc1), ("xrows5", xrows), ("hscale5", hscale)]:
        sch.add("sync", lambda q, nm=nm, buf=buf: q.dma_start(out=buf[:, :], in_=io[nm]),
                writes=(K(buf),), dma=True, tag=nm)
    for c in range(NCH):
        sch.add("gpsimd", lambda g, c=c: g.dma_start(
            out=s3e[:, EXT * c + 1:EXT * c + 1 + T],
            in_=io["s3i"][128 * c:128 * (c + 1), :]),
            writes=(K(s3e),), dma=True, tag=f"s3ld{c}")
    sch.add("vector", lambda v: v.tensor_copy(
        stgp[:, :], s3e.rearrange("p (c n) -> p c n", n=EXT)[:, :, 2048]),
        reads=(K(s3e),), writes=(K(stgp),))
    ptk = sch.add("gpsimd", lambda g: g.indirect_dma_start(
        out=X5.ap(), out_offset=IOA(ap=xrows[:, 0:1], axis=0),
        in_=stgp[:, :], in_offset=None),
        reads=(K(stgp), K(xrows)), writes=("X5",), dma=True, tag="e7_push")
    xg.push(xgk, [ptk])
    pk = xg.poll(xgk)
    sch.add("gpsimd", lambda g: g.indirect_dma_start(
        out=stg[:, :], out_offset=None,
        in_=X5.ap(), in_offset=IOA(ap=xrows[:, 1:2], axis=0)),
        reads=("X5", K(xrows)), writes=(K(stg),), dma=True, extra_deps=[pk],
        tag="e7_pull")
    sch.add("gpsimd", lambda g: g.tensor_scalar_mul(stg[:, :], stg[:, :], hscale[:, 0:1]),
            reads=(K(stg), K(hscale)), writes=(K(stg),))
    sch.add("vector", lambda v: v.tensor_copy(
        s3e.rearrange("p (c n) -> p c n", n=EXT)[:, :, 0], stg[:, :]),
        reads=(K(stg),), writes=(K(s3e),))

    m1 = ctx.mark()
    xrb = ctx.sb("p5xrb", [128, 8 * TG], BF16)
    xkb = ctx.sb("p5xkb", [128, 8 * TG], BF16)
    rb = ctx.sb("p5rb", [128, 8 * TG], BF16)
    kab = ctx.sb("p5kab", [128, 8 * TG], BF16)
    scr = [ctx.sb(f"p5scr{i}", [128, TG], F32) for i in range(4)]
    scb = ctx.sb("p5scb", [128, TG], BF16)
    fo = [ctx.sb(f"p5fo{i}", [128, TG], F32) for i in range(2)]

    for gi in range(NTG):
        lo = gi * TG
        for c in range(NCH):
            xs_ = s3e[:, EXT * c + 1 + lo:EXT * c + 1 + lo + TG]
            xp_ = s3e[:, EXT * c + lo:EXT * c + lo + TG]
            sch.add("vector", lambda v, c=c, xp_=xp_: v.tensor_scalar_mul(
                scb[:, :], xp_, murc1[:, c:c + 1]),
                reads=(K(s3e), K(murc1)), writes=(K(scb),))
            sch.add("vector", lambda v, c=c, xs_=xs_: v.scalar_tensor_tensor(
                out=xrb[:, TG * c:TG * (c + 1)], in0=xs_, scalar=murc[:, c:c + 1],
                in1=scb[:, :], op0=ALU.mult, op1=ALU.add),
                reads=(K(s3e), K(scb), K(murc)), writes=(K(xrb),), tag=f"xr{gi}_{c}")
            sch.add("vector", lambda v, c=c, xp_=xp_: v.tensor_scalar_mul(
                scb[:, :], xp_, mukc1[:, c:c + 1]),
                reads=(K(s3e), K(mukc1)), writes=(K(scb),))
            sch.add("vector", lambda v, c=c, xs_=xs_: v.scalar_tensor_tensor(
                out=xkb[:, TG * c:TG * (c + 1)], in0=xs_, scalar=mukc[:, c:c + 1],
                in1=scb[:, :], op0=ALU.mult, op1=ALU.add),
                reads=(K(s3e), K(scb), K(mukc)), writes=(K(xkb),), tag=f"xk{gi}_{c}")
        for oc in range(NCH):
            brr = pb[oc % 2]
            sch.add("tensor", MMG_(pb, brr[:, :], [
                (lambda t, c=c, oc=oc: wr[:, c, 128 * oc:128 * (oc + 1)],
                 lambda t, c=c: xrb[:, TG * c:TG * (c + 1)]) for c in range(NCH)]),
                reads=(K(wr), K(xrb)), writes=(K(brr),), tag=f"rmm{gi}_{oc}")
            sch.add("scalar", lambda s, oc=oc, brr=brr: s.activation(
                out=rb[:, TG * oc:TG * (oc + 1)], in_=brr[:, :], func=AF.Sigmoid),
                reads=(K(brr),), writes=(K(rb),), tag=f"sig{gi}_{oc}")
            bkk = pb[2 + oc % 2]
            sch.add("tensor", MMG_(pb, bkk[:, :], [
                (lambda t, c=c, oc=oc: wk_[:, c, 128 * oc:128 * (oc + 1)],
                 lambda t, c=c: xkb[:, TG * c:TG * (c + 1)]) for c in range(NCH)]),
                reads=(K(wk_), K(xkb)), writes=(K(bkk),), tag=f"kmm{gi}_{oc}")
            sch.add("vector", lambda v, bkk=bkk: v.tensor_scalar_max(
                scr[0][:, :], bkk[:, :], 0.0),
                reads=(K(bkk),), writes=(K(scr[0]),))
            sch.add("scalar", lambda s, oc=oc: s.activation(
                out=kab[:, TG * oc:TG * (oc + 1)], in_=scr[0][:, :], func=AF.Square),
                reads=(K(scr[0]),), writes=(K(kab),), tag=f"ksq{gi}_{oc}")
        for oc in range(NCH):
            bvv = pb[4 + oc % 2]
            sch.add("tensor", MMG_(pb, bvv[:, :], [
                (lambda t, c=c, oc=oc: wv_[:, c, 128 * oc:128 * (oc + 1)],
                 lambda t, c=c: kab[:, TG * c:TG * (c + 1)]) for c in range(NCH)]),
                reads=(K(wv_), K(kab)), writes=(K(bvv),), tag=f"vmm{gi}_{oc}")
            fv = fo[oc % 2]
            sch.add("vector", lambda v, oc=oc, bvv=bvv, fv=fv: v.tensor_mul(
                fv[:, :], bvv[:, :], rb[:, TG * oc:TG * (oc + 1)]),
                reads=(K(bvv), K(rb)), writes=(K(fv),), tag=f"fo{gi}_{oc}")
            sch.add("sync", lambda q, oc=oc, lo=lo, fv=fv: q.dma_start(
                out=io["fout"][128 * oc:128 * (oc + 1), lo:lo + TG], in_=fv[:, :]),
                reads=(K(fv),), writes=(f"fout{gi}_{oc}",), dma=True,
                tag=f"fst{gi}_{oc}")
    ctx.reset(m0)
    sch.set_fence()


def build_p5():
    nc = bass.Bass("TRN2", target_bir_lowering=False)
    ctx = Ctx(nc)
    io = p5_decls(nc)
    io["fout"] = nc.dram_tensor("fout", [D, T], F32, kind="ExternalOutput").ap()
    io["out"] = nc.dram_tensor("out_final", [D, T], F32, kind="ExternalOutput").ap()
    io.update(Exchange.input_decls(nc, 1))

    class MiniCst:
        pass
    with nc.Block() as blk:
        sch = Sched(ctx)
        xg = Exchange(ctx, sch, 1)
        xg.emit_load(io)
        cst = MiniCst()
        cst.ones_f = ctx.sb("c_ones_f", [128, 128], F32)
        cst.ones_b = ctx.sb("c_ones_b", [128, 128], BF16)
        sch.add("gpsimd", lambda g: g.memset(cst.ones_f[:, :], 1.0), writes=(K(cst.ones_f),))
        sch.add("gpsimd", lambda g: g.memset(cst.ones_b[:, :], 1.0), writes=(K(cst.ones_b),))
        emit_p5(ctx, sch, cst, xg, io, 0)
        emit_postln(ctx, sch, cst, io["s3i"], io["fout"], io["out"], "pl5")
        sch.flush(blk)
    ctx.st.close()
    return nc


def run_p5(inputs, s3_percore):
    nc = build_p5()
    in_maps = []
    for c in range(NCORE):
        d = p5_host(inputs, c)
        d["s3i"] = s3_percore[c]
        in_maps.append(d)
    seed = int(np.random.default_rng().integers(2 ** 31))
    for c in range(NCORE):
        pair_rng = np.random.default_rng(seed + (c // 2))
        xgi = Exchange.__new__(Exchange)
        xgi.n_flags = 1
        in_maps[c].update(Exchange.host_inputs(xgi, c % 2, pair_rng))
    res = run_bass_kernel_spmd(nc, in_maps, core_ids=list(range(NCORE)))
    return res.results


_prog_cache = {}


def _get_prog(name, builder):
    if name not in _prog_cache:
        _prog_cache[name] = builder()
    return _prog_cache[name]


def _pair_xg_inputs(n_flags, seed_base):
    maps = []
    for c in range(NCORE):
        pair_rng = np.random.default_rng(seed_base + (c // 2))
        xgi = Exchange.__new__(Exchange)
        xgi.n_flags = n_flags
        maps.append(Exchange.host_inputs(xgi, c % 2, pair_rng))
    return maps


def kernel(**inputs):
    rng = np.random.default_rng()

    xpc = []
    for c in range(NCORE):
        b, h = c // 2, c % 2
        xpc.append(to_fm(np.asarray(inputs["L0_attn_x"][b, 2048 * h:2048 * (h + 1), :],
                                    np.float32)))

    nc1 = _get_prog("p1", build_p1)
    in_maps = [p1_host_inputs(c, inputs, rng) for c in range(NCORE)]
    xgmaps = _pair_xg_inputs(1, int(rng.integers(2 ** 31)))
    for c in range(NCORE):
        in_maps[c].update(xgmaps[c])
    res = run_bass_kernel_spmd(nc1, in_maps, core_ids=list(range(NCORE)))
    s1 = [np.ascontiguousarray(r["out_s1"]) for r in res.results]

    nc2 = _get_prog("p2", lambda: build_p2(0))
    in_maps = []
    for c in range(NCORE):
        dd = {"yin": s1[c], "xrows": xrows_host(c)}
        dd.update(l0ffn_const_host(c))
        dd.update(l0ffn_layer_host(inputs, 0))
        in_maps.append(dd)
    xgmaps = _pair_xg_inputs(3, int(rng.integers(2 ** 31)))
    for c in range(NCORE):
        in_maps[c].update(xgmaps[c])
    res = run_bass_kernel_spmd(nc2, in_maps, core_ids=list(range(NCORE)))
    y1 = [np.ascontiguousarray(r["yout"]) for r in res.results]

    nc3 = _get_prog("p3", lambda: build_p3(1))
    in_maps = []
    for c in range(NCORE):
        dd = {"yin": y1[c], "s1": s1[c], "xrows": xrows_host(c)}
        dd.update(l0ffn_const_host(c))
        dd.update(l0ffn_layer_host(inputs, 1))
        in_maps.append(dd)
    xgmaps = _pair_xg_inputs(3, int(rng.integers(2 ** 31)))
    for c in range(NCORE):
        in_maps[c].update(xgmaps[c])
    res = run_bass_kernel_spmd(nc3, in_maps, core_ids=list(range(NCORE)))
    s2 = [np.ascontiguousarray(r["s2"]) for r in res.results]

    nc4 = _get_prog("p4", build_p4)
    in_maps = []
    for c in range(NCORE):
        dd = p4_host(inputs, c)
        dd["s2i"] = s2[c]
        ch = l0ffn_const_host(c)
        dd["cst_hpat2"] = ch["cst_hpat2"]
        dd["cst_bpat2"] = ch["cst_bpat2"]
        dd["cst_e01"] = ch["cst_e01"]
        in_maps.append(dd)
    xgmaps = _pair_xg_inputs(1, int(rng.integers(2 ** 31)))
    for c in range(NCORE):
        in_maps[c].update(xgmaps[c])
    res = run_bass_kernel_spmd(nc4, in_maps, core_ids=list(range(NCORE)))
    s3 = [np.ascontiguousarray(r["s3"]) for r in res.results]

    nc5 = _get_prog("p5", build_p5)
    in_maps = []
    for c in range(NCORE):
        dd = p5_host(inputs, c)
        dd["s3i"] = s3[c]
        in_maps.append(dd)
    xgmaps = _pair_xg_inputs(1, int(rng.integers(2 ** 31)))
    for c in range(NCORE):
        in_maps[c].update(xgmaps[c])
    res = run_bass_kernel_spmd(nc5, in_maps, core_ids=list(range(NCORE)))

    out = np.zeros((B, L, D), np.float32)
    for c in range(NCORE):
        b, h = c // 2, c % 2
        out[b, 2048 * h:2048 * (h + 1), :] = res.results[c]["out_final"].T
    return out


def gcst_decls(nc):
    return {
        "cst_hpat2": nc.dram_tensor("cst_hpat2", [128, 2], F32, kind="ExternalInput").ap(),
        "cst_bpat2": nc.dram_tensor("cst_bpat2", [2, 128], F32, kind="ExternalInput").ap(),
        "cst_e01": nc.dram_tensor("cst_e01", [2, 2], F32, kind="ExternalInput").ap(),
    }


def gcst_host(core):
    ch = l0ffn_const_host(core)
    return {k: ch[k] for k in ("cst_hpat2", "cst_bpat2", "cst_e01")}


def build_fused():
    nc = bass.Bass("TRN2", target_bir_lowering=False)
    ctx = Ctx(nc)
    io = {}
    io["x"] = nc.dram_tensor("x", [D, T], F32, kind="ExternalInput").ap()
    io["WqT"] = nc.dram_tensor("WqT", [D, D], F32, kind="ExternalInput").ap()
    io["WkT"] = nc.dram_tensor("WkT", [D, D], F32, kind="ExternalInput").ap()
    io["WvT"] = nc.dram_tensor("WvT", [D, D], F32, kind="ExternalInput").ap()
    io["proj"] = nc.dram_tensor("proj", [D, 256], F32, kind="ExternalInput").ap()
    io["kvrows_my"] = nc.dram_tensor("kvrows_my", [128, 2], U32, kind="ExternalInput").ap()
    io["kvrows_peer"] = nc.dram_tensor("kvrows_peer", [128, 2], U32, kind="ExternalInput").ap()
    io["xrows"] = nc.dram_tensor("xrows", [128, 22], U32, kind="ExternalInput").ap()
    io.update(l0ffn_const_decls(nc))
    io.update(gcst_decls(nc))
    io.update(l0ffn_layer_decls(nc, 0))
    io.update(l0ffn_layer_decls(nc, 1))
    io.update(p4_decls(nc, with_stream=False))
    io.update(p5_decls(nc, with_stream=False))
    io.update(Exchange.input_decls(nc, 9))

    kind = "ExternalOutput" if FUSED_DEBUG else "Internal"
    s1 = nc.dram_tensor("st_s1", [D, T], F32, kind=kind).ap()
    ymid0 = nc.dram_tensor("st_ymid0", [D, T], F32, kind=kind).ap()
    y1 = nc.dram_tensor("st_y1", [D, T], F32, kind=kind).ap()
    ymid1 = nc.dram_tensor("st_ymid1", [D, T], F32, kind=kind).ap()
    y2 = nc.dram_tensor("st_y2", [D, T], F32, kind=kind).ap()
    s2 = nc.dram_tensor("st_s2", [D, T], F32, kind=kind).ap()
    aout = nc.dram_tensor("st_aout", [D, T], F32, kind=kind).ap()
    s3 = nc.dram_tensor("st_s3", [D, T], F32, kind=kind).ap()
    fout = nc.dram_tensor("st_fout", [D, T], F32, kind=kind).ap()
    out = nc.dram_tensor("out_final", [D, T], F32, kind="ExternalOutput").ap()

    with nc.Block() as blk:
        sch = Sched(ctx)
        xg = Exchange(ctx, sch, 9)
        xg.emit_load(io)
        gc = GlobalCst(ctx, sch, io)
        io1 = dict(io)
        io1["out"] = s1
        emit_p1(ctx, sch, xg, io1, xgk0=0)
        m_c = ctx.mark()
        cst = L0Consts(ctx, sch, io, gc)
        io2 = dict(io)
        io2.update({"yin": s1, "ymid": ymid0, "yout": y1})
        if FUSED_STAGES >= 2:
            emit_l0ffn_layer(ctx, sch, xg, cst, io2, 0, 1)
        io3 = dict(io)
        io3.update({"yin": y1, "ymid": ymid1, "yout": y2})
        if FUSED_STAGES >= 3:
            emit_l0ffn_layer(ctx, sch, xg, cst, io3, 1, 4)
            emit_postln(ctx, sch, gc, s1, y2, s2, "pl3")
        ctx.reset(m_c)
        sch.set_fence()
        if FUSED_STAGES >= 4:
            io4 = dict(io)
            io4.update({"s2i": s2, "aout": aout})
            emit_p4(ctx, sch, xg, gc, io4, 7)
            emit_postln(ctx, sch, gc, s2, aout, s3, "pl4")
        if FUSED_STAGES >= 5:
            io5 = dict(io)
            io5.update({"s3i": s3, "fout": fout})
            emit_p5(ctx, sch, gc, xg, io5, 8)
            emit_postln(ctx, sch, gc, s3, fout, out, "pl5")
        sch.flush(blk)
    ctx.st.close()
    return nc


def fused_host_inputs(core, inputs, rng):
    d = p1_host_inputs(core, inputs, rng)
    d["xrows"] = xrows_host(core)
    d.update(l0ffn_const_host(core))
    d.update(l0ffn_layer_host(inputs, 0))
    d.update(l0ffn_layer_host(inputs, 1))
    p4d = p4_host(inputs, core)
    p4d.pop("s2i")
    d.update(p4d)
    d.update(p5_host(inputs, core))
    return d


def kernel(**inputs):
    rng = np.random.default_rng()
    nc = _get_prog("fused", build_fused)
    in_maps = [fused_host_inputs(c, inputs, rng) for c in range(NCORE)]
    xgmaps = _pair_xg_inputs(9, int(rng.integers(2 ** 31)))
    for c in range(NCORE):
        in_maps[c].update(xgmaps[c])
    res = run_bass_kernel_spmd(nc, in_maps, core_ids=list(range(NCORE)))
    out = np.zeros((B, L, D), np.float32)
    for c in range(NCORE):
        b, h = c // 2, c % 2
        out[b, 2048 * h:2048 * (h + 1), :] = res.results[c]["out_final"].T
    return out


# version 08
# speedup vs baseline: 1.2754x; 1.2754x over previous
import numpy as np
import ml_dtypes
from contextlib import ExitStack

import concourse.bass as bass
import concourse.mybir as mybir
from concourse.bass_utils import run_bass_kernel_spmd

F32 = mybir.dt.float32
BF16 = mybir.dt.bfloat16
U32 = mybir.dt.uint32
I32 = mybir.dt.int32
AF = mybir.ActivationFunctionType
ALU = mybir.AluOpType
IOA = bass.IndirectOffsetOnAxis

B, L, D = 4, 4096, 1024
T = 2048
NCORE = 8
TG = 512
NTG = T // TG
NCH = D // 128
POLL_BOUND = 400000
P1_DEBUG = False
P2_DEBUG = False
FUSED_DEBUG = False
FUSED_STAGES = 9
bf16 = ml_dtypes.bfloat16

ENGINES = ("tensor", "vector", "scalar", "gpsimd", "sync")
DMA_QUEUES = ("sync", "gpsimd", "scalar")


class Op:
    __slots__ = ("engine", "emit", "reads", "writes", "amount", "deps", "idx",
                 "tag", "pool", "pool_pre")

    def __init__(self, engine, emit, reads, writes, amount, tag=""):
        self.engine = engine
        self.emit = emit
        self.reads = reads
        self.writes = writes
        self.amount = amount
        self.deps = []
        self.idx = 0
        self.tag = tag


class Sched:

    NPOOL = {"sync": 14, "gpsimd": 14, "scalar": 6}

    def __init__(self, ctx):
        self.ctx = ctx
        self.ops = []
        self.cum = {e: 0 for e in ENGINES}
        self.last_writer = {}
        self.readers = {}
        self.sems = {e: ctx.sem(f"E_{e}") for e in ENGINES}
        self.pool = {q: [ctx.sem(f"P_{q}{i}") for i in range(self.NPOOL[q])]
                     for q in DMA_QUEUES}
        self.pool_cum = {q: [0] * self.NPOOL[q] for q in DMA_QUEUES}
        self.pool_next = {q: 0 for q in DMA_QUEUES}
        self.fence = []

    def set_fence(self):
        self.fence = self.barrier_token()

    def add(self, engine, emit, reads=(), writes=(), dma=False, tag="",
            extra_deps=()):
        op = Op(engine, emit, tuple(reads), tuple(writes), 16 if dma else 1, tag)
        deps = {}

        def need(tok):
            if tok is None:
                return
            kind, key, c = tok
            if kind == "eng" and key == engine and not dma:
                return
            k = (kind, key)
            if deps.get(k, 0) < c:
                deps[k] = c
        for k in op.reads:
            need(self.last_writer.get(k))
        for k in op.writes:
            need(self.last_writer.get(k))
            for r in self.readers.get(k, ()):
                need(r)
        for d in extra_deps:
            need(d)
        for d in self.fence:
            need(d)
        op.deps = sorted(deps.items())
        if dma:
            q = engine
            i = self.pool_next[q]
            self.pool_next[q] = (i + 1) % self.NPOOL[q]
            op.pool = (q, i)
            op.pool_pre = self.pool_cum[q][i]
            self.pool_cum[q][i] += 16
            me = ("dma", (q, i), self.pool_cum[q][i])
        else:
            op.pool = None
            self.cum[engine] += 1
            op.idx = self.cum[engine]
            me = ("eng", engine, self.cum[engine])
        self.ops.append(op)
        for k in op.writes:
            self.last_writer[k] = me
            self.readers[k] = []
        for k in op.reads:
            self.readers.setdefault(k, []).append(me)
        return me

    def barrier_token(self):
        toks = [("eng", e, c) for e, c in self.cum.items() if c > 0]
        for q in DMA_QUEUES:
            for i, c in enumerate(self.pool_cum[q]):
                if c > 0:
                    toks.append(("dma", (q, i), c))
        return toks

    def _sem_of(self, kind, key):
        if kind == "eng":
            return self.sems[key].h
        q, i = key
        return self.pool[q][i].h

    def flush(self, blk):
        per_engine = {e: [] for e in ENGINES}
        for op in self.ops:
            per_engine[op.engine].append(op)
        for ename in ENGINES:
            ops = per_engine[ename]
            if not ops and ename not in DMA_QUEUES:
                continue

            def make(ops=ops, ename=ename):
                def body(eng):
                    waited = {}
                    for op in ops:
                        for (kind, key), c in op.deps:
                            if waited.get((kind, key), 0) < c:
                                eng.wait_ge(self._sem_of(kind, key), c)
                                waited[(kind, key)] = c
                        if op.pool is not None:
                            q, i = op.pool
                            if op.pool_pre > 0 and waited.get(("dma", (q, i)), 0) < op.pool_pre:
                                eng.wait_ge(self.pool[q][i].h, op.pool_pre)
                                waited[("dma", (q, i))] = op.pool_pre
                            inst = op.emit(eng)
                            assert inst is not None, f"dma op {op.tag} returned None"
                            inst.then_inc(self.pool[q][i].h, 16)
                        else:
                            inst = op.emit(eng)
                            if inst is None:
                                eng.sem_inc(self.sems[ename].h, 1)
                            else:
                                inst.then_inc(self.sems[ename].h, 1)
                    if ename in DMA_QUEUES:
                        for i, c in enumerate(self.pool_cum[ename]):
                            if c > 0 and waited.get(("dma", (ename, i)), 0) < c:
                                eng.wait_ge(self.pool[ename][i].h, c)
                return body
            getattr(blk, ename)(make())


class Ctx:

    def __init__(self, nc):
        self.nc = nc
        self.st = ExitStack()
        self._i = 0
        self.cursor = (nc.sbuf_base + 31) // 32 * 32
        self.top = nc.sbuf_top
        self.peak = self.cursor
        self.pbanks = [self.ps(f"pbank{i}", [128, 512], F32) for i in range(8)]

    def sem(self, name=None):
        if name is None:
            name = f"s{self._i}"
            self._i += 1
        return _SemH(self.st.enter_context(self.nc.semaphore(name)))

    def sb(self, name, shape, dt, at=None, region=None):
        nbytes = int(np.prod(shape[1:])) * mybir.dt.size(dt)
        nbytes = (nbytes + 31) // 32 * 32
        if at is None:
            off = self.cursor
            self.cursor += nbytes
            self.peak = max(self.peak, self.cursor)
        else:
            off = at
        assert off + nbytes <= self.top, (
            f"SBUF overflow allocating {name}: off={off} nbytes={nbytes} top={self.top}")
        self._i += 1
        t = self.nc.alloc_sbuf_tensor_at(f"{name}_{self._i}", shape, dt, offset=off)
        REGION[id(t)] = region if region is not None else id(t)
        OFFSET[id(t)] = off
        return t

    def overlay(self, name, shape, dt, over):
        return self.sb(name, shape, dt, at=OFFSET[id(over)], region=REGION[id(over)])

    def mark(self):
        return self.cursor

    def reset(self, m):
        self.cursor = m

    def ps(self, name, shape, dt=F32):
        return self.st.enter_context(self.nc.psum_tensor(name, shape, dt))


class _SemH:
    __slots__ = ("h",)

    def __init__(self, h):
        self.h = h


REGION = {}
OFFSET = {}


def K(t):
    return REGION.get(id(t), id(t))


def to_fm(x2d):
    return np.ascontiguousarray(np.asarray(x2d).T)


def rows_col(a, b=None, off=0):
    if b is None:
        a, b = 0, a
    return (np.arange(a, b, dtype=np.uint32) + off)[:, None]


class Exchange:

    def __init__(self, ctx, sch, n_flags):
        self.ctx = ctx
        self.sch = sch
        nc = ctx.nc
        self.n_flags = n_flags
        self.FLG = nc.dram_tensor("FLGshare", [max(2 * n_flags, 2), 1], I32,
                                  addr_space="Shared")
        self.s_magic = ctx.sb("xgs_magic", [2, max(n_flags, 1)], I32)
        self.s_myslot = ctx.sb("xgs_myslot", [2, max(n_flags, 1)], U32)
        self.s_peerslot = ctx.sb("xgs_peerslot", [2, max(n_flags, 1)], U32)
        self.s_pf = ctx.sb("xg_pf", [2, 1], I32)
        self.iters = ctx.sb("xg_iters", [1, max(n_flags, 1)], I32)
        self.loaded = None

    def emit_load(self, io):
        sch = self.sch
        sch.add("sync", lambda q: q.dma_start(out=self.s_magic[:, :], in_=io["xg_magic"]),
                writes=(K(self.s_magic),), dma=True, tag="xgld0")
        sch.add("sync", lambda q: q.dma_start(out=self.s_myslot[:, :], in_=io["xg_myslot"]),
                writes=(K(self.s_myslot),), dma=True, tag="xgld1")
        self.loaded = sch.add(
            "sync", lambda q: q.dma_start(out=self.s_peerslot[:, :], in_=io["xg_peerslot"]),
            writes=(K(self.s_peerslot),), dma=True, tag="xgld2")

    def push(self, k, data_deps):
        sch = self.sch
        ctx = self.ctx
        dma = ctx.sem()

        def em(g):
            g.indirect_dma_start(
                out=self.FLG.ap(),
                out_offset=IOA(ap=self.s_myslot[:, k:k + 1], axis=0),
                in_=self.s_magic[:, k:k + 1],
                in_offset=None,
            ).then_inc(dma.h, 16)
            g.wait_ge(dma.h, 16)
            return None
        return sch.add("gpsimd", em, reads=(K(self.s_magic), K(self.s_myslot)),
                       writes=(), tag=f"push{k}",
                       extra_deps=list(data_deps) + [self.loaded])

    def poll(self, k):
        sch = self.sch
        ctx = self.ctx
        poll = ctx.sem()

        def em(g):
            with (g.register(f"xit{k}") as rit, g.register(f"xc{k}") as rc,
                  g.register(f"xw{k}") as rw, g.register(f"xf{k}") as rf,
                  g.register(f"xm{k}") as rm):
                g.reg_mov(rit, 0)
                g.reg_mov(rw, 0)
                g.reg_load(rm, self.s_magic[0:1, k:k + 1])
                g.reg_mov(rc, 1)
                with g.While(rc):
                    g.indirect_dma_start(
                        out=self.s_pf[:, :], out_offset=None,
                        in_=self.FLG.ap(),
                        in_offset=IOA(ap=self.s_peerslot[:, k:k + 1], axis=0),
                    ).then_inc(poll.h, 16)
                    g.reg_add(rw, rw, 16)
                    g.wait_ge(poll.h, rw)
                    g.reg_load(rf, self.s_pf[0:1, 0:1])
                    g.reg_add(rit, rit, 1)
                    with g.register(f"xne{k}") as rne, g.register(f"xlt{k}") as rlt:
                        g.reg_alu(rne, rf, rm, ALU.not_equal)
                        g.reg_alu(rlt, rit, POLL_BOUND, ALU.is_lt)
                        g.reg_alu(rc, rne, rlt, ALU.bitwise_and)
                g.reg_save(self.iters[0:1, k:k + 1], rit)
            return None
        return sch.add("gpsimd", em,
                       reads=(K(self.s_magic), K(self.s_peerslot)),
                       writes=(K(self.s_pf),), tag=f"poll{k}",
                       extra_deps=[self.loaded])

    def host_inputs(self, h, rng):
        n = max(self.n_flags, 1)
        magics = rng.integers(1, 2 ** 30, size=(n,), dtype=np.int64).astype(np.int32)
        magic = np.stack([magics, magics], axis=0)
        ks = np.arange(n, dtype=np.uint32)
        my = np.stack([2 * ks + h, 2 * ks + h], axis=0).astype(np.uint32)
        peer = np.stack([2 * ks + (1 - h), 2 * ks + (1 - h)], axis=0).astype(np.uint32)
        return {"xg_magic": magic, "xg_myslot": my, "xg_peerslot": peer}

    @staticmethod
    def input_decls(nc, n_flags):
        n = max(n_flags, 1)
        return {
            "xg_magic": nc.dram_tensor("xg_magic", [2, n], I32, kind="ExternalInput").ap(),
            "xg_myslot": nc.dram_tensor("xg_myslot", [2, n], U32, kind="ExternalInput").ap(),
            "xg_peerslot": nc.dram_tensor("xg_peerslot", [2, n], U32, kind="ExternalInput").ap(),
        }


def emit_p1(ctx, sch, xg, io, xgk0=0):
    nc = ctx.nc
    m_p1 = ctx.mark()

    xb = [ctx.sb(f"p1xb{c}", [128, T], BF16) for c in range(NCH)]
    wq = [ctx.sb(f"p1wq{c}", [128, D], BF16) for c in range(NCH)]
    wk = [ctx.sb(f"p1wk{c}", [128, D], BF16) for c in range(NCH)]
    wv = [ctx.sb(f"p1wv{c}", [128, D], BF16) for c in range(NCH)]
    pj = [ctx.sb(f"p1pj{c}", [128, 256], BF16) for c in range(NCH)]
    qf = [ctx.sb(f"p1qf{c}", [128, TG], BF16) for c in range(NCH)]
    kf = [ctx.sb(f"p1kf{c}", [128, TG], BF16) for c in range(NCH)]
    qprime = [ctx.sb(f"p1qpr{p}", [128, T], BF16) for p in range(2)]
    kprime = [ctx.sb(f"p1kpr{t}", [128, 256], BF16) for t in range(4)]
    vt = [ctx.sb(f"p1vt{t}", [128, D], BF16) for t in range(4)]
    ones_b = ctx.sb("p1ones_b", [128, 128], BF16)
    ones_f = ctx.sb("p1ones_f", [128, 128], F32)
    kvs = [ctx.sb(f"p1kvs{c}", [128, D], F32) for c in range(2)]
    kvb = [ctx.sb(f"p1kvb{c}", [128, D], BF16) for c in range(2)]
    kvp = ctx.sb("p1kvp", [128, D], F32)
    zrow = ctx.sb("p1zrow", [1, T], F32)
    zrec = ctx.sb("p1zrec", [1, T], F32)
    scr = [ctx.sb(f"p1scr{i}", [128, TG], F32) for i in range(6)]
    srow = [ctx.sb(f"p1srow{i}", [1, TG], F32) for i in range(6)]
    yg = [ctx.sb(f"p1yg{c}", [128, TG], F32) for c in range(NCH)]
    xs = [ctx.sb(f"p1xs{i}", [128, TG], F32) for i in range(2)]
    s1 = [ctx.sb(f"p1s1{c}", [128, TG], F32) for c in range(2)]
    kvrows_my = ctx.sb("p1kvrows_my", [128, 2], U32)
    kvrows_peer = ctx.sb("p1kvrows_peer", [128, 2], U32)

    KVSH = nc.dram_tensor("KVshare", [512, D], F32, addr_space="Shared")

    pb = ctx.pbanks

    for c in range(NCH):
        sch.add("gpsimd", lambda g, c=c: g.dma_start(
            out=xb[c][:, :], in_=io["x"][128 * c:128 * (c + 1), :]),
            writes=(K(xb[c]),), dma=True, tag=f"ldxb{c}")
        sch.add("gpsimd", lambda g, c=c: g.dma_start(
            out=wq[c][:, :], in_=io["WqT"][128 * c:128 * (c + 1), :]),
            writes=(K(wq[c]),), dma=True)
        sch.add("gpsimd", lambda g, c=c: g.dma_start(
            out=wk[c][:, :], in_=io["WkT"][128 * c:128 * (c + 1), :]),
            writes=(K(wk[c]),), dma=True)
        sch.add("gpsimd", lambda g, c=c: g.dma_start(
            out=wv[c][:, :], in_=io["WvT"][128 * c:128 * (c + 1), :]),
            writes=(K(wv[c]),), dma=True)
        sch.add("gpsimd", lambda g, c=c: g.dma_start(
            out=pj[c][:, :], in_=io["proj"][128 * c:128 * (c + 1), :]),
            writes=(K(pj[c]),), dma=True)
    sch.add("sync", lambda q: q.dma_start(out=kvrows_my[:, :], in_=io["kvrows_my"]),
            writes=(K(kvrows_my),), dma=True)
    sch.add("sync", lambda q: q.dma_start(out=kvrows_peer[:, :], in_=io["kvrows_peer"]),
            writes=(K(kvrows_peer),), dma=True)
    sch.add("gpsimd", lambda g: g.memset(ones_b[:, :], 1.0), writes=(K(ones_b),))
    if P1_DEBUG:
        def dbg_store(name, buf, region=None):
            ap = buf[:, :] if region is None else region
            sch.add("sync", lambda q, ap=ap, name=name: q.dma_start(
                out=io[name], in_=ap), reads=(K(buf),), writes=(name,), dma=True)
        io["_dbg_store"] = dbg_store
    sch.add("gpsimd", lambda g: g.memset(ones_f[:, :], 1.0), writes=(K(ones_f),))

    def MMG(bank, pairs, tag=""):
        def em(t):
            n = len(pairs)
            last = None
            for i, (lf, rf) in enumerate(pairs):
                last = t.matmul(bank[:, :] if callable(bank) else bank,
                                lf(t), rf(t), start=(i == 0), stop=(i == n - 1))
            return last
        return em

    for gi in range(NTG):
        lo = gi * TG
        for e in range(NCH):
            bq = pb[e % 2]
            sch.add("tensor", MMG(bq[:, :], [
                (lambda t, d=d, e=e: wq[d][:, 128 * e:128 * (e + 1)],
                 lambda t, d=d, lo=lo: xb[d][:, lo:lo + TG]) for d in range(NCH)]),
                reads=tuple(K(wq[d]) for d in range(NCH)) + tuple(K(xb[d]) for d in range(NCH)),
                writes=(K(bq),), tag=f"qmm{gi}_{e}")
            sch.add("scalar", lambda s, e=e, bq=bq: s.copy(qf[e][:, :], bq[:, :]),
                    reads=(K(bq),), writes=(K(qf[e]),), tag=f"qev{gi}_{e}")
            bk = pb[2 + e % 2]
            sch.add("tensor", MMG(bk[:, :], [
                (lambda t, d=d, e=e: wk[d][:, 128 * e:128 * (e + 1)],
                 lambda t, d=d, lo=lo: xb[d][:, lo:lo + TG]) for d in range(NCH)]),
                reads=tuple(K(wk[d]) for d in range(NCH)) + tuple(K(xb[d]) for d in range(NCH)),
                writes=(K(bk),), tag=f"kmm{gi}_{e}")
            sch.add("vector", lambda v, e=e, bk=bk: v.tensor_copy(kf[e][:, :], bk[:, :]),
                    reads=(K(bk),), writes=(K(kf[e]),), tag=f"kev{gi}_{e}")
        for p in range(2):
            bqp = pb[p]
            sch.add("tensor", MMG(bqp[:, :], [
                (lambda t, e=e, p=p: pj[e][:, 128 * p:128 * (p + 1)],
                 lambda t, e=e: qf[e][:, :]) for e in range(NCH)]),
                reads=tuple(K(pj[e]) for e in range(NCH)) + tuple(K(qf[e]) for e in range(NCH)),
                writes=(K(bqp),), tag=f"qpmm{gi}_{p}")
            sc0, sc1 = scr[2 * p], scr[2 * p + 1]
            sch.add("vector", lambda v, bqp=bqp, sc0=sc0: v.tensor_scalar_min(
                sc0[:, :], bqp[:, :], 0.0), reads=(K(bqp),), writes=(K(sc0),))
            sch.add("scalar", lambda s, sc0=sc0, sc1=sc1: s.activation(
                out=sc1[:, :], in_=sc0[:, :], func=AF.Exp),
                reads=(K(sc0),), writes=(K(sc1),))
            sch.add("vector", lambda v, bqp=bqp, sc1=sc1, p=p, lo=lo: v.scalar_tensor_tensor(
                out=qprime[p][:, lo:lo + TG], in0=bqp[:, :], scalar=0.0,
                in1=sc1[:, :], op0=ALU.max, op1=ALU.add),
                reads=(K(bqp), K(sc1)), writes=(K(qprime[p]),), tag=f"qpr{gi}_{p}")
        bz = pb[2]
        sch.add("tensor", MMG(bz[0:1, :], [
            (lambda t, p=p: ones_b[:, 0:1],
             lambda t, p=p, lo=lo: qprime[p][:, lo:lo + TG]) for p in range(2)]),
            reads=(K(ones_b), K(qprime[0]), K(qprime[1])),
            writes=(K(bz),), tag=f"zmm{gi}")
        sch.add("vector", lambda v, bz=bz, lo=lo: v.tensor_copy(zrow[0:1, lo:lo + TG], bz[0:1, :]),
                reads=(K(bz),), writes=(K(zrow),))
        for lt in range(4):
            ls = lo + 128 * lt
            bkp = pb[3]
            sch.add("tensor", MMG(bkp[:, 0:256], [
                (lambda t, e=e, lt=lt: kf[e][:, 128 * lt:128 * (lt + 1)],
                 lambda t, e=e: pj[e][:, :]) for e in range(NCH)]),
                reads=tuple(K(kf[e]) for e in range(NCH)) + tuple(K(pj[e]) for e in range(NCH)),
                writes=(K(bkp),), tag=f"kpmm{gi}_{lt}")
            sc0, sc1 = scr[4], scr[5]
            sch.add("vector", lambda v, bkp=bkp, sc0=sc0: v.tensor_scalar_min(
                sc0[:, 0:256], bkp[:, 0:256], 0.0), reads=(K(bkp),), writes=(K(sc0),))
            sch.add("scalar", lambda s, sc0=sc0, sc1=sc1: s.activation(
                out=sc1[:, 0:256], in_=sc0[:, 0:256], func=AF.Exp),
                reads=(K(sc0),), writes=(K(sc1),))
            sch.add("vector", lambda v, bkp=bkp, sc1=sc1, lt=lt: v.scalar_tensor_tensor(
                out=kprime[lt][:, :], in0=bkp[:, 0:256], scalar=0.0,
                in1=sc1[:, 0:256], op0=ALU.max, op1=ALU.add),
                reads=(K(bkp), K(sc1)), writes=(K(kprime[lt]),), tag=f"kpr{gi}_{lt}")
            for vh in range(2):
                bv = pb[3]
                sch.add("tensor", MMG(bv[:, :], [
                    (lambda t, d=d, ls=ls: xb[d][:, ls:ls + 128],
                     lambda t, d=d, vh=vh: wv[d][:, TG * vh:TG * (vh + 1)]) for d in range(NCH)]),
                    reads=tuple(K(xb[d]) for d in range(NCH)) + tuple(K(wv[d]) for d in range(NCH)),
                    writes=(K(bv),), tag=f"vmm{gi}_{lt}_{vh}")
                sch.add("vector", lambda v, bv=bv, lt=lt, vh=vh: v.tensor_copy(
                    vt[lt][:, TG * vh:TG * (vh + 1)], bv[:, :]),
                    reads=(K(bv),), writes=(K(vt[lt]),))
            first = (gi == 0 and lt == 0)
            last = (gi == NTG - 1 and lt == 3)
            for p in range(2):
                for vh in range(2):
                    bank = pb[4 + 2 * p + vh]
                    def em(t, p=p, vh=vh, lt=lt, first=first, last=last, bank=bank):
                        return t.matmul(bank[:, :], kprime[lt][:, 128 * p:128 * (p + 1)],
                                        vt[lt][:, TG * vh:TG * (vh + 1)],
                                        start=first, stop=last)
                    sch.add("tensor", em, reads=(K(kprime[lt]), K(vt[lt])),
                            writes=(K(bank),), tag=f"kvmm{gi}_{lt}_{p}{vh}")

    kv_deps = []
    for p in range(2):
        for vh in range(2):
            bank = pb[4 + 2 * p + vh]
            sch.add("vector", lambda v, p=p, vh=vh, bank=bank: v.tensor_copy(
                kvs[p][:, TG * vh:TG * (vh + 1)], bank[:, :]),
                reads=(K(bank),), writes=(K(kvs[p]),), tag=f"kvev{p}{vh}")
    for p in range(2):
        tok = sch.add("gpsimd", lambda g, p=p: g.indirect_dma_start(
            out=KVSH.ap(), out_offset=IOA(ap=kvrows_my[:, p:p + 1], axis=0),
            in_=kvs[p][:, :], in_offset=None),
            reads=(K(kvs[p]), K(kvrows_my)), writes=("KVSH",), dma=True,
            tag=f"kvpush{p}")
        kv_deps.append(tok)
    xg.push(xgk0 + 0, kv_deps)
    ptok = xg.poll(xgk0 + 0)
    for p in range(2):
        sch.add("gpsimd", lambda g, p=p: g.indirect_dma_start(
            out=kvp[:, :], out_offset=None,
            in_=KVSH.ap(),
            in_offset=IOA(ap=kvrows_peer[:, p:p + 1], axis=0)),
            reads=("KVSH", K(kvrows_peer)), writes=(K(kvp),), dma=True,
            extra_deps=[ptok], tag=f"kvpull{p}")
        sch.add("vector", lambda v, p=p: v.tensor_add(
            kvs[p][:, :], kvs[p][:, :], kvp[:, :]),
            reads=(K(kvs[p]), K(kvp)), writes=(K(kvs[p]),))
        sch.add("scalar", lambda s, p=p: s.copy(kvb[p][:, :], kvs[p][:, :]),
                reads=(K(kvs[p]),), writes=(K(kvb[p]),))
        if P1_DEBUG:
            io["_dbg_store"](f"dbg_kv{p}", kvs[p])

    if P1_DEBUG:
        io["_dbg_store"]("dbg_z", zrow, zrow[0:1, :])
        for p in range(2):
            io["_dbg_store"](f"dbg_qpr{p}", qprime[p])
    sch.add("vector", lambda v: v.tensor_scalar_add(zrow[0:1, :], zrow[0:1, :], 1e-8),
            reads=(K(zrow),), writes=(K(zrow),))
    sch.add("vector", lambda v: v.reciprocal(zrec[0:1, :], zrow[0:1, :]),
            reads=(K(zrow),), writes=(K(zrec),))
    for gi in range(NTG):
        lo = gi * TG
        bsum = pb[6]
        bsq = pb[7]
        zib = scr[5]
        bbc = pb[2]
        sch.add("tensor", lambda t, lo=lo, bbc=bbc: t.matmul(
            bbc[:, :], ones_f[0:1, :], zrec[0:1, lo:lo + TG], start=True, stop=True),
            reads=(K(ones_f), K(zrec)), writes=(K(bbc),), tag=f"zbc{gi}")
        sch.add("scalar", lambda s, bbc=bbc, zib=zib: s.copy(zib[:, :], bbc[:, :]),
                reads=(K(bbc),), writes=(K(zib),))
        for e in range(NCH):
            bn = pb[e % 2]
            sch.add("tensor", MMG(bn[:, :], [
                (lambda t, p=p, e=e: kvb[p][:, 128 * e:128 * (e + 1)],
                 lambda t, p=p, lo=lo: qprime[p][:, lo:lo + TG]) for p in range(2)]),
                reads=(K(kvb[0]), K(kvb[1]), K(qprime[0]), K(qprime[1])),
                writes=(K(bn),), tag=f"num{gi}_{e}")
            xse = xs[e % 2]
            sch.add("sync", lambda q, e=e, lo=lo, xse=xse: q.dma_start(
                out=xse[:, :], in_=io["x"][128 * e:128 * (e + 1), lo:lo + TG]),
                reads=(), writes=(K(xse),), dma=True, tag=f"xld{gi}_{e}")
            sc = scr[e % 2]
            sch.add("vector", lambda v, bn=bn, sc=sc, zib=zib: v.tensor_mul(
                sc[:, :], bn[:, :], zib[:, :]),
                reads=(K(bn), K(zib)), writes=(K(sc),))
            sch.add("vector", lambda g, sc=sc, e=e, xse=xse: g.tensor_add(
                yg[e][:, :], sc[:, :], xse[:, :]),
                reads=(K(sc), K(xse)), writes=(K(yg[e]),), tag=f"y{gi}_{e}")
            if P1_DEBUG and gi == 0 and e == 0:
                io["_dbg_store"]("dbg_y00", yg[0])
            def em_sum(t, e=e, bsum=bsum):
                return t.matmul(bsum[0:1, :], ones_f[:, 0:1], yg[e][:, :],
                                start=(e == 0), stop=(e == NCH - 1))
            sch.add("tensor", em_sum, reads=(K(ones_f), K(yg[e])), writes=(K(bsum),),
                    tag=f"sum{gi}_{e}")
            scq = scr[2 + e % 2]
            sch.add("scalar", lambda s, e=e, scq=scq: s.activation(
                out=scq[:, :], in_=yg[e][:, :], func=AF.Square),
                reads=(K(yg[e]),), writes=(K(scq),))
            def em_sq(t, e=e, scq=scq, bsq=bsq):
                return t.matmul(bsq[0:1, :], ones_f[:, 0:1], scq[:, :],
                                start=(e == 0), stop=(e == NCH - 1))
            sch.add("tensor", em_sq, reads=(K(ones_f), K(scq)), writes=(K(bsq),),
                    tag=f"sq{gi}_{e}")
        sch.add("scalar", lambda s, bsum=bsum: s.activation(
            out=srow[0][0:1, :], in_=bsum[0:1, :], func=AF.Copy,
            scale=1.0 / D), reads=(K(bsum),), writes=(K(srow[0]),))
        sch.add("scalar", lambda s, bsq=bsq: s.activation(
            out=srow[1][0:1, :], in_=bsq[0:1, :], func=AF.Copy,
            scale=1.0 / D), reads=(K(bsq),), writes=(K(srow[1]),))
        sch.add("scalar", lambda s: s.activation(
            out=srow[2][0:1, :], in_=srow[0][0:1, :], func=AF.Square),
            reads=(K(srow[0]),), writes=(K(srow[2]),))
        sch.add("vector", lambda v: v.tensor_sub(
            srow[3][0:1, :], srow[1][0:1, :], srow[2][0:1, :]),
            reads=(K(srow[1]), K(srow[2])), writes=(K(srow[3]),))
        sch.add("vector", lambda v: v.tensor_scalar_add(
            srow[3][0:1, :], srow[3][0:1, :], 1e-5),
            reads=(K(srow[3]),), writes=(K(srow[3]),))
        sch.add("scalar", lambda s: s.activation(
            out=srow[4][0:1, :], in_=srow[3][0:1, :], func=AF.Ln),
            reads=(K(srow[3]),), writes=(K(srow[4]),))
        sch.add("scalar", lambda s: s.activation(
            out=srow[5][0:1, :], in_=srow[4][0:1, :], func=AF.Exp, scale=-0.5),
            reads=(K(srow[4]),), writes=(K(srow[5]),), tag=f"rstd{gi}")
        if P1_DEBUG and gi == 0:
            io["_dbg_store"]("dbg_mu0", srow[0], srow[0][0:1, :])
            io["_dbg_store"]("dbg_rstd0", srow[5], srow[5][0:1, :])
        sch.add("vector", lambda v: v.tensor_mul(
            srow[2][0:1, :], srow[0][0:1, :], srow[5][0:1, :]),
            reads=(K(srow[0]), K(srow[5])), writes=(K(srow[2]),), tag=f"mr{gi}")
        br = pb[4]
        bm = pb[5]
        sch.add("tensor", lambda t, br=br: t.matmul(
            br[:, :], ones_f[0:1, :], srow[5][0:1, :], start=True, stop=True),
            reads=(K(ones_f), K(srow[5])), writes=(K(br),))
        sch.add("tensor", lambda t, bm=bm: t.matmul(
            bm[:, :], ones_f[0:1, :], srow[2][0:1, :], start=True, stop=True),
            reads=(K(ones_f), K(srow[2])), writes=(K(bm),))
        for e in range(NCH):
            so = s1[e % 2]
            sch.add("vector", lambda v, e=e, br=br, so=so: v.tensor_mul(
                so[:, :], yg[e][:, :], br[:, :]),
                reads=(K(yg[e]), K(br)), writes=(K(so),))
            sch.add("vector", lambda v, bm=bm, so=so: v.tensor_sub(
                so[:, :], so[:, :], bm[:, :]),
                reads=(K(so), K(bm)), writes=(K(so),))
            sch.add("sync", lambda q, e=e, lo=lo, so=so: q.dma_start(
                out=io["out"][128 * e:128 * (e + 1), lo:lo + TG], in_=so[:, :]),
                reads=(K(so),), writes=(f"out{e}_{gi}",), dma=True,
                tag=f"st{gi}_{e}")
    ctx.reset(m_p1)
    sch.set_fence()


def p1_host_inputs(core, inputs, rng):
    b, h = core // 2, core % 2
    x = np.asarray(inputs["L0_attn_x"][b, 2048 * h:2048 * (h + 1), :], np.float32)
    d = {
        "x": to_fm(x),
        "WqT": np.ascontiguousarray(np.asarray(inputs["L0_attn_Wq"], np.float32).T),
        "WkT": np.ascontiguousarray(np.asarray(inputs["L0_attn_Wk"], np.float32).T),
        "WvT": np.ascontiguousarray(np.asarray(inputs["L0_attn_Wv"], np.float32).T),
        "proj": np.asarray(inputs["L0_attn_proj"], np.float32),
        "kvrows_my": np.concatenate([rows_col(128, off=256 * h),
                                     rows_col(128, off=256 * h + 128)], axis=1).astype(np.uint32),
        "kvrows_peer": np.concatenate([rows_col(128, off=256 * (1 - h)),
                                       rows_col(128, off=256 * (1 - h) + 128)], axis=1).astype(np.uint32),
    }
    return d


def p1_decls(nc):
    return {
        "x": nc.dram_tensor("x", [D, T], F32, kind="ExternalInput").ap(),
        "WqT": nc.dram_tensor("WqT", [D, D], F32, kind="ExternalInput").ap(),
        "WkT": nc.dram_tensor("WkT", [D, D], F32, kind="ExternalInput").ap(),
        "WvT": nc.dram_tensor("WvT", [D, D], F32, kind="ExternalInput").ap(),
        "proj": nc.dram_tensor("proj", [D, 256], F32, kind="ExternalInput").ap(),
        "kvrows_my": nc.dram_tensor("kvrows_my", [128, 2], U32, kind="ExternalInput").ap(),
        "kvrows_peer": nc.dram_tensor("kvrows_peer", [128, 2], U32, kind="ExternalInput").ap(),
        "out": nc.dram_tensor("out_s1", [D, T], F32, kind="ExternalOutput").ap(),
    } | ({
        "dbg_kv0": nc.dram_tensor("dbg_kv0", [128, D], F32, kind="ExternalOutput").ap(),
        "dbg_kv1": nc.dram_tensor("dbg_kv1", [128, D], F32, kind="ExternalOutput").ap(),
        "dbg_z": nc.dram_tensor("dbg_z", [1, T], F32, kind="ExternalOutput").ap(),
        "dbg_qpr0": nc.dram_tensor("dbg_qpr0", [128, T], BF16, kind="ExternalOutput").ap(),
        "dbg_qpr1": nc.dram_tensor("dbg_qpr1", [128, T], BF16, kind="ExternalOutput").ap(),
        "dbg_y00": nc.dram_tensor("dbg_y00", [128, TG], F32, kind="ExternalOutput").ap(),
        "dbg_mu0": nc.dram_tensor("dbg_mu0", [1, TG], F32, kind="ExternalOutput").ap(),
        "dbg_rstd0": nc.dram_tensor("dbg_rstd0", [1, TG], F32, kind="ExternalOutput").ap(),
    } if P1_DEBUG else {})


def build_p1():
    nc = bass.Bass("TRN2", target_bir_lowering=False)
    ctx = Ctx(nc)
    io = p1_decls(nc)
    io.update(Exchange.input_decls(nc, 1))
    with nc.Block() as blk:
        sch = Sched(ctx)
        xg = Exchange(ctx, sch, 1)
        xg.emit_load(io)
        emit_p1(ctx, sch, xg, io)
        sch.flush(blk)
    ctx.st.close()
    return nc


def run_p1(inputs):
    rng = np.random.default_rng()
    nc = build_p1()
    in_maps = [p1_host_inputs(c, inputs, rng) for c in range(NCORE)]
    seed = int(np.random.default_rng().integers(2 ** 31))
    for c in range(NCORE):
        pair_rng = np.random.default_rng(seed + (c // 2))
        xgi = Exchange.__new__(Exchange)
        xgi.n_flags = 1
        in_maps[c].update(Exchange.host_inputs(xgi, c % 2, pair_rng))
    res = run_bass_kernel_spmd(nc, in_maps, core_ids=list(range(NCORE)))
    return res.results



H0 = 8
HD0 = 64
NW = 4
NJ = 22


def l0ffn_const_decls(nc):
    d = {}
    d["cst_tcos"] = nc.dram_tensor("cst_tcos", [128, T], F32, kind="ExternalInput").ap()
    d["cst_tsin"] = nc.dram_tensor("cst_tsin", [128, T], F32, kind="ExternalInput").ap()
    d["cst_tsq"] = nc.dram_tensor("cst_tsq", [128, T], F32, kind="ExternalInput").ap()
    d["cst_tski"] = nc.dram_tensor("cst_tski", [128, T], F32, kind="ExternalInput").ap()
    d["cst_masks"] = nc.dram_tensor("cst_masks", [128, 8 * TG], F32, kind="ExternalInput").ap()
    d["cst_perm"] = nc.dram_tensor("cst_perm", [128, 128], F32, kind="ExternalInput").ap()
    d["cst_vkill"] = nc.dram_tensor("cst_vkill", [128, 1], F32, kind="ExternalInput").ap()
    d["cst_hscale"] = nc.dram_tensor("cst_hscale", [128, 1], F32, kind="ExternalInput").ap()
    return d


def l0ffn_const_host(core):
    h = core % 2
    base_pos = 2048 * h
    dd = np.arange(64)
    inv_freq = 10000.0 ** (-(2.0 * (dd % 32)) / 64.0)
    base = (2.0 * (dd % 32) + 0.4 * 64) / (1.4 * 64)
    i = base_pos + np.arange(T)
    freqs = i[None, :] * inv_freq[:, None]
    power = (i - L // 2) / float(512 // 2)
    scale = base[:, None] ** power[None, :]
    tc = np.cos(freqs).astype(np.float32)
    ts = np.sin(freqs).astype(np.float32)
    two = lambda a: np.concatenate([a, a], axis=0).astype(np.float32)
    masks = np.zeros((128, 8, TG), np.float32)
    jr = np.arange(128)[:, None]
    qq = np.arange(TG)[None, :]
    for kt in range(8):
        dlt = 512 - 128 * kt - jr + qq
        masks[:, kt, :] = ((dlt >= 0) & (dlt <= 512)).astype(np.float32)
    perm = np.zeros((128, 128), np.float32)
    for blk in range(2):
        o = 64 * blk
        for j2 in range(32):
            perm[o + 32 + j2, o + j2] = -1.0
            perm[o + j2, o + 32 + j2] = 1.0
    hpat2 = np.zeros((128, 2), np.float32)
    hpat2[0:64, 0] = 1.0
    hpat2[64:128, 1] = 1.0
    bpat2 = hpat2.T.copy()
    e01 = np.eye(2, dtype=np.float32)
    vkill = np.full((128, 1), 1.0 if h == 1 else 0.0, np.float32)
    hscale = vkill.copy()
    return {
        "cst_tcos": two(tc), "cst_tsin": two(ts),
        "cst_tsq": two(scale), "cst_tski": two(1.0 / scale),
        "cst_masks": masks.reshape(128, 8 * TG),
        "cst_perm": perm, "cst_hpat2": hpat2, "cst_bpat2": bpat2,
        "cst_e01": e01, "cst_vkill": vkill, "cst_hscale": hscale,
    }


class GlobalCst:
    def __init__(self, ctx, sch, io):
        self.ones_f = ctx.sb("g_ones_f", [128, 128], F32)
        self.ones_b = ctx.sb("g_ones_b", [128, 128], BF16)
        self.hpat2 = ctx.sb("g_hpat2", [128, 2], F32)
        self.bpat2 = ctx.sb("g_bpat2", [2, 128], F32)
        self.e01 = ctx.sb("g_e01", [2, 2], F32)
        sch.add("gpsimd", lambda g: g.memset(self.ones_f[:, :], 1.0), writes=(K(self.ones_f),))
        sch.add("gpsimd", lambda g: g.memset(self.ones_b[:, :], 1.0), writes=(K(self.ones_b),))
        for nm, buf in [("cst_hpat2", self.hpat2), ("cst_bpat2", self.bpat2),
                        ("cst_e01", self.e01)]:
            sch.add("sync", lambda q, nm=nm, buf=buf: q.dma_start(out=buf[:, :], in_=io[nm]),
                    writes=(K(buf),), dma=True, tag=nm)


class L0Consts:

    def __init__(self, ctx, sch, io, g):
        for a in ("ones_f", "ones_b", "hpat2", "bpat2", "e01"):
            setattr(self, a, getattr(g, a))
        self.tcos = ctx.sb("tcos", [128, T], BF16)
        self.tsin = ctx.sb("tsin", [128, T], BF16)
        self.tsq = ctx.sb("tsq", [128, T], BF16)
        self.tski = ctx.sb("tski", [128, T], BF16)
        self.masks = ctx.sb("masks", [128, 8 * TG], BF16)
        self.perm = ctx.sb("permc", [128, 128], F32)
        self.vkill = ctx.sb("vkill", [128, 1], F32)
        self.hscale = ctx.sb("hscale", [128, 1], F32)
        for name, buf in [("cst_perm", self.perm),
                          ("cst_vkill", self.vkill), ("cst_hscale", self.hscale)]:
            sch.add("sync", lambda q, buf=buf, name=name: q.dma_start(
                out=buf[:, :], in_=io[name]), writes=(K(buf),), dma=True, tag=name)
        for name, buf in [("cst_tcos", self.tcos), ("cst_tsin", self.tsin),
                          ("cst_tsq", self.tsq), ("cst_tski", self.tski),
                          ("cst_masks", self.masks)]:
            sch.add("gpsimd", lambda g, buf=buf, name=name: g.dma_start(
                out=buf[:, :], in_=io[name]), writes=(K(buf),), dma=True, tag=name)


def l0ffn_layer_decls(nc, li):
    d = {}
    d[f"wqkv{li}"] = nc.dram_tensor(f"wqkv{li}", [D, 1536], F32, kind="ExternalInput").ap()
    d[f"wo{li}"] = nc.dram_tensor(f"wo{li}", [512, D], F32, kind="ExternalInput").ap()
    d[f"w1p{li}"] = nc.dram_tensor(f"w1p{li}", [D, 2 * NJ * 128], F32, kind="ExternalInput").ap()
    d[f"b1p{li}"] = nc.dram_tensor(f"b1p{li}", [128, 2 * NJ], F32, kind="ExternalInput").ap()
    d[f"w2{li}"] = nc.dram_tensor(f"w2{li}", [2730, D], F32, kind="ExternalInput").ap()
    d[f"qs{li}"] = nc.dram_tensor(f"qs{li}", [128, 1], F32, kind="ExternalInput").ap()
    d[f"ks{li}"] = nc.dram_tensor(f"ks{li}", [128, 1], F32, kind="ExternalInput").ap()
    d[f"lg{li}"] = nc.dram_tensor(f"lg{li}", [128, 8], F32, kind="ExternalInput").ap()
    d[f"lb{li}"] = nc.dram_tensor(f"lb{li}", [128, 8], F32, kind="ExternalInput").ap()
    d[f"b2c{li}"] = nc.dram_tensor(f"b2c{li}", [128, 8], F32, kind="ExternalInput").ap()
    return d


def l0ffn_layer_host(inputs, li):
    w1 = np.asarray(inputs["L0_ffn_w1"][li], np.float32)
    b1 = np.asarray(inputs["L0_ffn_b1"][li], np.float32)
    HID = 2730
    w1p = np.zeros((D, 2 * NJ * 128), np.float32)
    b1p = np.zeros((128, 2 * NJ), np.float32)
    for j in range(NJ):
        nj = 128 if j < 21 else 42
        a = w1[:, 128 * j:128 * j + nj]
        g = w1[:, HID + 128 * j:HID + 128 * j + nj]
        base = 256 * j
        w1p[:, base:base + nj] = a
        w1p[:, base + 128:base + 128 + nj] = g
        b1p[0:nj, 2 * j] = b1[128 * j:128 * j + nj]
        b1p[0:nj, 2 * j + 1] = b1[HID + 128 * j:HID + 128 * j + nj]
    qs = np.asarray(inputs["L0_ffn_q_scale"][li], np.float32)
    ks = np.asarray(inputs["L0_ffn_k_scale"][li], np.float32)
    col = lambda v: np.concatenate([v, v])[:, None].astype(np.float32)
    pack8 = lambda v: np.asarray(v, np.float32).reshape(8, 128).T.copy()
    return {
        f"wqkv{li}": np.asarray(inputs["L0_ffn_wqkv"][li], np.float32),
        f"wo{li}": np.asarray(inputs["L0_ffn_wo"][li], np.float32),
        f"w1p{li}": w1p, f"b1p{li}": b1p,
        f"w2{li}": np.asarray(inputs["L0_ffn_w2"][li], np.float32),
        f"qs{li}": col(qs), f"ks{li}": col(ks),
        f"lg{li}": pack8(inputs["L0_ffn_ln_g"][li]),
        f"lb{li}": pack8(inputs["L0_ffn_ln_b"][li]),
        f"b2c{li}": pack8(inputs["L0_ffn_b2"][li]),
    }


def emit_l0ffn_layer(ctx, sch, xg, cst, io, li, xgk):
    nc = ctx.nc
    pb = ctx.pbanks
    EXT = 2049

    XHA = nc.dram_tensor(f"XHA{li}", [256, 8], BF16, addr_space="Shared")
    XHK = nc.dram_tensor(f"XHK{li}", [2 * 512, 512], BF16, addr_space="Shared")
    XHV = nc.dram_tensor(f"XHV{li}", [2 * 512, 520], BF16, addr_space="Shared")
    XBM = nc.dram_tensor(f"XBM{li}", [4, 4], F32, addr_space="Shared")
    XMB = nc.dram_tensor(f"XMB{li}", [256, 8], BF16, addr_space="Shared")

    m0 = ctx.mark()
    ymid = ctx.sb(f"ymid{li}", [128, 8 * EXT], BF16)
    amax = ctx.sb(f"amax{li}", [2, 16], F32)
    bmax = ctx.sb(f"bmax{li}", [2, 20], F32)
    bval = ctx.sb(f"bval{li}", [2, 16], F32)
    mrow = ctx.sb(f"mrow{li}", [1, 32], F32)
    bcol = ctx.sb(f"bcol{li}", [128, 32], F32)
    xrows = ctx.sb(f"xrows{li}", [128, 22], U32)
    sch.add("sync", lambda q: q.dma_start(out=xrows[:, :], in_=io["xrows"]),
            writes=(K(xrows),), dma=True, tag="xrowsld")

    m1 = ctx.mark()
    qrot = [ctx.sb(f"qrot{li}_{hp}", [128, T], BF16) for hp in range(4)]
    krot = [ctx.sb(f"krot{li}_{hp}", [128, 512 + T], BF16) for hp in range(4)]
    vbuf = [ctx.sb(f"vbuf{li}_{t}", [128, 520], BF16) for t in range(20)]
    wq3 = ctx.sb(f"wq3_{li}", [128, 8, 1536], BF16)
    wo4 = ctx.sb(f"wo4_{li}", [128, 4, 1024], BF16)

    sch.add("gpsimd", lambda g: g.dma_start(
        out=wq3[:, :, :], in_=io[f"wqkv{li}"].rearrange("(c p) n -> p c n", p=128)),
        writes=(K(wq3),), dma=True, tag="wqkvld")
    sch.add("gpsimd", lambda g: g.dma_start(
        out=wo4[:, :, :], in_=io[f"wo{li}"].rearrange("(c p) n -> p c n", p=128)),
        writes=(K(wo4),), dma=True, tag="wold")
    qs = ctx.sb(f"qsc{li}", [128, 1], F32)
    ks = ctx.sb(f"ksc{li}", [128, 1], F32)
    for nm, buf in [(f"qs{li}", qs), (f"ks{li}", ks)]:
        sch.add("sync", lambda q, nm=nm, buf=buf: q.dma_start(out=buf[:, :], in_=io[nm]),
                writes=(K(buf),), dma=True)
    for t in range(20):
        sch.add("gpsimd", lambda g, t=t: g.memset(vbuf[t][:, :], 1.0),
                writes=(K(vbuf[t]),), tag=f"vones{t}")

    m2 = ctx.mark()
    ybf = ctx.sb(f"ybf{li}", [128, 8 * EXT], BF16)
    scr2set = [[ctx.sb(f"s1scr{li}_{s}_{i}", [128, TG], F32) for i in range(6)]
               for s in range(2)]
    scr = scr2set[0]
    stg = ctx.sb(f"stg{li}", [128, 8], BF16)
    rq2set = [ctx.sb(f"rq2_{li}_{s}", [2, TG], F32) for s in range(2)]
    rq2 = rq2set[0]

    def ysl(buf, c, lo, n, shifted):
        off = lo + (0 if (shifted and c >= 4) else 1)
        return buf[:, EXT * c + off: EXT * c + off + n]

    for c in range(NCH):
        sch.add("gpsimd", lambda g, c=c: g.dma_start(
            out=ybf[:, EXT * c + 1:EXT * c + 1 + T],
            in_=io["yin"][128 * c:128 * (c + 1), :]),
            writes=(K(ybf),), dma=True, tag=f"yld{c}")

    stgp = ctx.sb(f"stgp{li}", [128, 8], BF16)
    sch.add("vector", lambda v: v.tensor_copy(
        stgp[:, :], ybf.rearrange("p (c n) -> p c n", n=EXT)[:, :, 2048]),
        reads=(K(ybf),), writes=(K(stgp),), tag="e2a_stage")
    ptk = sch.add("gpsimd", lambda g: g.indirect_dma_start(
        out=XHA.ap(), out_offset=IOA(ap=xrows[:, 0:1], axis=0),
        in_=stgp[:, :], in_offset=None),
        reads=(K(stgp), K(xrows)), writes=("XHA" + str(li),), dma=True, tag="e2a_push")
    xg.push(xgk + 0, [ptk])
    pk = xg.poll(xgk + 0)
    sch.add("gpsimd", lambda g: g.indirect_dma_start(
        out=stg[:, :], out_offset=None,
        in_=XHA.ap(), in_offset=IOA(ap=xrows[:, 1:2], axis=0)),
        reads=("XHA" + str(li), K(xrows)), writes=(K(stg),), dma=True,
        extra_deps=[pk], tag="e2a_pull")
    sch.add("gpsimd", lambda g: g.tensor_scalar_mul(stg[:, :], stg[:, :], cst.hscale[:, 0:1]),
            reads=(K(stg), K(cst.hscale)), writes=(K(stg),))
    sch.add("vector", lambda v: v.tensor_copy(
        ybf.rearrange("p (c n) -> p c n", n=EXT)[:, :, 0], stg[:, :]),
        reads=(K(stg),), writes=(K(ybf),), tag="e2a_wr")

    def qk_side(gi, hp, is_q):
        cid = gi * 8 + hp * 2 + (0 if is_q else 1)
        scr = scr2set[cid % 2]
        rq2 = rq2set[cid % 2]
        pbA = pb[cid % 4]
        pbB = pb[4 + cid % 4]
        lo = gi * TG
        colbase = (0 if is_q else 512) + 128 * hp
        bmm = pbA
        tab_c, tab_s = cst.tcos, cst.tsin
        tsc = cst.tsq if is_q else cst.tski
        sccol = qs if is_q else ks
        dst = qrot[hp] if is_q else krot[hp]
        dst_lo = lo if is_q else 512 + lo
        tab = amax if is_q else bmax
        tabcol = (4 * gi + hp) if is_q else (4 * (gi + 1) + hp)
        sch.add("tensor", MMG_(pb, bmm[:, :], [
            (lambda t, d=d, colbase=colbase: wq3[:, d, colbase:colbase + 128],
             lambda t, d=d, lo=lo: ysl(ybf, d, lo, TG, True)) for d in range(NCH)]),
            reads=(K(wq3), K(ybf)), writes=(K(bmm),), tag=f"qk{is_q}{gi}_{hp}")
        sch.add("scalar", lambda s, bmm=bmm: s.activation(
            out=scr[0][:, :], in_=bmm[:, :], func=AF.Square),
            reads=(K(bmm),), writes=(K(scr[0]),))
        sch.add("tensor", lambda t, scr=scr, pbB=pbB: t.matmul(
            pbB[0:2, :], cst.hpat2[:, :], scr[0][:, :], start=True, stop=True),
            reads=(K(cst.hpat2), K(scr[0])), writes=(K(pbB),))
        sch.add("vector", lambda v, rq2=rq2, pbB=pbB: v.tensor_scalar_max(rq2[0:2, :], pbB[0:2, :], 1e-24),
                reads=(K(pbB),), writes=(K(rq2),))
        sch.add("scalar", lambda s, rq2=rq2: s.activation(out=rq2[0:2, :], in_=rq2[0:2, :], func=AF.Ln),
                reads=(K(rq2),), writes=(K(rq2),))
        sch.add("scalar", lambda s, rq2=rq2: s.activation(out=rq2[0:2, :], in_=rq2[0:2, :], func=AF.Exp, scale=-0.5),
                reads=(K(rq2),), writes=(K(rq2),))
        sch.add("tensor", lambda t, rq2=rq2, pbB=pbB: t.matmul(
            pbB[:, :], cst.bpat2[:, :], rq2[0:2, :], start=True, stop=True),
            reads=(K(cst.bpat2), K(rq2)), writes=(K(pbB),))
        sch.add("scalar", lambda s, scr=scr, pbB=pbB: s.copy(scr[1][:, :], pbB[:, :]),
                reads=(K(pbB),), writes=(K(scr[1]),))
        sch.add("vector", lambda v, bmm=bmm, sccol=sccol, scr=scr: v.scalar_tensor_tensor(
            out=scr[2][:, :], in0=bmm[:, :], scalar=sccol[:, 0:1],
            in1=scr[1][:, :], op0=ALU.mult, op1=ALU.mult),
            reads=(K(bmm), K(scr[1]), K(sccol)), writes=(K(scr[2]),))
        sch.add("tensor", lambda t, scr=scr, pbB=pbB: t.matmul(
            pbB[:, :], cst.perm[:, :], scr[2][:, :], start=True, stop=True),
            reads=(K(cst.perm), K(scr[2])), writes=(K(pbB),))
        sch.add("vector", lambda v, lo=lo, tab_c=tab_c, scr=scr: v.tensor_mul(
            scr[3][:, :], scr[2][:, :], tab_c[:, lo:lo + TG]),
            reads=(K(scr[2]), K(tab_c)), writes=(K(scr[3]),))
        sch.add("vector", lambda v, lo=lo, tab_s=tab_s, scr=scr, pbB=pbB: v.tensor_mul(
            scr[4][:, :], pbB[:, :], tab_s[:, lo:lo + TG]),
            reads=(K(pbB), K(tab_s)), writes=(K(scr[4]),))
        sch.add("vector", lambda g, scr=scr: g.tensor_add(scr[5][:, :], scr[3][:, :], scr[4][:, :]),
                reads=(K(scr[3]), K(scr[4])), writes=(K(scr[5]),))
        sch.add("scalar", lambda s, scr=scr: s.activation(
            out=scr[0][:, :], in_=scr[5][:, :], func=AF.Square),
            reads=(K(scr[5]),), writes=(K(scr[0]),))
        sch.add("tensor", lambda t, scr=scr, pbB=pbB: t.matmul(
            pbB[0:2, :], cst.hpat2[:, :], scr[0][:, :], start=True, stop=True),
            reads=(K(cst.hpat2), K(scr[0])), writes=(K(pbB),))
        sch.add("vector", lambda v, rq2=rq2, pbB=pbB: v.reduce_max(rq2[0:2, 0:1], pbB[0:2, :],
                                                 axis=mybir.AxisListType.X),
                reads=(K(pbB),), writes=(K(rq2),))
        sch.add("scalar", lambda s, rq2=rq2: s.activation(
            out=rq2[0:2, 0:1], in_=rq2[0:2, 0:1], func=AF.Ln),
            reads=(K(rq2),), writes=(K(rq2),))
        sch.add("scalar", lambda s, tab=tab, tabcol=tabcol, rq2=rq2: s.activation(
            out=tab[0:2, tabcol:tabcol + 1], in_=rq2[0:2, 0:1], func=AF.Exp, scale=0.5),
            reads=(K(rq2),), writes=(K(tab),))
        sch.add("vector", lambda v, dst=dst, dst_lo=dst_lo, lo=lo, tsc=tsc, scr=scr: v.tensor_mul(
            dst[:, dst_lo:dst_lo + TG], scr[5][:, :], tsc[:, lo:lo + TG]),
            reads=(K(scr[5]), K(tsc)), writes=(K(dst),), tag=f"rot{is_q}{gi}{hp}")

    for gi in range(NTG):
        for hp in range(4):
            qk_side(gi, hp, True)
            qk_side(gi, hp, False)
        for lt in range(4):
            ls = gi * TG + 128 * lt
            bv = pb[lt % 4]
            sch.add("tensor", MMG_(pb, bv[:, :], [
                (lambda t, d=d, ls=ls: ysl(ybf, d, ls, 128, True),
                 lambda t, d=d: wq3[:, d, 1024:1536]) for d in range(NCH)]),
                reads=(K(ybf), K(wq3)), writes=(K(bv),), tag=f"vmm{gi}_{lt}")
            ti = 4 + 4 * gi + lt
            sch.add("vector", lambda v, bv=bv, ti=ti: v.tensor_copy(
                vbuf[ti].rearrange("p (h n) -> p h n", n=65)[:, :, 0:64], bv[:, :]),
                reads=(K(bv),), writes=(K(vbuf[ti]),), tag=f"vev{gi}_{lt}")

    deps = []
    for hp in range(4):
        deps.append(sch.add("gpsimd", lambda g, hp=hp: g.indirect_dma_start(
            out=XHK.ap(), out_offset=IOA(ap=xrows[:, 4 + hp:5 + hp], axis=0),
            in_=krot[hp][:, 2048:2560], in_offset=None),
            reads=(K(krot[hp]), K(xrows)), writes=(f"XHK{li}",), dma=True,
            tag=f"e2b_pk{hp}"))
    for i, t in enumerate(range(16, 20)):
        deps.append(sch.add("gpsimd", lambda g, t=t, i=i: g.indirect_dma_start(
            out=XHV.ap(), out_offset=IOA(ap=xrows[:, 8 + i:9 + i], axis=0),
            in_=vbuf[t][:, :], in_offset=None),
            reads=(K(vbuf[t]), K(xrows)), writes=(f"XHV{li}",), dma=True,
            tag=f"e2b_pv{i}"))
    deps.append(sch.add("gpsimd", lambda g: g.indirect_dma_start(
        out=XBM.ap(), out_offset=IOA(ap=xrows[0:2, 12:13], axis=0),
        in_=bmax[0:2, 16:20], in_offset=None),
        reads=(K(bmax), K(xrows)), writes=(f"XBM{li}",), dma=True, tag="e2b_pb"))
    xg.push(xgk + 1, deps)
    pk2 = xg.poll(xgk + 1)
    for hp in range(4):
        sch.add("gpsimd", lambda g, hp=hp: g.indirect_dma_start(
            out=krot[hp][:, 0:512], out_offset=None,
            in_=XHK.ap(), in_offset=IOA(ap=xrows[:, 13 + hp:14 + hp], axis=0)),
            reads=(f"XHK{li}", K(xrows)), writes=(K(krot[hp]),), dma=True,
            extra_deps=[pk2], tag=f"e2b_gk{hp}")
        sch.add("gpsimd", lambda g, hp=hp: g.tensor_scalar_mul(
            krot[hp][:, 0:512], krot[hp][:, 0:512], cst.vkill[:, 0:1]),
            reads=(K(krot[hp]), K(cst.vkill)), writes=(K(krot[hp]),))
    for i in range(4):
        sch.add("gpsimd", lambda g, i=i: g.indirect_dma_start(
            out=vbuf[i][:, 0:520], out_offset=None,
            in_=XHV.ap(), in_offset=IOA(ap=xrows[:, 17 + i:18 + i], axis=0)),
            reads=(f"XHV{li}", K(xrows)), writes=(K(vbuf[i]),), dma=True,
            extra_deps=[pk2], tag=f"e2b_gv{i}")
        sch.add("gpsimd", lambda g, i=i: g.tensor_scalar_mul(
            vbuf[i][:, :], vbuf[i][:, :], cst.vkill[:, 0:1]),
            reads=(K(vbuf[i]), K(cst.vkill)), writes=(K(vbuf[i]),))
    sch.add("gpsimd", lambda g: g.indirect_dma_start(
        out=bmax[0:2, 0:4], out_offset=None,
        in_=XBM.ap(), in_offset=IOA(ap=xrows[0:2, 21:22], axis=0)),
        reads=(f"XBM{li}", K(xrows)), writes=(K(bmax),), dma=True,
        extra_deps=[pk2], tag="e2b_gb")

    sch.add("vector", lambda v: v.tensor_max(bval[0:2, :], bmax[0:2, 4:20], bmax[0:2, 0:16]),
            reads=(K(bmax),), writes=(K(bval),))
    sch.add("vector", lambda v: v.tensor_mul(bval[0:2, :], bval[0:2, :], amax[0:2, :]),
            reads=(K(bval), K(amax)), writes=(K(bval),))
    sch.add("vector", lambda v: v.tensor_scalar(
        out=bval[0:2, :], in0=bval[0:2, :], scalar1=-8.0, scalar2=45.0,
        op0=ALU.mult, op1=ALU.add),
        reads=(K(bval),), writes=(K(bval),))
    for r in range(2):
        sch.add("tensor", lambda t, r=r: t.matmul(
            pb[2][0:1, 16 * r:16 * r + 16], cst.e01[:, r:r + 1], bval[0:2, :],
            start=True, stop=True),
            reads=(K(cst.e01), K(bval)), writes=(K(pb[2]),), tag=f"mrow{r}")
    sch.add("vector", lambda v: v.tensor_copy(mrow[0:1, :], pb[2][0:1, 0:32]),
            reads=(K(pb[2]),), writes=(K(mrow),))
    sch.add("tensor", lambda t: t.matmul(
        pb[3][:, 0:32], cst.ones_f[0:1, :], mrow[0:1, :], start=True, stop=True),
        reads=(K(cst.ones_f), K(mrow)), writes=(K(pb[3]),))
    sch.add("scalar", lambda s: s.copy(bcol[:, :], pb[3][:, 0:32]),
            reads=(K(pb[3]),), writes=(K(bcol),))

    ctx.reset(m2)
    sch.set_fence()
    pT = [ctx.sb(f"pT{li}_{i}", [128, TG], BF16) for i in range(4)]
    ofb = [ctx.sb(f"ofb{li}_{i}", [128, TG], BF16) for i in range(4)]
    recs = [ctx.sb(f"recs{li}_{i}", [1, TG], F32) for i in range(2)]
    scrR = ctx.sb(f"scrR{li}", [128, TG], F32)
    scrX = [ctx.sb(f"scrX{li}_{i}", [128, TG], F32) for i in range(2)]
    scrY = [ctx.sb(f"scrY{li}_{i}", [128, TG], F32) for i in range(2)]

    for w in range(NW):
        for hp in range(4):
            bO = [pb[4], pb[5]]
            for kt in range(8):
                bS = [pb[2 * (kt % 2)], pb[2 * (kt % 2) + 1]]
                for par in range(2):
                    rs = 64 * par
                    def em_s(t, w=w, hp=hp, kt=kt, rs=rs, par=par, bS=bS):
                        return t.matmul(
                            bS[par][:, :],
                            krot[hp][rs:rs + 64, 512 * w + 128 * kt:512 * w + 128 * kt + 128],
                            qrot[hp][rs:rs + 64, 512 * w:512 * w + 512],
                            start=True, stop=True)
                    sch.add("tensor", em_s, reads=(K(krot[hp]), K(qrot[hp])),
                            writes=(K(bS[par]),), tag=f"S{w}_{hp}_{kt}_{par}")
                    hgl = 2 * hp + par
                    bidx = 16 * par + 4 * w + hp
                    def em_e(s, par=par, bidx=bidx, kt=kt, bS=bS):
                        return s.activation(
                            out=pT[2 * (kt % 2) + par][:, :], in_=bS[par][:, :],
                            func=AF.Exp, bias=bcol[:, bidx:bidx + 1], scale=8.0)
                    sch.add("scalar", em_e, reads=(K(bS[par]), K(bcol)),
                            writes=(K(pT[2 * (kt % 2) + par]),), tag=f"E{w}_{hp}_{kt}_{par}")
                    def em_m(g, par=par, kt=kt):
                        buf = pT[2 * (kt % 2) + par]
                        return g.tensor_mul(buf[:, :], buf[:, :],
                                            cst.masks[:, TG * kt:TG * (kt + 1)])
                    sch.add("vector", em_m, reads=(K(pT[2 * (kt % 2) + par]), K(cst.masks)),
                            writes=(K(pT[2 * (kt % 2) + par]),), tag=f"M{w}_{hp}_{kt}_{par}")
                    def em_pv(t, w=w, kt=kt, hgl=hgl, par=par, bO=bO):
                        return t.matmul(
                            bO[par][0:65, :], vbuf[4 * w + kt][:, 65 * hgl:65 * hgl + 65],
                            pT[2 * (kt % 2) + par][:, :], start=(kt == 0), stop=(kt == 7))
                    sch.add("tensor", em_pv,
                            reads=(K(vbuf[4 * w + kt]), K(pT[2 * (kt % 2) + par])),
                            writes=(K(bO[par]),), tag=f"PV{w}_{hp}_{kt}_{par}")
            for par in range(2):
                sch.add("vector", lambda v, par=par, bO=bO: v.reciprocal(
                    recs[par][0:1, :], bO[par][64:65, :]),
                    reads=(K(bO[par]),), writes=(K(recs[par]),))
            def em_rb(t, bO=bO):
                t.matmul(pb[6][0:64, :], cst.ones_f[0:1, 0:64], recs[0][0:1, :],
                         start=True, stop=True)
                return t.matmul(pb[6][64:128, :], cst.ones_f[0:1, 0:64], recs[1][0:1, :],
                                start=True, stop=True)
            sch.add("tensor", em_rb, reads=(K(cst.ones_f), K(recs[0]), K(recs[1])),
                    writes=(K(pb[6]),), tag=f"RB{w}_{hp}")
            sch.add("scalar", lambda s: s.copy(scrR[:, :], pb[6][:, :]),
                    reads=(K(pb[6]),), writes=(K(scrR),))
            for par in range(2):
                rs = 64 * par
                sch.add("vector", lambda v, par=par, rs=rs, hp=hp, bO=bO: v.tensor_mul(
                    ofb[hp][rs:rs + 64, :], bO[par][0:64, :], scrR[rs:rs + 64, :]),
                    reads=(K(bO[par]), K(scrR)), writes=(K(ofb[hp]),), tag=f"OF{w}_{hp}_{par}")
        for oc in range(NCH):
            bY = pb[2 + oc % 2]
            sch.add("tensor", MMG_(pb, bY[:, :], [
                (lambda t, hp=hp2, oc=oc: wo4[:, hp, 128 * oc:128 * oc + 128],
                 lambda t, hp=hp2: ofb[hp][:, :]) for hp2 in range(4)]),
                reads=tuple(K(ofb[i]) for i in range(4)) + (K(wo4),),
                writes=(K(bY),), tag=f"WO{w}_{oc}")
            xse = scrX[oc % 2]
            sch.add("sync", lambda q, oc=oc, w=w, xse=xse: q.dma_start(
                out=xse[:, :], in_=io["yin"][128 * oc:128 * (oc + 1), TG * w:TG * (w + 1)]),
                writes=(K(xse),), dma=True, tag=f"yld2_{w}_{oc}")
            yse = scrY[oc % 2]
            sch.add("vector", lambda v, bY=bY, xse=xse, yse=yse: v.tensor_add(
                yse[:, :], bY[:, :], xse[:, :]),
                reads=(K(bY), K(xse)), writes=(K(yse),), tag=f"YM{w}_{oc}")
            sch.add("sync", lambda q, oc=oc, w=w, yse=yse: q.dma_start(
                out=io["ymid"][128 * oc:128 * (oc + 1), TG * w:TG * (w + 1)], in_=yse[:, :]),
                reads=(K(yse),), writes=(f"ymidd{li}_{w}_{oc}",), dma=True,
                tag=f"YMS{w}_{oc}")
            sch.add("scalar", lambda s, oc=oc, w=w, yse=yse: s.copy(
                ymid[:, EXT * oc + 1 + TG * w: EXT * oc + 1 + TG * (w + 1)], yse[:, :]),
                reads=(K(yse),), writes=(K(ymid),), tag=f"YMB{w}_{oc}")

    stg2 = ctx.sb(f"stg2{li}", [128, 8], BF16)
    stg2p = ctx.sb(f"stg2p{li}", [128, 8], BF16)
    sch.add("vector", lambda v: v.tensor_copy(
        stg2p[:, :], ymid.rearrange("p (c n) -> p c n", n=EXT)[:, :, 2048]),
        reads=(K(ymid),), writes=(K(stg2p),), tag="e3_stage")
    ptk3 = sch.add("gpsimd", lambda g: g.indirect_dma_start(
        out=XMB.ap(), out_offset=IOA(ap=xrows[:, 2:3], axis=0),
        in_=stg2p[:, :], in_offset=None),
        reads=(K(stg2p), K(xrows)), writes=(f"XMB{li}",), dma=True, tag="e3_push")
    xg.push(xgk + 2, [ptk3])
    pk3 = xg.poll(xgk + 2)
    sch.add("gpsimd", lambda g: g.indirect_dma_start(
        out=stg2[:, :], out_offset=None,
        in_=XMB.ap(), in_offset=IOA(ap=xrows[:, 3:4], axis=0)),
        reads=(f"XMB{li}", K(xrows)), writes=(K(stg2),), dma=True,
        extra_deps=[pk3], tag="e3_pull")
    if P2_DEBUG:
        sch.add("sync", lambda q: q.dma_start(out=io["dbg_stg2"], in_=stg2[:, :]),
                reads=(K(stg2),), writes=("dbg_stg2",), dma=True)
    sch.add("gpsimd", lambda g: g.tensor_scalar_mul(stg2[:, :], stg2[:, :], cst.hscale[:, 0:1]),
            reads=(K(stg2), K(cst.hscale)), writes=(K(stg2),))
    sch.add("vector", lambda v: v.tensor_copy(
        ymid.rearrange("p (c n) -> p c n", n=EXT)[:, :, 0], stg2[:, :]),
        reads=(K(stg2),), writes=(K(ymid),), tag="e3_wr")

    ctx.reset(m1)
    sch.set_fence()
    hnb = ctx.sb(f"hnb{li}", [128, 8 * TG], BF16)
    h2g = ctx.sb(f"h2g{li}", [128, NJ * TG], BF16)
    w1t = [ctx.sb(f"w1t{li}_{i}", [128, 8, 256], BF16) for i in range(2)]
    w2t = [ctx.sb(f"w2t{li}_{i}", [128, 1024], BF16) for i in range(2)]
    scr3 = [ctx.sb(f"s3scr{li}_{i}", [128, TG], F32) for i in range(6)]
    scr3m = ctx.sb(f"s3scrm{li}", [128, TG], F32)
    scr3b = [ctx.sb(f"s3scrb{li}_{i}", [128, TG], BF16) for i in range(2)]
    srow = [ctx.sb(f"s3row{li}_{i}", [1, TG], F32) for i in range(6)]
    lgc = ctx.sb(f"lgc{li}", [128, 8], F32)
    lbc = ctx.sb(f"lbc{li}", [128, 8], F32)
    b1c = ctx.sb(f"b1c{li}", [128, 2 * NJ], F32)
    b2c = ctx.sb(f"b2cc{li}", [128, 8], F32)
    for nm, buf in [(f"lg{li}", lgc), (f"lb{li}", lbc), (f"b1p{li}", b1c),
                    (f"b2c{li}", b2c)]:
        sch.add("sync", lambda q, nm=nm, buf=buf: q.dma_start(out=buf[:, :], in_=io[nm]),
                writes=(K(buf),), dma=True)

    for gi in range(NTG):
        lo = gi * TG
        bsum, bsq = pb[6], pb[7]
        for c in range(NCH):
            def em_sum(t, c=c, lo=lo, bsum=bsum):
                return t.matmul(bsum[0:1, :], cst.ones_b[:, 0:1],
                                ysl(ymid, c, lo, TG, True), start=(c == 0), stop=(c == NCH - 1))
            sch.add("tensor", em_sum, reads=(K(ymid), K(cst.ones_b)), writes=(K(bsum),),
                    tag=f"hsum{gi}_{c}")
            sq = scr3b[c % 2]
            sch.add("scalar", lambda s, c=c, lo=lo, sq=sq: s.activation(
                out=sq[:, :], in_=ysl(ymid, c, lo, TG, True), func=AF.Square),
                reads=(K(ymid),), writes=(K(sq),))
            def em_sq(t, c=c, sq=sq, bsq=bsq):
                return t.matmul(bsq[0:1, :], cst.ones_b[:, 0:1], sq[:, :],
                                start=(c == 0), stop=(c == NCH - 1))
            sch.add("tensor", em_sq, reads=(K(sq), K(cst.ones_b)), writes=(K(bsq),),
                    tag=f"hsq{gi}_{c}")
        sch.add("scalar", lambda s, bsum=bsum: s.activation(
            out=srow[0][0:1, :], in_=bsum[0:1, :], func=AF.Copy, scale=1.0 / D),
            reads=(K(bsum),), writes=(K(srow[0]),))
        sch.add("scalar", lambda s, bsq=bsq: s.activation(
            out=srow[1][0:1, :], in_=bsq[0:1, :], func=AF.Copy, scale=1.0 / D),
            reads=(K(bsq),), writes=(K(srow[1]),))
        sch.add("scalar", lambda s: s.activation(
            out=srow[2][0:1, :], in_=srow[0][0:1, :], func=AF.Square),
            reads=(K(srow[0]),), writes=(K(srow[2]),))
        sch.add("vector", lambda v: v.tensor_sub(srow[3][0:1, :], srow[1][0:1, :], srow[2][0:1, :]),
                reads=(K(srow[1]), K(srow[2])), writes=(K(srow[3]),))
        sch.add("vector", lambda v: v.tensor_scalar_add(srow[3][0:1, :], srow[3][0:1, :], 1e-5),
                reads=(K(srow[3]),), writes=(K(srow[3]),))
        sch.add("scalar", lambda s: s.activation(out=srow[4][0:1, :], in_=srow[3][0:1, :],
                                                 func=AF.Ln),
                reads=(K(srow[3]),), writes=(K(srow[4]),))
        sch.add("scalar", lambda s: s.activation(out=srow[5][0:1, :], in_=srow[4][0:1, :],
                                                 func=AF.Exp, scale=-0.5),
                reads=(K(srow[4]),), writes=(K(srow[5]),))
        sch.add("vector", lambda v: v.tensor_mul(srow[2][0:1, :], srow[0][0:1, :], srow[5][0:1, :]),
                reads=(K(srow[0]), K(srow[5])), writes=(K(srow[2]),))
        br, bm = pb[4], pb[5]
        sch.add("tensor", lambda t, br=br: t.matmul(
            br[:, :], cst.ones_f[0:1, :], srow[5][0:1, :], start=True, stop=True),
            reads=(K(cst.ones_f), K(srow[5])), writes=(K(br),))
        sch.add("tensor", lambda t, bm=bm: t.matmul(
            bm[:, :], cst.ones_f[0:1, :], srow[2][0:1, :], start=True, stop=True),
            reads=(K(cst.ones_f), K(srow[2])), writes=(K(bm),))
        sch.add("scalar", lambda s, bm=bm: s.copy(scr3m[:, :], bm[:, :]),
                reads=(K(bm),), writes=(K(scr3m),))
        for c in range(NCH):
            sa = scr3[c % 2]
            sb_ = scr3[4 + c % 2]
            sch.add("vector", lambda v, c=c, lo=lo, br=br, sa=sa: v.tensor_mul(
                sa[:, :], ysl(ymid, c, lo, TG, True), br[:, :]),
                reads=(K(ymid), K(br)), writes=(K(sa),))
            sch.add("vector", lambda g, sa=sa, sb_=sb_: g.tensor_sub(sb_[:, :], sa[:, :], scr3m[:, :]),
                    reads=(K(sa), K(scr3m)), writes=(K(sb_),))
            sch.add("vector", lambda v, c=c, sb_=sb_: v.tensor_scalar(
                out=hnb[:, TG * c:TG * (c + 1)], in0=sb_[:, :],
                scalar1=lgc[:, c:c + 1], scalar2=lbc[:, c:c + 1],
                op0=ALU.mult, op1=ALU.add),
                reads=(K(sb_), K(lgc), K(lbc)), writes=(K(hnb),), tag=f"hnb{gi}_{c}")
        if P2_DEBUG and gi == 0:
            sch.add("sync", lambda q: q.dma_start(out=io["dbg_hnb"], in_=hnb[:, :]),
                    reads=(K(hnb),), writes=("dbg_hnb",), dma=True)
            sch.add("sync", lambda q: q.dma_start(out=io["dbg_mu"], in_=srow[0][0:1, :]),
                    reads=(K(srow[0]),), writes=("dbg_mu",), dma=True)
            sch.add("sync", lambda q: q.dma_start(out=io["dbg_rstd"], in_=srow[5][0:1, :]),
                    reads=(K(srow[5]),), writes=("dbg_rstd",), dma=True)
        for j in range(NJ):
            nj = 128 if j < 21 else 42
            wt = w1t[j % 2]
            sch.add("gpsimd", lambda g, j=j, wt=wt: g.dma_start(
                out=wt[:, :, :],
                in_=io[f"w1p{li}"][:, 256 * j:256 * (j + 1)].rearrange("(c p) n -> p c n", p=128)),
                writes=(K(wt),), dma=True, tag=f"w1ld{gi}_{j}")
            bA, bG = pb[2 * (j % 2)], pb[2 * (j % 2) + 1]
            sch.add("tensor", MMG_(pb, bA[0:nj, :], [
                (lambda t, c=c, wt=wt, nj=nj: wt[:, c, 0:nj],
                 lambda t, c=c: hnb[:, TG * c:TG * (c + 1)]) for c in range(NCH)]),
                reads=(K(wt), K(hnb)), writes=(K(bA),), tag=f"amm{gi}_{j}")
            sch.add("tensor", MMG_(pb, bG[0:nj, :], [
                (lambda t, c=c, wt=wt, nj=nj: wt[:, c, 128:128 + nj],
                 lambda t, c=c: hnb[:, TG * c:TG * (c + 1)]) for c in range(NCH)]),
                reads=(K(wt), K(hnb)), writes=(K(bG),), tag=f"gmm{gi}_{j}")
            gsc = scr3[2 + j % 2]
            sch.add("scalar", lambda s, j=j, nj=nj, bG=bG, gsc=gsc: s.activation(
                out=gsc[0:nj, :], in_=bG[0:nj, :], func=AF.Gelu,
                bias=b1c[0:nj, 2 * j + 1:2 * j + 2]),
                reads=(K(bG), K(b1c)), writes=(K(gsc),))
            sch.add("vector", lambda v, j=j, nj=nj, bA=bA, gsc=gsc: v.scalar_tensor_tensor(
                out=h2g[0:nj, TG * j:TG * (j + 1)], in0=bA[0:nj, :],
                scalar=b1c[0:nj, 2 * j:2 * j + 1], in1=gsc[0:nj, :],
                op0=ALU.add, op1=ALU.mult),
                reads=(K(bA), K(gsc), K(b1c)), writes=(K(h2g),), tag=f"h2{gi}_{j}")
        if P2_DEBUG and gi == 0:
            sch.add("sync", lambda q: q.dma_start(out=io["dbg_h2"], in_=h2g[:, 0:2 * TG]),
                    reads=(K(h2g),), writes=("dbg_h2",), dma=True)
        for j in range(NJ):
            nj = 128 if j < 21 else 42
            wt = w2t[j % 2]
            sch.add("gpsimd", lambda g, j=j, nj=nj, wt=wt: g.dma_start(
                out=wt[0:nj, :], in_=io[f"w2{li}"][128 * j:128 * j + nj, :]),
                writes=(K(wt),), dma=True, tag=f"w2ld{gi}_{j}")
            for oc in range(NCH):
                def em_w2(t, j=j, nj=nj, oc=oc, wt=wt):
                    return t.matmul(pb[oc][:, :], wt[0:nj, 128 * oc:128 * (oc + 1)],
                                    h2g[0:nj, TG * j:TG * (j + 1)],
                                    start=(j == 0), stop=(j == NJ - 1))
                sch.add("tensor", em_w2, reads=(K(wt), K(h2g)), writes=(K(pb[oc]),),
                        tag=f"w2mm{gi}_{j}_{oc}")
        for oc in range(NCH):
            xse = scr3[oc % 2]
            sch.add("sync", lambda q, oc=oc, lo=lo, xse=xse: q.dma_start(
                out=xse[:, :], in_=io["ymid"][128 * oc:128 * (oc + 1), lo:lo + TG]),
                writes=(K(xse),), dma=True, tag=f"ymld{gi}_{oc}")
            yse = scr3[4 + oc % 2]
            sch.add("vector", lambda v, oc=oc, xse=xse, yse=yse: v.scalar_tensor_tensor(
                out=yse[:, :], in0=pb[oc][:, :], scalar=b2c[:, oc:oc + 1],
                in1=xse[:, :], op0=ALU.add, op1=ALU.add),
                reads=(K(pb[oc]), K(xse), K(b2c)), writes=(K(yse),), tag=f"yo{gi}_{oc}")
            sch.add("sync", lambda q, oc=oc, lo=lo, yse=yse: q.dma_start(
                out=io["yout"][128 * oc:128 * (oc + 1), lo:lo + TG], in_=yse[:, :]),
                reads=(K(yse),), writes=(f"yout{li}_{gi}_{oc}",), dma=True,
                tag=f"yos{gi}_{oc}")
    if P2_DEBUG:
        sch.add("sync", lambda q: q.dma_start(out=io["dbg_polls"], in_=xg.iters[0:1, 0:3]),
                reads=(K(xg.iters),), writes=("dbg_polls",), dma=True)
    ctx.reset(m0)
    sch.set_fence()


def MMG_(pb, bank_ap, pairs):
    def em(t):
        n = len(pairs)
        last = None
        for i, (lf, rf) in enumerate(pairs):
            last = t.matmul(bank_ap, lf(t), rf(t), start=(i == 0), stop=(i == n - 1))
        return last
    return em


def xrows_host(core):
    h = core % 2
    p = np.arange(128, dtype=np.int64)
    cols = []
    cols.append(128 * h + p)
    cols.append(128 * (1 - h) + p)
    cols.append(128 * h + p)
    cols.append(128 * (1 - h) + p)
    for hp in range(4):
        cols.append(512 * h + 128 * hp + p)
    for i in range(4):
        cols.append(512 * h + 128 * i + p)
    cols.append(2 * h + p)
    for hp in range(4):
        cols.append(512 * (1 - h) + 128 * hp + p)
    for i in range(4):
        cols.append(512 * (1 - h) + 128 * i + p)
    cols.append(2 * (1 - h) + p)
    return np.stack(cols, axis=1).astype(np.uint32)


def build_p2(li=0):
    nc = bass.Bass("TRN2", target_bir_lowering=False)
    ctx = Ctx(nc)
    io = {}
    io["yin"] = nc.dram_tensor("yin", [D, T], F32, kind="ExternalInput").ap()
    io["ymid"] = nc.dram_tensor("ymid_d", [D, T], F32, kind="ExternalOutput").ap()
    io["yout"] = nc.dram_tensor("yout", [D, T], F32, kind="ExternalOutput").ap()
    io["xrows"] = nc.dram_tensor("xrows", [128, 22], U32, kind="ExternalInput").ap()
    if P2_DEBUG:
        io["dbg_hnb"] = nc.dram_tensor("dbg_hnb", [128, 8 * TG], BF16, kind="ExternalOutput").ap()
        io["dbg_mu"] = nc.dram_tensor("dbg_mu", [1, TG], F32, kind="ExternalOutput").ap()
        io["dbg_rstd"] = nc.dram_tensor("dbg_rstd", [1, TG], F32, kind="ExternalOutput").ap()
        io["dbg_h2"] = nc.dram_tensor("dbg_h2", [128, 2 * TG], BF16, kind="ExternalOutput").ap()
        io["dbg_stg2"] = nc.dram_tensor("dbg_stg2", [128, 8], BF16, kind="ExternalOutput").ap()
        io["dbg_polls"] = nc.dram_tensor("dbg_polls", [1, 3], I32, kind="ExternalOutput").ap()
    io.update(l0ffn_const_decls(nc))
    io.update(gcst_decls(nc))
    io.update(l0ffn_layer_decls(nc, li))
    io.update(Exchange.input_decls(nc, 3))
    with nc.Block() as blk:
        sch = Sched(ctx)
        xg = Exchange(ctx, sch, 3)
        xg.emit_load(io)
        gc = GlobalCst(ctx, sch, io)
        cst = L0Consts(ctx, sch, io, gc)
        emit_l0ffn_layer(ctx, sch, xg, cst, io, li, 0)
        sch.flush(blk)
    ctx.st.close()
    return nc


def run_p2(inputs, yin_percore, li=0):
    nc = build_p2(li)
    in_maps = []
    for c in range(NCORE):
        d = {"yin": yin_percore[c], "xrows": xrows_host(c)}
        d.update(l0ffn_const_host(c))
        d.update(l0ffn_layer_host(inputs, li))
        in_maps.append(d)
    seed = int(np.random.default_rng().integers(2 ** 31))
    for c in range(NCORE):
        pair_rng = np.random.default_rng(seed + (c // 2))
        xgi = Exchange.__new__(Exchange)
        xgi.n_flags = 3
        in_maps[c].update(Exchange.host_inputs(xgi, c % 2, pair_rng))
    res = run_bass_kernel_spmd(nc, in_maps, core_ids=list(range(NCORE)))
    return res.results


def emit_postln(ctx, sch, cst, io_a, io_b, io_out, tagp):
    pb = ctx.pbanks
    m = ctx.mark()
    yg = [ctx.sb(f"{tagp}yg{c}", [128, TG], F32) for c in range(NCH)]
    xs = [ctx.sb(f"{tagp}xs{i}", [128, TG], F32) for i in range(4)]
    so = [ctx.sb(f"{tagp}so{i}", [128, TG], F32) for i in range(2)]
    scq = [ctx.sb(f"{tagp}scq{i}", [128, TG], BF16) for i in range(2)]
    srow = [ctx.sb(f"{tagp}srow{i}", [1, TG], F32) for i in range(6)]
    for gi in range(NTG):
        lo = gi * TG
        bsum, bsq = pb[6], pb[7]
        for c in range(NCH):
            xa = xs[2 * (c % 2)]
            xb_ = xs[2 * (c % 2) + 1]
            sch.add("sync", lambda q, c=c, lo=lo, xa=xa: q.dma_start(
                out=xa[:, :], in_=io_a[128 * c:128 * (c + 1), lo:lo + TG]),
                writes=(K(xa),), dma=True, tag=f"{tagp}la{gi}_{c}")
            sch.add("scalar", lambda q, c=c, lo=lo, xb_=xb_: q.dma_start(
                out=xb_[:, :], in_=io_b[128 * c:128 * (c + 1), lo:lo + TG]),
                writes=(K(xb_),), dma=True, tag=f"{tagp}lb{gi}_{c}")
            sch.add("vector", lambda g, c=c, xa=xa, xb_=xb_: g.tensor_add(
                yg[c][:, :], xa[:, :], xb_[:, :]),
                reads=(K(xa), K(xb_)), writes=(K(yg[c]),), tag=f"{tagp}y{gi}_{c}")
            def em_sum(t, c=c, bsum=bsum):
                return t.matmul(bsum[0:1, :], cst.ones_f[:, 0:1], yg[c][:, :],
                                start=(c == 0), stop=(c == NCH - 1))
            sch.add("tensor", em_sum, reads=(K(yg[c]), K(cst.ones_f)), writes=(K(bsum),))
            sq = scq[c % 2]
            sch.add("scalar", lambda s, c=c, sq=sq: s.activation(
                out=sq[:, :], in_=yg[c][:, :], func=AF.Square),
                reads=(K(yg[c]),), writes=(K(sq),))
            def em_sq(t, c=c, sq=sq, bsq=bsq):
                return t.matmul(bsq[0:1, :], cst.ones_b[:, 0:1], sq[:, :],
                                start=(c == 0), stop=(c == NCH - 1))
            sch.add("tensor", em_sq, reads=(K(sq), K(cst.ones_b)), writes=(K(bsq),))
        sch.add("scalar", lambda s, bsum=bsum: s.activation(
            out=srow[0][0:1, :], in_=bsum[0:1, :], func=AF.Copy, scale=1.0 / D),
            reads=(K(bsum),), writes=(K(srow[0]),))
        sch.add("scalar", lambda s, bsq=bsq: s.activation(
            out=srow[1][0:1, :], in_=bsq[0:1, :], func=AF.Copy, scale=1.0 / D),
            reads=(K(bsq),), writes=(K(srow[1]),))
        sch.add("scalar", lambda s: s.activation(
            out=srow[2][0:1, :], in_=srow[0][0:1, :], func=AF.Square),
            reads=(K(srow[0]),), writes=(K(srow[2]),))
        sch.add("vector", lambda v: v.tensor_sub(srow[3][0:1, :], srow[1][0:1, :], srow[2][0:1, :]),
                reads=(K(srow[1]), K(srow[2])), writes=(K(srow[3]),))
        sch.add("vector", lambda v: v.tensor_scalar_add(srow[3][0:1, :], srow[3][0:1, :], 1e-5),
                reads=(K(srow[3]),), writes=(K(srow[3]),))
        sch.add("scalar", lambda s: s.activation(out=srow[4][0:1, :], in_=srow[3][0:1, :],
                                                 func=AF.Ln),
                reads=(K(srow[3]),), writes=(K(srow[4]),))
        sch.add("scalar", lambda s: s.activation(out=srow[5][0:1, :], in_=srow[4][0:1, :],
                                                 func=AF.Exp, scale=-0.5),
                reads=(K(srow[4]),), writes=(K(srow[5]),))
        sch.add("vector", lambda v: v.tensor_mul(srow[2][0:1, :], srow[0][0:1, :], srow[5][0:1, :]),
                reads=(K(srow[0]), K(srow[5])), writes=(K(srow[2]),))
        br, bm = pb[4], pb[5]
        sch.add("tensor", lambda t, br=br: t.matmul(
            br[:, :], cst.ones_f[0:1, :], srow[5][0:1, :], start=True, stop=True),
            reads=(K(cst.ones_f), K(srow[5])), writes=(K(br),))
        sch.add("tensor", lambda t, bm=bm: t.matmul(
            bm[:, :], cst.ones_f[0:1, :], srow[2][0:1, :], start=True, stop=True),
            reads=(K(cst.ones_f), K(srow[2])), writes=(K(bm),))
        for c in range(NCH):
            sv = so[c % 2]
            sch.add("vector", lambda v, c=c, br=br, sv=sv: v.tensor_mul(
                sv[:, :], yg[c][:, :], br[:, :]),
                reads=(K(yg[c]), K(br)), writes=(K(sv),))
            sch.add("vector", lambda v, bm=bm, sv=sv: v.tensor_sub(
                sv[:, :], sv[:, :], bm[:, :]),
                reads=(K(sv), K(bm)), writes=(K(sv),))
            sch.add("sync", lambda q, c=c, lo=lo, sv=sv: q.dma_start(
                out=io_out[128 * c:128 * (c + 1), lo:lo + TG], in_=sv[:, :]),
                reads=(K(sv),), writes=(f"{tagp}o{gi}_{c}",), dma=True,
                tag=f"{tagp}st{gi}_{c}")
    ctx.reset(m)
    sch.set_fence()


def build_p3(li=1):
    nc = bass.Bass("TRN2", target_bir_lowering=False)
    ctx = Ctx(nc)
    io = {}
    io["yin"] = nc.dram_tensor("yin", [D, T], F32, kind="ExternalInput").ap()
    io["s1"] = nc.dram_tensor("s1", [D, T], F32, kind="ExternalInput").ap()
    io["ymid"] = nc.dram_tensor("ymid_d", [D, T], F32, kind="ExternalOutput").ap()
    io["yout"] = nc.dram_tensor("yout", [D, T], F32, kind="ExternalOutput").ap()
    io["s2"] = nc.dram_tensor("s2", [D, T], F32, kind="ExternalOutput").ap()
    io["xrows"] = nc.dram_tensor("xrows", [128, 22], U32, kind="ExternalInput").ap()
    io.update(l0ffn_const_decls(nc))
    io.update(gcst_decls(nc))
    io.update(l0ffn_layer_decls(nc, li))
    io.update(Exchange.input_decls(nc, 3))
    with nc.Block() as blk:
        sch = Sched(ctx)
        xg = Exchange(ctx, sch, 3)
        xg.emit_load(io)
        gc = GlobalCst(ctx, sch, io)
        cst = L0Consts(ctx, sch, io, gc)
        emit_l0ffn_layer(ctx, sch, xg, cst, io, li, 0)
        emit_postln(ctx, sch, cst, io["s1"], io["yout"], io["s2"], "pl3")
        sch.flush(blk)
    ctx.st.close()
    return nc


def run_p3(inputs, y1_percore, s1_percore, li=1):
    nc = build_p3(li)
    in_maps = []
    for c in range(NCORE):
        d = {"yin": y1_percore[c], "s1": s1_percore[c], "xrows": xrows_host(c)}
        d.update(l0ffn_const_host(c))
        d.update(l0ffn_layer_host(inputs, li))
        in_maps.append(d)
    seed = int(np.random.default_rng().integers(2 ** 31))
    for c in range(NCORE):
        pair_rng = np.random.default_rng(seed + (c // 2))
        xgi = Exchange.__new__(Exchange)
        xgi.n_flags = 3
        in_maps[c].update(Exchange.host_inputs(xgi, c % 2, pair_rng))
    res = run_bass_kernel_spmd(nc, in_maps, core_ids=list(range(NCORE)))
    return res.results


NH1 = 16
LAT = 128


def p4_decls(nc, with_stream=True):
    d = {}
    if with_stream:
        d["s2i"] = nc.dram_tensor("s2i", [D, T], F32, kind="ExternalInput").ap()
    d["Wq1p"] = nc.dram_tensor("Wq1p", [D, D], F32, kind="ExternalInput").ap()
    d["Wkv1"] = nc.dram_tensor("Wkv1", [D, LAT], F32, kind="ExternalInput").ap()
    d["Wk1p"] = nc.dram_tensor("Wk1p", [LAT, 768], F32, kind="ExternalInput").ap()
    d["Wv1"] = nc.dram_tensor("Wv1", [LAT, 512], F32, kind="ExternalInput").ap()
    d["Wo1p"] = nc.dram_tensor("Wo1p", [768, D], F32, kind="ExternalInput").ap()
    d["qg1"] = nc.dram_tensor("qg1", [128, 1], F32, kind="ExternalInput").ap()
    d["kg1"] = nc.dram_tensor("kg1", [128, 1], F32, kind="ExternalInput").ap()
    d["xrows4"] = nc.dram_tensor("xrows4", [128, 3], U32, kind="ExternalInput").ap()
    d["vkill4"] = nc.dram_tensor("vkill4", [128, 32], F32, kind="ExternalInput").ap()
    d["qsel32"] = nc.dram_tensor("qsel32", [128, 2], F32, kind="ExternalInput").ap()
    d["hpat3"] = nc.dram_tensor("hpat3", [128, 3], F32, kind="ExternalInput").ap()
    d["bpat3"] = nc.dram_tensor("bpat3", [3, 128], F32, kind="ExternalInput").ap()
    d["e3x"] = nc.dram_tensor("e3x", [3, 3], F32, kind="ExternalInput").ap()
    d["maskd0"] = nc.dram_tensor("maskd0", [128, 640], F32, kind="ExternalInput").ap()
    d["maskd1"] = nc.dram_tensor("maskd1", [128, 640], F32, kind="ExternalInput").ap()
    d["mkill4"] = nc.dram_tensor("mkill4", [128, 128], F32, kind="ExternalInput").ap()
    return d


def p4_host(inputs, core):
    h = core % 2
    Wq = np.asarray(inputs["L1_attn_Wq"], np.float32)
    Wk = np.asarray(inputs["L1_attn_Wk"], np.float32)
    Wo = np.asarray(inputs["L1_attn_Wo"], np.float32)
    Wk1p = np.zeros((LAT, 768), np.float32)
    Wo1p = np.zeros((768, D), np.float32)
    for H in range(NH1):
        cc, r = H // 3, H % 3
        Wk1p[:, 128 * cc + 32 * r:128 * cc + 32 * r + 32] = Wk[:, 32 * H:32 * H + 32]
        Wo1p[128 * cc + 32 * r:128 * cc + 32 * r + 32, :] = Wo[64 * H:64 * H + 32, :]
    qg = np.asarray(inputs["L1_attn_q_g"], np.float32)
    kg = np.asarray(inputs["L1_attn_k_g"], np.float32)
    kgcol = np.zeros((128, 1), np.float32)
    for r in range(3):
        kgcol[32 * r:32 * r + 32, 0] = kg
    p = np.arange(128, dtype=np.uint32)
    xrows4 = np.stack([128 * h + p, p, 128 + p], axis=1).astype(np.uint32)
    vkill4 = np.ones((128, 32), np.float32)
    if h == 0:
        vkill4[:, 16:] = 0.0
    qsel32 = np.zeros((128, 2), np.float32)
    qsel32[0:32, 0] = 1.0
    qsel32[64:96, 1] = 1.0
    hpat3 = np.zeros((128, 3), np.float32)
    bpat3 = np.zeros((3, 128), np.float32)
    for r in range(3):
        hpat3[32 * r:32 * r + 32, r] = 1.0
        bpat3[r, 32 * r:32 * r + 32] = 1.0
    jr = np.arange(128)[:, None]
    qq = np.arange(128)[None, :]
    tri = np.concatenate([np.zeros((128, 512), np.float32),
                          (qq >= jr).astype(np.float32)], axis=1)
    ones = np.ones((128, 640), np.float32)
    mkill = np.ones((128, 128), np.float32)
    if h == 0:
        for qc in range(4):
            for kt in range(32):
                if kt > 4 * qc + 3:
                    mkill[:, 32 * qc + kt] = 0.0
    return {
        "s2i": None,
        "Wq1p": np.ascontiguousarray(Wq),
        "Wkv1": np.asarray(inputs["L1_attn_Wkv"], np.float32),
        "Wk1p": Wk1p,
        "Wv1": np.asarray(inputs["L1_attn_Wv"], np.float32),
        "Wo1p": Wo1p,
        "qg1": np.concatenate([qg, qg])[:, None].astype(np.float32),
        "kg1": kgcol,
        "xrows4": xrows4, "vkill4": vkill4, "qsel32": qsel32,
        "hpat3": hpat3, "bpat3": bpat3, "e3x": np.eye(3, dtype=np.float32),
        "maskd0": tri if h == 0 else ones,
        "maskd1": tri if h == 1 else ones,
        "mkill4": mkill,
    }


def emit_p4(ctx, sch, xg, cst, io, xgk):
    nc = ctx.nc
    pb = ctx.pbanks
    TL = 2 * T

    XL = nc.dram_tensor("XLshare", [256, T], BF16, addr_space="Shared")

    m0 = ctx.mark()
    lat = ctx.sb("p4lat", [128, TL], BF16)
    kn = ctx.sb("p4kn", [128, 6 * TL], BF16)
    qsc = ctx.sb("p4qsc", [128, 6 * T], BF16)
    vbuf = [ctx.sb(f"p4v{t}", [128, 16 * 33], BF16) for t in range(32)]
    wq = ctx.sb("p4wq", [128, 8, D], BF16)
    wkv = ctx.sb("p4wkv", [128, 8, LAT], BF16)
    wk = ctx.sb("p4wk", [128, 768], BF16)
    wv = ctx.sb("p4wv", [128, 512], BF16)
    wo6 = ctx.sb("p4wo", [128, 6, D], BF16)
    qg = ctx.sb("p4qg", [128, 1], F32)
    kg = ctx.sb("p4kg", [128, 1], F32)
    xrows = ctx.sb("p4xr", [128, 3], U32)
    vkill = ctx.sb("p4vk", [128, 32], F32)
    qsel = ctx.sb("p4qsel", [128, 2], F32)
    hpat3 = ctx.sb("p4hp3", [128, 3], F32)
    bpat3 = ctx.sb("p4bp3", [3, 128], F32)
    maskd = [ctx.sb(f"p4md{i}", [128, 640], BF16) for i in range(2)]
    mkill = ctx.sb("p4mk", [128, 128], F32)
    qtab = ctx.sb("p4qtab", [2, 8], F32)
    ktab = ctx.sb("p4ktab", [3, 6], F32)
    qrow = ctx.sb("p4qrow", [1, 16], F32)
    krow = ctx.sb("p4krow", [1, 18], F32)
    bcol = ctx.sb("p4bcol", [128, 16], F32)
    for nm, buf, qd in [("Wq1p", wq, 1), ("Wkv1", wkv, 1), ("Wk1p", wk, 0),
                        ("Wv1", wv, 0), ("Wo1p", wo6, 1)]:
        if qd:
            sch.add("gpsimd", lambda g, nm=nm, buf=buf: g.dma_start(
                out=buf[:, :, :], in_=io[nm].rearrange("(c p) n -> p c n", p=128)),
                writes=(K(buf),), dma=True, tag=nm)
        else:
            sch.add("gpsimd", lambda g, nm=nm, buf=buf: g.dma_start(
                out=buf[:, :], in_=io[nm]), writes=(K(buf),), dma=True, tag=nm)
    for nm, buf in [("qg1", qg), ("kg1", kg), ("xrows4", xrows), ("vkill4", vkill),
                    ("qsel32", qsel), ("hpat3", hpat3), ("bpat3", bpat3)]:
        sch.add("sync", lambda q, nm=nm, buf=buf: q.dma_start(out=buf[:, :], in_=io[nm]),
                writes=(K(buf),), dma=True, tag=nm)
    for i in range(2):
        sch.add("gpsimd", lambda g, i=i: g.dma_start(out=maskd[i][:, :], in_=io[f"maskd{i}"]),
                writes=(K(maskd[i]),), dma=True)
    sch.add("sync", lambda q: q.dma_start(out=mkill[:, :], in_=io["mkill4"]),
            writes=(K(mkill),), dma=True)
    for t in range(32):
        sch.add("gpsimd", lambda g, t=t: g.memset(vbuf[t][:, :], 1.0),
                writes=(K(vbuf[t]),))
    sch.add("gpsimd", lambda g: g.memset(qtab[:, :], 0.0), writes=(K(qtab),))
    sch.add("gpsimd", lambda g: g.memset(ktab[:, :], 0.0), writes=(K(ktab),))

    m1 = ctx.mark()
    s2b = ctx.sb("p4s2b", [128, 8, T], BF16)
    scr = [ctx.sb(f"p4scr{i}", [128, TG], F32) for i in range(6)]
    rq2 = ctx.sb("p4rq2", [3, TG], F32)
    r21 = ctx.sb("p4r21", [3, 1], F32)

    for c in range(NCH):
        sch.add("gpsimd", lambda g, c=c: g.dma_start(
            out=s2b[:, c, :], in_=io["s2i"][128 * c:128 * (c + 1), :]),
            writes=(K(s2b),), dma=True, tag=f"s2ld{c}")

    latmy = ctx.sb("p4latmy", [128, T], BF16)
    for gi in range(NTG):
        lo = gi * TG
        bl = pb[0]
        sch.add("tensor", MMG_(pb, bl[:, :], [
            (lambda t, d=d: wkv[:, d, :],
             lambda t, d=d, lo=lo: s2b[:, d, lo:lo + TG]) for d in range(NCH)]),
            reads=(K(wkv), K(s2b)), writes=(K(bl),), tag=f"lat{gi}")
        sch.add("vector", lambda v, lo=lo, bl=bl: v.tensor_copy(latmy[:, lo:lo + TG], bl[:, :]),
                reads=(K(bl),), writes=(K(latmy),))
    ptk = sch.add("gpsimd", lambda g: g.indirect_dma_start(
        out=XL.ap(), out_offset=IOA(ap=xrows[:, 0:1], axis=0),
        in_=latmy[:, :], in_offset=None),
        reads=(K(latmy), K(xrows)), writes=("XL",), dma=True, tag="e6_push")
    xg.push(xgk, [ptk])

    for gi in range(NTG):
        lo = gi * TG
        for c in range(NCH):
            bq = pb[1]
            sch.add("tensor", MMG_(pb, bq[:, :], [
                (lambda t, d=d, c=c: wq[:, d, 128 * c:128 * (c + 1)],
                 lambda t, d=d, lo=lo: s2b[:, d, lo:lo + TG]) for d in range(NCH)]),
                reads=(K(wq), K(s2b)), writes=(K(bq),), tag=f"qmm{gi}_{c}")
            sch.add("scalar", lambda s, bq=bq: s.activation(
                out=scr[0][:, :], in_=bq[:, :], func=AF.Square),
                reads=(K(bq),), writes=(K(scr[0]),))
            sch.add("tensor", lambda t: t.matmul(
                pb[2][0:2, :], cst.hpat2[:, :], scr[0][:, :], start=True, stop=True),
                reads=(K(cst.hpat2), K(scr[0])), writes=(K(pb[2]),))
            sch.add("vector", lambda v: v.tensor_scalar(
                out=rq2[0:2, :], in0=pb[2][0:2, :], scalar1=1.0 / 64, scalar2=1e-6,
                op0=ALU.mult, op1=ALU.add),
                reads=(K(pb[2]),), writes=(K(rq2),))
            sch.add("scalar", lambda s: s.activation(out=rq2[0:2, :], in_=rq2[0:2, :],
                                                     func=AF.Ln),
                    reads=(K(rq2),), writes=(K(rq2),))
            sch.add("scalar", lambda s: s.activation(out=rq2[0:2, :], in_=rq2[0:2, :],
                                                     func=AF.Exp, scale=-0.5),
                    reads=(K(rq2),), writes=(K(rq2),))
            sch.add("tensor", lambda t: t.matmul(
                pb[3][:, :], cst.bpat2[:, :], rq2[0:2, :], start=True, stop=True),
                reads=(K(cst.bpat2), K(rq2)), writes=(K(pb[3]),))
            sch.add("scalar", lambda s: s.copy(scr[1][:, :], pb[3][:, :]),
                    reads=(K(pb[3]),), writes=(K(scr[1]),))
            sch.add("vector", lambda v, bq=bq: v.scalar_tensor_tensor(
                out=scr[2][:, :], in0=bq[:, :], scalar=qg[:, 0:1],
                in1=scr[1][:, :], op0=ALU.mult, op1=ALU.mult),
                reads=(K(bq), K(scr[1]), K(qg)), writes=(K(scr[2]),), tag=f"qn{gi}_{c}")
            for par in range(2):
                H = 2 * c + par
                cc, r = H // 3, H % 3
                sch.add("vector", lambda v, par=par, cc=cc, r=r, lo=lo: v.tensor_copy(
                    qsc[32 * r:32 * r + 32, T * cc + lo:T * cc + lo + TG],
                    scr[2][64 * par:64 * par + 32, :]),
                    reads=(K(scr[2]),), writes=(K(qsc),), tag=f"qx{gi}_{c}_{par}")
            sch.add("scalar", lambda s: s.activation(
                out=scr[3][:, :], in_=scr[2][:, :], func=AF.Square),
                reads=(K(scr[2]),), writes=(K(scr[3]),))
            sch.add("tensor", lambda t: t.matmul(
                pb[2][0:2, :], qsel[:, :], scr[3][:, :], start=True, stop=True),
                reads=(K(qsel), K(scr[3])), writes=(K(pb[2]),))
            sch.add("vector", lambda v: v.reduce_max(rq2[0:2, 0:1], pb[2][0:2, :],
                                                     axis=mybir.AxisListType.X),
                    reads=(K(pb[2]),), writes=(K(rq2),))
            sch.add("vector", lambda v, c=c: v.tensor_max(
                qtab[0:2, c:c + 1], qtab[0:2, c:c + 1], rq2[0:2, 0:1]),
                reads=(K(qtab), K(rq2)), writes=(K(qtab),), tag=f"qmax{gi}_{c}")

    pk = xg.poll(xgk)
    for blkk in range(2):
        sch.add("gpsimd", lambda g, blkk=blkk: g.indirect_dma_start(
            out=lat[:, T * blkk:T * (blkk + 1)], out_offset=None,
            in_=XL.ap(), in_offset=IOA(ap=xrows[:, 1 + blkk:2 + blkk], axis=0)),
            reads=("XL", K(xrows)), writes=(K(lat),), dma=True,
            extra_deps=[pk], tag=f"e6_pull{blkk}")

    for kg8 in range(8):
        ko = kg8 * TG
        for cc in range(6):
            bk = pb[1]
            sch.add("tensor", lambda t, cc=cc, ko=ko, bk=bk: t.matmul(
                bk[:, :], wk[:, 128 * cc:128 * (cc + 1)], lat[:, ko:ko + TG],
                start=True, stop=True),
                reads=(K(wk), K(lat)), writes=(K(bk),), tag=f"kmm{kg8}_{cc}")
            sch.add("scalar", lambda s, bk=bk: s.activation(
                out=scr[0][:, :], in_=bk[:, :], func=AF.Square),
                reads=(K(bk),), writes=(K(scr[0]),))
            sch.add("tensor", lambda t: t.matmul(
                pb[2][0:3, :], hpat3[:, :], scr[0][:, :], start=True, stop=True),
                reads=(K(hpat3), K(scr[0])), writes=(K(pb[2]),))
            sch.add("vector", lambda v: v.tensor_scalar(
                out=rq2[0:3, :], in0=pb[2][0:3, :], scalar1=1.0 / 32, scalar2=1e-6,
                op0=ALU.mult, op1=ALU.add),
                reads=(K(pb[2]),), writes=(K(rq2),))
            sch.add("scalar", lambda s: s.activation(out=rq2[0:3, :], in_=rq2[0:3, :],
                                                     func=AF.Ln),
                    reads=(K(rq2),), writes=(K(rq2),))
            sch.add("scalar", lambda s: s.activation(out=rq2[0:3, :], in_=rq2[0:3, :],
                                                     func=AF.Exp, scale=-0.5),
                    reads=(K(rq2),), writes=(K(rq2),))
            sch.add("tensor", lambda t: t.matmul(
                pb[3][:, :], bpat3[:, :], rq2[0:3, :], start=True, stop=True),
                reads=(K(bpat3), K(rq2)), writes=(K(pb[3]),))
            sch.add("scalar", lambda s: s.copy(scr[1][:, :], pb[3][:, :]),
                    reads=(K(pb[3]),), writes=(K(scr[1]),))
            sch.add("vector", lambda v, cc=cc, ko=ko, bk=bk: v.scalar_tensor_tensor(
                out=kn[:, TL * cc + ko:TL * cc + ko + TG], in0=bk[:, :], scalar=kg[:, 0:1],
                in1=scr[1][:, :], op0=ALU.mult, op1=ALU.mult),
                reads=(K(bk), K(scr[1]), K(kg)), writes=(K(kn),), tag=f"knw{kg8}_{cc}")
            sch.add("scalar", lambda s, cc=cc, ko=ko: s.activation(
                out=scr[3][:, :], in_=kn[:, TL * cc + ko:TL * cc + ko + TG], func=AF.Square),
                reads=(K(kn),), writes=(K(scr[3]),))
            sch.add("tensor", lambda t: t.matmul(
                pb[2][0:3, :], hpat3[:, :], scr[3][:, :], start=True, stop=True),
                reads=(K(hpat3), K(scr[3])), writes=(K(pb[2]),))
            sch.add("vector", lambda v: v.reduce_max(r21[0:3, 0:1], pb[2][0:3, :],
                                                     axis=mybir.AxisListType.X),
                    reads=(K(pb[2]),), writes=(K(r21),))
            sch.add("vector", lambda v, cc=cc: v.tensor_max(
                ktab[0:3, cc:cc + 1], ktab[0:3, cc:cc + 1], r21[0:3, 0:1]),
                reads=(K(ktab), K(r21)), writes=(K(ktab),), tag=f"kmax{kg8}_{cc}")
        for lt in range(4):
            ks = ko + 128 * lt
            bv = pb[5]
            sch.add("tensor", lambda t, ks=ks, bv=bv: t.matmul(
                bv[:, :], lat[:, ks:ks + 128], wv[:, :], start=True, stop=True),
                reads=(K(lat), K(wv)), writes=(K(bv),), tag=f"vmm{kg8}_{lt}")
            ti = 4 * kg8 + lt
            sch.add("vector", lambda v, bv=bv, ti=ti: v.tensor_copy(
                vbuf[ti].rearrange("p (h n) -> p h n", n=33)[:, :, 0:32], bv[:, :]),
                reads=(K(bv),), writes=(K(vbuf[ti]),))
            sch.add("vector", lambda g, ti=ti: g.tensor_scalar_mul(
                vbuf[ti][:, :], vbuf[ti][:, :], vkill[:, ti:ti + 1]),
                reads=(K(vbuf[ti]), K(vkill)), writes=(K(vbuf[ti]),), tag=f"vk{ti}")

    for tb, rr in ((qtab, 2), (ktab, 3)):
        sch.add("scalar", lambda s, tb=tb, rr=rr: s.activation(out=tb[0:rr, :], in_=tb[0:rr, :], func=AF.Ln),
                reads=(K(tb),), writes=(K(tb),))
        sch.add("scalar", lambda s, tb=tb, rr=rr: s.activation(out=tb[0:rr, :], in_=tb[0:rr, :], func=AF.Exp, scale=0.5),
                reads=(K(tb),), writes=(K(tb),))
    for r in range(2):
        sch.add("tensor", lambda t, r=r: t.matmul(
            pb[2][0:1, 0:8], cst.e01[:, r:r + 1], qtab[0:2, :], start=True, stop=True),
            reads=(K(cst.e01), K(qtab)), writes=(K(pb[2]),))
        sch.add("vector", lambda v, r=r: v.tensor_copy(
            qrow.rearrange("a (c two) -> a c two", two=2)[0:1, :, r], pb[2][0:1, 0:8]),
            reads=(K(pb[2]),), writes=(K(qrow),))
    e3 = ctx.sb("p4e3", [3, 3], F32)
    sch.add("sync", lambda q: q.dma_start(out=e3[:, :], in_=io["e3x"]),
            writes=(K(e3),), dma=True)
    for r in range(3):
        sch.add("tensor", lambda t, r=r: t.matmul(
            pb[2][0:1, 0:6], e3[:, r:r + 1], ktab[0:3, :], start=True, stop=True),
            reads=(K(e3), K(ktab)), writes=(K(pb[2]),))
        sch.add("vector", lambda v, r=r: v.tensor_copy(
            krow.rearrange("a (c three) -> a c three", three=3)[0:1, :, r],
            pb[2][0:1, 0:6]),
            reads=(K(pb[2]),), writes=(K(krow),))
    sch.add("vector", lambda v: v.tensor_mul(qrow[0:1, :], qrow[0:1, :], krow[0:1, 0:16]),
            reads=(K(qrow), K(krow)), writes=(K(qrow),))
    sch.add("vector", lambda v: v.tensor_scalar(
        out=qrow[0:1, :], in0=qrow[0:1, :], scalar1=-1.0 / 8, scalar2=45.0,
        op0=ALU.mult, op1=ALU.add),
        reads=(K(qrow),), writes=(K(qrow),))
    sch.add("tensor", lambda t: t.matmul(
        pb[3][:, 0:16], cst.ones_f[0:1, :], qrow[0:1, :], start=True, stop=True),
        reads=(K(cst.ones_f), K(qrow)), writes=(K(pb[3]),))
    sch.add("scalar", lambda s: s.copy(bcol[:, :], pb[3][:, 0:16]),
            reads=(K(pb[3]),), writes=(K(bcol),))

    ctx.reset(m1)
    sch.set_fence()
    pT = [ctx.sb(f"p4pT{i}", [128, TG], BF16) for i in range(4)]
    ofm = ctx.sb("p4ofm", [128, 6 * TG], BF16)
    recs = ctx.sb("p4recs", [1, TG], F32)
    rbc = ctx.sb("p4rbc", [128, TG], F32)
    so = [ctx.sb(f"p4so{i}", [128, TG], F32) for i in range(2)]

    for qc in range(NTG):
        nkt = 16 + 4 * (qc + 1)
        sch.add("gpsimd", lambda g: g.memset(ofm[:, :], 0.0), writes=(K(ofm),))
        for H in range(NH1):
            cc, r = H // 3, H % 3
            bO = pb[4 + H % 2]
            for kt in range(nkt):
                bS = pb[kt % 4]
                def em_s(t, H=H, cc=cc, r=r, kt=kt, qc=qc, bS=bS):
                    return t.matmul(
                        bS[:, :],
                        kn[32 * r:32 * r + 32, TL * cc + 128 * kt:TL * cc + 128 * kt + 128],
                        qsc[32 * r:32 * r + 32, T * cc + TG * qc:T * cc + TG * (qc + 1)],
                        start=True, stop=True)
                sch.add("tensor", em_s, reads=(K(kn), K(qsc)), writes=(K(bS),),
                        tag=f"S{qc}_{H}_{kt}")
                def em_e(s, H=H, kt=kt, bS=bS):
                    return s.activation(
                        out=pT[kt % 4][:, :], in_=bS[:, :], func=AF.Exp,
                        bias=bcol[:, H:H + 1], scale=0.125)
                sch.add("scalar", em_e, reads=(K(bS), K(bcol)),
                        writes=(K(pT[kt % 4]),), tag=f"E{qc}_{H}_{kt}")
                if 4 * qc + 3 < kt < 16:
                    def em_k(g, kt=kt, qc=qc):
                        return g.tensor_scalar_mul(
                            pT[kt % 4][:, :], pT[kt % 4][:, :],
                            mkill[:, 32 * qc + kt:32 * qc + kt + 1])
                    sch.add("vector", em_k, reads=(K(pT[kt % 4]), K(mkill)),
                            writes=(K(pT[kt % 4]),), tag=f"MK{qc}_{H}_{kt}")
                sub0 = 128 * kt - TG * qc
                sub1 = 128 * kt - T - TG * qc
                for mi, sub in ((0, sub0), (1, sub1)):
                    if 0 <= sub < TG:
                        def em_m(g, mi=mi, sub=sub, kt=kt):
                            return g.tensor_mul(
                                pT[kt % 4][:, 0:sub + 128], pT[kt % 4][:, 0:sub + 128],
                                maskd[mi][:, 512 - sub:640])
                        sch.add("vector", em_m,
                                reads=(K(pT[kt % 4]), K(maskd[mi])),
                                writes=(K(pT[kt % 4]),), tag=f"M{qc}_{H}_{kt}_{mi}")
                def em_pv(t, H=H, kt=kt, bO=bO, nkt=nkt):
                    return t.matmul(
                        bO[0:33, :], vbuf[kt][:, 33 * H:33 * H + 33],
                        pT[kt % 4][:, :], start=(kt == 0), stop=(kt == nkt - 1))
                sch.add("tensor", em_pv, reads=(K(vbuf[kt]), K(pT[kt % 4])),
                        writes=(K(bO),), tag=f"PV{qc}_{H}_{kt}")
            sch.add("vector", lambda v, bO=bO: v.reciprocal(recs[0:1, :], bO[32:33, :]),
                    reads=(K(bO),), writes=(K(recs),))
            sch.add("tensor", lambda t: t.matmul(
                pb[6][0:32, :], cst.ones_f[0:1, 0:32], recs[0:1, :], start=True, stop=True),
                reads=(K(cst.ones_f), K(recs)), writes=(K(pb[6]),))
            sch.add("scalar", lambda s: s.copy(rbc[0:32, :], pb[6][0:32, :]),
                    reads=(K(pb[6]),), writes=(K(rbc),))
            sch.add("vector", lambda v, cc=cc, r=r, bO=bO: v.tensor_mul(
                ofm[32 * r:32 * r + 32, TG * cc:TG * (cc + 1)], bO[0:32, :], rbc[0:32, :]),
                reads=(K(bO), K(rbc)), writes=(K(ofm),), tag=f"OF{qc}_{H}")
        for oc in range(NCH):
            bY = pb[6 + oc % 2]
            sch.add("tensor", MMG_(pb, bY[:, :], [
                (lambda t, cc=cc2, oc=oc: wo6[:, cc, 128 * oc:128 * (oc + 1)],
                 lambda t, cc=cc2: ofm[:, TG * cc:TG * (cc + 1)]) for cc2 in range(6)]),
                reads=(K(wo6), K(ofm)), writes=(K(bY),), tag=f"WO{qc}_{oc}")
            sv = so[oc % 2]
            sch.add("vector", lambda v, bY=bY, sv=sv: v.tensor_copy(sv[:, :], bY[:, :]),
                    reads=(K(bY),), writes=(K(sv),))
            sch.add("sync", lambda q, oc=oc, qc=qc, sv=sv: q.dma_start(
                out=io["aout"][128 * oc:128 * (oc + 1), TG * qc:TG * (qc + 1)], in_=sv[:, :]),
                reads=(K(sv),), writes=(f"aout{qc}_{oc}",), dma=True,
                tag=f"ast{qc}_{oc}")
    ctx.reset(m0)
    sch.set_fence()


def build_p4():
    nc = bass.Bass("TRN2", target_bir_lowering=False)
    ctx = Ctx(nc)
    io = p4_decls(nc)
    io["aout"] = nc.dram_tensor("aout", [D, T], F32, kind="ExternalOutput").ap()
    io["s3"] = nc.dram_tensor("s3", [D, T], F32, kind="ExternalOutput").ap()
    io["cst_hpat2"] = nc.dram_tensor("cst_hpat2", [128, 2], F32, kind="ExternalInput").ap()
    io["cst_bpat2"] = nc.dram_tensor("cst_bpat2", [2, 128], F32, kind="ExternalInput").ap()
    io["cst_e01"] = nc.dram_tensor("cst_e01", [2, 2], F32, kind="ExternalInput").ap()
    io.update(Exchange.input_decls(nc, 1))

    class MiniCst:
        pass
    with nc.Block() as blk:
        sch = Sched(ctx)
        xg = Exchange(ctx, sch, 1)
        xg.emit_load(io)
        cst = MiniCst()
        cst.ones_f = ctx.sb("c_ones_f", [128, 128], F32)
        cst.ones_b = ctx.sb("c_ones_b", [128, 128], BF16)
        cst.hpat2 = ctx.sb("hpat2", [128, 2], F32)
        cst.bpat2 = ctx.sb("bpat2", [2, 128], F32)
        cst.e01 = ctx.sb("e01", [2, 2], F32)
        sch.add("gpsimd", lambda g: g.memset(cst.ones_f[:, :], 1.0), writes=(K(cst.ones_f),))
        sch.add("gpsimd", lambda g: g.memset(cst.ones_b[:, :], 1.0), writes=(K(cst.ones_b),))
        for nm, buf in [("cst_hpat2", cst.hpat2), ("cst_bpat2", cst.bpat2),
                        ("cst_e01", cst.e01)]:
            sch.add("sync", lambda q, nm=nm, buf=buf: q.dma_start(out=buf[:, :], in_=io[nm]),
                    writes=(K(buf),), dma=True)
        emit_p4(ctx, sch, xg, cst, io, 0)
        emit_postln(ctx, sch, cst, io["s2i"], io["aout"], io["s3"], "pl4")
        sch.flush(blk)
    ctx.st.close()
    return nc


def run_p4(inputs, s2_percore):
    nc = build_p4()
    in_maps = []
    for c in range(NCORE):
        d = p4_host(inputs, c)
        d["s2i"] = s2_percore[c]
        d["cst_hpat2"] = l0ffn_const_host(c)["cst_hpat2"]
        d["cst_bpat2"] = l0ffn_const_host(c)["cst_bpat2"]
        d["cst_e01"] = l0ffn_const_host(c)["cst_e01"]
        in_maps.append(d)
    seed = int(np.random.default_rng().integers(2 ** 31))
    for c in range(NCORE):
        pair_rng = np.random.default_rng(seed + (c // 2))
        xgi = Exchange.__new__(Exchange)
        xgi.n_flags = 1
        in_maps[c].update(Exchange.host_inputs(xgi, c % 2, pair_rng))
    res = run_bass_kernel_spmd(nc, in_maps, core_ids=list(range(NCORE)))
    return res.results


def p5_decls(nc, with_stream=True):
    d = {}
    if with_stream:
        d["s3i"] = nc.dram_tensor("s3i", [D, T], F32, kind="ExternalInput").ap()
    d["Wr5T"] = nc.dram_tensor("Wr5T", [D, D], F32, kind="ExternalInput").ap()
    d["Wk5T"] = nc.dram_tensor("Wk5T", [D, D], F32, kind="ExternalInput").ap()
    d["Wv5T"] = nc.dram_tensor("Wv5T", [D, D], F32, kind="ExternalInput").ap()
    d["murc"] = nc.dram_tensor("murc", [128, 8], F32, kind="ExternalInput").ap()
    d["murc1"] = nc.dram_tensor("murc1", [128, 8], F32, kind="ExternalInput").ap()
    d["mukc"] = nc.dram_tensor("mukc", [128, 8], F32, kind="ExternalInput").ap()
    d["mukc1"] = nc.dram_tensor("mukc1", [128, 8], F32, kind="ExternalInput").ap()
    d["xrows5"] = nc.dram_tensor("xrows5", [128, 2], U32, kind="ExternalInput").ap()
    d["hscale5"] = nc.dram_tensor("hscale5", [128, 1], F32, kind="ExternalInput").ap()
    return d


def p5_host(inputs, core):
    h = core % 2
    pack8 = lambda v: np.asarray(v, np.float32).reshape(8, 128).T.copy()
    mur = np.asarray(inputs["L1_ffn_mu_r"], np.float32)
    muk = np.asarray(inputs["L1_ffn_mu_k"], np.float32)
    p = np.arange(128, dtype=np.uint32)
    return {
        "Wr5T": np.ascontiguousarray(np.asarray(inputs["L1_ffn_W_r"], np.float32).T),
        "Wk5T": np.ascontiguousarray(np.asarray(inputs["L1_ffn_W_k"], np.float32).T),
        "Wv5T": np.ascontiguousarray(np.asarray(inputs["L1_ffn_W_v"], np.float32).T),
        "murc": pack8(mur), "murc1": pack8(1.0 - mur),
        "mukc": pack8(muk), "mukc1": pack8(1.0 - muk),
        "xrows5": np.stack([128 * h + p, 128 * (1 - h) + p], axis=1).astype(np.uint32),
        "hscale5": np.full((128, 1), float(h), np.float32),
    }


def emit_p5(ctx, sch, cst, xg, io, xgk):
    nc = ctx.nc
    pb = ctx.pbanks
    EXT = 2049
    X5 = nc.dram_tensor("X5share", [256, 8], BF16, addr_space="Shared")

    m0 = ctx.mark()
    s3e = ctx.sb("p5s3e", [128, 8 * EXT], BF16)
    wr = ctx.sb("p5wr", [128, 8, D], BF16)
    wk_ = ctx.sb("p5wk", [128, 8, D], BF16)
    wv_ = ctx.sb("p5wv", [128, 8, D], BF16)
    murc = ctx.sb("p5mur", [128, 8], F32)
    murc1 = ctx.sb("p5mur1", [128, 8], F32)
    mukc = ctx.sb("p5muk", [128, 8], F32)
    mukc1 = ctx.sb("p5muk1", [128, 8], F32)
    xrows = ctx.sb("p5xr", [128, 2], U32)
    hscale = ctx.sb("p5hs", [128, 1], F32)
    stg = ctx.sb("p5stg", [128, 8], BF16)
    stgp = ctx.sb("p5stgp", [128, 8], BF16)
    for nm, buf in [("Wr5T", wr), ("Wk5T", wk_), ("Wv5T", wv_)]:
        sch.add("gpsimd", lambda g, nm=nm, buf=buf: g.dma_start(
            out=buf[:, :, :], in_=io[nm].rearrange("(c p) n -> p c n", p=128)),
            writes=(K(buf),), dma=True, tag=nm)
    for nm, buf in [("murc", murc), ("murc1", murc1), ("mukc", mukc),
                    ("mukc1", mukc1), ("xrows5", xrows), ("hscale5", hscale)]:
        sch.add("sync", lambda q, nm=nm, buf=buf: q.dma_start(out=buf[:, :], in_=io[nm]),
                writes=(K(buf),), dma=True, tag=nm)
    for c in range(NCH):
        sch.add("gpsimd", lambda g, c=c: g.dma_start(
            out=s3e[:, EXT * c + 1:EXT * c + 1 + T],
            in_=io["s3i"][128 * c:128 * (c + 1), :]),
            writes=(K(s3e),), dma=True, tag=f"s3ld{c}")
    sch.add("vector", lambda v: v.tensor_copy(
        stgp[:, :], s3e.rearrange("p (c n) -> p c n", n=EXT)[:, :, 2048]),
        reads=(K(s3e),), writes=(K(stgp),))
    ptk = sch.add("gpsimd", lambda g: g.indirect_dma_start(
        out=X5.ap(), out_offset=IOA(ap=xrows[:, 0:1], axis=0),
        in_=stgp[:, :], in_offset=None),
        reads=(K(stgp), K(xrows)), writes=("X5",), dma=True, tag="e7_push")
    xg.push(xgk, [ptk])
    pk = xg.poll(xgk)
    sch.add("gpsimd", lambda g: g.indirect_dma_start(
        out=stg[:, :], out_offset=None,
        in_=X5.ap(), in_offset=IOA(ap=xrows[:, 1:2], axis=0)),
        reads=("X5", K(xrows)), writes=(K(stg),), dma=True, extra_deps=[pk],
        tag="e7_pull")
    sch.add("gpsimd", lambda g: g.tensor_scalar_mul(stg[:, :], stg[:, :], hscale[:, 0:1]),
            reads=(K(stg), K(hscale)), writes=(K(stg),))
    sch.add("vector", lambda v: v.tensor_copy(
        s3e.rearrange("p (c n) -> p c n", n=EXT)[:, :, 0], stg[:, :]),
        reads=(K(stg),), writes=(K(s3e),))

    m1 = ctx.mark()
    xrb = ctx.sb("p5xrb", [128, 8 * TG], BF16)
    xkb = ctx.sb("p5xkb", [128, 8 * TG], BF16)
    rb = ctx.sb("p5rb", [128, 8 * TG], BF16)
    kab = ctx.sb("p5kab", [128, 8 * TG], BF16)
    scr = [ctx.sb(f"p5scr{i}", [128, TG], F32) for i in range(4)]
    scb = ctx.sb("p5scb", [128, TG], BF16)
    fo = [ctx.sb(f"p5fo{i}", [128, TG], F32) for i in range(2)]

    for gi in range(NTG):
        lo = gi * TG
        for c in range(NCH):
            xs_ = s3e[:, EXT * c + 1 + lo:EXT * c + 1 + lo + TG]
            xp_ = s3e[:, EXT * c + lo:EXT * c + lo + TG]
            sch.add("vector", lambda v, c=c, xp_=xp_: v.tensor_scalar_mul(
                scb[:, :], xp_, murc1[:, c:c + 1]),
                reads=(K(s3e), K(murc1)), writes=(K(scb),))
            sch.add("vector", lambda v, c=c, xs_=xs_: v.scalar_tensor_tensor(
                out=xrb[:, TG * c:TG * (c + 1)], in0=xs_, scalar=murc[:, c:c + 1],
                in1=scb[:, :], op0=ALU.mult, op1=ALU.add),
                reads=(K(s3e), K(scb), K(murc)), writes=(K(xrb),), tag=f"xr{gi}_{c}")
            sch.add("vector", lambda v, c=c, xp_=xp_: v.tensor_scalar_mul(
                scb[:, :], xp_, mukc1[:, c:c + 1]),
                reads=(K(s3e), K(mukc1)), writes=(K(scb),))
            sch.add("vector", lambda v, c=c, xs_=xs_: v.scalar_tensor_tensor(
                out=xkb[:, TG * c:TG * (c + 1)], in0=xs_, scalar=mukc[:, c:c + 1],
                in1=scb[:, :], op0=ALU.mult, op1=ALU.add),
                reads=(K(s3e), K(scb), K(mukc)), writes=(K(xkb),), tag=f"xk{gi}_{c}")
        for oc in range(NCH):
            brr = pb[oc % 2]
            sch.add("tensor", MMG_(pb, brr[:, :], [
                (lambda t, c=c, oc=oc: wr[:, c, 128 * oc:128 * (oc + 1)],
                 lambda t, c=c: xrb[:, TG * c:TG * (c + 1)]) for c in range(NCH)]),
                reads=(K(wr), K(xrb)), writes=(K(brr),), tag=f"rmm{gi}_{oc}")
            sch.add("scalar", lambda s, oc=oc, brr=brr: s.activation(
                out=rb[:, TG * oc:TG * (oc + 1)], in_=brr[:, :], func=AF.Sigmoid),
                reads=(K(brr),), writes=(K(rb),), tag=f"sig{gi}_{oc}")
            bkk = pb[2 + oc % 2]
            sch.add("tensor", MMG_(pb, bkk[:, :], [
                (lambda t, c=c, oc=oc: wk_[:, c, 128 * oc:128 * (oc + 1)],
                 lambda t, c=c: xkb[:, TG * c:TG * (c + 1)]) for c in range(NCH)]),
                reads=(K(wk_), K(xkb)), writes=(K(bkk),), tag=f"kmm{gi}_{oc}")
            sch.add("vector", lambda v, bkk=bkk: v.tensor_scalar_max(
                scr[0][:, :], bkk[:, :], 0.0),
                reads=(K(bkk),), writes=(K(scr[0]),))
            sch.add("scalar", lambda s, oc=oc: s.activation(
                out=kab[:, TG * oc:TG * (oc + 1)], in_=scr[0][:, :], func=AF.Square),
                reads=(K(scr[0]),), writes=(K(kab),), tag=f"ksq{gi}_{oc}")
        for oc in range(NCH):
            bvv = pb[4 + oc % 2]
            sch.add("tensor", MMG_(pb, bvv[:, :], [
                (lambda t, c=c, oc=oc: wv_[:, c, 128 * oc:128 * (oc + 1)],
                 lambda t, c=c: kab[:, TG * c:TG * (c + 1)]) for c in range(NCH)]),
                reads=(K(wv_), K(kab)), writes=(K(bvv),), tag=f"vmm{gi}_{oc}")
            fv = fo[oc % 2]
            sch.add("vector", lambda v, oc=oc, bvv=bvv, fv=fv: v.tensor_mul(
                fv[:, :], bvv[:, :], rb[:, TG * oc:TG * (oc + 1)]),
                reads=(K(bvv), K(rb)), writes=(K(fv),), tag=f"fo{gi}_{oc}")
            sch.add("sync", lambda q, oc=oc, lo=lo, fv=fv: q.dma_start(
                out=io["fout"][128 * oc:128 * (oc + 1), lo:lo + TG], in_=fv[:, :]),
                reads=(K(fv),), writes=(f"fout{gi}_{oc}",), dma=True,
                tag=f"fst{gi}_{oc}")
    ctx.reset(m0)
    sch.set_fence()


def build_p5():
    nc = bass.Bass("TRN2", target_bir_lowering=False)
    ctx = Ctx(nc)
    io = p5_decls(nc)
    io["fout"] = nc.dram_tensor("fout", [D, T], F32, kind="ExternalOutput").ap()
    io["out"] = nc.dram_tensor("out_final", [D, T], F32, kind="ExternalOutput").ap()
    io.update(Exchange.input_decls(nc, 1))

    class MiniCst:
        pass
    with nc.Block() as blk:
        sch = Sched(ctx)
        xg = Exchange(ctx, sch, 1)
        xg.emit_load(io)
        cst = MiniCst()
        cst.ones_f = ctx.sb("c_ones_f", [128, 128], F32)
        cst.ones_b = ctx.sb("c_ones_b", [128, 128], BF16)
        sch.add("gpsimd", lambda g: g.memset(cst.ones_f[:, :], 1.0), writes=(K(cst.ones_f),))
        sch.add("gpsimd", lambda g: g.memset(cst.ones_b[:, :], 1.0), writes=(K(cst.ones_b),))
        emit_p5(ctx, sch, cst, xg, io, 0)
        emit_postln(ctx, sch, cst, io["s3i"], io["fout"], io["out"], "pl5")
        sch.flush(blk)
    ctx.st.close()
    return nc


def run_p5(inputs, s3_percore):
    nc = build_p5()
    in_maps = []
    for c in range(NCORE):
        d = p5_host(inputs, c)
        d["s3i"] = s3_percore[c]
        in_maps.append(d)
    seed = int(np.random.default_rng().integers(2 ** 31))
    for c in range(NCORE):
        pair_rng = np.random.default_rng(seed + (c // 2))
        xgi = Exchange.__new__(Exchange)
        xgi.n_flags = 1
        in_maps[c].update(Exchange.host_inputs(xgi, c % 2, pair_rng))
    res = run_bass_kernel_spmd(nc, in_maps, core_ids=list(range(NCORE)))
    return res.results


_prog_cache = {}


def _get_prog(name, builder):
    if name not in _prog_cache:
        _prog_cache[name] = builder()
    return _prog_cache[name]


def _pair_xg_inputs(n_flags, seed_base):
    maps = []
    for c in range(NCORE):
        pair_rng = np.random.default_rng(seed_base + (c // 2))
        xgi = Exchange.__new__(Exchange)
        xgi.n_flags = n_flags
        maps.append(Exchange.host_inputs(xgi, c % 2, pair_rng))
    return maps


def kernel(**inputs):
    rng = np.random.default_rng()

    xpc = []
    for c in range(NCORE):
        b, h = c // 2, c % 2
        xpc.append(to_fm(np.asarray(inputs["L0_attn_x"][b, 2048 * h:2048 * (h + 1), :],
                                    np.float32)))

    nc1 = _get_prog("p1", build_p1)
    in_maps = [p1_host_inputs(c, inputs, rng) for c in range(NCORE)]
    xgmaps = _pair_xg_inputs(1, int(rng.integers(2 ** 31)))
    for c in range(NCORE):
        in_maps[c].update(xgmaps[c])
    res = run_bass_kernel_spmd(nc1, in_maps, core_ids=list(range(NCORE)))
    s1 = [np.ascontiguousarray(r["out_s1"]) for r in res.results]

    nc2 = _get_prog("p2", lambda: build_p2(0))
    in_maps = []
    for c in range(NCORE):
        dd = {"yin": s1[c], "xrows": xrows_host(c)}
        dd.update(l0ffn_const_host(c))
        dd.update(l0ffn_layer_host(inputs, 0))
        in_maps.append(dd)
    xgmaps = _pair_xg_inputs(3, int(rng.integers(2 ** 31)))
    for c in range(NCORE):
        in_maps[c].update(xgmaps[c])
    res = run_bass_kernel_spmd(nc2, in_maps, core_ids=list(range(NCORE)))
    y1 = [np.ascontiguousarray(r["yout"]) for r in res.results]

    nc3 = _get_prog("p3", lambda: build_p3(1))
    in_maps = []
    for c in range(NCORE):
        dd = {"yin": y1[c], "s1": s1[c], "xrows": xrows_host(c)}
        dd.update(l0ffn_const_host(c))
        dd.update(l0ffn_layer_host(inputs, 1))
        in_maps.append(dd)
    xgmaps = _pair_xg_inputs(3, int(rng.integers(2 ** 31)))
    for c in range(NCORE):
        in_maps[c].update(xgmaps[c])
    res = run_bass_kernel_spmd(nc3, in_maps, core_ids=list(range(NCORE)))
    s2 = [np.ascontiguousarray(r["s2"]) for r in res.results]

    nc4 = _get_prog("p4", build_p4)
    in_maps = []
    for c in range(NCORE):
        dd = p4_host(inputs, c)
        dd["s2i"] = s2[c]
        ch = l0ffn_const_host(c)
        dd["cst_hpat2"] = ch["cst_hpat2"]
        dd["cst_bpat2"] = ch["cst_bpat2"]
        dd["cst_e01"] = ch["cst_e01"]
        in_maps.append(dd)
    xgmaps = _pair_xg_inputs(1, int(rng.integers(2 ** 31)))
    for c in range(NCORE):
        in_maps[c].update(xgmaps[c])
    res = run_bass_kernel_spmd(nc4, in_maps, core_ids=list(range(NCORE)))
    s3 = [np.ascontiguousarray(r["s3"]) for r in res.results]

    nc5 = _get_prog("p5", build_p5)
    in_maps = []
    for c in range(NCORE):
        dd = p5_host(inputs, c)
        dd["s3i"] = s3[c]
        in_maps.append(dd)
    xgmaps = _pair_xg_inputs(1, int(rng.integers(2 ** 31)))
    for c in range(NCORE):
        in_maps[c].update(xgmaps[c])
    res = run_bass_kernel_spmd(nc5, in_maps, core_ids=list(range(NCORE)))

    out = np.zeros((B, L, D), np.float32)
    for c in range(NCORE):
        b, h = c // 2, c % 2
        out[b, 2048 * h:2048 * (h + 1), :] = res.results[c]["out_final"].T
    return out


def gcst_decls(nc):
    return {
        "cst_hpat2": nc.dram_tensor("cst_hpat2", [128, 2], F32, kind="ExternalInput").ap(),
        "cst_bpat2": nc.dram_tensor("cst_bpat2", [2, 128], F32, kind="ExternalInput").ap(),
        "cst_e01": nc.dram_tensor("cst_e01", [2, 2], F32, kind="ExternalInput").ap(),
    }


def gcst_host(core):
    ch = l0ffn_const_host(core)
    return {k: ch[k] for k in ("cst_hpat2", "cst_bpat2", "cst_e01")}


def build_fused():
    nc = bass.Bass("TRN2", target_bir_lowering=False)
    ctx = Ctx(nc)
    io = {}
    io["x"] = nc.dram_tensor("x", [D, T], F32, kind="ExternalInput").ap()
    io["WqT"] = nc.dram_tensor("WqT", [D, D], F32, kind="ExternalInput").ap()
    io["WkT"] = nc.dram_tensor("WkT", [D, D], F32, kind="ExternalInput").ap()
    io["WvT"] = nc.dram_tensor("WvT", [D, D], F32, kind="ExternalInput").ap()
    io["proj"] = nc.dram_tensor("proj", [D, 256], F32, kind="ExternalInput").ap()
    io["kvrows_my"] = nc.dram_tensor("kvrows_my", [128, 2], U32, kind="ExternalInput").ap()
    io["kvrows_peer"] = nc.dram_tensor("kvrows_peer", [128, 2], U32, kind="ExternalInput").ap()
    io["xrows"] = nc.dram_tensor("xrows", [128, 22], U32, kind="ExternalInput").ap()
    io.update(l0ffn_const_decls(nc))
    io.update(gcst_decls(nc))
    io.update(l0ffn_layer_decls(nc, 0))
    io.update(l0ffn_layer_decls(nc, 1))
    io.update(p4_decls(nc, with_stream=False))
    io.update(p5_decls(nc, with_stream=False))
    io.update(Exchange.input_decls(nc, 9))

    kind = "ExternalOutput" if FUSED_DEBUG else "Internal"
    s1 = nc.dram_tensor("st_s1", [D, T], F32, kind=kind).ap()
    ymid0 = nc.dram_tensor("st_ymid0", [D, T], F32, kind=kind).ap()
    y1 = nc.dram_tensor("st_y1", [D, T], F32, kind=kind).ap()
    ymid1 = nc.dram_tensor("st_ymid1", [D, T], F32, kind=kind).ap()
    y2 = nc.dram_tensor("st_y2", [D, T], F32, kind=kind).ap()
    s2 = nc.dram_tensor("st_s2", [D, T], F32, kind=kind).ap()
    aout = nc.dram_tensor("st_aout", [D, T], F32, kind=kind).ap()
    s3 = nc.dram_tensor("st_s3", [D, T], F32, kind=kind).ap()
    fout = nc.dram_tensor("st_fout", [D, T], F32, kind=kind).ap()
    out = nc.dram_tensor("out_final", [D, T], F32, kind="ExternalOutput").ap()

    with nc.Block() as blk:
        sch = Sched(ctx)
        xg = Exchange(ctx, sch, 9)
        xg.emit_load(io)
        gc = GlobalCst(ctx, sch, io)
        io1 = dict(io)
        io1["out"] = s1
        emit_p1(ctx, sch, xg, io1, xgk0=0)
        m_c = ctx.mark()
        cst = L0Consts(ctx, sch, io, gc)
        io2 = dict(io)
        io2.update({"yin": s1, "ymid": ymid0, "yout": y1})
        if FUSED_STAGES >= 2:
            emit_l0ffn_layer(ctx, sch, xg, cst, io2, 0, 1)
        io3 = dict(io)
        io3.update({"yin": y1, "ymid": ymid1, "yout": y2})
        if FUSED_STAGES >= 3:
            emit_l0ffn_layer(ctx, sch, xg, cst, io3, 1, 4)
            emit_postln(ctx, sch, gc, s1, y2, s2, "pl3")
        ctx.reset(m_c)
        sch.set_fence()
        if FUSED_STAGES >= 4:
            io4 = dict(io)
            io4.update({"s2i": s2, "aout": aout})
            emit_p4(ctx, sch, xg, gc, io4, 7)
            emit_postln(ctx, sch, gc, s2, aout, s3, "pl4")
        if FUSED_STAGES >= 5:
            io5 = dict(io)
            io5.update({"s3i": s3, "fout": fout})
            emit_p5(ctx, sch, gc, xg, io5, 8)
            emit_postln(ctx, sch, gc, s3, fout, out, "pl5")
        sch.flush(blk)
    ctx.st.close()
    return nc


def fused_host_inputs(core, inputs, rng):
    d = p1_host_inputs(core, inputs, rng)
    d["xrows"] = xrows_host(core)
    d.update(l0ffn_const_host(core))
    d.update(l0ffn_layer_host(inputs, 0))
    d.update(l0ffn_layer_host(inputs, 1))
    p4d = p4_host(inputs, core)
    p4d.pop("s2i")
    d.update(p4d)
    d.update(p5_host(inputs, core))
    return d


def kernel(**inputs):
    rng = np.random.default_rng()
    nc = _get_prog("fused", build_fused)
    in_maps = [fused_host_inputs(c, inputs, rng) for c in range(NCORE)]
    xgmaps = _pair_xg_inputs(9, int(rng.integers(2 ** 31)))
    for c in range(NCORE):
        in_maps[c].update(xgmaps[c])
    res = run_bass_kernel_spmd(nc, in_maps, core_ids=list(range(NCORE)))
    out = np.zeros((B, L, D), np.float32)
    for c in range(NCORE):
        b, h = c // 2, c % 2
        out[b, 2048 * h:2048 * (h + 1), :] = res.results[c]["out_final"].T
    return out


# version 09 (final)
# speedup vs baseline: 1.4861x; 1.1653x over previous
import numpy as np
import ml_dtypes
from contextlib import ExitStack

import concourse.bass as bass
import concourse.mybir as mybir
from concourse.bass_utils import run_bass_kernel_spmd

F32 = mybir.dt.float32
BF16 = mybir.dt.bfloat16
U32 = mybir.dt.uint32
I32 = mybir.dt.int32
AF = mybir.ActivationFunctionType
ALU = mybir.AluOpType
IOA = bass.IndirectOffsetOnAxis

B, L, D = 4, 4096, 1024
T = 2048
NCORE = 8
TG = 512
NTG = T // TG
NCH = D // 128
POLL_BOUND = 400000
P1_DEBUG = False
P2_DEBUG = False
FUSED_DEBUG = False
FUSED_STAGES = 9
bf16 = ml_dtypes.bfloat16

ENGINES = ("tensor", "vector", "scalar", "gpsimd", "sync")
DMA_QUEUES = ("sync", "gpsimd", "scalar")


class Op:
    __slots__ = ("engine", "emit", "reads", "writes", "amount", "deps", "idx",
                 "tag", "pool", "pool_pre")

    def __init__(self, engine, emit, reads, writes, amount, tag=""):
        self.engine = engine
        self.emit = emit
        self.reads = reads
        self.writes = writes
        self.amount = amount
        self.deps = []
        self.idx = 0
        self.tag = tag


class Sched:

    NPOOL = {"sync": 14, "gpsimd": 14, "scalar": 6}

    def __init__(self, ctx):
        self.ctx = ctx
        self.ops = []
        self.cum = {e: 0 for e in ENGINES}
        self.last_writer = {}
        self.readers = {}
        self.sems = {e: ctx.sem(f"E_{e}") for e in ENGINES}
        self.pool = {q: [ctx.sem(f"P_{q}{i}") for i in range(self.NPOOL[q])]
                     for q in DMA_QUEUES}
        self.pool_cum = {q: [0] * self.NPOOL[q] for q in DMA_QUEUES}
        self.pool_next = {q: 0 for q in DMA_QUEUES}
        self.fence = []

    def set_fence(self):
        self.fence = self.barrier_token()

    def add(self, engine, emit, reads=(), writes=(), dma=False, tag="",
            extra_deps=()):
        op = Op(engine, emit, tuple(reads), tuple(writes), 16 if dma else 1, tag)
        deps = {}

        def need(tok):
            if tok is None:
                return
            kind, key, c = tok
            if kind == "eng" and key == engine and not dma:
                return
            k = (kind, key)
            if deps.get(k, 0) < c:
                deps[k] = c
        for k in op.reads:
            need(self.last_writer.get(k))
        for k in op.writes:
            need(self.last_writer.get(k))
            for r in self.readers.get(k, ()):
                need(r)
        for d in extra_deps:
            need(d)
        for d in self.fence:
            need(d)
        op.deps = sorted(deps.items())
        if dma:
            q = engine
            i = self.pool_next[q]
            self.pool_next[q] = (i + 1) % self.NPOOL[q]
            op.pool = (q, i)
            op.pool_pre = self.pool_cum[q][i]
            self.pool_cum[q][i] += 16
            me = ("dma", (q, i), self.pool_cum[q][i])
        else:
            op.pool = None
            self.cum[engine] += 1
            op.idx = self.cum[engine]
            me = ("eng", engine, self.cum[engine])
        self.ops.append(op)
        for k in op.writes:
            self.last_writer[k] = me
            self.readers[k] = []
        for k in op.reads:
            self.readers.setdefault(k, []).append(me)
        return me

    def barrier_token(self):
        toks = [("eng", e, c) for e, c in self.cum.items() if c > 0]
        for q in DMA_QUEUES:
            for i, c in enumerate(self.pool_cum[q]):
                if c > 0:
                    toks.append(("dma", (q, i), c))
        return toks

    def _sem_of(self, kind, key):
        if kind == "eng":
            return self.sems[key].h
        q, i = key
        return self.pool[q][i].h

    def flush(self, blk):
        per_engine = {e: [] for e in ENGINES}
        for op in self.ops:
            per_engine[op.engine].append(op)
        for ename in ENGINES:
            ops = per_engine[ename]
            if not ops and ename not in DMA_QUEUES:
                continue

            def make(ops=ops, ename=ename):
                def body(eng):
                    waited = {}
                    for op in ops:
                        for (kind, key), c in op.deps:
                            if waited.get((kind, key), 0) < c:
                                eng.wait_ge(self._sem_of(kind, key), c)
                                waited[(kind, key)] = c
                        if op.pool is not None:
                            q, i = op.pool
                            if op.pool_pre > 0 and waited.get(("dma", (q, i)), 0) < op.pool_pre:
                                eng.wait_ge(self.pool[q][i].h, op.pool_pre)
                                waited[("dma", (q, i))] = op.pool_pre
                            inst = op.emit(eng)
                            assert inst is not None, f"dma op {op.tag} returned None"
                            inst.then_inc(self.pool[q][i].h, 16)
                        else:
                            inst = op.emit(eng)
                            if inst is None:
                                eng.sem_inc(self.sems[ename].h, 1)
                            else:
                                inst.then_inc(self.sems[ename].h, 1)
                    if ename in DMA_QUEUES:
                        for i, c in enumerate(self.pool_cum[ename]):
                            if c > 0 and waited.get(("dma", (ename, i)), 0) < c:
                                eng.wait_ge(self.pool[ename][i].h, c)
                return body
            getattr(blk, ename)(make())


class Ctx:

    def __init__(self, nc):
        self.nc = nc
        self.st = ExitStack()
        self._i = 0
        self.cursor = (nc.sbuf_base + 31) // 32 * 32
        self.top = nc.sbuf_top
        self.peak = self.cursor
        self.pbanks = [self.ps(f"pbank{i}", [128, 512], F32) for i in range(8)]

    def sem(self, name=None):
        if name is None:
            name = f"s{self._i}"
            self._i += 1
        return _SemH(self.st.enter_context(self.nc.semaphore(name)))

    def sb(self, name, shape, dt, at=None, region=None):
        nbytes = int(np.prod(shape[1:])) * mybir.dt.size(dt)
        nbytes = (nbytes + 31) // 32 * 32
        if at is None:
            off = self.cursor
            self.cursor += nbytes
            self.peak = max(self.peak, self.cursor)
        else:
            off = at
        assert off + nbytes <= self.top, (
            f"SBUF overflow allocating {name}: off={off} nbytes={nbytes} top={self.top}")
        self._i += 1
        t = self.nc.alloc_sbuf_tensor_at(f"{name}_{self._i}", shape, dt, offset=off)
        REGION[id(t)] = region if region is not None else id(t)
        OFFSET[id(t)] = off
        return t

    def overlay(self, name, shape, dt, over):
        return self.sb(name, shape, dt, at=OFFSET[id(over)], region=REGION[id(over)])

    def mark(self):
        return self.cursor

    def reset(self, m):
        self.cursor = m

    def ps(self, name, shape, dt=F32):
        return self.st.enter_context(self.nc.psum_tensor(name, shape, dt))


class _SemH:
    __slots__ = ("h",)

    def __init__(self, h):
        self.h = h


REGION = {}
OFFSET = {}


def K(t):
    return REGION.get(id(t), id(t))


def to_fm(x2d):
    return np.ascontiguousarray(np.asarray(x2d).T)


def rows_col(a, b=None, off=0):
    if b is None:
        a, b = 0, a
    return (np.arange(a, b, dtype=np.uint32) + off)[:, None]


class Exchange:

    def __init__(self, ctx, sch, n_flags):
        self.ctx = ctx
        self.sch = sch
        nc = ctx.nc
        self.n_flags = n_flags
        self.FLG = nc.dram_tensor("FLGshare", [max(2 * n_flags, 2), 1], I32,
                                  addr_space="Shared")
        self.s_magic = ctx.sb("xgs_magic", [2, max(n_flags, 1)], I32)
        self.s_myslot = ctx.sb("xgs_myslot", [2, max(n_flags, 1)], U32)
        self.s_peerslot = ctx.sb("xgs_peerslot", [2, max(n_flags, 1)], U32)
        self.s_pf = ctx.sb("xg_pf", [2, 1], I32)
        self.iters = ctx.sb("xg_iters", [1, max(n_flags, 1)], I32)
        self.loaded = None

    def emit_load(self, io):
        sch = self.sch
        sch.add("sync", lambda q: q.dma_start(out=self.s_magic[:, :], in_=io["xg_magic"]),
                writes=(K(self.s_magic),), dma=True, tag="xgld0")
        sch.add("sync", lambda q: q.dma_start(out=self.s_myslot[:, :], in_=io["xg_myslot"]),
                writes=(K(self.s_myslot),), dma=True, tag="xgld1")
        self.loaded = sch.add(
            "sync", lambda q: q.dma_start(out=self.s_peerslot[:, :], in_=io["xg_peerslot"]),
            writes=(K(self.s_peerslot),), dma=True, tag="xgld2")

    def push(self, k, data_deps):
        sch = self.sch
        ctx = self.ctx
        dma = ctx.sem()

        def em(g):
            g.indirect_dma_start(
                out=self.FLG.ap(),
                out_offset=IOA(ap=self.s_myslot[:, k:k + 1], axis=0),
                in_=self.s_magic[:, k:k + 1],
                in_offset=None,
            ).then_inc(dma.h, 16)
            g.wait_ge(dma.h, 16)
            return None
        return sch.add("gpsimd", em, reads=(K(self.s_magic), K(self.s_myslot)),
                       writes=(), tag=f"push{k}",
                       extra_deps=list(data_deps) + [self.loaded])

    def poll(self, k):
        sch = self.sch
        ctx = self.ctx
        poll = ctx.sem()

        def em(g):
            with (g.register(f"xit{k}") as rit, g.register(f"xc{k}") as rc,
                  g.register(f"xw{k}") as rw, g.register(f"xf{k}") as rf,
                  g.register(f"xm{k}") as rm):
                g.reg_mov(rit, 0)
                g.reg_mov(rw, 0)
                g.reg_load(rm, self.s_magic[0:1, k:k + 1])
                g.reg_mov(rc, 1)
                with g.While(rc):
                    g.indirect_dma_start(
                        out=self.s_pf[:, :], out_offset=None,
                        in_=self.FLG.ap(),
                        in_offset=IOA(ap=self.s_peerslot[:, k:k + 1], axis=0),
                    ).then_inc(poll.h, 16)
                    g.reg_add(rw, rw, 16)
                    g.wait_ge(poll.h, rw)
                    g.reg_load(rf, self.s_pf[0:1, 0:1])
                    g.reg_add(rit, rit, 1)
                    with g.register(f"xne{k}") as rne, g.register(f"xlt{k}") as rlt:
                        g.reg_alu(rne, rf, rm, ALU.not_equal)
                        g.reg_alu(rlt, rit, POLL_BOUND, ALU.is_lt)
                        g.reg_alu(rc, rne, rlt, ALU.bitwise_and)
                g.reg_save(self.iters[0:1, k:k + 1], rit)
            return None
        return sch.add("gpsimd", em,
                       reads=(K(self.s_magic), K(self.s_peerslot)),
                       writes=(K(self.s_pf),), tag=f"poll{k}",
                       extra_deps=[self.loaded])

    def host_inputs(self, h, rng):
        n = max(self.n_flags, 1)
        magics = rng.integers(1, 2 ** 30, size=(n,), dtype=np.int64).astype(np.int32)
        magic = np.stack([magics, magics], axis=0)
        ks = np.arange(n, dtype=np.uint32)
        my = np.stack([2 * ks + h, 2 * ks + h], axis=0).astype(np.uint32)
        peer = np.stack([2 * ks + (1 - h), 2 * ks + (1 - h)], axis=0).astype(np.uint32)
        return {"xg_magic": magic, "xg_myslot": my, "xg_peerslot": peer}

    @staticmethod
    def input_decls(nc, n_flags):
        n = max(n_flags, 1)
        return {
            "xg_magic": nc.dram_tensor("xg_magic", [2, n], I32, kind="ExternalInput").ap(),
            "xg_myslot": nc.dram_tensor("xg_myslot", [2, n], U32, kind="ExternalInput").ap(),
            "xg_peerslot": nc.dram_tensor("xg_peerslot", [2, n], U32, kind="ExternalInput").ap(),
        }


def emit_p1(ctx, sch, xg, io, xgk0=0):
    nc = ctx.nc
    m_p1 = ctx.mark()

    xb = [ctx.sb(f"p1xb{c}", [128, T], BF16) for c in range(NCH)]
    wq = [ctx.sb(f"p1wq{c}", [128, D], BF16) for c in range(NCH)]
    wk = [ctx.sb(f"p1wk{c}", [128, D], BF16) for c in range(NCH)]
    wv = [ctx.sb(f"p1wv{c}", [128, D], BF16) for c in range(NCH)]
    pj = [ctx.sb(f"p1pj{c}", [128, 256], BF16) for c in range(NCH)]
    qf = [ctx.sb(f"p1qf{c}", [128, TG], BF16) for c in range(NCH)]
    kf = [ctx.sb(f"p1kf{c}", [128, TG], BF16) for c in range(NCH)]
    qprime = [ctx.sb(f"p1qpr{p}", [128, T], BF16) for p in range(2)]
    kprime = [ctx.sb(f"p1kpr{t}", [128, 256], BF16) for t in range(4)]
    vt = [ctx.sb(f"p1vt{t}", [128, D], BF16) for t in range(4)]
    ones_b = ctx.sb("p1ones_b", [128, 128], BF16)
    ones_f = ctx.sb("p1ones_f", [128, 128], F32)
    kvs = [ctx.sb(f"p1kvs{c}", [128, D], F32) for c in range(2)]
    kvb = [ctx.sb(f"p1kvb{c}", [128, D], BF16) for c in range(2)]
    kvp = ctx.sb("p1kvp", [128, D], F32)
    zrow = ctx.sb("p1zrow", [1, T], F32)
    zrec = ctx.sb("p1zrec", [1, T], F32)
    scr = [ctx.sb(f"p1scr{i}", [128, TG], F32) for i in range(6)]
    srow = [ctx.sb(f"p1srow{i}", [1, TG], F32) for i in range(6)]
    yg = [ctx.sb(f"p1yg{c}", [128, TG], F32) for c in range(NCH)]
    xs = [ctx.sb(f"p1xs{i}", [128, TG], F32) for i in range(2)]
    s1 = [ctx.sb(f"p1s1{c}", [128, TG], F32) for c in range(2)]
    kvrows_my = ctx.sb("p1kvrows_my", [128, 2], U32)
    kvrows_peer = ctx.sb("p1kvrows_peer", [128, 2], U32)

    KVSH = nc.dram_tensor("KVshare", [512, D], F32, addr_space="Shared")

    pb = ctx.pbanks

    for c in range(NCH):
        sch.add("gpsimd", lambda g, c=c: g.dma_start(
            out=xb[c][:, :], in_=io["x"][128 * c:128 * (c + 1), :]),
            writes=(K(xb[c]),), dma=True, tag=f"ldxb{c}")
        sch.add("gpsimd", lambda g, c=c: g.dma_start(
            out=wq[c][:, :], in_=io["WqT"][128 * c:128 * (c + 1), :]),
            writes=(K(wq[c]),), dma=True)
        sch.add("gpsimd", lambda g, c=c: g.dma_start(
            out=wk[c][:, :], in_=io["WkT"][128 * c:128 * (c + 1), :]),
            writes=(K(wk[c]),), dma=True)
        sch.add("gpsimd", lambda g, c=c: g.dma_start(
            out=wv[c][:, :], in_=io["WvT"][128 * c:128 * (c + 1), :]),
            writes=(K(wv[c]),), dma=True)
        sch.add("gpsimd", lambda g, c=c: g.dma_start(
            out=pj[c][:, :], in_=io["proj"][128 * c:128 * (c + 1), :]),
            writes=(K(pj[c]),), dma=True)
    sch.add("sync", lambda q: q.dma_start(out=kvrows_my[:, :], in_=io["kvrows_my"]),
            writes=(K(kvrows_my),), dma=True)
    sch.add("sync", lambda q: q.dma_start(out=kvrows_peer[:, :], in_=io["kvrows_peer"]),
            writes=(K(kvrows_peer),), dma=True)
    sch.add("gpsimd", lambda g: g.memset(ones_b[:, :], 1.0), writes=(K(ones_b),))
    if P1_DEBUG:
        def dbg_store(name, buf, region=None):
            ap = buf[:, :] if region is None else region
            sch.add("sync", lambda q, ap=ap, name=name: q.dma_start(
                out=io[name], in_=ap), reads=(K(buf),), writes=(name,), dma=True)
        io["_dbg_store"] = dbg_store
    sch.add("gpsimd", lambda g: g.memset(ones_f[:, :], 1.0), writes=(K(ones_f),))

    def MMG(bank, pairs, tag=""):
        def em(t):
            n = len(pairs)
            last = None
            for i, (lf, rf) in enumerate(pairs):
                last = t.matmul(bank[:, :] if callable(bank) else bank,
                                lf(t), rf(t), start=(i == 0), stop=(i == n - 1))
            return last
        return em

    for gi in range(NTG):
        lo = gi * TG
        for e in range(NCH):
            bq = pb[e % 2]
            sch.add("tensor", MMG(bq[:, :], [
                (lambda t, d=d, e=e: wq[d][:, 128 * e:128 * (e + 1)],
                 lambda t, d=d, lo=lo: xb[d][:, lo:lo + TG]) for d in range(NCH)]),
                reads=tuple(K(wq[d]) for d in range(NCH)) + tuple(K(xb[d]) for d in range(NCH)),
                writes=(K(bq),), tag=f"qmm{gi}_{e}")
            sch.add("scalar", lambda s, e=e, bq=bq: s.copy(qf[e][:, :], bq[:, :]),
                    reads=(K(bq),), writes=(K(qf[e]),), tag=f"qev{gi}_{e}")
            bk = pb[2 + e % 2]
            sch.add("tensor", MMG(bk[:, :], [
                (lambda t, d=d, e=e: wk[d][:, 128 * e:128 * (e + 1)],
                 lambda t, d=d, lo=lo: xb[d][:, lo:lo + TG]) for d in range(NCH)]),
                reads=tuple(K(wk[d]) for d in range(NCH)) + tuple(K(xb[d]) for d in range(NCH)),
                writes=(K(bk),), tag=f"kmm{gi}_{e}")
            sch.add("vector", lambda v, e=e, bk=bk: v.tensor_copy(kf[e][:, :], bk[:, :]),
                    reads=(K(bk),), writes=(K(kf[e]),), tag=f"kev{gi}_{e}")
        for p in range(2):
            bqp = pb[p]
            sch.add("tensor", MMG(bqp[:, :], [
                (lambda t, e=e, p=p: pj[e][:, 128 * p:128 * (p + 1)],
                 lambda t, e=e: qf[e][:, :]) for e in range(NCH)]),
                reads=tuple(K(pj[e]) for e in range(NCH)) + tuple(K(qf[e]) for e in range(NCH)),
                writes=(K(bqp),), tag=f"qpmm{gi}_{p}")
            sc0, sc1 = scr[2 * p], scr[2 * p + 1]
            sch.add("vector", lambda v, bqp=bqp, sc0=sc0: v.tensor_scalar_min(
                sc0[:, :], bqp[:, :], 0.0), reads=(K(bqp),), writes=(K(sc0),))
            sch.add("scalar", lambda s, sc0=sc0, sc1=sc1: s.activation(
                out=sc1[:, :], in_=sc0[:, :], func=AF.Exp),
                reads=(K(sc0),), writes=(K(sc1),))
            sch.add("vector", lambda v, bqp=bqp, sc1=sc1, p=p, lo=lo: v.scalar_tensor_tensor(
                out=qprime[p][:, lo:lo + TG], in0=bqp[:, :], scalar=0.0,
                in1=sc1[:, :], op0=ALU.max, op1=ALU.add),
                reads=(K(bqp), K(sc1)), writes=(K(qprime[p]),), tag=f"qpr{gi}_{p}")
        bz = pb[2]
        sch.add("tensor", MMG(bz[0:1, :], [
            (lambda t, p=p: ones_b[:, 0:1],
             lambda t, p=p, lo=lo: qprime[p][:, lo:lo + TG]) for p in range(2)]),
            reads=(K(ones_b), K(qprime[0]), K(qprime[1])),
            writes=(K(bz),), tag=f"zmm{gi}")
        sch.add("vector", lambda v, bz=bz, lo=lo: v.tensor_copy(zrow[0:1, lo:lo + TG], bz[0:1, :]),
                reads=(K(bz),), writes=(K(zrow),))
        for lt in range(4):
            ls = lo + 128 * lt
            bkp = pb[3]
            sch.add("tensor", MMG(bkp[:, 0:256], [
                (lambda t, e=e, lt=lt: kf[e][:, 128 * lt:128 * (lt + 1)],
                 lambda t, e=e: pj[e][:, :]) for e in range(NCH)]),
                reads=tuple(K(kf[e]) for e in range(NCH)) + tuple(K(pj[e]) for e in range(NCH)),
                writes=(K(bkp),), tag=f"kpmm{gi}_{lt}")
            sc0, sc1 = scr[4], scr[5]
            sch.add("vector", lambda v, bkp=bkp, sc0=sc0: v.tensor_scalar_min(
                sc0[:, 0:256], bkp[:, 0:256], 0.0), reads=(K(bkp),), writes=(K(sc0),))
            sch.add("scalar", lambda s, sc0=sc0, sc1=sc1: s.activation(
                out=sc1[:, 0:256], in_=sc0[:, 0:256], func=AF.Exp),
                reads=(K(sc0),), writes=(K(sc1),))
            sch.add("vector", lambda v, bkp=bkp, sc1=sc1, lt=lt: v.scalar_tensor_tensor(
                out=kprime[lt][:, :], in0=bkp[:, 0:256], scalar=0.0,
                in1=sc1[:, 0:256], op0=ALU.max, op1=ALU.add),
                reads=(K(bkp), K(sc1)), writes=(K(kprime[lt]),), tag=f"kpr{gi}_{lt}")
            for vh in range(2):
                bv = pb[3]
                sch.add("tensor", MMG(bv[:, :], [
                    (lambda t, d=d, ls=ls: xb[d][:, ls:ls + 128],
                     lambda t, d=d, vh=vh: wv[d][:, TG * vh:TG * (vh + 1)]) for d in range(NCH)]),
                    reads=tuple(K(xb[d]) for d in range(NCH)) + tuple(K(wv[d]) for d in range(NCH)),
                    writes=(K(bv),), tag=f"vmm{gi}_{lt}_{vh}")
                sch.add("vector", lambda v, bv=bv, lt=lt, vh=vh: v.tensor_copy(
                    vt[lt][:, TG * vh:TG * (vh + 1)], bv[:, :]),
                    reads=(K(bv),), writes=(K(vt[lt]),))
            first = (gi == 0 and lt == 0)
            last = (gi == NTG - 1 and lt == 3)
            for p in range(2):
                for vh in range(2):
                    bank = pb[4 + 2 * p + vh]
                    def em(t, p=p, vh=vh, lt=lt, first=first, last=last, bank=bank):
                        return t.matmul(bank[:, :], kprime[lt][:, 128 * p:128 * (p + 1)],
                                        vt[lt][:, TG * vh:TG * (vh + 1)],
                                        start=first, stop=last)
                    sch.add("tensor", em, reads=(K(kprime[lt]), K(vt[lt])),
                            writes=(K(bank),), tag=f"kvmm{gi}_{lt}_{p}{vh}")

    kv_deps = []
    for p in range(2):
        for vh in range(2):
            bank = pb[4 + 2 * p + vh]
            sch.add("vector", lambda v, p=p, vh=vh, bank=bank: v.tensor_copy(
                kvs[p][:, TG * vh:TG * (vh + 1)], bank[:, :]),
                reads=(K(bank),), writes=(K(kvs[p]),), tag=f"kvev{p}{vh}")
    for p in range(2):
        tok = sch.add("gpsimd", lambda g, p=p: g.indirect_dma_start(
            out=KVSH.ap(), out_offset=IOA(ap=kvrows_my[:, p:p + 1], axis=0),
            in_=kvs[p][:, :], in_offset=None),
            reads=(K(kvs[p]), K(kvrows_my)), writes=("KVSH",), dma=True,
            tag=f"kvpush{p}")
        kv_deps.append(tok)
    xg.push(xgk0 + 0, kv_deps)
    ptok = xg.poll(xgk0 + 0)
    for p in range(2):
        sch.add("gpsimd", lambda g, p=p: g.indirect_dma_start(
            out=kvp[:, :], out_offset=None,
            in_=KVSH.ap(),
            in_offset=IOA(ap=kvrows_peer[:, p:p + 1], axis=0)),
            reads=("KVSH", K(kvrows_peer)), writes=(K(kvp),), dma=True,
            extra_deps=[ptok], tag=f"kvpull{p}")
        sch.add("vector", lambda v, p=p: v.tensor_add(
            kvs[p][:, :], kvs[p][:, :], kvp[:, :]),
            reads=(K(kvs[p]), K(kvp)), writes=(K(kvs[p]),))
        sch.add("scalar", lambda s, p=p: s.copy(kvb[p][:, :], kvs[p][:, :]),
                reads=(K(kvs[p]),), writes=(K(kvb[p]),))
        if P1_DEBUG:
            io["_dbg_store"](f"dbg_kv{p}", kvs[p])

    if P1_DEBUG:
        io["_dbg_store"]("dbg_z", zrow, zrow[0:1, :])
        for p in range(2):
            io["_dbg_store"](f"dbg_qpr{p}", qprime[p])
    sch.add("vector", lambda v: v.tensor_scalar_add(zrow[0:1, :], zrow[0:1, :], 1e-8),
            reads=(K(zrow),), writes=(K(zrow),))
    sch.add("vector", lambda v: v.reciprocal(zrec[0:1, :], zrow[0:1, :]),
            reads=(K(zrow),), writes=(K(zrec),))
    for gi in range(NTG):
        lo = gi * TG
        bsum = pb[6]
        bsq = pb[7]
        zib = scr[5]
        bbc = pb[2]
        sch.add("tensor", lambda t, lo=lo, bbc=bbc: t.matmul(
            bbc[:, :], ones_f[0:1, :], zrec[0:1, lo:lo + TG], start=True, stop=True),
            reads=(K(ones_f), K(zrec)), writes=(K(bbc),), tag=f"zbc{gi}")
        sch.add("scalar", lambda s, bbc=bbc, zib=zib: s.copy(zib[:, :], bbc[:, :]),
                reads=(K(bbc),), writes=(K(zib),))
        for e in range(NCH):
            bn = pb[e % 2]
            sch.add("tensor", MMG(bn[:, :], [
                (lambda t, p=p, e=e: kvb[p][:, 128 * e:128 * (e + 1)],
                 lambda t, p=p, lo=lo: qprime[p][:, lo:lo + TG]) for p in range(2)]),
                reads=(K(kvb[0]), K(kvb[1]), K(qprime[0]), K(qprime[1])),
                writes=(K(bn),), tag=f"num{gi}_{e}")
            xse = xs[e % 2]
            sch.add("sync", lambda q, e=e, lo=lo, xse=xse: q.dma_start(
                out=xse[:, :], in_=io["x"][128 * e:128 * (e + 1), lo:lo + TG]),
                reads=(), writes=(K(xse),), dma=True, tag=f"xld{gi}_{e}")
            sc = scr[e % 2]
            sch.add("vector", lambda v, bn=bn, sc=sc, zib=zib: v.tensor_mul(
                sc[:, :], bn[:, :], zib[:, :]),
                reads=(K(bn), K(zib)), writes=(K(sc),))
            sch.add("vector", lambda g, sc=sc, e=e, xse=xse: g.tensor_add(
                yg[e][:, :], sc[:, :], xse[:, :]),
                reads=(K(sc), K(xse)), writes=(K(yg[e]),), tag=f"y{gi}_{e}")
            if P1_DEBUG and gi == 0 and e == 0:
                io["_dbg_store"]("dbg_y00", yg[0])
            def em_sum(t, e=e, bsum=bsum):
                return t.matmul(bsum[0:1, :], ones_f[:, 0:1], yg[e][:, :],
                                start=(e == 0), stop=(e == NCH - 1))
            sch.add("tensor", em_sum, reads=(K(ones_f), K(yg[e])), writes=(K(bsum),),
                    tag=f"sum{gi}_{e}")
            scq = scr[2 + e % 2]
            sch.add("scalar", lambda s, e=e, scq=scq: s.activation(
                out=scq[:, :], in_=yg[e][:, :], func=AF.Square),
                reads=(K(yg[e]),), writes=(K(scq),))
            def em_sq(t, e=e, scq=scq, bsq=bsq):
                return t.matmul(bsq[0:1, :], ones_f[:, 0:1], scq[:, :],
                                start=(e == 0), stop=(e == NCH - 1))
            sch.add("tensor", em_sq, reads=(K(ones_f), K(scq)), writes=(K(bsq),),
                    tag=f"sq{gi}_{e}")
        sch.add("scalar", lambda s, bsum=bsum: s.activation(
            out=srow[0][0:1, :], in_=bsum[0:1, :], func=AF.Copy,
            scale=1.0 / D), reads=(K(bsum),), writes=(K(srow[0]),))
        sch.add("scalar", lambda s, bsq=bsq: s.activation(
            out=srow[1][0:1, :], in_=bsq[0:1, :], func=AF.Copy,
            scale=1.0 / D), reads=(K(bsq),), writes=(K(srow[1]),))
        sch.add("scalar", lambda s: s.activation(
            out=srow[2][0:1, :], in_=srow[0][0:1, :], func=AF.Square),
            reads=(K(srow[0]),), writes=(K(srow[2]),))
        sch.add("vector", lambda v: v.tensor_sub(
            srow[3][0:1, :], srow[1][0:1, :], srow[2][0:1, :]),
            reads=(K(srow[1]), K(srow[2])), writes=(K(srow[3]),))
        sch.add("vector", lambda v: v.tensor_scalar_add(
            srow[3][0:1, :], srow[3][0:1, :], 1e-5),
            reads=(K(srow[3]),), writes=(K(srow[3]),))
        sch.add("scalar", lambda s: s.activation(
            out=srow[4][0:1, :], in_=srow[3][0:1, :], func=AF.Ln),
            reads=(K(srow[3]),), writes=(K(srow[4]),))
        sch.add("scalar", lambda s: s.activation(
            out=srow[5][0:1, :], in_=srow[4][0:1, :], func=AF.Exp, scale=-0.5),
            reads=(K(srow[4]),), writes=(K(srow[5]),), tag=f"rstd{gi}")
        if P1_DEBUG and gi == 0:
            io["_dbg_store"]("dbg_mu0", srow[0], srow[0][0:1, :])
            io["_dbg_store"]("dbg_rstd0", srow[5], srow[5][0:1, :])
        sch.add("vector", lambda v: v.tensor_mul(
            srow[2][0:1, :], srow[0][0:1, :], srow[5][0:1, :]),
            reads=(K(srow[0]), K(srow[5])), writes=(K(srow[2]),), tag=f"mr{gi}")
        br = pb[4]
        bm = pb[5]
        sch.add("tensor", lambda t, br=br: t.matmul(
            br[:, :], ones_f[0:1, :], srow[5][0:1, :], start=True, stop=True),
            reads=(K(ones_f), K(srow[5])), writes=(K(br),))
        sch.add("tensor", lambda t, bm=bm: t.matmul(
            bm[:, :], ones_f[0:1, :], srow[2][0:1, :], start=True, stop=True),
            reads=(K(ones_f), K(srow[2])), writes=(K(bm),))
        for e in range(NCH):
            so = s1[e % 2]
            sch.add("vector", lambda v, e=e, br=br, so=so: v.tensor_mul(
                so[:, :], yg[e][:, :], br[:, :]),
                reads=(K(yg[e]), K(br)), writes=(K(so),))
            sch.add("vector", lambda v, bm=bm, so=so: v.tensor_sub(
                so[:, :], so[:, :], bm[:, :]),
                reads=(K(so), K(bm)), writes=(K(so),))
            sch.add("sync", lambda q, e=e, lo=lo, so=so: q.dma_start(
                out=io["out"][128 * e:128 * (e + 1), lo:lo + TG], in_=so[:, :]),
                reads=(K(so),), writes=(f"out{e}_{gi}",), dma=True,
                tag=f"st{gi}_{e}")
    ctx.reset(m_p1)
    sch.set_fence()


def p1_host_inputs(core, inputs, rng):
    b, h = core // 2, core % 2
    x = np.asarray(inputs["L0_attn_x"][b, 2048 * h:2048 * (h + 1), :], np.float32)
    d = {
        "x": to_fm(x),
        "WqT": np.ascontiguousarray(np.asarray(inputs["L0_attn_Wq"], np.float32).T),
        "WkT": np.ascontiguousarray(np.asarray(inputs["L0_attn_Wk"], np.float32).T),
        "WvT": np.ascontiguousarray(np.asarray(inputs["L0_attn_Wv"], np.float32).T),
        "proj": np.asarray(inputs["L0_attn_proj"], np.float32),
        "kvrows_my": np.concatenate([rows_col(128, off=256 * h),
                                     rows_col(128, off=256 * h + 128)], axis=1).astype(np.uint32),
        "kvrows_peer": np.concatenate([rows_col(128, off=256 * (1 - h)),
                                       rows_col(128, off=256 * (1 - h) + 128)], axis=1).astype(np.uint32),
    }
    return d


def p1_decls(nc):
    return {
        "x": nc.dram_tensor("x", [D, T], F32, kind="ExternalInput").ap(),
        "WqT": nc.dram_tensor("WqT", [D, D], F32, kind="ExternalInput").ap(),
        "WkT": nc.dram_tensor("WkT", [D, D], F32, kind="ExternalInput").ap(),
        "WvT": nc.dram_tensor("WvT", [D, D], F32, kind="ExternalInput").ap(),
        "proj": nc.dram_tensor("proj", [D, 256], F32, kind="ExternalInput").ap(),
        "kvrows_my": nc.dram_tensor("kvrows_my", [128, 2], U32, kind="ExternalInput").ap(),
        "kvrows_peer": nc.dram_tensor("kvrows_peer", [128, 2], U32, kind="ExternalInput").ap(),
        "out": nc.dram_tensor("out_s1", [D, T], F32, kind="ExternalOutput").ap(),
    } | ({
        "dbg_kv0": nc.dram_tensor("dbg_kv0", [128, D], F32, kind="ExternalOutput").ap(),
        "dbg_kv1": nc.dram_tensor("dbg_kv1", [128, D], F32, kind="ExternalOutput").ap(),
        "dbg_z": nc.dram_tensor("dbg_z", [1, T], F32, kind="ExternalOutput").ap(),
        "dbg_qpr0": nc.dram_tensor("dbg_qpr0", [128, T], BF16, kind="ExternalOutput").ap(),
        "dbg_qpr1": nc.dram_tensor("dbg_qpr1", [128, T], BF16, kind="ExternalOutput").ap(),
        "dbg_y00": nc.dram_tensor("dbg_y00", [128, TG], F32, kind="ExternalOutput").ap(),
        "dbg_mu0": nc.dram_tensor("dbg_mu0", [1, TG], F32, kind="ExternalOutput").ap(),
        "dbg_rstd0": nc.dram_tensor("dbg_rstd0", [1, TG], F32, kind="ExternalOutput").ap(),
    } if P1_DEBUG else {})


def build_p1():
    nc = bass.Bass("TRN2", target_bir_lowering=False)
    ctx = Ctx(nc)
    io = p1_decls(nc)
    io.update(Exchange.input_decls(nc, 1))
    with nc.Block() as blk:
        sch = Sched(ctx)
        xg = Exchange(ctx, sch, 1)
        xg.emit_load(io)
        emit_p1(ctx, sch, xg, io)
        sch.flush(blk)
    ctx.st.close()
    return nc


def run_p1(inputs):
    rng = np.random.default_rng()
    nc = build_p1()
    in_maps = [p1_host_inputs(c, inputs, rng) for c in range(NCORE)]
    seed = int(np.random.default_rng().integers(2 ** 31))
    for c in range(NCORE):
        pair_rng = np.random.default_rng(seed + (c // 2))
        xgi = Exchange.__new__(Exchange)
        xgi.n_flags = 1
        in_maps[c].update(Exchange.host_inputs(xgi, c % 2, pair_rng))
    res = run_bass_kernel_spmd(nc, in_maps, core_ids=list(range(NCORE)))
    return res.results



H0 = 8
HD0 = 64
NW = 4
NJ = 22


def l0ffn_const_decls(nc):
    d = {}
    d["cst_tcos"] = nc.dram_tensor("cst_tcos", [128, T], F32, kind="ExternalInput").ap()
    d["cst_tsin"] = nc.dram_tensor("cst_tsin", [128, T], F32, kind="ExternalInput").ap()
    d["cst_tsq"] = nc.dram_tensor("cst_tsq", [128, T], F32, kind="ExternalInput").ap()
    d["cst_tski"] = nc.dram_tensor("cst_tski", [128, T], F32, kind="ExternalInput").ap()
    d["cst_masks"] = nc.dram_tensor("cst_masks", [128, 8 * TG], F32, kind="ExternalInput").ap()
    d["cst_perm"] = nc.dram_tensor("cst_perm", [128, 128], F32, kind="ExternalInput").ap()
    d["cst_vkill"] = nc.dram_tensor("cst_vkill", [128, 1], F32, kind="ExternalInput").ap()
    d["cst_hscale"] = nc.dram_tensor("cst_hscale", [128, 1], F32, kind="ExternalInput").ap()
    return d


def l0ffn_const_host(core):
    h = core % 2
    base_pos = 2048 * h
    dd = np.arange(64)
    inv_freq = 10000.0 ** (-(2.0 * (dd % 32)) / 64.0)
    base = (2.0 * (dd % 32) + 0.4 * 64) / (1.4 * 64)
    i = base_pos + np.arange(T)
    freqs = i[None, :] * inv_freq[:, None]
    power = (i - L // 2) / float(512 // 2)
    scale = base[:, None] ** power[None, :]
    tc = np.cos(freqs).astype(np.float32)
    ts = np.sin(freqs).astype(np.float32)
    two = lambda a: np.concatenate([a, a], axis=0).astype(np.float32)
    masks = np.zeros((128, 8, TG), np.float32)
    jr = np.arange(128)[:, None]
    qq = np.arange(TG)[None, :]
    for kt in range(8):
        dlt = 512 - 128 * kt - jr + qq
        masks[:, kt, :] = ((dlt >= 0) & (dlt <= 512)).astype(np.float32)
    perm = np.zeros((128, 128), np.float32)
    for blk in range(2):
        o = 64 * blk
        for j2 in range(32):
            perm[o + 32 + j2, o + j2] = -1.0
            perm[o + j2, o + 32 + j2] = 1.0
    hpat2 = np.zeros((128, 2), np.float32)
    hpat2[0:64, 0] = 1.0
    hpat2[64:128, 1] = 1.0
    bpat2 = hpat2.T.copy()
    e01 = np.eye(2, dtype=np.float32)
    vkill = np.full((128, 1), 1.0 if h == 1 else 0.0, np.float32)
    hscale = vkill.copy()
    return {
        "cst_tcos": two(tc), "cst_tsin": two(ts),
        "cst_tsq": two(scale), "cst_tski": two(1.0 / scale),
        "cst_masks": masks.reshape(128, 8 * TG),
        "cst_perm": perm, "cst_hpat2": hpat2, "cst_bpat2": bpat2,
        "cst_e01": e01, "cst_vkill": vkill, "cst_hscale": hscale,
    }


class GlobalCst:
    def __init__(self, ctx, sch, io):
        self.ones_f = ctx.sb("g_ones_f", [128, 128], F32)
        self.ones_b = ctx.sb("g_ones_b", [128, 128], BF16)
        self.hpat2 = ctx.sb("g_hpat2", [128, 2], F32)
        self.bpat2 = ctx.sb("g_bpat2", [2, 128], F32)
        self.e01 = ctx.sb("g_e01", [2, 2], F32)
        sch.add("gpsimd", lambda g: g.memset(self.ones_f[:, :], 1.0), writes=(K(self.ones_f),))
        sch.add("gpsimd", lambda g: g.memset(self.ones_b[:, :], 1.0), writes=(K(self.ones_b),))
        for nm, buf in [("cst_hpat2", self.hpat2), ("cst_bpat2", self.bpat2),
                        ("cst_e01", self.e01)]:
            sch.add("sync", lambda q, nm=nm, buf=buf: q.dma_start(out=buf[:, :], in_=io[nm]),
                    writes=(K(buf),), dma=True, tag=nm)


class L0Consts:

    def __init__(self, ctx, sch, io, g):
        for a in ("ones_f", "ones_b", "hpat2", "bpat2", "e01"):
            setattr(self, a, getattr(g, a))
        self.tcos = ctx.sb("tcos", [128, T], BF16)
        self.tsin = ctx.sb("tsin", [128, T], BF16)
        self.tsq = ctx.sb("tsq", [128, T], BF16)
        self.tski = ctx.sb("tski", [128, T], BF16)
        self.masks = ctx.sb("masks", [128, 8 * TG], BF16)
        self.perm = ctx.sb("permc", [128, 128], F32)
        self.vkill = ctx.sb("vkill", [128, 1], F32)
        self.hscale = ctx.sb("hscale", [128, 1], F32)
        for name, buf in [("cst_perm", self.perm),
                          ("cst_vkill", self.vkill), ("cst_hscale", self.hscale)]:
            sch.add("sync", lambda q, buf=buf, name=name: q.dma_start(
                out=buf[:, :], in_=io[name]), writes=(K(buf),), dma=True, tag=name)
        for name, buf in [("cst_tcos", self.tcos), ("cst_tsin", self.tsin),
                          ("cst_tsq", self.tsq), ("cst_tski", self.tski),
                          ("cst_masks", self.masks)]:
            sch.add("gpsimd", lambda g, buf=buf, name=name: g.dma_start(
                out=buf[:, :], in_=io[name]), writes=(K(buf),), dma=True, tag=name)


def l0ffn_layer_decls(nc, li):
    d = {}
    d[f"wqkv{li}"] = nc.dram_tensor(f"wqkv{li}", [D, 1536], F32, kind="ExternalInput").ap()
    d[f"wo{li}"] = nc.dram_tensor(f"wo{li}", [512, D], F32, kind="ExternalInput").ap()
    d[f"w1p{li}"] = nc.dram_tensor(f"w1p{li}", [D, 2 * NJ * 128], F32, kind="ExternalInput").ap()
    d[f"b1p{li}"] = nc.dram_tensor(f"b1p{li}", [128, 2 * NJ], F32, kind="ExternalInput").ap()
    d[f"w2{li}"] = nc.dram_tensor(f"w2{li}", [2730, D], F32, kind="ExternalInput").ap()
    d[f"qs{li}"] = nc.dram_tensor(f"qs{li}", [128, 1], F32, kind="ExternalInput").ap()
    d[f"ks{li}"] = nc.dram_tensor(f"ks{li}", [128, 1], F32, kind="ExternalInput").ap()
    d[f"lg{li}"] = nc.dram_tensor(f"lg{li}", [128, 8], F32, kind="ExternalInput").ap()
    d[f"lb{li}"] = nc.dram_tensor(f"lb{li}", [128, 8], F32, kind="ExternalInput").ap()
    d[f"b2c{li}"] = nc.dram_tensor(f"b2c{li}", [128, 8], F32, kind="ExternalInput").ap()
    return d


def l0ffn_layer_host(inputs, li):
    w1 = np.asarray(inputs["L0_ffn_w1"][li], np.float32)
    b1 = np.asarray(inputs["L0_ffn_b1"][li], np.float32)
    HID = 2730
    w1p = np.zeros((D, 2 * NJ * 128), np.float32)
    b1p = np.zeros((128, 2 * NJ), np.float32)
    for j in range(NJ):
        nj = 128 if j < 21 else 42
        a = w1[:, 128 * j:128 * j + nj]
        g = w1[:, HID + 128 * j:HID + 128 * j + nj]
        base = 256 * j
        w1p[:, base:base + nj] = a
        w1p[:, base + 128:base + 128 + nj] = g
        b1p[0:nj, 2 * j] = b1[128 * j:128 * j + nj]
        b1p[0:nj, 2 * j + 1] = b1[HID + 128 * j:HID + 128 * j + nj]
    qs = np.asarray(inputs["L0_ffn_q_scale"][li], np.float32)
    ks = np.asarray(inputs["L0_ffn_k_scale"][li], np.float32)
    col = lambda v: np.concatenate([v, v])[:, None].astype(np.float32)
    pack8 = lambda v: np.asarray(v, np.float32).reshape(8, 128).T.copy()
    return {
        f"wqkv{li}": np.asarray(inputs["L0_ffn_wqkv"][li], np.float32),
        f"wo{li}": np.asarray(inputs["L0_ffn_wo"][li], np.float32),
        f"w1p{li}": w1p, f"b1p{li}": b1p,
        f"w2{li}": np.asarray(inputs["L0_ffn_w2"][li], np.float32),
        f"qs{li}": col(qs), f"ks{li}": col(ks),
        f"lg{li}": pack8(inputs["L0_ffn_ln_g"][li]),
        f"lb{li}": pack8(inputs["L0_ffn_ln_b"][li]),
        f"b2c{li}": pack8(inputs["L0_ffn_b2"][li]),
    }


def emit_l0ffn_layer(ctx, sch, xg, cst, io, li, xgk):
    nc = ctx.nc
    pb = ctx.pbanks
    EXT = 2049

    XHA = nc.dram_tensor(f"XHA{li}", [256, 8], BF16, addr_space="Shared")
    XHK = nc.dram_tensor(f"XHK{li}", [2 * 512, 512], BF16, addr_space="Shared")
    XHV = nc.dram_tensor(f"XHV{li}", [2 * 512, 520], BF16, addr_space="Shared")
    XBM = nc.dram_tensor(f"XBM{li}", [4, 4], F32, addr_space="Shared")
    XMB = nc.dram_tensor(f"XMB{li}", [256, 8], BF16, addr_space="Shared")

    m0 = ctx.mark()
    ymid = ctx.sb(f"ymid{li}", [128, 8 * EXT], BF16)
    amax = ctx.sb(f"amax{li}", [2, 16], F32)
    bmax = ctx.sb(f"bmax{li}", [2, 20], F32)
    bval = ctx.sb(f"bval{li}", [2, 16], F32)
    mrow = ctx.sb(f"mrow{li}", [1, 32], F32)
    bcol = ctx.sb(f"bcol{li}", [128, 32], F32)
    xrows = ctx.sb(f"xrows{li}", [128, 22], U32)
    sch.add("sync", lambda q: q.dma_start(out=xrows[:, :], in_=io["xrows"]),
            writes=(K(xrows),), dma=True, tag="xrowsld")

    m1 = ctx.mark()
    qrot = [ctx.sb(f"qrot{li}_{hp}", [128, T], BF16) for hp in range(4)]
    krot = [ctx.sb(f"krot{li}_{hp}", [128, 512 + T], BF16) for hp in range(4)]
    vbuf = [ctx.sb(f"vbuf{li}_{t}", [128, 520], BF16) for t in range(20)]
    wq3 = ctx.sb(f"wq3_{li}", [128, 8, 1536], BF16)
    wo4 = ctx.sb(f"wo4_{li}", [128, 4, 1024], BF16)

    sch.add("gpsimd", lambda g: g.dma_start(
        out=wq3[:, :, :], in_=io[f"wqkv{li}"].rearrange("(c p) n -> p c n", p=128)),
        writes=(K(wq3),), dma=True, tag="wqkvld")
    sch.add("gpsimd", lambda g: g.dma_start(
        out=wo4[:, :, :], in_=io[f"wo{li}"].rearrange("(c p) n -> p c n", p=128)),
        writes=(K(wo4),), dma=True, tag="wold")
    qs = ctx.sb(f"qsc{li}", [128, 1], F32)
    ks = ctx.sb(f"ksc{li}", [128, 1], F32)
    for nm, buf in [(f"qs{li}", qs), (f"ks{li}", ks)]:
        sch.add("sync", lambda q, nm=nm, buf=buf: q.dma_start(out=buf[:, :], in_=io[nm]),
                writes=(K(buf),), dma=True)
    for t in range(20):
        sch.add("gpsimd", lambda g, t=t: g.memset(vbuf[t][:, :], 1.0),
                writes=(K(vbuf[t]),), tag=f"vones{t}")

    m2 = ctx.mark()
    ybf = ctx.sb(f"ybf{li}", [128, 8 * EXT], BF16)
    scr2set = [[ctx.sb(f"s1scr{li}_{s}_{i}", [128, TG], F32) for i in range(6)]
               for s in range(2)]
    scr = scr2set[0]
    stg = ctx.sb(f"stg{li}", [128, 8], BF16)
    rq2set = [ctx.sb(f"rq2_{li}_{s}", [2, TG], F32) for s in range(2)]
    rq2 = rq2set[0]

    def ysl(buf, c, lo, n, shifted):
        off = lo + (0 if (shifted and c >= 4) else 1)
        return buf[:, EXT * c + off: EXT * c + off + n]

    for c in range(NCH):
        sch.add("gpsimd", lambda g, c=c: g.dma_start(
            out=ybf[:, EXT * c + 1:EXT * c + 1 + T],
            in_=io["yin"][128 * c:128 * (c + 1), :]),
            writes=(K(ybf),), dma=True, tag=f"yld{c}")

    stgp = ctx.sb(f"stgp{li}", [128, 8], BF16)
    sch.add("vector", lambda v: v.tensor_copy(
        stgp[:, :], ybf.rearrange("p (c n) -> p c n", n=EXT)[:, :, 2048]),
        reads=(K(ybf),), writes=(K(stgp),), tag="e2a_stage")
    ptk = sch.add("gpsimd", lambda g: g.indirect_dma_start(
        out=XHA.ap(), out_offset=IOA(ap=xrows[:, 0:1], axis=0),
        in_=stgp[:, :], in_offset=None),
        reads=(K(stgp), K(xrows)), writes=("XHA" + str(li),), dma=True, tag="e2a_push")
    xg.push(xgk + 0, [ptk])
    pk = xg.poll(xgk + 0)
    sch.add("gpsimd", lambda g: g.indirect_dma_start(
        out=stg[:, :], out_offset=None,
        in_=XHA.ap(), in_offset=IOA(ap=xrows[:, 1:2], axis=0)),
        reads=("XHA" + str(li), K(xrows)), writes=(K(stg),), dma=True,
        extra_deps=[pk], tag="e2a_pull")
    sch.add("gpsimd", lambda g: g.tensor_scalar_mul(stg[:, :], stg[:, :], cst.hscale[:, 0:1]),
            reads=(K(stg), K(cst.hscale)), writes=(K(stg),))
    sch.add("vector", lambda v: v.tensor_copy(
        ybf.rearrange("p (c n) -> p c n", n=EXT)[:, :, 0], stg[:, :]),
        reads=(K(stg),), writes=(K(ybf),), tag="e2a_wr")

    def qk_side(gi, hp, is_q):
        cid = gi * 8 + hp * 2 + (0 if is_q else 1)
        scr = scr2set[cid % 2]
        rq2 = rq2set[cid % 2]
        pbA = pb[cid % 4]
        pbB = pb[4 + cid % 4]
        lo = gi * TG
        colbase = (0 if is_q else 512) + 128 * hp
        bmm = pbA
        tab_c, tab_s = cst.tcos, cst.tsin
        tsc = cst.tsq if is_q else cst.tski
        sccol = qs if is_q else ks
        dst = qrot[hp] if is_q else krot[hp]
        dst_lo = lo if is_q else 512 + lo
        tab = amax if is_q else bmax
        tabcol = (4 * gi + hp) if is_q else (4 * (gi + 1) + hp)
        sch.add("tensor", MMG_(pb, bmm[:, :], [
            (lambda t, d=d, colbase=colbase: wq3[:, d, colbase:colbase + 128],
             lambda t, d=d, lo=lo: ysl(ybf, d, lo, TG, True)) for d in range(NCH)]),
            reads=(K(wq3), K(ybf)), writes=(K(bmm),), tag=f"qk{is_q}{gi}_{hp}")
        sch.add("scalar", lambda s, bmm=bmm: s.activation(
            out=scr[0][:, :], in_=bmm[:, :], func=AF.Square),
            reads=(K(bmm),), writes=(K(scr[0]),))
        sch.add("tensor", lambda t, scr=scr, pbB=pbB: t.matmul(
            pbB[0:2, :], cst.hpat2[:, :], scr[0][:, :], start=True, stop=True),
            reads=(K(cst.hpat2), K(scr[0])), writes=(K(pbB),))
        sch.add("vector", lambda v, rq2=rq2, pbB=pbB: v.tensor_scalar_max(rq2[0:2, :], pbB[0:2, :], 1e-24),
                reads=(K(pbB),), writes=(K(rq2),))
        sch.add("scalar", lambda s, rq2=rq2: s.activation(out=rq2[0:2, :], in_=rq2[0:2, :], func=AF.Ln),
                reads=(K(rq2),), writes=(K(rq2),))
        sch.add("scalar", lambda s, rq2=rq2: s.activation(out=rq2[0:2, :], in_=rq2[0:2, :], func=AF.Exp, scale=-0.5),
                reads=(K(rq2),), writes=(K(rq2),))
        sch.add("tensor", lambda t, rq2=rq2, pbB=pbB: t.matmul(
            pbB[:, :], cst.bpat2[:, :], rq2[0:2, :], start=True, stop=True),
            reads=(K(cst.bpat2), K(rq2)), writes=(K(pbB),))
        sch.add("scalar", lambda s, scr=scr, pbB=pbB: s.copy(scr[1][:, :], pbB[:, :]),
                reads=(K(pbB),), writes=(K(scr[1]),))
        sch.add("vector", lambda v, bmm=bmm, sccol=sccol, scr=scr: v.scalar_tensor_tensor(
            out=scr[2][:, :], in0=bmm[:, :], scalar=sccol[:, 0:1],
            in1=scr[1][:, :], op0=ALU.mult, op1=ALU.mult),
            reads=(K(bmm), K(scr[1]), K(sccol)), writes=(K(scr[2]),))
        sch.add("tensor", lambda t, scr=scr, pbB=pbB: t.matmul(
            pbB[:, :], cst.perm[:, :], scr[2][:, :], start=True, stop=True),
            reads=(K(cst.perm), K(scr[2])), writes=(K(pbB),))
        sch.add("vector", lambda v, lo=lo, tab_c=tab_c, scr=scr: v.tensor_mul(
            scr[3][:, :], scr[2][:, :], tab_c[:, lo:lo + TG]),
            reads=(K(scr[2]), K(tab_c)), writes=(K(scr[3]),))
        sch.add("vector", lambda v, lo=lo, tab_s=tab_s, scr=scr, pbB=pbB: v.tensor_mul(
            scr[4][:, :], pbB[:, :], tab_s[:, lo:lo + TG]),
            reads=(K(pbB), K(tab_s)), writes=(K(scr[4]),))
        sch.add("vector", lambda g, scr=scr: g.tensor_add(scr[5][:, :], scr[3][:, :], scr[4][:, :]),
                reads=(K(scr[3]), K(scr[4])), writes=(K(scr[5]),))
        sch.add("scalar", lambda s, scr=scr: s.activation(
            out=scr[0][:, :], in_=scr[5][:, :], func=AF.Square),
            reads=(K(scr[5]),), writes=(K(scr[0]),))
        sch.add("tensor", lambda t, scr=scr, pbB=pbB: t.matmul(
            pbB[0:2, :], cst.hpat2[:, :], scr[0][:, :], start=True, stop=True),
            reads=(K(cst.hpat2), K(scr[0])), writes=(K(pbB),))
        sch.add("vector", lambda v, rq2=rq2, pbB=pbB: v.reduce_max(rq2[0:2, 0:1], pbB[0:2, :],
                                                 axis=mybir.AxisListType.X),
                reads=(K(pbB),), writes=(K(rq2),))
        sch.add("scalar", lambda s, rq2=rq2: s.activation(
            out=rq2[0:2, 0:1], in_=rq2[0:2, 0:1], func=AF.Ln),
            reads=(K(rq2),), writes=(K(rq2),))
        sch.add("scalar", lambda s, tab=tab, tabcol=tabcol, rq2=rq2: s.activation(
            out=tab[0:2, tabcol:tabcol + 1], in_=rq2[0:2, 0:1], func=AF.Exp, scale=0.5),
            reads=(K(rq2),), writes=(K(tab),))
        sch.add("vector", lambda v, dst=dst, dst_lo=dst_lo, lo=lo, tsc=tsc, scr=scr: v.tensor_mul(
            dst[:, dst_lo:dst_lo + TG], scr[5][:, :], tsc[:, lo:lo + TG]),
            reads=(K(scr[5]), K(tsc)), writes=(K(dst),), tag=f"rot{is_q}{gi}{hp}")

    for gi in range(NTG):
        for hp in range(4):
            qk_side(gi, hp, True)
            qk_side(gi, hp, False)
        for lt in range(4):
            ls = gi * TG + 128 * lt
            bv = pb[lt % 4]
            sch.add("tensor", MMG_(pb, bv[:, :], [
                (lambda t, d=d, ls=ls: ysl(ybf, d, ls, 128, True),
                 lambda t, d=d: wq3[:, d, 1024:1536]) for d in range(NCH)]),
                reads=(K(ybf), K(wq3)), writes=(K(bv),), tag=f"vmm{gi}_{lt}")
            ti = 4 + 4 * gi + lt
            sch.add("vector", lambda v, bv=bv, ti=ti: v.tensor_copy(
                vbuf[ti].rearrange("p (h n) -> p h n", n=65)[:, :, 0:64], bv[:, :]),
                reads=(K(bv),), writes=(K(vbuf[ti]),), tag=f"vev{gi}_{lt}")

    deps = []
    for hp in range(4):
        deps.append(sch.add("gpsimd", lambda g, hp=hp: g.indirect_dma_start(
            out=XHK.ap(), out_offset=IOA(ap=xrows[:, 4 + hp:5 + hp], axis=0),
            in_=krot[hp][:, 2048:2560], in_offset=None),
            reads=(K(krot[hp]), K(xrows)), writes=(f"XHK{li}",), dma=True,
            tag=f"e2b_pk{hp}"))
    for i, t in enumerate(range(16, 20)):
        deps.append(sch.add("gpsimd", lambda g, t=t, i=i: g.indirect_dma_start(
            out=XHV.ap(), out_offset=IOA(ap=xrows[:, 8 + i:9 + i], axis=0),
            in_=vbuf[t][:, :], in_offset=None),
            reads=(K(vbuf[t]), K(xrows)), writes=(f"XHV{li}",), dma=True,
            tag=f"e2b_pv{i}"))
    deps.append(sch.add("gpsimd", lambda g: g.indirect_dma_start(
        out=XBM.ap(), out_offset=IOA(ap=xrows[0:2, 12:13], axis=0),
        in_=bmax[0:2, 16:20], in_offset=None),
        reads=(K(bmax), K(xrows)), writes=(f"XBM{li}",), dma=True, tag="e2b_pb"))
    xg.push(xgk + 1, deps)
    pk2 = xg.poll(xgk + 1)
    for hp in range(4):
        sch.add("gpsimd", lambda g, hp=hp: g.indirect_dma_start(
            out=krot[hp][:, 0:512], out_offset=None,
            in_=XHK.ap(), in_offset=IOA(ap=xrows[:, 13 + hp:14 + hp], axis=0)),
            reads=(f"XHK{li}", K(xrows)), writes=(K(krot[hp]),), dma=True,
            extra_deps=[pk2], tag=f"e2b_gk{hp}")
        sch.add("gpsimd", lambda g, hp=hp: g.tensor_scalar_mul(
            krot[hp][:, 0:512], krot[hp][:, 0:512], cst.vkill[:, 0:1]),
            reads=(K(krot[hp]), K(cst.vkill)), writes=(K(krot[hp]),))
    for i in range(4):
        sch.add("gpsimd", lambda g, i=i: g.indirect_dma_start(
            out=vbuf[i][:, 0:520], out_offset=None,
            in_=XHV.ap(), in_offset=IOA(ap=xrows[:, 17 + i:18 + i], axis=0)),
            reads=(f"XHV{li}", K(xrows)), writes=(K(vbuf[i]),), dma=True,
            extra_deps=[pk2], tag=f"e2b_gv{i}")
        sch.add("gpsimd", lambda g, i=i: g.tensor_scalar_mul(
            vbuf[i][:, :], vbuf[i][:, :], cst.vkill[:, 0:1]),
            reads=(K(vbuf[i]), K(cst.vkill)), writes=(K(vbuf[i]),))
    sch.add("gpsimd", lambda g: g.indirect_dma_start(
        out=bmax[0:2, 0:4], out_offset=None,
        in_=XBM.ap(), in_offset=IOA(ap=xrows[0:2, 21:22], axis=0)),
        reads=(f"XBM{li}", K(xrows)), writes=(K(bmax),), dma=True,
        extra_deps=[pk2], tag="e2b_gb")

    sch.add("vector", lambda v: v.tensor_max(bval[0:2, :], bmax[0:2, 4:20], bmax[0:2, 0:16]),
            reads=(K(bmax),), writes=(K(bval),))
    sch.add("vector", lambda v: v.tensor_mul(bval[0:2, :], bval[0:2, :], amax[0:2, :]),
            reads=(K(bval), K(amax)), writes=(K(bval),))
    sch.add("vector", lambda v: v.tensor_scalar(
        out=bval[0:2, :], in0=bval[0:2, :], scalar1=-8.0, scalar2=45.0,
        op0=ALU.mult, op1=ALU.add),
        reads=(K(bval),), writes=(K(bval),))
    for r in range(2):
        sch.add("tensor", lambda t, r=r: t.matmul(
            pb[2][0:1, 16 * r:16 * r + 16], cst.e01[:, r:r + 1], bval[0:2, :],
            start=True, stop=True),
            reads=(K(cst.e01), K(bval)), writes=(K(pb[2]),), tag=f"mrow{r}")
    sch.add("vector", lambda v: v.tensor_copy(mrow[0:1, :], pb[2][0:1, 0:32]),
            reads=(K(pb[2]),), writes=(K(mrow),))
    sch.add("tensor", lambda t: t.matmul(
        pb[3][:, 0:32], cst.ones_f[0:1, :], mrow[0:1, :], start=True, stop=True),
        reads=(K(cst.ones_f), K(mrow)), writes=(K(pb[3]),))
    sch.add("scalar", lambda s: s.copy(bcol[:, :], pb[3][:, 0:32]),
            reads=(K(pb[3]),), writes=(K(bcol),))

    ctx.reset(m2)
    sch.set_fence()
    pT = [ctx.sb(f"pT{li}_{i}", [128, TG], BF16) for i in range(8)]
    ofb = [ctx.sb(f"ofb{li}_{i}", [128, TG], BF16) for i in range(4)]
    recs = [ctx.sb(f"recs{li}_{i}", [1, TG], F32) for i in range(2)]
    scrR = ctx.sb(f"scrR{li}", [128, TG], F32)
    scrX = [ctx.sb(f"scrX{li}_{i}", [128, TG], F32) for i in range(2)]
    scrY = [ctx.sb(f"scrY{li}_{i}", [128, TG], F32) for i in range(2)]

    for w in range(NW):
        for hp in range(4):
            bO = [pb[4], pb[5]]
            for kt in range(8):
                bS = [pb[2 * (kt % 2)], pb[2 * (kt % 2) + 1]]
                for par in range(2):
                    rs = 64 * par
                    def em_s(t, w=w, hp=hp, kt=kt, rs=rs, par=par, bS=bS):
                        return t.matmul(
                            bS[par][:, :],
                            krot[hp][rs:rs + 64, 512 * w + 128 * kt:512 * w + 128 * kt + 128],
                            qrot[hp][rs:rs + 64, 512 * w:512 * w + 512],
                            start=True, stop=True)
                    sch.add("tensor", em_s, reads=(K(krot[hp]), K(qrot[hp])),
                            writes=(K(bS[par]),), tag=f"S{w}_{hp}_{kt}_{par}")
                    hgl = 2 * hp + par
                    bidx = 16 * par + 4 * w + hp
                    def em_e(s, par=par, bidx=bidx, kt=kt, bS=bS):
                        return s.activation(
                            out=pT[2 * (kt % 4) + par][:, :], in_=bS[par][:, :],
                            func=AF.Exp, bias=bcol[:, bidx:bidx + 1], scale=8.0)
                    sch.add("scalar", em_e, reads=(K(bS[par]), K(bcol)),
                            writes=(K(pT[2 * (kt % 4) + par]),), tag=f"E{w}_{hp}_{kt}_{par}")
                    def em_m(g, par=par, kt=kt):
                        buf = pT[2 * (kt % 4) + par]
                        return g.tensor_mul(buf[:, :], buf[:, :],
                                            cst.masks[:, TG * kt:TG * (kt + 1)])
                    sch.add("vector", em_m, reads=(K(pT[2 * (kt % 4) + par]), K(cst.masks)),
                            writes=(K(pT[2 * (kt % 4) + par]),), tag=f"M{w}_{hp}_{kt}_{par}")
                    def em_pv(t, w=w, kt=kt, hgl=hgl, par=par, bO=bO):
                        return t.matmul(
                            bO[par][0:65, :], vbuf[4 * w + kt][:, 65 * hgl:65 * hgl + 65],
                            pT[2 * (kt % 4) + par][:, :], start=(kt == 0), stop=(kt == 7))
                    sch.add("tensor", em_pv,
                            reads=(K(vbuf[4 * w + kt]), K(pT[2 * (kt % 4) + par])),
                            writes=(K(bO[par]),), tag=f"PV{w}_{hp}_{kt}_{par}")
            for par in range(2):
                sch.add("vector", lambda v, par=par, bO=bO: v.reciprocal(
                    recs[par][0:1, :], bO[par][64:65, :]),
                    reads=(K(bO[par]),), writes=(K(recs[par]),))
            def em_rb(t, bO=bO):
                t.matmul(pb[6][0:64, :], cst.ones_f[0:1, 0:64], recs[0][0:1, :],
                         start=True, stop=True)
                return t.matmul(pb[6][64:128, :], cst.ones_f[0:1, 0:64], recs[1][0:1, :],
                                start=True, stop=True)
            sch.add("tensor", em_rb, reads=(K(cst.ones_f), K(recs[0]), K(recs[1])),
                    writes=(K(pb[6]),), tag=f"RB{w}_{hp}")
            sch.add("scalar", lambda s: s.copy(scrR[:, :], pb[6][:, :]),
                    reads=(K(pb[6]),), writes=(K(scrR),))
            for par in range(2):
                rs = 64 * par
                sch.add("vector", lambda v, par=par, rs=rs, hp=hp, bO=bO: v.tensor_mul(
                    ofb[hp][rs:rs + 64, :], bO[par][0:64, :], scrR[rs:rs + 64, :]),
                    reads=(K(bO[par]), K(scrR)), writes=(K(ofb[hp]),), tag=f"OF{w}_{hp}_{par}")
        for oc in range(NCH):
            bY = pb[2 + oc % 2]
            sch.add("tensor", MMG_(pb, bY[:, :], [
                (lambda t, hp=hp2, oc=oc: wo4[:, hp, 128 * oc:128 * oc + 128],
                 lambda t, hp=hp2: ofb[hp][:, :]) for hp2 in range(4)]),
                reads=tuple(K(ofb[i]) for i in range(4)) + (K(wo4),),
                writes=(K(bY),), tag=f"WO{w}_{oc}")
            xse = scrX[oc % 2]
            sch.add("sync", lambda q, oc=oc, w=w, xse=xse: q.dma_start(
                out=xse[:, :], in_=io["yin"][128 * oc:128 * (oc + 1), TG * w:TG * (w + 1)]),
                writes=(K(xse),), dma=True, tag=f"yld2_{w}_{oc}")
            yse = scrY[oc % 2]
            sch.add("vector", lambda v, bY=bY, xse=xse, yse=yse: v.tensor_add(
                yse[:, :], bY[:, :], xse[:, :]),
                reads=(K(bY), K(xse)), writes=(K(yse),), tag=f"YM{w}_{oc}")
            sch.add("sync", lambda q, oc=oc, w=w, yse=yse: q.dma_start(
                out=io["ymid"][128 * oc:128 * (oc + 1), TG * w:TG * (w + 1)], in_=yse[:, :]),
                reads=(K(yse),), writes=(f"ymidd{li}_{w}_{oc}",), dma=True,
                tag=f"YMS{w}_{oc}")
            sch.add("scalar", lambda s, oc=oc, w=w, yse=yse: s.copy(
                ymid[:, EXT * oc + 1 + TG * w: EXT * oc + 1 + TG * (w + 1)], yse[:, :]),
                reads=(K(yse),), writes=(K(ymid),), tag=f"YMB{w}_{oc}")

    stg2 = ctx.sb(f"stg2{li}", [128, 8], BF16)
    stg2p = ctx.sb(f"stg2p{li}", [128, 8], BF16)
    sch.add("vector", lambda v: v.tensor_copy(
        stg2p[:, :], ymid.rearrange("p (c n) -> p c n", n=EXT)[:, :, 2048]),
        reads=(K(ymid),), writes=(K(stg2p),), tag="e3_stage")
    ptk3 = sch.add("gpsimd", lambda g: g.indirect_dma_start(
        out=XMB.ap(), out_offset=IOA(ap=xrows[:, 2:3], axis=0),
        in_=stg2p[:, :], in_offset=None),
        reads=(K(stg2p), K(xrows)), writes=(f"XMB{li}",), dma=True, tag="e3_push")
    xg.push(xgk + 2, [ptk3])
    pk3 = xg.poll(xgk + 2)
    sch.add("gpsimd", lambda g: g.indirect_dma_start(
        out=stg2[:, :], out_offset=None,
        in_=XMB.ap(), in_offset=IOA(ap=xrows[:, 3:4], axis=0)),
        reads=(f"XMB{li}", K(xrows)), writes=(K(stg2),), dma=True,
        extra_deps=[pk3], tag="e3_pull")
    if P2_DEBUG:
        sch.add("sync", lambda q: q.dma_start(out=io["dbg_stg2"], in_=stg2[:, :]),
                reads=(K(stg2),), writes=("dbg_stg2",), dma=True)
    sch.add("gpsimd", lambda g: g.tensor_scalar_mul(stg2[:, :], stg2[:, :], cst.hscale[:, 0:1]),
            reads=(K(stg2), K(cst.hscale)), writes=(K(stg2),))
    sch.add("vector", lambda v: v.tensor_copy(
        ymid.rearrange("p (c n) -> p c n", n=EXT)[:, :, 0], stg2[:, :]),
        reads=(K(stg2),), writes=(K(ymid),), tag="e3_wr")

    ctx.reset(m1)
    sch.set_fence()
    hnb = ctx.sb(f"hnb{li}", [128, 8 * TG], BF16)
    h2g = ctx.sb(f"h2g{li}", [128, NJ * TG], BF16)
    w1t = [ctx.sb(f"w1t{li}_{i}", [128, 8, 256], BF16) for i in range(2)]
    w2t = [ctx.sb(f"w2t{li}_{i}", [128, 1024], BF16) for i in range(2)]
    scr3 = [ctx.sb(f"s3scr{li}_{i}", [128, TG], F32) for i in range(6)]
    scr3m = ctx.sb(f"s3scrm{li}", [128, TG], F32)
    scr3b = [ctx.sb(f"s3scrb{li}_{i}", [128, TG], BF16) for i in range(2)]
    srow = [ctx.sb(f"s3row{li}_{i}", [1, TG], F32) for i in range(6)]
    lgc = ctx.sb(f"lgc{li}", [128, 8], F32)
    lbc = ctx.sb(f"lbc{li}", [128, 8], F32)
    b1c = ctx.sb(f"b1c{li}", [128, 2 * NJ], F32)
    b2c = ctx.sb(f"b2cc{li}", [128, 8], F32)
    for nm, buf in [(f"lg{li}", lgc), (f"lb{li}", lbc), (f"b1p{li}", b1c),
                    (f"b2c{li}", b2c)]:
        sch.add("sync", lambda q, nm=nm, buf=buf: q.dma_start(out=buf[:, :], in_=io[nm]),
                writes=(K(buf),), dma=True)

    for gi in range(NTG):
        lo = gi * TG
        bsum, bsq = pb[6], pb[7]
        for c in range(NCH):
            def em_sum(t, c=c, lo=lo, bsum=bsum):
                return t.matmul(bsum[0:1, :], cst.ones_b[:, 0:1],
                                ysl(ymid, c, lo, TG, True), start=(c == 0), stop=(c == NCH - 1))
            sch.add("tensor", em_sum, reads=(K(ymid), K(cst.ones_b)), writes=(K(bsum),),
                    tag=f"hsum{gi}_{c}")
            sq = scr3b[c % 2]
            sch.add("scalar", lambda s, c=c, lo=lo, sq=sq: s.activation(
                out=sq[:, :], in_=ysl(ymid, c, lo, TG, True), func=AF.Square),
                reads=(K(ymid),), writes=(K(sq),))
            def em_sq(t, c=c, sq=sq, bsq=bsq):
                return t.matmul(bsq[0:1, :], cst.ones_b[:, 0:1], sq[:, :],
                                start=(c == 0), stop=(c == NCH - 1))
            sch.add("tensor", em_sq, reads=(K(sq), K(cst.ones_b)), writes=(K(bsq),),
                    tag=f"hsq{gi}_{c}")
        sch.add("scalar", lambda s, bsum=bsum: s.activation(
            out=srow[0][0:1, :], in_=bsum[0:1, :], func=AF.Copy, scale=1.0 / D),
            reads=(K(bsum),), writes=(K(srow[0]),))
        sch.add("scalar", lambda s, bsq=bsq: s.activation(
            out=srow[1][0:1, :], in_=bsq[0:1, :], func=AF.Copy, scale=1.0 / D),
            reads=(K(bsq),), writes=(K(srow[1]),))
        sch.add("scalar", lambda s: s.activation(
            out=srow[2][0:1, :], in_=srow[0][0:1, :], func=AF.Square),
            reads=(K(srow[0]),), writes=(K(srow[2]),))
        sch.add("vector", lambda v: v.tensor_sub(srow[3][0:1, :], srow[1][0:1, :], srow[2][0:1, :]),
                reads=(K(srow[1]), K(srow[2])), writes=(K(srow[3]),))
        sch.add("vector", lambda v: v.tensor_scalar_add(srow[3][0:1, :], srow[3][0:1, :], 1e-5),
                reads=(K(srow[3]),), writes=(K(srow[3]),))
        sch.add("scalar", lambda s: s.activation(out=srow[4][0:1, :], in_=srow[3][0:1, :],
                                                 func=AF.Ln),
                reads=(K(srow[3]),), writes=(K(srow[4]),))
        sch.add("scalar", lambda s: s.activation(out=srow[5][0:1, :], in_=srow[4][0:1, :],
                                                 func=AF.Exp, scale=-0.5),
                reads=(K(srow[4]),), writes=(K(srow[5]),))
        sch.add("vector", lambda v: v.tensor_mul(srow[2][0:1, :], srow[0][0:1, :], srow[5][0:1, :]),
                reads=(K(srow[0]), K(srow[5])), writes=(K(srow[2]),))
        br, bm = pb[4], pb[5]
        sch.add("tensor", lambda t, br=br: t.matmul(
            br[:, :], cst.ones_f[0:1, :], srow[5][0:1, :], start=True, stop=True),
            reads=(K(cst.ones_f), K(srow[5])), writes=(K(br),))
        sch.add("tensor", lambda t, bm=bm: t.matmul(
            bm[:, :], cst.ones_f[0:1, :], srow[2][0:1, :], start=True, stop=True),
            reads=(K(cst.ones_f), K(srow[2])), writes=(K(bm),))
        sch.add("scalar", lambda s, bm=bm: s.copy(scr3m[:, :], bm[:, :]),
                reads=(K(bm),), writes=(K(scr3m),))
        for c in range(NCH):
            sa = scr3[c % 2]
            sb_ = scr3[4 + c % 2]
            sch.add("vector", lambda v, c=c, lo=lo, br=br, sa=sa: v.tensor_mul(
                sa[:, :], ysl(ymid, c, lo, TG, True), br[:, :]),
                reads=(K(ymid), K(br)), writes=(K(sa),))
            sch.add("vector", lambda g, sa=sa, sb_=sb_: g.tensor_sub(sb_[:, :], sa[:, :], scr3m[:, :]),
                    reads=(K(sa), K(scr3m)), writes=(K(sb_),))
            sch.add("vector", lambda v, c=c, sb_=sb_: v.tensor_scalar(
                out=hnb[:, TG * c:TG * (c + 1)], in0=sb_[:, :],
                scalar1=lgc[:, c:c + 1], scalar2=lbc[:, c:c + 1],
                op0=ALU.mult, op1=ALU.add),
                reads=(K(sb_), K(lgc), K(lbc)), writes=(K(hnb),), tag=f"hnb{gi}_{c}")
        if P2_DEBUG and gi == 0:
            sch.add("sync", lambda q: q.dma_start(out=io["dbg_hnb"], in_=hnb[:, :]),
                    reads=(K(hnb),), writes=("dbg_hnb",), dma=True)
            sch.add("sync", lambda q: q.dma_start(out=io["dbg_mu"], in_=srow[0][0:1, :]),
                    reads=(K(srow[0]),), writes=("dbg_mu",), dma=True)
            sch.add("sync", lambda q: q.dma_start(out=io["dbg_rstd"], in_=srow[5][0:1, :]),
                    reads=(K(srow[5]),), writes=("dbg_rstd",), dma=True)
        for j in range(NJ):
            nj = 128 if j < 21 else 42
            wt = w1t[j % 2]
            sch.add("gpsimd", lambda g, j=j, wt=wt: g.dma_start(
                out=wt[:, :, :],
                in_=io[f"w1p{li}"][:, 256 * j:256 * (j + 1)].rearrange("(c p) n -> p c n", p=128)),
                writes=(K(wt),), dma=True, tag=f"w1ld{gi}_{j}")
            bA, bG = pb[2 * (j % 2)], pb[2 * (j % 2) + 1]
            sch.add("tensor", MMG_(pb, bA[0:nj, :], [
                (lambda t, c=c, wt=wt, nj=nj: wt[:, c, 0:nj],
                 lambda t, c=c: hnb[:, TG * c:TG * (c + 1)]) for c in range(NCH)]),
                reads=(K(wt), K(hnb)), writes=(K(bA),), tag=f"amm{gi}_{j}")
            sch.add("tensor", MMG_(pb, bG[0:nj, :], [
                (lambda t, c=c, wt=wt, nj=nj: wt[:, c, 128:128 + nj],
                 lambda t, c=c: hnb[:, TG * c:TG * (c + 1)]) for c in range(NCH)]),
                reads=(K(wt), K(hnb)), writes=(K(bG),), tag=f"gmm{gi}_{j}")
            gsc = scr3[2 + j % 2]
            sch.add("scalar", lambda s, j=j, nj=nj, bG=bG, gsc=gsc: s.activation(
                out=gsc[0:nj, :], in_=bG[0:nj, :], func=AF.Gelu,
                bias=b1c[0:nj, 2 * j + 1:2 * j + 2]),
                reads=(K(bG), K(b1c)), writes=(K(gsc),))
            sch.add("vector", lambda v, j=j, nj=nj, bA=bA, gsc=gsc: v.scalar_tensor_tensor(
                out=h2g[0:nj, TG * j:TG * (j + 1)], in0=bA[0:nj, :],
                scalar=b1c[0:nj, 2 * j:2 * j + 1], in1=gsc[0:nj, :],
                op0=ALU.add, op1=ALU.mult),
                reads=(K(bA), K(gsc), K(b1c)), writes=(K(h2g),), tag=f"h2{gi}_{j}")
        if P2_DEBUG and gi == 0:
            sch.add("sync", lambda q: q.dma_start(out=io["dbg_h2"], in_=h2g[:, 0:2 * TG]),
                    reads=(K(h2g),), writes=("dbg_h2",), dma=True)
        for j in range(NJ):
            nj = 128 if j < 21 else 42
            wt = w2t[j % 2]
            sch.add("gpsimd", lambda g, j=j, nj=nj, wt=wt: g.dma_start(
                out=wt[0:nj, :], in_=io[f"w2{li}"][128 * j:128 * j + nj, :]),
                writes=(K(wt),), dma=True, tag=f"w2ld{gi}_{j}")
            for oc in range(NCH):
                def em_w2(t, j=j, nj=nj, oc=oc, wt=wt):
                    return t.matmul(pb[oc][:, :], wt[0:nj, 128 * oc:128 * (oc + 1)],
                                    h2g[0:nj, TG * j:TG * (j + 1)],
                                    start=(j == 0), stop=(j == NJ - 1))
                sch.add("tensor", em_w2, reads=(K(wt), K(h2g)), writes=(K(pb[oc]),),
                        tag=f"w2mm{gi}_{j}_{oc}")
        for oc in range(NCH):
            xse = scr3[oc % 2]
            sch.add("sync", lambda q, oc=oc, lo=lo, xse=xse: q.dma_start(
                out=xse[:, :], in_=io["ymid"][128 * oc:128 * (oc + 1), lo:lo + TG]),
                writes=(K(xse),), dma=True, tag=f"ymld{gi}_{oc}")
            yse = scr3[4 + oc % 2]
            sch.add("vector", lambda v, oc=oc, xse=xse, yse=yse: v.scalar_tensor_tensor(
                out=yse[:, :], in0=pb[oc][:, :], scalar=b2c[:, oc:oc + 1],
                in1=xse[:, :], op0=ALU.add, op1=ALU.add),
                reads=(K(pb[oc]), K(xse), K(b2c)), writes=(K(yse),), tag=f"yo{gi}_{oc}")
            sch.add("sync", lambda q, oc=oc, lo=lo, yse=yse: q.dma_start(
                out=io["yout"][128 * oc:128 * (oc + 1), lo:lo + TG], in_=yse[:, :]),
                reads=(K(yse),), writes=(f"yout{li}_{gi}_{oc}",), dma=True,
                tag=f"yos{gi}_{oc}")
    if P2_DEBUG:
        sch.add("sync", lambda q: q.dma_start(out=io["dbg_polls"], in_=xg.iters[0:1, 0:3]),
                reads=(K(xg.iters),), writes=("dbg_polls",), dma=True)
    ctx.reset(m0)
    sch.set_fence()


def MMG_(pb, bank_ap, pairs):
    def em(t):
        n = len(pairs)
        last = None
        for i, (lf, rf) in enumerate(pairs):
            last = t.matmul(bank_ap, lf(t), rf(t), start=(i == 0), stop=(i == n - 1))
        return last
    return em


def xrows_host(core):
    h = core % 2
    p = np.arange(128, dtype=np.int64)
    cols = []
    cols.append(128 * h + p)
    cols.append(128 * (1 - h) + p)
    cols.append(128 * h + p)
    cols.append(128 * (1 - h) + p)
    for hp in range(4):
        cols.append(512 * h + 128 * hp + p)
    for i in range(4):
        cols.append(512 * h + 128 * i + p)
    cols.append(2 * h + p)
    for hp in range(4):
        cols.append(512 * (1 - h) + 128 * hp + p)
    for i in range(4):
        cols.append(512 * (1 - h) + 128 * i + p)
    cols.append(2 * (1 - h) + p)
    return np.stack(cols, axis=1).astype(np.uint32)


def build_p2(li=0):
    nc = bass.Bass("TRN2", target_bir_lowering=False)
    ctx = Ctx(nc)
    io = {}
    io["yin"] = nc.dram_tensor("yin", [D, T], F32, kind="ExternalInput").ap()
    io["ymid"] = nc.dram_tensor("ymid_d", [D, T], F32, kind="ExternalOutput").ap()
    io["yout"] = nc.dram_tensor("yout", [D, T], F32, kind="ExternalOutput").ap()
    io["xrows"] = nc.dram_tensor("xrows", [128, 22], U32, kind="ExternalInput").ap()
    if P2_DEBUG:
        io["dbg_hnb"] = nc.dram_tensor("dbg_hnb", [128, 8 * TG], BF16, kind="ExternalOutput").ap()
        io["dbg_mu"] = nc.dram_tensor("dbg_mu", [1, TG], F32, kind="ExternalOutput").ap()
        io["dbg_rstd"] = nc.dram_tensor("dbg_rstd", [1, TG], F32, kind="ExternalOutput").ap()
        io["dbg_h2"] = nc.dram_tensor("dbg_h2", [128, 2 * TG], BF16, kind="ExternalOutput").ap()
        io["dbg_stg2"] = nc.dram_tensor("dbg_stg2", [128, 8], BF16, kind="ExternalOutput").ap()
        io["dbg_polls"] = nc.dram_tensor("dbg_polls", [1, 3], I32, kind="ExternalOutput").ap()
    io.update(l0ffn_const_decls(nc))
    io.update(gcst_decls(nc))
    io.update(l0ffn_layer_decls(nc, li))
    io.update(Exchange.input_decls(nc, 3))
    with nc.Block() as blk:
        sch = Sched(ctx)
        xg = Exchange(ctx, sch, 3)
        xg.emit_load(io)
        gc = GlobalCst(ctx, sch, io)
        cst = L0Consts(ctx, sch, io, gc)
        emit_l0ffn_layer(ctx, sch, xg, cst, io, li, 0)
        sch.flush(blk)
    ctx.st.close()
    return nc


def run_p2(inputs, yin_percore, li=0):
    nc = build_p2(li)
    in_maps = []
    for c in range(NCORE):
        d = {"yin": yin_percore[c], "xrows": xrows_host(c)}
        d.update(l0ffn_const_host(c))
        d.update(l0ffn_layer_host(inputs, li))
        in_maps.append(d)
    seed = int(np.random.default_rng().integers(2 ** 31))
    for c in range(NCORE):
        pair_rng = np.random.default_rng(seed + (c // 2))
        xgi = Exchange.__new__(Exchange)
        xgi.n_flags = 3
        in_maps[c].update(Exchange.host_inputs(xgi, c % 2, pair_rng))
    res = run_bass_kernel_spmd(nc, in_maps, core_ids=list(range(NCORE)))
    return res.results


def emit_postln(ctx, sch, cst, io_a, io_b, io_out, tagp):
    pb = ctx.pbanks
    m = ctx.mark()
    yg = [ctx.sb(f"{tagp}yg{c}", [128, TG], F32) for c in range(NCH)]
    xs = [ctx.sb(f"{tagp}xs{i}", [128, TG], F32) for i in range(4)]
    so = [ctx.sb(f"{tagp}so{i}", [128, TG], F32) for i in range(2)]
    scq = [ctx.sb(f"{tagp}scq{i}", [128, TG], BF16) for i in range(2)]
    srow = [ctx.sb(f"{tagp}srow{i}", [1, TG], F32) for i in range(6)]
    for gi in range(NTG):
        lo = gi * TG
        bsum, bsq = pb[6], pb[7]
        for c in range(NCH):
            xa = xs[2 * (c % 2)]
            xb_ = xs[2 * (c % 2) + 1]
            sch.add("sync", lambda q, c=c, lo=lo, xa=xa: q.dma_start(
                out=xa[:, :], in_=io_a[128 * c:128 * (c + 1), lo:lo + TG]),
                writes=(K(xa),), dma=True, tag=f"{tagp}la{gi}_{c}")
            sch.add("scalar", lambda q, c=c, lo=lo, xb_=xb_: q.dma_start(
                out=xb_[:, :], in_=io_b[128 * c:128 * (c + 1), lo:lo + TG]),
                writes=(K(xb_),), dma=True, tag=f"{tagp}lb{gi}_{c}")
            sch.add("vector", lambda g, c=c, xa=xa, xb_=xb_: g.tensor_add(
                yg[c][:, :], xa[:, :], xb_[:, :]),
                reads=(K(xa), K(xb_)), writes=(K(yg[c]),), tag=f"{tagp}y{gi}_{c}")
            def em_sum(t, c=c, bsum=bsum):
                return t.matmul(bsum[0:1, :], cst.ones_f[:, 0:1], yg[c][:, :],
                                start=(c == 0), stop=(c == NCH - 1))
            sch.add("tensor", em_sum, reads=(K(yg[c]), K(cst.ones_f)), writes=(K(bsum),))
            sq = scq[c % 2]
            sch.add("scalar", lambda s, c=c, sq=sq: s.activation(
                out=sq[:, :], in_=yg[c][:, :], func=AF.Square),
                reads=(K(yg[c]),), writes=(K(sq),))
            def em_sq(t, c=c, sq=sq, bsq=bsq):
                return t.matmul(bsq[0:1, :], cst.ones_b[:, 0:1], sq[:, :],
                                start=(c == 0), stop=(c == NCH - 1))
            sch.add("tensor", em_sq, reads=(K(sq), K(cst.ones_b)), writes=(K(bsq),))
        sch.add("scalar", lambda s, bsum=bsum: s.activation(
            out=srow[0][0:1, :], in_=bsum[0:1, :], func=AF.Copy, scale=1.0 / D),
            reads=(K(bsum),), writes=(K(srow[0]),))
        sch.add("scalar", lambda s, bsq=bsq: s.activation(
            out=srow[1][0:1, :], in_=bsq[0:1, :], func=AF.Copy, scale=1.0 / D),
            reads=(K(bsq),), writes=(K(srow[1]),))
        sch.add("scalar", lambda s: s.activation(
            out=srow[2][0:1, :], in_=srow[0][0:1, :], func=AF.Square),
            reads=(K(srow[0]),), writes=(K(srow[2]),))
        sch.add("vector", lambda v: v.tensor_sub(srow[3][0:1, :], srow[1][0:1, :], srow[2][0:1, :]),
                reads=(K(srow[1]), K(srow[2])), writes=(K(srow[3]),))
        sch.add("vector", lambda v: v.tensor_scalar_add(srow[3][0:1, :], srow[3][0:1, :], 1e-5),
                reads=(K(srow[3]),), writes=(K(srow[3]),))
        sch.add("scalar", lambda s: s.activation(out=srow[4][0:1, :], in_=srow[3][0:1, :],
                                                 func=AF.Ln),
                reads=(K(srow[3]),), writes=(K(srow[4]),))
        sch.add("scalar", lambda s: s.activation(out=srow[5][0:1, :], in_=srow[4][0:1, :],
                                                 func=AF.Exp, scale=-0.5),
                reads=(K(srow[4]),), writes=(K(srow[5]),))
        sch.add("vector", lambda v: v.tensor_mul(srow[2][0:1, :], srow[0][0:1, :], srow[5][0:1, :]),
                reads=(K(srow[0]), K(srow[5])), writes=(K(srow[2]),))
        br, bm = pb[4], pb[5]
        sch.add("tensor", lambda t, br=br: t.matmul(
            br[:, :], cst.ones_f[0:1, :], srow[5][0:1, :], start=True, stop=True),
            reads=(K(cst.ones_f), K(srow[5])), writes=(K(br),))
        sch.add("tensor", lambda t, bm=bm: t.matmul(
            bm[:, :], cst.ones_f[0:1, :], srow[2][0:1, :], start=True, stop=True),
            reads=(K(cst.ones_f), K(srow[2])), writes=(K(bm),))
        for c in range(NCH):
            sv = so[c % 2]
            sch.add("vector", lambda v, c=c, br=br, sv=sv: v.tensor_mul(
                sv[:, :], yg[c][:, :], br[:, :]),
                reads=(K(yg[c]), K(br)), writes=(K(sv),))
            sch.add("vector", lambda v, bm=bm, sv=sv: v.tensor_sub(
                sv[:, :], sv[:, :], bm[:, :]),
                reads=(K(sv), K(bm)), writes=(K(sv),))
            sch.add("sync", lambda q, c=c, lo=lo, sv=sv: q.dma_start(
                out=io_out[128 * c:128 * (c + 1), lo:lo + TG], in_=sv[:, :]),
                reads=(K(sv),), writes=(f"{tagp}o{gi}_{c}",), dma=True,
                tag=f"{tagp}st{gi}_{c}")
    ctx.reset(m)
    sch.set_fence()


def build_p3(li=1):
    nc = bass.Bass("TRN2", target_bir_lowering=False)
    ctx = Ctx(nc)
    io = {}
    io["yin"] = nc.dram_tensor("yin", [D, T], F32, kind="ExternalInput").ap()
    io["s1"] = nc.dram_tensor("s1", [D, T], F32, kind="ExternalInput").ap()
    io["ymid"] = nc.dram_tensor("ymid_d", [D, T], F32, kind="ExternalOutput").ap()
    io["yout"] = nc.dram_tensor("yout", [D, T], F32, kind="ExternalOutput").ap()
    io["s2"] = nc.dram_tensor("s2", [D, T], F32, kind="ExternalOutput").ap()
    io["xrows"] = nc.dram_tensor("xrows", [128, 22], U32, kind="ExternalInput").ap()
    io.update(l0ffn_const_decls(nc))
    io.update(gcst_decls(nc))
    io.update(l0ffn_layer_decls(nc, li))
    io.update(Exchange.input_decls(nc, 3))
    with nc.Block() as blk:
        sch = Sched(ctx)
        xg = Exchange(ctx, sch, 3)
        xg.emit_load(io)
        gc = GlobalCst(ctx, sch, io)
        cst = L0Consts(ctx, sch, io, gc)
        emit_l0ffn_layer(ctx, sch, xg, cst, io, li, 0)
        emit_postln(ctx, sch, cst, io["s1"], io["yout"], io["s2"], "pl3")
        sch.flush(blk)
    ctx.st.close()
    return nc


def run_p3(inputs, y1_percore, s1_percore, li=1):
    nc = build_p3(li)
    in_maps = []
    for c in range(NCORE):
        d = {"yin": y1_percore[c], "s1": s1_percore[c], "xrows": xrows_host(c)}
        d.update(l0ffn_const_host(c))
        d.update(l0ffn_layer_host(inputs, li))
        in_maps.append(d)
    seed = int(np.random.default_rng().integers(2 ** 31))
    for c in range(NCORE):
        pair_rng = np.random.default_rng(seed + (c // 2))
        xgi = Exchange.__new__(Exchange)
        xgi.n_flags = 3
        in_maps[c].update(Exchange.host_inputs(xgi, c % 2, pair_rng))
    res = run_bass_kernel_spmd(nc, in_maps, core_ids=list(range(NCORE)))
    return res.results


NH1 = 16
LAT = 128


def p4_decls(nc, with_stream=True):
    d = {}
    if with_stream:
        d["s2i"] = nc.dram_tensor("s2i", [D, T], F32, kind="ExternalInput").ap()
    d["Wq1p"] = nc.dram_tensor("Wq1p", [D, D], F32, kind="ExternalInput").ap()
    d["Wkv1"] = nc.dram_tensor("Wkv1", [D, LAT], F32, kind="ExternalInput").ap()
    d["Wk1p"] = nc.dram_tensor("Wk1p", [LAT, 768], F32, kind="ExternalInput").ap()
    d["Wv1"] = nc.dram_tensor("Wv1", [LAT, 512], F32, kind="ExternalInput").ap()
    d["Wo1p"] = nc.dram_tensor("Wo1p", [768, D], F32, kind="ExternalInput").ap()
    d["qg1"] = nc.dram_tensor("qg1", [128, 1], F32, kind="ExternalInput").ap()
    d["kg1"] = nc.dram_tensor("kg1", [128, 1], F32, kind="ExternalInput").ap()
    d["xrows4"] = nc.dram_tensor("xrows4", [128, 3], U32, kind="ExternalInput").ap()
    d["vkill4"] = nc.dram_tensor("vkill4", [128, 32], F32, kind="ExternalInput").ap()
    d["qsel32"] = nc.dram_tensor("qsel32", [128, 2], F32, kind="ExternalInput").ap()
    d["hpat3"] = nc.dram_tensor("hpat3", [128, 3], F32, kind="ExternalInput").ap()
    d["bpat3"] = nc.dram_tensor("bpat3", [3, 128], F32, kind="ExternalInput").ap()
    d["e3x"] = nc.dram_tensor("e3x", [3, 3], F32, kind="ExternalInput").ap()
    d["maskd0"] = nc.dram_tensor("maskd0", [128, 640], F32, kind="ExternalInput").ap()
    d["maskd1"] = nc.dram_tensor("maskd1", [128, 640], F32, kind="ExternalInput").ap()
    d["mkill4"] = nc.dram_tensor("mkill4", [128, 128], F32, kind="ExternalInput").ap()
    return d


def p4_host(inputs, core):
    h = core % 2
    Wq = np.asarray(inputs["L1_attn_Wq"], np.float32)
    Wk = np.asarray(inputs["L1_attn_Wk"], np.float32)
    Wo = np.asarray(inputs["L1_attn_Wo"], np.float32)
    Wk1p = np.zeros((LAT, 768), np.float32)
    Wo1p = np.zeros((768, D), np.float32)
    for H in range(NH1):
        cc, r = H // 3, H % 3
        Wk1p[:, 128 * cc + 32 * r:128 * cc + 32 * r + 32] = Wk[:, 32 * H:32 * H + 32]
        Wo1p[128 * cc + 32 * r:128 * cc + 32 * r + 32, :] = Wo[64 * H:64 * H + 32, :]
    qg = np.asarray(inputs["L1_attn_q_g"], np.float32)
    kg = np.asarray(inputs["L1_attn_k_g"], np.float32)
    kgcol = np.zeros((128, 1), np.float32)
    for r in range(3):
        kgcol[32 * r:32 * r + 32, 0] = kg
    p = np.arange(128, dtype=np.uint32)
    xrows4 = np.stack([128 * h + p, p, 128 + p], axis=1).astype(np.uint32)
    vkill4 = np.ones((128, 32), np.float32)
    if h == 0:
        vkill4[:, 16:] = 0.0
    qsel32 = np.zeros((128, 2), np.float32)
    qsel32[0:32, 0] = 1.0
    qsel32[64:96, 1] = 1.0
    hpat3 = np.zeros((128, 3), np.float32)
    bpat3 = np.zeros((3, 128), np.float32)
    for r in range(3):
        hpat3[32 * r:32 * r + 32, r] = 1.0
        bpat3[r, 32 * r:32 * r + 32] = 1.0
    jr = np.arange(128)[:, None]
    qq = np.arange(128)[None, :]
    tri = np.concatenate([np.zeros((128, 512), np.float32),
                          (qq >= jr).astype(np.float32)], axis=1)
    ones = np.ones((128, 640), np.float32)
    mkill = np.ones((128, 128), np.float32)
    if h == 0:
        for qc in range(4):
            for kt in range(32):
                if kt > 4 * qc + 3:
                    mkill[:, 32 * qc + kt] = 0.0
    return {
        "s2i": None,
        "Wq1p": np.ascontiguousarray(Wq),
        "Wkv1": np.asarray(inputs["L1_attn_Wkv"], np.float32),
        "Wk1p": Wk1p,
        "Wv1": np.asarray(inputs["L1_attn_Wv"], np.float32),
        "Wo1p": Wo1p,
        "qg1": np.concatenate([qg, qg])[:, None].astype(np.float32),
        "kg1": kgcol,
        "xrows4": xrows4, "vkill4": vkill4, "qsel32": qsel32,
        "hpat3": hpat3, "bpat3": bpat3, "e3x": np.eye(3, dtype=np.float32),
        "maskd0": tri if h == 0 else ones,
        "maskd1": tri if h == 1 else ones,
        "mkill4": mkill,
    }


def emit_p4(ctx, sch, xg, cst, io, xgk):
    nc = ctx.nc
    pb = ctx.pbanks
    TL = 2 * T

    XL = nc.dram_tensor("XLshare", [256, T], BF16, addr_space="Shared")

    m0 = ctx.mark()
    lat = ctx.sb("p4lat", [128, TL], BF16)
    kn = ctx.sb("p4kn", [128, 6 * TL], BF16)
    qsc = ctx.sb("p4qsc", [128, 6 * T], BF16)
    vbuf = [ctx.sb(f"p4v{t}", [128, 16 * 33], BF16) for t in range(32)]
    wq = ctx.sb("p4wq", [128, 8, D], BF16)
    wkv = ctx.sb("p4wkv", [128, 8, LAT], BF16)
    wk = ctx.sb("p4wk", [128, 768], BF16)
    wv = ctx.sb("p4wv", [128, 512], BF16)
    wo6 = ctx.sb("p4wo", [128, 6, D], BF16)
    qg = ctx.sb("p4qg", [128, 1], F32)
    kg = ctx.sb("p4kg", [128, 1], F32)
    xrows = ctx.sb("p4xr", [128, 3], U32)
    vkill = ctx.sb("p4vk", [128, 32], F32)
    qsel = ctx.sb("p4qsel", [128, 2], F32)
    hpat3 = ctx.sb("p4hp3", [128, 3], F32)
    bpat3 = ctx.sb("p4bp3", [3, 128], F32)
    maskd = [ctx.sb(f"p4md{i}", [128, 640], BF16) for i in range(2)]
    mkill = ctx.sb("p4mk", [128, 128], F32)
    qtab = ctx.sb("p4qtab", [2, 8], F32)
    ktab = ctx.sb("p4ktab", [3, 6], F32)
    qrow = ctx.sb("p4qrow", [1, 16], F32)
    krow = ctx.sb("p4krow", [1, 18], F32)
    bcol = ctx.sb("p4bcol", [128, 16], F32)
    for nm, buf, qd in [("Wq1p", wq, 1), ("Wkv1", wkv, 1), ("Wk1p", wk, 0),
                        ("Wv1", wv, 0), ("Wo1p", wo6, 1)]:
        if qd:
            sch.add("gpsimd", lambda g, nm=nm, buf=buf: g.dma_start(
                out=buf[:, :, :], in_=io[nm].rearrange("(c p) n -> p c n", p=128)),
                writes=(K(buf),), dma=True, tag=nm)
        else:
            sch.add("gpsimd", lambda g, nm=nm, buf=buf: g.dma_start(
                out=buf[:, :], in_=io[nm]), writes=(K(buf),), dma=True, tag=nm)
    for nm, buf in [("qg1", qg), ("kg1", kg), ("xrows4", xrows), ("vkill4", vkill),
                    ("qsel32", qsel), ("hpat3", hpat3), ("bpat3", bpat3)]:
        sch.add("sync", lambda q, nm=nm, buf=buf: q.dma_start(out=buf[:, :], in_=io[nm]),
                writes=(K(buf),), dma=True, tag=nm)
    for i in range(2):
        sch.add("gpsimd", lambda g, i=i: g.dma_start(out=maskd[i][:, :], in_=io[f"maskd{i}"]),
                writes=(K(maskd[i]),), dma=True)
    sch.add("sync", lambda q: q.dma_start(out=mkill[:, :], in_=io["mkill4"]),
            writes=(K(mkill),), dma=True)
    for t in range(32):
        sch.add("gpsimd", lambda g, t=t: g.memset(vbuf[t][:, :], 1.0),
                writes=(K(vbuf[t]),))
    sch.add("gpsimd", lambda g: g.memset(qtab[:, :], 0.0), writes=(K(qtab),))
    sch.add("gpsimd", lambda g: g.memset(ktab[:, :], 0.0), writes=(K(ktab),))

    m1 = ctx.mark()
    s2b = ctx.sb("p4s2b", [128, 8, T], BF16)
    scr = [ctx.sb(f"p4scr{i}", [128, TG], F32) for i in range(6)]
    rq2 = ctx.sb("p4rq2", [3, TG], F32)
    r21 = ctx.sb("p4r21", [3, 1], F32)

    for c in range(NCH):
        sch.add("gpsimd", lambda g, c=c: g.dma_start(
            out=s2b[:, c, :], in_=io["s2i"][128 * c:128 * (c + 1), :]),
            writes=(K(s2b),), dma=True, tag=f"s2ld{c}")

    latmy = ctx.sb("p4latmy", [128, T], BF16)
    for gi in range(NTG):
        lo = gi * TG
        bl = pb[0]
        sch.add("tensor", MMG_(pb, bl[:, :], [
            (lambda t, d=d: wkv[:, d, :],
             lambda t, d=d, lo=lo: s2b[:, d, lo:lo + TG]) for d in range(NCH)]),
            reads=(K(wkv), K(s2b)), writes=(K(bl),), tag=f"lat{gi}")
        sch.add("vector", lambda v, lo=lo, bl=bl: v.tensor_copy(latmy[:, lo:lo + TG], bl[:, :]),
                reads=(K(bl),), writes=(K(latmy),))
    ptk = sch.add("gpsimd", lambda g: g.indirect_dma_start(
        out=XL.ap(), out_offset=IOA(ap=xrows[:, 0:1], axis=0),
        in_=latmy[:, :], in_offset=None),
        reads=(K(latmy), K(xrows)), writes=("XL",), dma=True, tag="e6_push")
    xg.push(xgk, [ptk])

    for gi in range(NTG):
        lo = gi * TG
        for c in range(NCH):
            bq = pb[1]
            sch.add("tensor", MMG_(pb, bq[:, :], [
                (lambda t, d=d, c=c: wq[:, d, 128 * c:128 * (c + 1)],
                 lambda t, d=d, lo=lo: s2b[:, d, lo:lo + TG]) for d in range(NCH)]),
                reads=(K(wq), K(s2b)), writes=(K(bq),), tag=f"qmm{gi}_{c}")
            sch.add("scalar", lambda s, bq=bq: s.activation(
                out=scr[0][:, :], in_=bq[:, :], func=AF.Square),
                reads=(K(bq),), writes=(K(scr[0]),))
            sch.add("tensor", lambda t: t.matmul(
                pb[2][0:2, :], cst.hpat2[:, :], scr[0][:, :], start=True, stop=True),
                reads=(K(cst.hpat2), K(scr[0])), writes=(K(pb[2]),))
            sch.add("vector", lambda v: v.tensor_scalar(
                out=rq2[0:2, :], in0=pb[2][0:2, :], scalar1=1.0 / 64, scalar2=1e-6,
                op0=ALU.mult, op1=ALU.add),
                reads=(K(pb[2]),), writes=(K(rq2),))
            sch.add("scalar", lambda s: s.activation(out=rq2[0:2, :], in_=rq2[0:2, :],
                                                     func=AF.Ln),
                    reads=(K(rq2),), writes=(K(rq2),))
            sch.add("scalar", lambda s: s.activation(out=rq2[0:2, :], in_=rq2[0:2, :],
                                                     func=AF.Exp, scale=-0.5),
                    reads=(K(rq2),), writes=(K(rq2),))
            sch.add("tensor", lambda t: t.matmul(
                pb[3][:, :], cst.bpat2[:, :], rq2[0:2, :], start=True, stop=True),
                reads=(K(cst.bpat2), K(rq2)), writes=(K(pb[3]),))
            sch.add("scalar", lambda s: s.copy(scr[1][:, :], pb[3][:, :]),
                    reads=(K(pb[3]),), writes=(K(scr[1]),))
            sch.add("vector", lambda v, bq=bq: v.scalar_tensor_tensor(
                out=scr[2][:, :], in0=bq[:, :], scalar=qg[:, 0:1],
                in1=scr[1][:, :], op0=ALU.mult, op1=ALU.mult),
                reads=(K(bq), K(scr[1]), K(qg)), writes=(K(scr[2]),), tag=f"qn{gi}_{c}")
            for par in range(2):
                H = 2 * c + par
                cc, r = H // 3, H % 3
                sch.add("vector", lambda v, par=par, cc=cc, r=r, lo=lo: v.tensor_copy(
                    qsc[32 * r:32 * r + 32, T * cc + lo:T * cc + lo + TG],
                    scr[2][64 * par:64 * par + 32, :]),
                    reads=(K(scr[2]),), writes=(K(qsc),), tag=f"qx{gi}_{c}_{par}")
            sch.add("scalar", lambda s: s.activation(
                out=scr[3][:, :], in_=scr[2][:, :], func=AF.Square),
                reads=(K(scr[2]),), writes=(K(scr[3]),))
            sch.add("tensor", lambda t: t.matmul(
                pb[2][0:2, :], qsel[:, :], scr[3][:, :], start=True, stop=True),
                reads=(K(qsel), K(scr[3])), writes=(K(pb[2]),))
            sch.add("vector", lambda v: v.reduce_max(rq2[0:2, 0:1], pb[2][0:2, :],
                                                     axis=mybir.AxisListType.X),
                    reads=(K(pb[2]),), writes=(K(rq2),))
            sch.add("vector", lambda v, c=c: v.tensor_max(
                qtab[0:2, c:c + 1], qtab[0:2, c:c + 1], rq2[0:2, 0:1]),
                reads=(K(qtab), K(rq2)), writes=(K(qtab),), tag=f"qmax{gi}_{c}")

    pk = xg.poll(xgk)
    for blkk in range(2):
        sch.add("gpsimd", lambda g, blkk=blkk: g.indirect_dma_start(
            out=lat[:, T * blkk:T * (blkk + 1)], out_offset=None,
            in_=XL.ap(), in_offset=IOA(ap=xrows[:, 1 + blkk:2 + blkk], axis=0)),
            reads=("XL", K(xrows)), writes=(K(lat),), dma=True,
            extra_deps=[pk], tag=f"e6_pull{blkk}")

    for kg8 in range(8):
        ko = kg8 * TG
        for cc in range(6):
            bk = pb[1]
            sch.add("tensor", lambda t, cc=cc, ko=ko, bk=bk: t.matmul(
                bk[:, :], wk[:, 128 * cc:128 * (cc + 1)], lat[:, ko:ko + TG],
                start=True, stop=True),
                reads=(K(wk), K(lat)), writes=(K(bk),), tag=f"kmm{kg8}_{cc}")
            sch.add("scalar", lambda s, bk=bk: s.activation(
                out=scr[0][:, :], in_=bk[:, :], func=AF.Square),
                reads=(K(bk),), writes=(K(scr[0]),))
            sch.add("tensor", lambda t: t.matmul(
                pb[2][0:3, :], hpat3[:, :], scr[0][:, :], start=True, stop=True),
                reads=(K(hpat3), K(scr[0])), writes=(K(pb[2]),))
            sch.add("vector", lambda v: v.tensor_scalar(
                out=rq2[0:3, :], in0=pb[2][0:3, :], scalar1=1.0 / 32, scalar2=1e-6,
                op0=ALU.mult, op1=ALU.add),
                reads=(K(pb[2]),), writes=(K(rq2),))
            sch.add("scalar", lambda s: s.activation(out=rq2[0:3, :], in_=rq2[0:3, :],
                                                     func=AF.Ln),
                    reads=(K(rq2),), writes=(K(rq2),))
            sch.add("scalar", lambda s: s.activation(out=rq2[0:3, :], in_=rq2[0:3, :],
                                                     func=AF.Exp, scale=-0.5),
                    reads=(K(rq2),), writes=(K(rq2),))
            sch.add("tensor", lambda t: t.matmul(
                pb[3][:, :], bpat3[:, :], rq2[0:3, :], start=True, stop=True),
                reads=(K(bpat3), K(rq2)), writes=(K(pb[3]),))
            sch.add("scalar", lambda s: s.copy(scr[1][:, :], pb[3][:, :]),
                    reads=(K(pb[3]),), writes=(K(scr[1]),))
            sch.add("vector", lambda v, cc=cc, ko=ko, bk=bk: v.scalar_tensor_tensor(
                out=kn[:, TL * cc + ko:TL * cc + ko + TG], in0=bk[:, :], scalar=kg[:, 0:1],
                in1=scr[1][:, :], op0=ALU.mult, op1=ALU.mult),
                reads=(K(bk), K(scr[1]), K(kg)), writes=(K(kn),), tag=f"knw{kg8}_{cc}")
            sch.add("scalar", lambda s, cc=cc, ko=ko: s.activation(
                out=scr[3][:, :], in_=kn[:, TL * cc + ko:TL * cc + ko + TG], func=AF.Square),
                reads=(K(kn),), writes=(K(scr[3]),))
            sch.add("tensor", lambda t: t.matmul(
                pb[2][0:3, :], hpat3[:, :], scr[3][:, :], start=True, stop=True),
                reads=(K(hpat3), K(scr[3])), writes=(K(pb[2]),))
            sch.add("vector", lambda v: v.reduce_max(r21[0:3, 0:1], pb[2][0:3, :],
                                                     axis=mybir.AxisListType.X),
                    reads=(K(pb[2]),), writes=(K(r21),))
            sch.add("vector", lambda v, cc=cc: v.tensor_max(
                ktab[0:3, cc:cc + 1], ktab[0:3, cc:cc + 1], r21[0:3, 0:1]),
                reads=(K(ktab), K(r21)), writes=(K(ktab),), tag=f"kmax{kg8}_{cc}")
        for lt in range(4):
            ks = ko + 128 * lt
            bv = pb[5]
            sch.add("tensor", lambda t, ks=ks, bv=bv: t.matmul(
                bv[:, :], lat[:, ks:ks + 128], wv[:, :], start=True, stop=True),
                reads=(K(lat), K(wv)), writes=(K(bv),), tag=f"vmm{kg8}_{lt}")
            ti = 4 * kg8 + lt
            sch.add("vector", lambda v, bv=bv, ti=ti: v.tensor_copy(
                vbuf[ti].rearrange("p (h n) -> p h n", n=33)[:, :, 0:32], bv[:, :]),
                reads=(K(bv),), writes=(K(vbuf[ti]),))
            sch.add("vector", lambda g, ti=ti: g.tensor_scalar_mul(
                vbuf[ti][:, :], vbuf[ti][:, :], vkill[:, ti:ti + 1]),
                reads=(K(vbuf[ti]), K(vkill)), writes=(K(vbuf[ti]),), tag=f"vk{ti}")

    for tb, rr in ((qtab, 2), (ktab, 3)):
        sch.add("scalar", lambda s, tb=tb, rr=rr: s.activation(out=tb[0:rr, :], in_=tb[0:rr, :], func=AF.Ln),
                reads=(K(tb),), writes=(K(tb),))
        sch.add("scalar", lambda s, tb=tb, rr=rr: s.activation(out=tb[0:rr, :], in_=tb[0:rr, :], func=AF.Exp, scale=0.5),
                reads=(K(tb),), writes=(K(tb),))
    for r in range(2):
        sch.add("tensor", lambda t, r=r: t.matmul(
            pb[2][0:1, 0:8], cst.e01[:, r:r + 1], qtab[0:2, :], start=True, stop=True),
            reads=(K(cst.e01), K(qtab)), writes=(K(pb[2]),))
        sch.add("vector", lambda v, r=r: v.tensor_copy(
            qrow.rearrange("a (c two) -> a c two", two=2)[0:1, :, r], pb[2][0:1, 0:8]),
            reads=(K(pb[2]),), writes=(K(qrow),))
    e3 = ctx.sb("p4e3", [3, 3], F32)
    sch.add("sync", lambda q: q.dma_start(out=e3[:, :], in_=io["e3x"]),
            writes=(K(e3),), dma=True)
    for r in range(3):
        sch.add("tensor", lambda t, r=r: t.matmul(
            pb[2][0:1, 0:6], e3[:, r:r + 1], ktab[0:3, :], start=True, stop=True),
            reads=(K(e3), K(ktab)), writes=(K(pb[2]),))
        sch.add("vector", lambda v, r=r: v.tensor_copy(
            krow.rearrange("a (c three) -> a c three", three=3)[0:1, :, r],
            pb[2][0:1, 0:6]),
            reads=(K(pb[2]),), writes=(K(krow),))
    sch.add("vector", lambda v: v.tensor_mul(qrow[0:1, :], qrow[0:1, :], krow[0:1, 0:16]),
            reads=(K(qrow), K(krow)), writes=(K(qrow),))
    sch.add("vector", lambda v: v.tensor_scalar(
        out=qrow[0:1, :], in0=qrow[0:1, :], scalar1=-1.0 / 8, scalar2=45.0,
        op0=ALU.mult, op1=ALU.add),
        reads=(K(qrow),), writes=(K(qrow),))
    sch.add("tensor", lambda t: t.matmul(
        pb[3][:, 0:16], cst.ones_f[0:1, :], qrow[0:1, :], start=True, stop=True),
        reads=(K(cst.ones_f), K(qrow)), writes=(K(pb[3]),))
    sch.add("scalar", lambda s: s.copy(bcol[:, :], pb[3][:, 0:16]),
            reads=(K(pb[3]),), writes=(K(bcol),))

    ctx.reset(m1)
    sch.set_fence()
    pT = [ctx.sb(f"p4pT{i}", [128, TG], BF16) for i in range(8)]
    ofm = ctx.sb("p4ofm", [128, 6 * TG], BF16)
    recs = ctx.sb("p4recs", [1, TG], F32)
    rbc = ctx.sb("p4rbc", [128, TG], F32)
    so = [ctx.sb(f"p4so{i}", [128, TG], F32) for i in range(2)]

    for qc in range(NTG):
        nkt = 16 + 4 * (qc + 1)
        sch.add("gpsimd", lambda g: g.memset(ofm[:, :], 0.0), writes=(K(ofm),))
        for H in range(NH1):
            cc, r = H // 3, H % 3
            bO = pb[4 + H % 2]
            for kt in range(nkt):
                bS = pb[kt % 4]
                def em_s(t, H=H, cc=cc, r=r, kt=kt, qc=qc, bS=bS):
                    return t.matmul(
                        bS[:, :],
                        kn[32 * r:32 * r + 32, TL * cc + 128 * kt:TL * cc + 128 * kt + 128],
                        qsc[32 * r:32 * r + 32, T * cc + TG * qc:T * cc + TG * (qc + 1)],
                        start=True, stop=True)
                sch.add("tensor", em_s, reads=(K(kn), K(qsc)), writes=(K(bS),),
                        tag=f"S{qc}_{H}_{kt}")
                def em_e(s, H=H, kt=kt, bS=bS):
                    return s.activation(
                        out=pT[kt % 8][:, :], in_=bS[:, :], func=AF.Exp,
                        bias=bcol[:, H:H + 1], scale=0.125)
                sch.add("scalar", em_e, reads=(K(bS), K(bcol)),
                        writes=(K(pT[kt % 8]),), tag=f"E{qc}_{H}_{kt}")
                if 4 * qc + 3 < kt < 16:
                    def em_k(g, kt=kt, qc=qc):
                        return g.tensor_scalar_mul(
                            pT[kt % 8][:, :], pT[kt % 8][:, :],
                            mkill[:, 32 * qc + kt:32 * qc + kt + 1])
                    sch.add("vector", em_k, reads=(K(pT[kt % 8]), K(mkill)),
                            writes=(K(pT[kt % 8]),), tag=f"MK{qc}_{H}_{kt}")
                sub0 = 128 * kt - TG * qc
                sub1 = 128 * kt - T - TG * qc
                for mi, sub in ((0, sub0), (1, sub1)):
                    if 0 <= sub < TG:
                        def em_m(g, mi=mi, sub=sub, kt=kt):
                            return g.tensor_mul(
                                pT[kt % 8][:, 0:sub + 128], pT[kt % 8][:, 0:sub + 128],
                                maskd[mi][:, 512 - sub:640])
                        sch.add("vector", em_m,
                                reads=(K(pT[kt % 8]), K(maskd[mi])),
                                writes=(K(pT[kt % 8]),), tag=f"M{qc}_{H}_{kt}_{mi}")
                def em_pv(t, H=H, kt=kt, bO=bO, nkt=nkt):
                    return t.matmul(
                        bO[0:33, :], vbuf[kt][:, 33 * H:33 * H + 33],
                        pT[kt % 8][:, :], start=(kt == 0), stop=(kt == nkt - 1))
                sch.add("tensor", em_pv, reads=(K(vbuf[kt]), K(pT[kt % 8])),
                        writes=(K(bO),), tag=f"PV{qc}_{H}_{kt}")
            sch.add("vector", lambda v, bO=bO: v.reciprocal(recs[0:1, :], bO[32:33, :]),
                    reads=(K(bO),), writes=(K(recs),))
            sch.add("tensor", lambda t: t.matmul(
                pb[6][0:32, :], cst.ones_f[0:1, 0:32], recs[0:1, :], start=True, stop=True),
                reads=(K(cst.ones_f), K(recs)), writes=(K(pb[6]),))
            sch.add("scalar", lambda s: s.copy(rbc[0:32, :], pb[6][0:32, :]),
                    reads=(K(pb[6]),), writes=(K(rbc),))
            sch.add("vector", lambda v, cc=cc, r=r, bO=bO: v.tensor_mul(
                ofm[32 * r:32 * r + 32, TG * cc:TG * (cc + 1)], bO[0:32, :], rbc[0:32, :]),
                reads=(K(bO), K(rbc)), writes=(K(ofm),), tag=f"OF{qc}_{H}")
        for oc in range(NCH):
            bY = pb[6 + oc % 2]
            sch.add("tensor", MMG_(pb, bY[:, :], [
                (lambda t, cc=cc2, oc=oc: wo6[:, cc, 128 * oc:128 * (oc + 1)],
                 lambda t, cc=cc2: ofm[:, TG * cc:TG * (cc + 1)]) for cc2 in range(6)]),
                reads=(K(wo6), K(ofm)), writes=(K(bY),), tag=f"WO{qc}_{oc}")
            sv = so[oc % 2]
            sch.add("vector", lambda v, bY=bY, sv=sv: v.tensor_copy(sv[:, :], bY[:, :]),
                    reads=(K(bY),), writes=(K(sv),))
            sch.add("sync", lambda q, oc=oc, qc=qc, sv=sv: q.dma_start(
                out=io["aout"][128 * oc:128 * (oc + 1), TG * qc:TG * (qc + 1)], in_=sv[:, :]),
                reads=(K(sv),), writes=(f"aout{qc}_{oc}",), dma=True,
                tag=f"ast{qc}_{oc}")
    ctx.reset(m0)
    sch.set_fence()


def build_p4():
    nc = bass.Bass("TRN2", target_bir_lowering=False)
    ctx = Ctx(nc)
    io = p4_decls(nc)
    io["aout"] = nc.dram_tensor("aout", [D, T], F32, kind="ExternalOutput").ap()
    io["s3"] = nc.dram_tensor("s3", [D, T], F32, kind="ExternalOutput").ap()
    io["cst_hpat2"] = nc.dram_tensor("cst_hpat2", [128, 2], F32, kind="ExternalInput").ap()
    io["cst_bpat2"] = nc.dram_tensor("cst_bpat2", [2, 128], F32, kind="ExternalInput").ap()
    io["cst_e01"] = nc.dram_tensor("cst_e01", [2, 2], F32, kind="ExternalInput").ap()
    io.update(Exchange.input_decls(nc, 1))

    class MiniCst:
        pass
    with nc.Block() as blk:
        sch = Sched(ctx)
        xg = Exchange(ctx, sch, 1)
        xg.emit_load(io)
        cst = MiniCst()
        cst.ones_f = ctx.sb("c_ones_f", [128, 128], F32)
        cst.ones_b = ctx.sb("c_ones_b", [128, 128], BF16)
        cst.hpat2 = ctx.sb("hpat2", [128, 2], F32)
        cst.bpat2 = ctx.sb("bpat2", [2, 128], F32)
        cst.e01 = ctx.sb("e01", [2, 2], F32)
        sch.add("gpsimd", lambda g: g.memset(cst.ones_f[:, :], 1.0), writes=(K(cst.ones_f),))
        sch.add("gpsimd", lambda g: g.memset(cst.ones_b[:, :], 1.0), writes=(K(cst.ones_b),))
        for nm, buf in [("cst_hpat2", cst.hpat2), ("cst_bpat2", cst.bpat2),
                        ("cst_e01", cst.e01)]:
            sch.add("sync", lambda q, nm=nm, buf=buf: q.dma_start(out=buf[:, :], in_=io[nm]),
                    writes=(K(buf),), dma=True)
        emit_p4(ctx, sch, xg, cst, io, 0)
        emit_postln(ctx, sch, cst, io["s2i"], io["aout"], io["s3"], "pl4")
        sch.flush(blk)
    ctx.st.close()
    return nc


def run_p4(inputs, s2_percore):
    nc = build_p4()
    in_maps = []
    for c in range(NCORE):
        d = p4_host(inputs, c)
        d["s2i"] = s2_percore[c]
        d["cst_hpat2"] = l0ffn_const_host(c)["cst_hpat2"]
        d["cst_bpat2"] = l0ffn_const_host(c)["cst_bpat2"]
        d["cst_e01"] = l0ffn_const_host(c)["cst_e01"]
        in_maps.append(d)
    seed = int(np.random.default_rng().integers(2 ** 31))
    for c in range(NCORE):
        pair_rng = np.random.default_rng(seed + (c // 2))
        xgi = Exchange.__new__(Exchange)
        xgi.n_flags = 1
        in_maps[c].update(Exchange.host_inputs(xgi, c % 2, pair_rng))
    res = run_bass_kernel_spmd(nc, in_maps, core_ids=list(range(NCORE)))
    return res.results


def p5_decls(nc, with_stream=True):
    d = {}
    if with_stream:
        d["s3i"] = nc.dram_tensor("s3i", [D, T], F32, kind="ExternalInput").ap()
    d["Wr5T"] = nc.dram_tensor("Wr5T", [D, D], F32, kind="ExternalInput").ap()
    d["Wk5T"] = nc.dram_tensor("Wk5T", [D, D], F32, kind="ExternalInput").ap()
    d["Wv5T"] = nc.dram_tensor("Wv5T", [D, D], F32, kind="ExternalInput").ap()
    d["murc"] = nc.dram_tensor("murc", [128, 8], F32, kind="ExternalInput").ap()
    d["murc1"] = nc.dram_tensor("murc1", [128, 8], F32, kind="ExternalInput").ap()
    d["mukc"] = nc.dram_tensor("mukc", [128, 8], F32, kind="ExternalInput").ap()
    d["mukc1"] = nc.dram_tensor("mukc1", [128, 8], F32, kind="ExternalInput").ap()
    d["xrows5"] = nc.dram_tensor("xrows5", [128, 2], U32, kind="ExternalInput").ap()
    d["hscale5"] = nc.dram_tensor("hscale5", [128, 1], F32, kind="ExternalInput").ap()
    return d


def p5_host(inputs, core):
    h = core % 2
    pack8 = lambda v: np.asarray(v, np.float32).reshape(8, 128).T.copy()
    mur = np.asarray(inputs["L1_ffn_mu_r"], np.float32)
    muk = np.asarray(inputs["L1_ffn_mu_k"], np.float32)
    p = np.arange(128, dtype=np.uint32)
    return {
        "Wr5T": np.ascontiguousarray(np.asarray(inputs["L1_ffn_W_r"], np.float32).T),
        "Wk5T": np.ascontiguousarray(np.asarray(inputs["L1_ffn_W_k"], np.float32).T),
        "Wv5T": np.ascontiguousarray(np.asarray(inputs["L1_ffn_W_v"], np.float32).T),
        "murc": pack8(mur), "murc1": pack8(1.0 - mur),
        "mukc": pack8(muk), "mukc1": pack8(1.0 - muk),
        "xrows5": np.stack([128 * h + p, 128 * (1 - h) + p], axis=1).astype(np.uint32),
        "hscale5": np.full((128, 1), float(h), np.float32),
    }


def emit_p5(ctx, sch, cst, xg, io, xgk):
    nc = ctx.nc
    pb = ctx.pbanks
    EXT = 2049
    X5 = nc.dram_tensor("X5share", [256, 8], BF16, addr_space="Shared")

    m0 = ctx.mark()
    s3e = ctx.sb("p5s3e", [128, 8 * EXT], BF16)
    wr = ctx.sb("p5wr", [128, 8, D], BF16)
    wk_ = ctx.sb("p5wk", [128, 8, D], BF16)
    wv_ = ctx.sb("p5wv", [128, 8, D], BF16)
    murc = ctx.sb("p5mur", [128, 8], F32)
    murc1 = ctx.sb("p5mur1", [128, 8], F32)
    mukc = ctx.sb("p5muk", [128, 8], F32)
    mukc1 = ctx.sb("p5muk1", [128, 8], F32)
    xrows = ctx.sb("p5xr", [128, 2], U32)
    hscale = ctx.sb("p5hs", [128, 1], F32)
    stg = ctx.sb("p5stg", [128, 8], BF16)
    stgp = ctx.sb("p5stgp", [128, 8], BF16)
    for nm, buf in [("Wr5T", wr), ("Wk5T", wk_), ("Wv5T", wv_)]:
        sch.add("gpsimd", lambda g, nm=nm, buf=buf: g.dma_start(
            out=buf[:, :, :], in_=io[nm].rearrange("(c p) n -> p c n", p=128)),
            writes=(K(buf),), dma=True, tag=nm)
    for nm, buf in [("murc", murc), ("murc1", murc1), ("mukc", mukc),
                    ("mukc1", mukc1), ("xrows5", xrows), ("hscale5", hscale)]:
        sch.add("sync", lambda q, nm=nm, buf=buf: q.dma_start(out=buf[:, :], in_=io[nm]),
                writes=(K(buf),), dma=True, tag=nm)
    for c in range(NCH):
        sch.add("gpsimd", lambda g, c=c: g.dma_start(
            out=s3e[:, EXT * c + 1:EXT * c + 1 + T],
            in_=io["s3i"][128 * c:128 * (c + 1), :]),
            writes=(K(s3e),), dma=True, tag=f"s3ld{c}")
    sch.add("vector", lambda v: v.tensor_copy(
        stgp[:, :], s3e.rearrange("p (c n) -> p c n", n=EXT)[:, :, 2048]),
        reads=(K(s3e),), writes=(K(stgp),))
    ptk = sch.add("gpsimd", lambda g: g.indirect_dma_start(
        out=X5.ap(), out_offset=IOA(ap=xrows[:, 0:1], axis=0),
        in_=stgp[:, :], in_offset=None),
        reads=(K(stgp), K(xrows)), writes=("X5",), dma=True, tag="e7_push")
    xg.push(xgk, [ptk])
    pk = xg.poll(xgk)
    sch.add("gpsimd", lambda g: g.indirect_dma_start(
        out=stg[:, :], out_offset=None,
        in_=X5.ap(), in_offset=IOA(ap=xrows[:, 1:2], axis=0)),
        reads=("X5", K(xrows)), writes=(K(stg),), dma=True, extra_deps=[pk],
        tag="e7_pull")
    sch.add("gpsimd", lambda g: g.tensor_scalar_mul(stg[:, :], stg[:, :], hscale[:, 0:1]),
            reads=(K(stg), K(hscale)), writes=(K(stg),))
    sch.add("vector", lambda v: v.tensor_copy(
        s3e.rearrange("p (c n) -> p c n", n=EXT)[:, :, 0], stg[:, :]),
        reads=(K(stg),), writes=(K(s3e),))

    m1 = ctx.mark()
    xrb = ctx.sb("p5xrb", [128, 8 * TG], BF16)
    xkb = ctx.sb("p5xkb", [128, 8 * TG], BF16)
    rb = ctx.sb("p5rb", [128, 8 * TG], BF16)
    kab = ctx.sb("p5kab", [128, 8 * TG], BF16)
    scr = [ctx.sb(f"p5scr{i}", [128, TG], F32) for i in range(4)]
    scb = ctx.sb("p5scb", [128, TG], BF16)
    fo = [ctx.sb(f"p5fo{i}", [128, TG], F32) for i in range(2)]

    for gi in range(NTG):
        lo = gi * TG
        for c in range(NCH):
            xs_ = s3e[:, EXT * c + 1 + lo:EXT * c + 1 + lo + TG]
            xp_ = s3e[:, EXT * c + lo:EXT * c + lo + TG]
            sch.add("vector", lambda v, c=c, xp_=xp_: v.tensor_scalar_mul(
                scb[:, :], xp_, murc1[:, c:c + 1]),
                reads=(K(s3e), K(murc1)), writes=(K(scb),))
            sch.add("vector", lambda v, c=c, xs_=xs_: v.scalar_tensor_tensor(
                out=xrb[:, TG * c:TG * (c + 1)], in0=xs_, scalar=murc[:, c:c + 1],
                in1=scb[:, :], op0=ALU.mult, op1=ALU.add),
                reads=(K(s3e), K(scb), K(murc)), writes=(K(xrb),), tag=f"xr{gi}_{c}")
            sch.add("vector", lambda v, c=c, xp_=xp_: v.tensor_scalar_mul(
                scb[:, :], xp_, mukc1[:, c:c + 1]),
                reads=(K(s3e), K(mukc1)), writes=(K(scb),))
            sch.add("vector", lambda v, c=c, xs_=xs_: v.scalar_tensor_tensor(
                out=xkb[:, TG * c:TG * (c + 1)], in0=xs_, scalar=mukc[:, c:c + 1],
                in1=scb[:, :], op0=ALU.mult, op1=ALU.add),
                reads=(K(s3e), K(scb), K(mukc)), writes=(K(xkb),), tag=f"xk{gi}_{c}")
        for oc in range(NCH):
            brr = pb[oc % 2]
            sch.add("tensor", MMG_(pb, brr[:, :], [
                (lambda t, c=c, oc=oc: wr[:, c, 128 * oc:128 * (oc + 1)],
                 lambda t, c=c: xrb[:, TG * c:TG * (c + 1)]) for c in range(NCH)]),
                reads=(K(wr), K(xrb)), writes=(K(brr),), tag=f"rmm{gi}_{oc}")
            sch.add("scalar", lambda s, oc=oc, brr=brr: s.activation(
                out=rb[:, TG * oc:TG * (oc + 1)], in_=brr[:, :], func=AF.Sigmoid),
                reads=(K(brr),), writes=(K(rb),), tag=f"sig{gi}_{oc}")
            bkk = pb[2 + oc % 2]
            sch.add("tensor", MMG_(pb, bkk[:, :], [
                (lambda t, c=c, oc=oc: wk_[:, c, 128 * oc:128 * (oc + 1)],
                 lambda t, c=c: xkb[:, TG * c:TG * (c + 1)]) for c in range(NCH)]),
                reads=(K(wk_), K(xkb)), writes=(K(bkk),), tag=f"kmm{gi}_{oc}")
            sch.add("vector", lambda v, bkk=bkk: v.tensor_scalar_max(
                scr[0][:, :], bkk[:, :], 0.0),
                reads=(K(bkk),), writes=(K(scr[0]),))
            sch.add("scalar", lambda s, oc=oc: s.activation(
                out=kab[:, TG * oc:TG * (oc + 1)], in_=scr[0][:, :], func=AF.Square),
                reads=(K(scr[0]),), writes=(K(kab),), tag=f"ksq{gi}_{oc}")
        for oc in range(NCH):
            bvv = pb[4 + oc % 2]
            sch.add("tensor", MMG_(pb, bvv[:, :], [
                (lambda t, c=c, oc=oc: wv_[:, c, 128 * oc:128 * (oc + 1)],
                 lambda t, c=c: kab[:, TG * c:TG * (c + 1)]) for c in range(NCH)]),
                reads=(K(wv_), K(kab)), writes=(K(bvv),), tag=f"vmm{gi}_{oc}")
            fv = fo[oc % 2]
            sch.add("vector", lambda v, oc=oc, bvv=bvv, fv=fv: v.tensor_mul(
                fv[:, :], bvv[:, :], rb[:, TG * oc:TG * (oc + 1)]),
                reads=(K(bvv), K(rb)), writes=(K(fv),), tag=f"fo{gi}_{oc}")
            sch.add("sync", lambda q, oc=oc, lo=lo, fv=fv: q.dma_start(
                out=io["fout"][128 * oc:128 * (oc + 1), lo:lo + TG], in_=fv[:, :]),
                reads=(K(fv),), writes=(f"fout{gi}_{oc}",), dma=True,
                tag=f"fst{gi}_{oc}")
    ctx.reset(m0)
    sch.set_fence()


def build_p5():
    nc = bass.Bass("TRN2", target_bir_lowering=False)
    ctx = Ctx(nc)
    io = p5_decls(nc)
    io["fout"] = nc.dram_tensor("fout", [D, T], F32, kind="ExternalOutput").ap()
    io["out"] = nc.dram_tensor("out_final", [D, T], F32, kind="ExternalOutput").ap()
    io.update(Exchange.input_decls(nc, 1))

    class MiniCst:
        pass
    with nc.Block() as blk:
        sch = Sched(ctx)
        xg = Exchange(ctx, sch, 1)
        xg.emit_load(io)
        cst = MiniCst()
        cst.ones_f = ctx.sb("c_ones_f", [128, 128], F32)
        cst.ones_b = ctx.sb("c_ones_b", [128, 128], BF16)
        sch.add("gpsimd", lambda g: g.memset(cst.ones_f[:, :], 1.0), writes=(K(cst.ones_f),))
        sch.add("gpsimd", lambda g: g.memset(cst.ones_b[:, :], 1.0), writes=(K(cst.ones_b),))
        emit_p5(ctx, sch, cst, xg, io, 0)
        emit_postln(ctx, sch, cst, io["s3i"], io["fout"], io["out"], "pl5")
        sch.flush(blk)
    ctx.st.close()
    return nc


def run_p5(inputs, s3_percore):
    nc = build_p5()
    in_maps = []
    for c in range(NCORE):
        d = p5_host(inputs, c)
        d["s3i"] = s3_percore[c]
        in_maps.append(d)
    seed = int(np.random.default_rng().integers(2 ** 31))
    for c in range(NCORE):
        pair_rng = np.random.default_rng(seed + (c // 2))
        xgi = Exchange.__new__(Exchange)
        xgi.n_flags = 1
        in_maps[c].update(Exchange.host_inputs(xgi, c % 2, pair_rng))
    res = run_bass_kernel_spmd(nc, in_maps, core_ids=list(range(NCORE)))
    return res.results


_prog_cache = {}


def _get_prog(name, builder):
    if name not in _prog_cache:
        _prog_cache[name] = builder()
    return _prog_cache[name]


def _pair_xg_inputs(n_flags, seed_base):
    maps = []
    for c in range(NCORE):
        pair_rng = np.random.default_rng(seed_base + (c // 2))
        xgi = Exchange.__new__(Exchange)
        xgi.n_flags = n_flags
        maps.append(Exchange.host_inputs(xgi, c % 2, pair_rng))
    return maps


def kernel(**inputs):
    rng = np.random.default_rng()

    xpc = []
    for c in range(NCORE):
        b, h = c // 2, c % 2
        xpc.append(to_fm(np.asarray(inputs["L0_attn_x"][b, 2048 * h:2048 * (h + 1), :],
                                    np.float32)))

    nc1 = _get_prog("p1", build_p1)
    in_maps = [p1_host_inputs(c, inputs, rng) for c in range(NCORE)]
    xgmaps = _pair_xg_inputs(1, int(rng.integers(2 ** 31)))
    for c in range(NCORE):
        in_maps[c].update(xgmaps[c])
    res = run_bass_kernel_spmd(nc1, in_maps, core_ids=list(range(NCORE)))
    s1 = [np.ascontiguousarray(r["out_s1"]) for r in res.results]

    nc2 = _get_prog("p2", lambda: build_p2(0))
    in_maps = []
    for c in range(NCORE):
        dd = {"yin": s1[c], "xrows": xrows_host(c)}
        dd.update(l0ffn_const_host(c))
        dd.update(l0ffn_layer_host(inputs, 0))
        in_maps.append(dd)
    xgmaps = _pair_xg_inputs(3, int(rng.integers(2 ** 31)))
    for c in range(NCORE):
        in_maps[c].update(xgmaps[c])
    res = run_bass_kernel_spmd(nc2, in_maps, core_ids=list(range(NCORE)))
    y1 = [np.ascontiguousarray(r["yout"]) for r in res.results]

    nc3 = _get_prog("p3", lambda: build_p3(1))
    in_maps = []
    for c in range(NCORE):
        dd = {"yin": y1[c], "s1": s1[c], "xrows": xrows_host(c)}
        dd.update(l0ffn_const_host(c))
        dd.update(l0ffn_layer_host(inputs, 1))
        in_maps.append(dd)
    xgmaps = _pair_xg_inputs(3, int(rng.integers(2 ** 31)))
    for c in range(NCORE):
        in_maps[c].update(xgmaps[c])
    res = run_bass_kernel_spmd(nc3, in_maps, core_ids=list(range(NCORE)))
    s2 = [np.ascontiguousarray(r["s2"]) for r in res.results]

    nc4 = _get_prog("p4", build_p4)
    in_maps = []
    for c in range(NCORE):
        dd = p4_host(inputs, c)
        dd["s2i"] = s2[c]
        ch = l0ffn_const_host(c)
        dd["cst_hpat2"] = ch["cst_hpat2"]
        dd["cst_bpat2"] = ch["cst_bpat2"]
        dd["cst_e01"] = ch["cst_e01"]
        in_maps.append(dd)
    xgmaps = _pair_xg_inputs(1, int(rng.integers(2 ** 31)))
    for c in range(NCORE):
        in_maps[c].update(xgmaps[c])
    res = run_bass_kernel_spmd(nc4, in_maps, core_ids=list(range(NCORE)))
    s3 = [np.ascontiguousarray(r["s3"]) for r in res.results]

    nc5 = _get_prog("p5", build_p5)
    in_maps = []
    for c in range(NCORE):
        dd = p5_host(inputs, c)
        dd["s3i"] = s3[c]
        in_maps.append(dd)
    xgmaps = _pair_xg_inputs(1, int(rng.integers(2 ** 31)))
    for c in range(NCORE):
        in_maps[c].update(xgmaps[c])
    res = run_bass_kernel_spmd(nc5, in_maps, core_ids=list(range(NCORE)))

    out = np.zeros((B, L, D), np.float32)
    for c in range(NCORE):
        b, h = c // 2, c % 2
        out[b, 2048 * h:2048 * (h + 1), :] = res.results[c]["out_final"].T
    return out


def gcst_decls(nc):
    return {
        "cst_hpat2": nc.dram_tensor("cst_hpat2", [128, 2], F32, kind="ExternalInput").ap(),
        "cst_bpat2": nc.dram_tensor("cst_bpat2", [2, 128], F32, kind="ExternalInput").ap(),
        "cst_e01": nc.dram_tensor("cst_e01", [2, 2], F32, kind="ExternalInput").ap(),
    }


def gcst_host(core):
    ch = l0ffn_const_host(core)
    return {k: ch[k] for k in ("cst_hpat2", "cst_bpat2", "cst_e01")}


def build_fused():
    nc = bass.Bass("TRN2", target_bir_lowering=False)
    ctx = Ctx(nc)
    io = {}
    io["x"] = nc.dram_tensor("x", [D, T], F32, kind="ExternalInput").ap()
    io["WqT"] = nc.dram_tensor("WqT", [D, D], F32, kind="ExternalInput").ap()
    io["WkT"] = nc.dram_tensor("WkT", [D, D], F32, kind="ExternalInput").ap()
    io["WvT"] = nc.dram_tensor("WvT", [D, D], F32, kind="ExternalInput").ap()
    io["proj"] = nc.dram_tensor("proj", [D, 256], F32, kind="ExternalInput").ap()
    io["kvrows_my"] = nc.dram_tensor("kvrows_my", [128, 2], U32, kind="ExternalInput").ap()
    io["kvrows_peer"] = nc.dram_tensor("kvrows_peer", [128, 2], U32, kind="ExternalInput").ap()
    io["xrows"] = nc.dram_tensor("xrows", [128, 22], U32, kind="ExternalInput").ap()
    io.update(l0ffn_const_decls(nc))
    io.update(gcst_decls(nc))
    io.update(l0ffn_layer_decls(nc, 0))
    io.update(l0ffn_layer_decls(nc, 1))
    io.update(p4_decls(nc, with_stream=False))
    io.update(p5_decls(nc, with_stream=False))
    io.update(Exchange.input_decls(nc, 9))

    kind = "ExternalOutput" if FUSED_DEBUG else "Internal"
    s1 = nc.dram_tensor("st_s1", [D, T], F32, kind=kind).ap()
    ymid0 = nc.dram_tensor("st_ymid0", [D, T], F32, kind=kind).ap()
    y1 = nc.dram_tensor("st_y1", [D, T], F32, kind=kind).ap()
    ymid1 = nc.dram_tensor("st_ymid1", [D, T], F32, kind=kind).ap()
    y2 = nc.dram_tensor("st_y2", [D, T], F32, kind=kind).ap()
    s2 = nc.dram_tensor("st_s2", [D, T], F32, kind=kind).ap()
    aout = nc.dram_tensor("st_aout", [D, T], F32, kind=kind).ap()
    s3 = nc.dram_tensor("st_s3", [D, T], F32, kind=kind).ap()
    fout = nc.dram_tensor("st_fout", [D, T], F32, kind=kind).ap()
    out = nc.dram_tensor("out_final", [D, T], F32, kind="ExternalOutput").ap()

    with nc.Block() as blk:
        sch = Sched(ctx)
        xg = Exchange(ctx, sch, 9)
        xg.emit_load(io)
        gc = GlobalCst(ctx, sch, io)
        io1 = dict(io)
        io1["out"] = s1
        emit_p1(ctx, sch, xg, io1, xgk0=0)
        m_c = ctx.mark()
        cst = L0Consts(ctx, sch, io, gc)
        io2 = dict(io)
        io2.update({"yin": s1, "ymid": ymid0, "yout": y1})
        if FUSED_STAGES >= 2:
            emit_l0ffn_layer(ctx, sch, xg, cst, io2, 0, 1)
        io3 = dict(io)
        io3.update({"yin": y1, "ymid": ymid1, "yout": y2})
        if FUSED_STAGES >= 3:
            emit_l0ffn_layer(ctx, sch, xg, cst, io3, 1, 4)
            emit_postln(ctx, sch, gc, s1, y2, s2, "pl3")
        ctx.reset(m_c)
        sch.set_fence()
        if FUSED_STAGES >= 4:
            io4 = dict(io)
            io4.update({"s2i": s2, "aout": aout})
            emit_p4(ctx, sch, xg, gc, io4, 7)
            emit_postln(ctx, sch, gc, s2, aout, s3, "pl4")
        if FUSED_STAGES >= 5:
            io5 = dict(io)
            io5.update({"s3i": s3, "fout": fout})
            emit_p5(ctx, sch, gc, xg, io5, 8)
            emit_postln(ctx, sch, gc, s3, fout, out, "pl5")
        sch.flush(blk)
    ctx.st.close()
    return nc


def fused_host_inputs(core, inputs, rng):
    d = p1_host_inputs(core, inputs, rng)
    d["xrows"] = xrows_host(core)
    d.update(l0ffn_const_host(core))
    d.update(l0ffn_layer_host(inputs, 0))
    d.update(l0ffn_layer_host(inputs, 1))
    p4d = p4_host(inputs, core)
    p4d.pop("s2i")
    d.update(p4d)
    d.update(p5_host(inputs, core))
    return d


def kernel(**inputs):
    rng = np.random.default_rng()
    nc = _get_prog("fused", build_fused)
    in_maps = [fused_host_inputs(c, inputs, rng) for c in range(NCORE)]
    xgmaps = _pair_xg_inputs(9, int(rng.integers(2 ** 31)))
    for c in range(NCORE):
        in_maps[c].update(xgmaps[c])
    res = run_bass_kernel_spmd(nc, in_maps, core_ids=list(range(NCORE)))
    out = np.zeros((B, L, D), np.float32)
    for c in range(NCORE):
        b, h = c // 2, c % 2
        out[b, 2048 * h:2048 * (h + 1), :] = res.results[c]["out_final"].T
    return out
